# version 16
# baseline (speedup 1.0000x reference)
import itertools
import numpy as np
from contextlib import ExitStack

import concourse.bass as bass
import concourse.mybir as mybir
from concourse.bass_utils import run_bass_kernel_spmd

F32 = mybir.dt.float32
BF16 = mybir.dt.bfloat16
U32 = mybir.dt.uint32
AF = mybir.ActivationFunctionType
ALU = mybir.AluOpType
AX = mybir.AxisListType

D = 1024
SEQ = 2048
NDC = 8
NJT = 16
IQ = 256
NIQ = SEQ // IQ
EPS = 1e-5
DFF = 2048
ISQ8 = 0.125
ISQ32 = float(1.0 / np.sqrt(32.0))
WIN = 512
NW = SEQ // WIN
ARENA_KB = 194

_TAPS = []
_MAXPHASE = 5
_HPN = 8


class Prog:
    ENGINES = ("Y", "G", "T", "V", "S")

    def __init__(self, nc):
        self.nc = nc
        self.steps = {e: [] for e in self.ENGINES}
        self.semc = {}
        self.chain_sems = {}
        self.es = ExitStack()
        self._uid = 0

    def _nm(self, p):
        self._uid += 1
        return "%s%d" % (p, self._uid)

    def sem(self, name=None):
        h = self.es.enter_context(self.nc.semaphore(name=name or self._nm("sm")))
        self.semc[h] = 0
        return h

    def sb(self, name, shape, dt):
        return self.es.enter_context(self.nc.sbuf_tensor(name, shape, dt))

    def psum(self, name, shape):
        return self.es.enter_context(self.nc.psum_tensor(name, shape, F32))

    def init_dma_pool(self, n=12):
        self.dma_pools = {q: [self.sem("dma%s%d" % (q, i)) for i in range(n if q != "S" else 4)]
                          for q in ("Y", "G", "S")}
        self.dma_prev = {q: [None] * len(self.dma_pools[q]) for q in ("Y", "G", "S")}
        self.dma_rr = {q: 0 for q in ("Y", "G", "S")}
        self.dma_pool = self.dma_pools["Y"] + self.dma_pools["G"] + self.dma_pools["S"]

    def all_dma_toks(self):
        return [self.tok(s) for s in self.dma_pool]

    def on(self, eng, fn, waits=(), inc=None, dma=False):
        w = []
        if fn is not None and eng in self.chain_sems:
            cs = self.chain_sems[eng]
            w.append((cs, self.semc[cs]))
            if inc is None and not dma:
                inc = (cs, 1)
        def _flat(ts):
            for t in ts:
                if t is None:
                    continue
                if isinstance(t, list):
                    _flat(t)
                else:
                    w.append(t)
        _flat(waits)
        if dma:
            pool = self.dma_pools[eng]
            i = self.dma_rr[eng] % len(pool)
            self.dma_rr[eng] += 1
            if self.dma_prev[eng][i] is not None:
                w.append(self.dma_prev[eng][i])
            inc = (pool[i], 16)
        self.steps[eng].append((w, fn, inc))
        if inc is not None:
            s, k = inc
            self.semc[s] += k
            tok = (s, self.semc[s])
            if dma:
                self.dma_prev[eng][i] = tok
            return tok
        return None

    def tok(self, sem):
        return (sem, self.semc[sem])

    def barrier(self, name, dma_sems=()):
        bar = self.sem("bar_" + name)
        dw = [self.tok(s) for s in dma_sems]
        for e in self.ENGINES:
            self.on(e, lambda eng: eng.nop(), waits=dw, inc=(bar, 1))
        t = self.tok(bar)
        for e in self.ENGINES:
            self.on(e, None, waits=[t])
        return t

    def emit(self):
        nc = self.nc
        with nc.Block() as block:
            def run(eng_obj, name):
                seen = {}
                for w, fn, inc in self.steps[name]:
                    for s, v in w:
                        if seen.get(s, -1) < v:
                            eng_obj.wait_ge(s, v)
                            seen[s] = v
                    if fn is not None:
                        ins = fn(eng_obj)
                        if inc is not None:
                            if ins is None:
                                eng_obj.nop().then_inc(*inc)
                            else:
                                ins.then_inc(*inc)
                    elif inc is not None:
                        eng_obj.nop().then_inc(*inc)

            @block.sync
            def _(e):
                run(e, "Y")

            @block.gpsimd
            def _(e):
                run(e, "G")

            @block.tensor
            def _(e):
                run(e, "T")

            @block.vector
            def _(e):
                run(e, "V")

            @block.scalar
            def _(e):
                run(e, "S")


def build_program(taps=(), maxphase=5, hpn=8):
    nc = bass.Bass("TRN2", target_bir_lowering=False)
    p = Prog(nc)

    def din(name, shape, dt=F32):
        return nc.dram_tensor(name, shape, dt, kind="ExternalInput").ap()

    xT = din("xT", [D, SEQ])
    wqT = din("wqT", [D, D]); wkT = din("wkT", [D, D])
    wvT = din("wvT", [D, 512]); wqvT = din("wqvT", [D, 512])
    wfT = din("wfT", [D, D])
    wr1T = din("wr1T", [D, DFF]); wr2T = din("wr2T", [DFF, D])
    mwqT = din("mwqT", [D, D]); mwkT = din("mwkT", [D, D]); mwvT = din("mwvT", [D, D])
    mwo = din("mwo", [256, D]); mwr = din("mwr", [D, 16])
    vecs = din("vecs", [128, 128])
    consts = din("consts", [128, 2048])
    bh3 = din("bh3", [16, 192])
    out_hbm = nc.dram_tensor("out", [D, SEQ], F32, kind="ExternalOutput").ap()

    taps = set(taps)

    p.init_dma_pool(16)
    sG = p.sem("sG")
    sT = p.sem("sT"); sV = p.sem("sV"); sS = p.sem("sS")
    p.chain_sems = {"V": sV, "S": sS, "G": sG}

    psS = p.psum("psS", [128, 1024])
    pb = [psS[:, 0:512], psS[:, 512:1024]] + [p.psum("pb%d" % i, [128, 512]) for i in range(2, 8)]
    bankfree = {i: None for i in range(8)}

    arena = p.sb("arena", [128, ARENA_KB * 512], BF16)

    _audit = []

    def carve(kb, shape, dt):
        cols = shape[-1] * (2 if dt == F32 else 1)
        start = int(kb * 512)
        _audit.append((float(kb), float(kb) + cols / 512.0, tuple(shape), "f32" if dt == F32 else "bf16"))
        assert start + cols <= ARENA_KB * 512, ("carve OOB", kb, shape)
        v = arena[0:shape[0], start:start + cols]
        if dt == F32:
            v = v.bitcast(F32)
        return v
    build_program.audit = _audit

    vecs_t = p.sb("vecs_t", [128, 128], F32)
    onesPP = p.sb("onesPP", [128, 128], F32)
    ones_col_f = p.sb("ones_col_f", [128, 1], F32)
    ones_col_b = p.sb("ones_col_b", [128, 1], BF16)
    E8b = p.sb("E8b", [72, 512], BF16)
    ISTb = p.sb("ISTb", [128, 64], BF16)
    Eselb = p.sb("Eselb", [16, 128], BF16)
    Ecol4b = p.sb("Ecol4b", [4, 256], BF16)
    RowSelb = p.sb("RowSelb", [16, 1024], BF16)
    M64b = p.sb("M64b", [128, 256], BF16)
    E64f = p.sb("E64f", [4, 256], F32)
    w1b = p.sb("w1b", [128, 64], BF16)
    w2b = p.sb("w2b", [128, 32], BF16)
    negM = p.sb("negM", [128, 8], F32)
    nrm = p.sb("nrm", [1, 96], F32)
    bqs = p.sb("bqs", [128, 8], F32)
    bqvs = p.sb("bqvs", [128, 4], F32)
    scr_row = p.sb("scr_row", [33, 512], F32)
    ones8b = p.sb("ones8b", [72, 8], BF16)
    H4T = p.sb("H4T", [4, 16], BF16)
    H4E = p.sb("H4E", [16, 256], BF16)
    gat = p.sb("gat", [16, 512], F32)
    gatb = p.sb("gatb", [4, 16], BF16)
    mi8 = p.sb("mi8", [1, 8], U32)
    gexp = p.sb("gexp", [128, 2], F32)
    gexpQ = p.sb("gexpQ", [128, 2], F32)
    bcpp = p.sb("bcpp", [128, 6], F32)
    bh3t = p.sb("bh3t", [16, 192], F32)
    mnegM = p.sb("mnegM", [128, 2], F32)
    wrf = p.sb("wrf", [128, 16 * NDC], F32)

    _sizes = [("bq", 8), ("bk", 8), ("bv", 4), ("bqv", 4), ("bf", 8), ("br1", 16),
              ("br2", 8), ("g1", 8), ("b1", 8), ("g2", 8), ("b2", 8), ("bs", 1),
              ("mbr", 1), ("mbo", 8)]
    VC = {}
    _off = 0
    for _n, _k in _sizes:
        VC[_n] = (_off, _k)
        _off += _k

    def vcol(name, i=0):
        o, n = VC[name]
        assert i < n
        return vecs_t[:, o + i:o + i + 1]

    def tap(name, ap, waits):
        if name not in taps:
            return
        if not isinstance(ap, bass.AP):
            ap = ap[:, :]
        t = nc.dram_tensor("tap_" + name, [ap.shape[0], ap.shape[1]], ap.dtype,
                           kind="ExternalOutput").ap()
        p.on("G", lambda e, t=t, ap=ap: e.dma_start(out=t[:, :], in_=ap), waits=waits, dma=True)

    cstage = carve(0, [128, 2048], F32)
    t_vec = p.on("Y", lambda e: e.dma_start(out=vecs_t[:], in_=vecs[:, :]), dma=True)
    t_cst = p.on("Y", lambda e: e.dma_start(out=cstage, in_=consts[:, :]), dma=True)
    t_bh3 = p.on("Y", lambda e: e.dma_start(out=bh3t[:], in_=bh3[:, :]), dma=True)

    def vc_cast(dst, src):
        return p.on("V", lambda e, d=dst, s=src: e.tensor_copy(d, s), waits=[t_cst], inc=(sV, 1))

    vc_cast(onesPP[:, :], cstage[:, 0:128])
    vc_cast(ones_col_f[:, :], cstage[:, 0:1])
    vc_cast(ones_col_b[:, :], cstage[:, 0:1])
    vc_cast(ones8b[64:72, :], cstage[0:8, 0:8])
    vc_cast(ISTb[:, :], cstage[:, 128:192])
    vc_cast(M64b[:, :], cstage[:, 192:448])
    vc_cast(w1b[:, :], cstage[:, 448:512])
    vc_cast(w2b[:, :], cstage[:, 512:544])
    vc_cast(E8b[64:72, :], cstage[0:8, 544:1056])
    vc_cast(E8b[32:40, :], cstage[0:8, 544:1056])
    vc_cast(Eselb[:, :], cstage[0:16, 1056:1184])
    vc_cast(Ecol4b[:, :], cstage[0:4, 1184:1440])
    vc_cast(E64f[:, :], cstage[0:4, 1440:1696])
    tCONST = vc_cast(RowSelb[:, :], cstage[32:48, 544:1568])
    tb1 = p.on("G", lambda e: e.tensor_scalar_mul(bqs[:], vecs_t[:, VC["bq"][0]:VC["bq"][0] + 8], ISQ8),
               waits=[t_vec], inc=(sG, 1))
    tb2 = p.on("G", lambda e: e.tensor_scalar_mul(bqvs[:], vecs_t[:, VC["bqv"][0]:VC["bqv"][0] + 4], ISQ32),
               waits=[t_vec], inc=(sG, 1))

    xf = carve(8, [128, NDC * SEQ], F32)
    x2s = carve(72, [128, SEQ], F32)
    xs1 = carve(80, [128, SEQ], F32)
    xs2 = carve(88, [128, SEQ], F32)
    tmpr = carve(96, [1, SEQ], F32)
    tmp2r = carve(104, [1, SEQ], F32)
    r2row = carve(112, [1, SEQ], F32)
    r2rep = carve(120, [128, SEQ], F32)
    m_rep = carve(128, [128, SEQ], F32)
    r_rep = carve(136, [128, SEQ], F32)
    xn2b = carve(144, [128, NDC * SEQ], BF16)
    statrows = carve(186, [65, SEQ], F32)

    tk_x = [p.on("Y", lambda e, c=c: e.dma_start(out=xf[:, c * SEQ:(c + 1) * SEQ],
                                                 in_=xT[c * 128:(c + 1) * 128, :]), dma=True)
            for c in range(NDC)]

    prev_sq = None
    for c in range(NDC):
        t_sq = p.on("V", lambda e, c=c: e.tensor_tensor(
            x2s, xf[:, c * SEQ:(c + 1) * SEQ], xf[:, c * SEQ:(c + 1) * SEQ], ALU.mult),
            waits=[tk_x[c], prev_sq], inc=(sV, 1))
        for cc in range(4):
            p.on("T", lambda e, c=c, cc=cc: e.matmul(
                pb[cc][0:1, 0:512], ones_col_f[:, :],
                xf[:, c * SEQ + cc * 512: c * SEQ + (cc + 1) * 512],
                start=(c == 0), stop=(c == NDC - 1)), waits=[tk_x[c], tCONST])
            prev_sq = p.on("T", lambda e, c=c, cc=cc: e.matmul(
                pb[4 + cc][0:1, 0:512], ones_col_f[:, :], x2s[:, cc * 512:(cc + 1) * 512],
                start=(c == 0), stop=(c == NDC - 1)), waits=[t_sq], inc=(sT, 1))
    tSTAT = prev_sq

    tv = None
    for cc in range(4):
        p.on("V", lambda e, cc=cc: e.tensor_scalar_mul(
            statrows[0:1, cc * 512:(cc + 1) * 512], pb[cc][0:1, 0:512], 1.0 / D),
            waits=[tSTAT], inc=(sV, 1))
        tv = p.on("V", lambda e, cc=cc: e.tensor_scalar_mul(
            tmpr[0:1, cc * 512:(cc + 1) * 512], pb[4 + cc][0:1, 0:512], 1.0 / D),
            waits=[tSTAT], inc=(sV, 1))
    tv = p.on("V", lambda e: e.tensor_tensor(tmp2r, statrows[0:1, :], statrows[0:1, :], ALU.mult),
              waits=[tv], inc=(sV, 1))
    tv = p.on("V", lambda e: e.tensor_tensor(tmpr, tmpr, tmp2r, ALU.subtract), inc=(sV, 1))
    tv = p.on("V", lambda e: e.tensor_scalar_add(tmpr, tmpr, EPS), inc=(sV, 1))
    p.on("S", lambda e: e.activation(tmpr, tmpr, AF.Ln), waits=[tv], inc=(sS, 1))
    tsq_ = p.on("S", lambda e: e.activation(statrows[64:65, :], tmpr, AF.Exp, scale=-0.5),
                inc=(sS, 1))
    tv = p.on("V", lambda e: e.tensor_tensor(tmpr, statrows[64:65, :], statrows[64:65, :], ALU.mult),
              waits=[tsq_], inc=(sV, 1))
    tv = p.on("V", lambda e: e.tensor_scalar(tmpr, tmpr, -EPS, 1.0 + EPS, ALU.mult, ALU.add), inc=(sV, 1))
    p.on("S", lambda e: e.activation(tmpr, tmpr, AF.Ln), waits=[tv], inc=(sS, 1))
    tr2 = p.on("S", lambda e: e.activation(r2row, tmpr, AF.Exp, scale=-0.5), inc=(sS, 1))

    def replicate(dst, src_row, base_part, waits, bank, width=SEQ):
        last = None
        ncc = width // 512
        for cc in range(ncc):
            w = list(waits) + [bankfree[bank], last]
            tm = p.on("T", lambda e, cc=cc, dst=dst, src_row=src_row, base_part=base_part, bank=bank: e.matmul(
                pb[bank][:, 0:512], onesPP[base_part:base_part + 1, 0:128],
                src_row[0:1, cc * 512:(cc + 1) * 512], start=True, stop=True),
                waits=w, inc=(sT, 1))
            last = p.on("V", lambda e, cc=cc, dst=dst, bank=bank: e.tensor_copy(
                dst[:, cc * 512:(cc + 1) * 512], pb[bank][:, 0:512]), waits=[tm], inc=(sV, 1))
        bankfree[bank] = last
        return last

    tmr = replicate(m_rep, statrows[0:1, :], 0, [tSTAT, p.tok(sV)], 0)
    trr = replicate(r_rep, statrows[64:65, :], 64, [tsq_], 1)
    tr2r = replicate(r2rep, r2row, 0, [tr2], 2)

    tXN2 = None
    for c in range(NDC):
        ta = p.on("V", lambda e, c=c: e.tensor_tensor(
            xs1, xf[:, c * SEQ:(c + 1) * SEQ], m_rep, ALU.subtract),
            waits=[tk_x[c], tmr, trr, tr2r, tXN2], inc=(sV, 1))
        p.on("V", lambda e, c=c: e.tensor_tensor(xs2, xs1, r_rep, ALU.mult), inc=(sV, 1))
        tc2 = p.on("V", lambda e, c=c: e.tensor_tensor(xs1, xs2, r2rep, ALU.mult), inc=(sV, 1))
        tXN2 = p.on("S", lambda e, c=c: e.activation(
            xn2b[:, c * SEQ:(c + 1) * SEQ], xs1, AF.Identity,
            bias=vcol("b1", c), scale=vcol("g1", c)), waits=[tc2, t_vec], inc=(sS, 1))
    tap("ISTb", ISTb, [tCONST])
    tap("E8b", E8b[64:72, :], [tCONST])
    tap("xf0", xf[:, 0:2048], [tk_x[0], tXN2])
    tap("ones_col", ones_col_f[:, :], [tCONST])
    tap("xn2b0", xn2b[:, 0:2048], [tXN2])
    tap("statm", statrows[0:1, :], [tsq_])
    tap("statr", statrows[64:65, :], [tsq_])

    p.barrier("p1", dma_sems=p.dma_pool)

    Qb = carve(0, [128, NDC * SEQ], BF16)
    Kb = carve(32, [128, NDC * SEQ], BF16)
    Vtm = carve(64, [128, NJT * 512], BF16)
    wstage = carve(80, [128, 2048], F32)
    wbf = carve(88, [128, 8192], BF16)
    wbf2 = carve(104, [128, 8192], BF16)
    qvb = carve(120, [128, 4 * SEQ], BF16)
    bvrow = carve(136, [1, 512], F32)
    bvrep = carve(138, [128, 512], F32)
    sTt = carve(140, [16, SEQ], BF16)

    wslot_free = [None, None]

    def stream_w(hbm, nchunks, width, dst, stage, pre=None, eng="G"):
        toks = []
        sem = {"G": sG, "S": sS, "V": sV}[eng]
        for c in range(nchunks):
            s = c % 2
            w = [wslot_free[s]]
            if pre is not None and c < 2:
                w.append(pre)
            tl = p.on("Y", lambda e, c=c, s=s, hbm=hbm, width=width, stage=stage: e.dma_start(
                out=stage[:, s * 1024:s * 1024 + width], in_=hbm[c * 128:(c + 1) * 128, :]),
                waits=w, dma=True)
            if eng == "S":
                tc_ = p.on("S", lambda e, c=c, s=s, dst=dst, width=width, stage=stage: e.copy(
                    dst[:, c * width:(c + 1) * width], stage[:, s * 1024:s * 1024 + width]),
                    waits=[tl], inc=(sem, 1))
            else:
                tc_ = p.on(eng, lambda e, c=c, s=s, dst=dst, width=width, stage=stage: e.tensor_copy(
                    dst[:, c * width:(c + 1) * width], stage[:, s * 1024:s * 1024 + width]),
                    waits=[tl], inc=(sem, 1))
            wslot_free[s] = tc_
            toks.append(tc_)
        return toks

    def fm_proj(wtoks, wtile, wwidth, nmt, evac):
        out_toks = []
        bank_i = 0
        for mt in range(nmt):
            for cc in range(4):
                bki = 4 + (bank_i % 4)
                bk = pb[bki]
                last = None
                for dc in range(NDC):
                    w = [wtoks[dc], tXN2]
                    if dc == 0:
                        w.append(bankfree[bki])
                    last = p.on("T", lambda e, mt=mt, cc=cc, dc=dc, bk=bk, wtile=wtile, wwidth=wwidth: e.matmul(
                        bk[:, 0:512],
                        wtile[:, dc * wwidth + mt * 128: dc * wwidth + (mt + 1) * 128],
                        xn2b[:, dc * SEQ + cc * 512: dc * SEQ + (cc + 1) * 512],
                        start=(dc == 0), stop=(dc == NDC - 1)),
                        waits=w, inc=(sT, 1) if dc == NDC - 1 else None)
                te = evac(mt, cc, bk, last)
                bankfree[bki] = te
                out_toks.append(te)
                bank_i += 1
        return out_toks

    wq_toks = stream_w(wqT, 8, 1024, wbf, wstage, eng="S")
    tQ = fm_proj(wq_toks, wbf, 1024, 8, lambda mt, cc, bk, tk: p.on(
        "V", lambda e, mt=mt, cc=cc, bk=bk: e.tensor_scalar(
            Qb[:, mt * SEQ + cc * 512: mt * SEQ + (cc + 1) * 512], bk[:, 0:512],
            ISQ8, bqs[:, mt:mt + 1], ALU.mult, ALU.add),
        waits=[tk, tb1], inc=(sV, 1)))

    wk_toks = stream_w(wkT, 8, 1024, wbf2, wstage)
    tK = fm_proj(wk_toks, wbf2, 1024, 8, lambda mt, cc, bk, tk: p.on(
        "V", lambda e, mt=mt, cc=cc, bk=bk: e.tensor_scalar(
            Kb[:, mt * SEQ + cc * 512: mt * SEQ + (cc + 1) * 512], bk[:, 0:512],
            vcol("bk", mt), None, ALU.add),
        waits=[tk], inc=(sV, 1)))

    tbv = [p.on("G", lambda e, c=c: e.dma_start(
        out=bvrow[0:1, c * 128:(c + 1) * 128],
        in_=vecs_t[:, VC["bv"][0] + c:VC["bv"][0] + c + 1]), waits=[t_vec], dma=True)
        for c in range(4)]
    tm_ = p.on("T", lambda e: e.matmul(pb[0][:, 0:512], onesPP[0:1, 0:128], bvrow[0:1, :],
                                       start=True, stop=True), waits=tbv + [bankfree[0]], inc=(sT, 1))
    tbvrep = p.on("V", lambda e: e.tensor_copy(bvrep, pb[0][:, 0:512]), waits=[tm_], inc=(sV, 1))
    bankfree[0] = tbvrep

    wv_toks = stream_w(wvT, 8, 512, wbf, wstage, pre=tQ[-1], eng="S")
    tV = None
    for jt in range(NJT):
        bki = 2 + (jt % 2)
        bk = pb[bki]
        last = None
        for dc in range(NDC):
            w = [wv_toks[dc], tXN2]
            if dc == 0:
                w.append(bankfree[bki])
            last = p.on("T", lambda e, jt=jt, dc=dc, bk=bk: e.matmul(
                bk[:, 0:512],
                xn2b[:, dc * SEQ + jt * 128: dc * SEQ + (jt + 1) * 128],
                wbf[:, dc * 512:(dc + 1) * 512],
                start=(dc == 0), stop=(dc == NDC - 1)),
                waits=w, inc=(sT, 1) if dc == NDC - 1 else None)
        tV = p.on("V", lambda e, jt=jt, bk=bk: e.tensor_tensor(
            Vtm[:, jt * 512:(jt + 1) * 512], bk[:, 0:512], bvrep, ALU.add),
            waits=[last, tbvrep], inc=(sV, 1))
        bankfree[bki] = tV

    wqv_toks = stream_w(wqvT, 8, 512, wbf2, wstage, pre=tK[-1])
    tQV = fm_proj(wqv_toks, wbf2, 512, 4, lambda mt, cc, bk, tk: p.on(
        "V", lambda e, mt=mt, cc=cc, bk=bk: e.tensor_scalar(
            qvb[:, mt * SEQ + cc * 512: mt * SEQ + (cc + 1) * 512], bk[:, 0:512],
            ISQ32, bqvs[:, mt:mt + 1], ALU.mult, ALU.add),
        waits=[tk, tb2], inc=(sV, 1)))

    tS_T = None
    for cc2 in range(4):
        last = None
        for mt in range(4):
            w = [tQV[-1], tCONST]
            if mt == 0:
                w += [bankfree[1], tS_T]
            last = p.on("T", lambda e, cc2=cc2, mt=mt: e.matmul(
                pb[1][0:16, 0:512], w1b[:, mt * 16:(mt + 1) * 16],
                qvb[:, mt * SEQ + cc2 * 512: mt * SEQ + (cc2 + 1) * 512],
                start=(mt == 0), stop=(mt == 3)),
                waits=w, inc=(sT, 1) if mt == 3 else None)
        tS_T = p.on("V", lambda e, cc2=cc2: e.tensor_scalar(
            sTt[:, cc2 * 512:(cc2 + 1) * 512], pb[1][0:16, 0:512],
            vcol("bs", 0)[0:16, :], None, ALU.add), waits=[last], inc=(sV, 1))
        bankfree[1] = tS_T

    sqs = wbf[:, 0:2048]
    tprev = None
    for src, base, tdep in ((Qb, 0, tQ[-1]), (Kb, 32, tK[-1])):
        for hp in range(8):
            tq = p.on("V", lambda e, hp=hp, src=src: e.tensor_tensor(
                sqs, src[:, hp * SEQ:(hp + 1) * SEQ], src[:, hp * SEQ:(hp + 1) * SEQ], ALU.mult),
                waits=[tdep, tV, tprev], inc=(sV, 1))
            for cc in range(4):
                tm2 = p.on("T", lambda e, cc=cc: e.matmul(
                    pb[0][0:1, 0:512], ones_col_b[:, :], sqs[:, cc * 512:(cc + 1) * 512],
                    start=True, stop=True), waits=[tq, bankfree[0]], inc=(sT, 1))
                tprev = p.on("V", lambda e, hp=hp, cc=cc, base=base: e.reduce_max(
                    nrm[0:1, base + hp * 4 + cc: base + hp * 4 + cc + 1],
                    pb[0][0:1, 0:512], axis=AX.X), waits=[tm2], inc=(sV, 1))
                bankfree[0] = tprev
    tlastn = tprev
    for hp in range(8):
        p.on("V", lambda e, hp=hp: e.reduce_max(
            nrm[0:1, 64 + hp:64 + hp + 1], nrm[0:1, hp * 4:hp * 4 + 4], axis=AX.X),
            waits=[tlastn], inc=(sV, 1))
        tprev = p.on("V", lambda e, hp=hp: e.reduce_max(
            nrm[0:1, 72 + hp:72 + hp + 1], nrm[0:1, 32 + hp * 4:32 + hp * 4 + 4], axis=AX.X),
            inc=(sV, 1))
    t_ = p.on("V", lambda e: e.tensor_tensor(nrm[0:1, 80:88], nrm[0:1, 64:72], nrm[0:1, 72:80], ALU.mult),
              waits=[tprev], inc=(sV, 1))
    t_ = p.on("S", lambda e: e.activation(nrm[0:1, 88:96], nrm[0:1, 80:88], AF.Sqrt),
              waits=[t_], inc=(sS, 1))
    t_ = p.on("V", lambda e: e.tensor_scalar_mul(nrm[0:1, 80:88], nrm[0:1, 88:96], -1.05),
              waits=[t_], inc=(sV, 1))
    tm2 = p.on("T", lambda e: e.matmul(pb[0][:, 0:8], onesPP[0:1, 0:128], nrm[0:1, 80:88],
                                       start=True, stop=True), waits=[t_, bankfree[0]], inc=(sT, 1))
    tNEGM = p.on("V", lambda e: e.tensor_copy(negM[:], pb[0][:, 0:8]), waits=[tm2], inc=(sV, 1))
    bankfree[0] = tNEGM

    tap("Qb0", Qb[:, 0:2048], [tQ[-1]])
    tap("Kb0", Kb[:, 0:2048], [tK[-1]])
    tap("Vtm0", Vtm[:, 0:512], [tV])
    tap("sTt", sTt, [tS_T])
    tap("negM", negM, [tNEGM])

    p.barrier("p2", dma_sems=p.dma_pool)

    if maxphase <= 2:
        p.on("G", None, waits=p.all_dma_toks())
        p.emit()
        return nc

    EAB = carve(80, [128, NJT * 512], BF16)
    outTA = carve(96, [128, 4 * SEQ], BF16)
    outTB = carve(112, [128, 4 * SEQ], BF16)
    prodb = [carve(132, [128, 512], BF16), carve(133, [128, 512], BF16)]
    def carveR(row, kb, shape, dt):
        cols = shape[-1] * (2 if dt == F32 else 1)
        start = int(kb * 512)
        v = arena[row:row + shape[0], start:start + cols]
        return v.bitcast(F32) if dt == F32 else v
    _rowp = {0: 64, 1: 32}
    cS_m = {(hh, par): carveR(_rowp[par], 134 + 2 * hh, [8, 512], F32)
            for hh in (0, 1) for par in (0, 1)}
    ivd_m = {(0, 0): carveR(64, 144, [8, 512], F32), (1, 0): carveR(64, 146, [8, 512], F32),
             (0, 1): carveR(32, 128, [8, 512], F32), (1, 1): carveR(32, 130, [8, 512], F32)}
    ivr_m = {(0, 0): carveR(64, 148, [8, 512], F32), (1, 0): carveR(64, 150, [8, 512], F32),
             (0, 1): carveR(32, 140, [8, 512], F32), (1, 1): carveR(32, 142, [8, 512], F32)}
    expcB_m = {(0, 0): carveR(64, 184, [8, 512], BF16), (1, 0): carveR(64, 185, [8, 512], BF16),
               (0, 1): carveR(64, 130, [8, 512], BF16), (1, 1): carveR(64, 131, [8, 512], BF16)}
    compfs_m = {(0, 0): carveR(64, 138, [8, 512], BF16), (1, 0): carveR(64, 139, [8, 512], BF16),
                (0, 1): carveR(32, 138, [8, 512], BF16), (1, 1): carveR(32, 139, [8, 512], BF16),
                (0, 2): carveR(64, 128, [8, 512], BF16), (1, 2): carveR(64, 129, [8, 512], BF16),
                (0, 3): carveR(32, 184, [8, 512], BF16), (1, 3): carveR(32, 185, [8, 512], BF16)}
    denAB = carve(144, [33, SEQ], F32)
    finall = carve(152, [128, NDC * SEQ], BF16)

    wfb = carve(0, [128, 8192], BF16)
    wstage4 = carve(16, [128, 2048], F32)
    wf_toks = None

    tok_pv = {}
    tok_sband = {}
    tFIN = None
    gate_out = {}
    gate_den = {}
    epi = {"bkr": {7: None, 0: None, 3: None, 1: None}, "bkfinr": {4: None, 5: None},
           "crepfree": {0: [None, None], 1: [None, None]},
           "prodfree": {0: None, 1: None}, "foldfree": {0: None, 1: None}, "tfin": None}

    def emit_epilogue(hp, evdc, denv):
        epi["foldfree"][0] = denv
        epi["foldfree"][1] = evdc[0]

        def comp_chain(hh, cc2, bank):
            head = 2 * hp + hh
            par = cc2 % 2
            outT = outTA if hh == 0 else outTB
            sl = slice(cc2 * 512, (cc2 + 1) * 512)
            ivd = ivd_m[(hh, par)]
            ivr = ivr_m[(hh, par)]
            cS = cS_m[(hh, par)]
            expcB = expcB_m[(hh, par)]
            compfs = compfs_m[(hh, cc2)]
            st = {}
            bk = pb[bank]

            def s0():
                last = None
                for dc in range(4):
                    w = [evdc[dc]]
                    if dc == 0:
                        w.append(epi["bkr"][bank])
                        if bank == 3:
                            w.append(evdc[1])
                    last = p.on("T", lambda e, dc=dc: e.matmul(
                        bk[0:8, 0:512], w2b[:, dc * 8:(dc + 1) * 8],
                        outT[:, dc * SEQ + sl.start: dc * SEQ + sl.stop],
                        start=(dc == 0), stop=(dc == 3)),
                        waits=w, inc=(sT, 1) if dc == 3 else None)
                st["c"] = last

            def s1():
                st["t1"] = p.on("V", lambda e: e.tensor_copy(cS, bk[0:8, 0:512]),
                                waits=[st["c"]], inc=(sV, 1))

            def s2():
                st["tsr"] = p.on("T", lambda e: e.matmul(
                    bk[32:40, 0:512], Eselb[:, head * 8:(head + 1) * 8], sTt[:, sl],
                    start=True, stop=True), waits=[tS_T, st["t1"]], inc=(sT, 1))
                st["tir"] = p.on("T", lambda e: e.matmul(
                    bk[64:72, 0:512], onesPP[32 * hh:32 * hh + 1, 0:8],
                    denAB[32 * hh:32 * hh + 1, sl],
                    start=True, stop=True), waits=[denv], inc=(sT, 1))
                gate_den.setdefault(cc2, [None, None])[hh] = st["tir"]

            def s2b():
                p.on("S", lambda e: e.activation(ivd, bk[64:72, 0:512], AF.Ln),
                     waits=[st["tir"]], inc=(sS, 1))
                st["trec"] = p.on("S", lambda e: e.activation(ivd, ivd, AF.Exp, scale=-1.0),
                                  inc=(sS, 1))

            def s3():
                p.on("V", lambda e: e.tensor_tensor(cS, cS, ivd, ALU.mult),
                     waits=[st["trec"]], inc=(sV, 1))
                st["t3"] = p.on("V", lambda e: e.tensor_tensor(
                    cS, cS, bk[32:40, 0:512], ALU.add), waits=[st["tsr"]], inc=(sV, 1))

            def s4():
                st["t4"] = p.on("S", lambda e: e.activation(cS, cS, AF.Exp),
                                waits=[st["t3"]], inc=(sS, 1))
                st["t4b"] = p.on("S", lambda e: e.copy(expcB, cS),
                                 waits=[st["t4"]], inc=(sS, 1))

            def s5():
                st["t5"] = p.on("T", lambda e: e.matmul(
                    bk[32:40, 0:512], ones8b[64:72, :], expcB,
                    start=True, stop=True), waits=[st["t4b"], st["t3"]], inc=(sT, 1))

            def s6():
                p.on("S", lambda e: e.activation(ivr, bk[32:40, 0:512], AF.Ln),
                     waits=[st["t5"]], inc=(sS, 1))
                st["trecr"] = p.on("S", lambda e: e.activation(ivr, ivr, AF.Exp, scale=-1.0),
                                   inc=(sS, 1))

            def s7():
                st["tc1"] = p.on("G", lambda e: e.tensor_tensor(
                    cS, cS, ivd, ALU.mult),
                    waits=[st["t4b"]], inc=(sG, 1))

            def s8():
                st["tcf"] = p.on("G", lambda e: e.tensor_tensor(
                    compfs[:, :], cS, ivr, ALU.mult),
                    waits=[st["trecr"], st["tc1"]], inc=(sG, 1))
                epi["bkr"][bank] = st["tcf"]

            return [s0, s1, s2, s2b, s3, s4, s5, s6, s7, s8], st

        def fin_chain(hh, cc2, st):
            outT = outTA if hh == 0 else outTB
            sl = slice(cc2 * 512, (cc2 + 1) * 512)
            compfs = compfs_m[(hh, cc2)]
            e8r = _rowp[cc2 % 2]
            rbank = 4 + hh
            steps = []
            tprs = []

            def mk_rep_prod(dc):
                def f():
                    w = [st["tcf"], epi["bkfinr"][rbank]]
                    trep = p.on("T", lambda e: e.matmul(
                        pb[rbank][:, 0:512], E8b[e8r:e8r + 8, dc * 128:(dc + 1) * 128], compfs[:, :],
                        start=True, stop=True), waits=w, inc=(sT, 1))
                    pbuf = prodb[hh]
                    tpr = p.on("V", lambda e: e.tensor_tensor(
                        pbuf[:, :], outT[:, dc * SEQ + sl.start: dc * SEQ + sl.stop],
                        pb[rbank][:, 0:512], ALU.mult),
                        waits=[trep, epi["prodfree"][hh]], inc=(sV, 1))
                    tprs.append(tpr)
                    epi["bkfinr"][rbank] = tpr
                    fb = 6 if hh == 0 else 2
                    w2 = [tpr]
                    if dc == 0:
                        w2.append(epi["foldfree"][hh])
                    tff = p.on("T", lambda e, dc=dc, fb=fb: e.matmul(
                        pb[fb][0:64, 0:512], ISTb[:, :], pbuf,
                        start=(dc == 0), stop=(dc == 3)), waits=w2, inc=(sT, 1))
                    epi["prodfree"][hh] = tff
                    st["tff"] = tff
                return f

            for dc in range(4):
                steps.append(mk_rep_prod(dc))

            def s_ev():
                fb = 6 if hh == 0 else 2
                tfin = p.on("V", lambda e, fb=fb: e.tensor_copy(
                    finall[64 * hh:64 * (hh + 1), hp * SEQ + sl.start: hp * SEQ + sl.stop],
                    pb[fb][0:64, 0:512]), waits=[st["tff"]], inc=(sV, 1))
                epi["foldfree"][hh] = tfin
                epi["tfin"] = tfin
                if hh == 1:
                    gate_out[cc2] = tprs[-1]
            steps.append(s_ev)
            return steps

        def zrun(chains):
            for ops in itertools.zip_longest(*chains):
                for op in ops:
                    if op:
                        op()
        metas = {}
        for wave in (0, 1):
            comps = []
            for cc2 in (2 * wave, 2 * wave + 1):
                ca, sta = comp_chain(0, cc2, (7, 3)[cc2 % 2])
                cb, stb = comp_chain(1, cc2, (0, 1)[cc2 % 2])
                comps += [ca, cb]
                metas[cc2] = (sta, stb)
            if wave == 1:
                comps += [fin_chain(0, 0, metas[0][0]), fin_chain(1, 0, metas[0][1])]
            zrun(comps)
        for cc2 in range(1, 4):
            sta, stb = metas[cc2]
            zrun([fin_chain(0, cc2, sta), fin_chain(1, cc2, stb)])
        return epi["tfin"]

    for hp in range(hpn):
        tok_pv_prevhp = tok_pv
        tok_exp = {}
        tok_pv = {}
        for iq in range(NIQ):
            def s_and_exp(j, iq=iq, hp=hp, tok_pv_prevhp=tok_pv_prevhp):
                w = [tQ[-1], tK[-1], tNEGM, tok_sband.get(0), bankfree[0], bankfree[1]]
                p.on("T", lambda e, j=j, iq=iq, hp=hp: e.matmul(
                    pb[0][:, 0:IQ],
                    Kb[0:64, hp * SEQ + j * 128: hp * SEQ + (j + 1) * 128],
                    Qb[0:64, hp * SEQ + iq * IQ: hp * SEQ + (iq + 1) * IQ],
                    start=True, stop=True, tile_position=(0, 0)), waits=w)
                tmm = p.on("T", lambda e, j=j, iq=iq, hp=hp: e.matmul(
                    pb[1][:, 0:IQ],
                    Kb[64:128, hp * SEQ + j * 128: hp * SEQ + (j + 1) * 128],
                    Qb[64:128, hp * SEQ + iq * IQ: hp * SEQ + (iq + 1) * IQ],
                    start=True, stop=True, tile_position=(64, 0)), inc=(sT, 1))
                if iq == 0:
                    prevpv = tok_pv_prevhp.get((NIQ - 1, j))
                else:
                    prevpv = tok_pv.get((iq - 1, j))
                te = p.on("S", lambda e, j=j, hp=hp: e.activation(
                    EAB[:, j * 512:(j + 1) * 512].rearrange("p (a b) -> p a b", b=IQ),
                    psS[:, 0:1024].rearrange("p (a b) -> p a b", b=512)[:, :, 0:IQ],
                    AF.Exp, bias=negM[:, hp:hp + 1]), waits=[tmm, prevpv], inc=(sS, 1))
                tok_exp[(iq, j)] = te
                tok_sband[0] = te

            s_and_exp(0)
            s_and_exp(1)
            for j in range(NJT):
                if j + 2 <= NJT - 1:
                    s_and_exp(j + 2)
                for dc in range(4):
                    bki = 2 + dc
                    bk = pb[bki]
                    w = [tok_exp[(iq, j)], tV]
                    if j == 0:
                        w.append(bankfree[bki])
                    p.on("T", lambda e, j=j, dc=dc, bk=bk: e.matmul(
                        bk[:, 0:2 * IQ],
                        Vtm[:, j * 512 + dc * 128: j * 512 + (dc + 1) * 128],
                        EAB[:, j * 512: (j + 1) * 512],
                        start=(j == 0), stop=(j == NJT - 1)), waits=w)
                w = [tok_exp[(iq, j)]]
                if j == 0:
                    w.append(bankfree[6])
                tok_pv[(iq, j)] = p.on("T", lambda e, j=j: e.matmul(
                    pb[6][0:1, 0:2 * IQ], ones_col_b[:, :], EAB[:, j * 512: (j + 1) * 512],
                    start=(j == 0), stop=(j == NJT - 1)), waits=w, inc=(sT, 1))
            tlast = tok_pv[(iq, NJT - 1)]
            go = gate_out.get(iq // 2)
            gd = gate_den.get(iq // 2)
            w = [tlast] + (gd if gd else [])
            p.on("V", lambda e, iq=iq: e.tensor_copy(
                denAB[0:1, iq * IQ:(iq + 1) * IQ], pb[6][0:1, 0:IQ]), waits=w, inc=(sV, 1))
            bankfree[6] = p.on("V", lambda e, iq=iq: e.tensor_copy(
                denAB[32:33, iq * IQ:(iq + 1) * IQ], pb[6][0:1, IQ:2 * IQ]),
                inc=(sV, 1))
            for dc in range(4):
                bk = pb[2 + dc]
                ta = p.on("S", lambda e, dc=dc, iq=iq, bk=bk: e.copy(
                    outTA[:, dc * SEQ + iq * IQ: dc * SEQ + (iq + 1) * IQ], bk[:, 0:IQ]),
                    waits=[tlast, go], inc=(sS, 1))
                tb = p.on("V", lambda e, dc=dc, iq=iq, bk=bk: e.tensor_copy(
                    outTB[:, dc * SEQ + iq * IQ: dc * SEQ + (iq + 1) * IQ], bk[:, IQ:2 * IQ]),
                    waits=[tlast, go], inc=(sV, 1))
                bankfree[2 + dc] = [ta, tb]
        tFIN = emit_epilogue(hp, [bankfree[2 + dc] for dc in range(4)], bankfree[6])
        bankfree[0] = epi["bkr"][0]
        bankfree[1] = epi["bkr"][1]
        bankfree[2] = epi["foldfree"][1]
        bankfree[3] = epi["bkr"][3]
        bankfree[4] = epi["bkfinr"][4]
        bankfree[5] = epi["bkfinr"][5]
        bankfree[6] = epi["foldfree"][0]
        if hp == 5 and maxphase >= 4:
            wslot_free[0] = wslot_free[1] = None
            wf_toks = stream_w(wfT, 8, 1024, wfb, wstage4, pre=p.tok(sT), eng="S")

    tap("finall0", finall[:, 0:2048], [tFIN])
    tap("outTB2", outTB[:, 2 * SEQ:3 * SEQ], [p.tok(sV)])
    tap("outTBf", outTB, [p.tok(sV)])
    tap("outTAf", outTA, [p.tok(sV)])
    tap("EABlast", EAB[:, 0:1024], [p.tok(sV)])
    tap("denA", denAB[0:1, :], [p.tok(sV)])
    tap("denB", denAB[32:33, :], [p.tok(sV)])
    tap("outTA0", outTA[:, 0:2048], [p.tok(sV)])

    p.barrier("p3", dma_sems=p.dma_pool)

    if maxphase <= 3:
        p.on("G", None, waits=p.all_dma_toks())
        p.emit()
        return nc

    mrepw = carve(24, [128, WIN], F32)
    rrepw = carve(26, [128, WIN], F32)
    xw = [carve(28, [128, WIN], F32), carve(30, [128, WIN], F32)]
    wr1b = carve(32, [128, 16384], BF16)
    wr2b = carve(64, [128, 16384], BF16)
    amidw = carve(96, [128, NDC * WIN], F32)
    h1w = carve(112, [128, 16 * WIN], BF16)
    xn2w = carve(128, [128, NDC * WIN], BF16)
    outw = [carve(136, [128, WIN], F32), carve(138, [128, WIN], F32)]
    sqw = carve(140, [128, WIN], F32)
    m2row = carve(142, [1, WIN], F32)
    vtmp = carve(144, [1, WIN], F32)
    r2wrow = carve(146, [1, WIN], F32)
    m2rep = carve(148, [128, WIN], F32)
    r2wrep = carve(150, [128, WIN], F32)
    tmp4 = carve(184, [128, WIN], F32)

    if wf_toks is None:
        wslot_free[0] = wslot_free[1] = None
        wf_toks = stream_w(wfT, 8, 1024, wfb, wstage4, eng="S")
    wslot_free[0] = wslot_free[1] = None
    wr1_toks = []
    for c in range(8):
        tl = p.on("Y", lambda e, c=c: e.dma_start(
            out=wstage4[:, 0:2048], in_=wr1T[c * 128:(c + 1) * 128, :]),
            waits=[wslot_free[0], wslot_free[1]], dma=True)
        tc_ = p.on("S", lambda e, c=c: e.copy(
            wr1b[:, c * 2048:(c + 1) * 2048], wstage4[:, 0:2048]), waits=[tl], inc=(sS, 1))
        wslot_free[0] = wslot_free[1] = tc_
        wr1_toks.append(tc_)
    wr2_toks = stream_w(wr2T, 16, 1024, wr2b, wstage4)

    tow = [None, None]
    th1 = {}
    txn2w_last = None
    wr1done_prev = None
    fsave = {}
    for w_i in range(NW + 1):
        if w_i < NW:
            w0 = w_i * WIN
            fpP = (4, 5) if w_i % 2 == 0 else (6, 7)
            tmrw = replicate(mrepw, statrows[0:1, w0:w0 + WIN], 0, [], 0, width=WIN)
            trrw = replicate(rrepw, statrows[64:65, w0:w0 + WIN], 64, [], 1, width=WIN)
            tamid = {}
            for dc in range(NDC):
                bki = fpP[dc % 2]
                bk = pb[bki]
                last = None
                for fc in range(NDC):
                    wts = [wf_toks[fc], tFIN]
                    if fc == 0:
                        wts.append(bankfree[bki])
                    last = p.on("T", lambda e, dc=dc, fc=fc, w0=w0, bk=bk: e.matmul(
                        bk[:, 0:WIN], wfb[:, fc * 1024 + dc * 128: fc * 1024 + (dc + 1) * 128],
                        finall[:, fc * SEQ + w0: fc * SEQ + w0 + WIN],
                        start=(fc == 0), stop=(fc == NDC - 1)),
                        waits=wts, inc=(sT, 1) if fc == NDC - 1 else None)
                txl = p.on("S", lambda e, dc=dc, w0=w0: e.dma_start(
                    out=xw[dc % 2], in_=xT[dc * 128:(dc + 1) * 128, w0:w0 + WIN]),
                    waits=[tow[dc % 2]], dma=True)
                t1 = p.on("G", lambda e, dc=dc: e.tensor_tensor(
                    tmp4, xw[dc % 2], mrepw, ALU.subtract),
                    waits=[txl, tmrw, trrw, txn2w_last], inc=(sG, 1))
                t2g = p.on("G", lambda e, dc=dc: e.tensor_tensor(tmp4, tmp4, rrepw, ALU.mult),
                           inc=(sG, 1))
                t3 = p.on("V", lambda e, dc=dc, bk=bk: e.tensor_scalar(
                    sqw, bk[:, 0:WIN], vcol("bf", dc), None, ALU.add),
                    waits=[last] + ([txn2w_last] if dc == 0 else []), inc=(sV, 1))
                bankfree[bki] = t3
                t4 = p.on("V", lambda e, dc=dc: e.tensor_tensor(
                    amidw[:, dc * WIN:(dc + 1) * WIN], tmp4, sqw, ALU.add),
                    waits=[t2g], inc=(sV, 1))
                t5 = p.on("V", lambda e, dc=dc: e.tensor_tensor(
                    outw[dc % 2], xw[dc % 2], amidw[:, dc * WIN:(dc + 1) * WIN], ALU.add),
                    waits=[tow[dc % 2]], inc=(sV, 1))
                _src = (lambda dc: amidw[:, dc * WIN:(dc + 1) * WIN]) if getattr(build_program, "_plain_amid_only", False) else (lambda dc: outw[dc % 2])
                tst = p.on("G", lambda e, dc=dc, w0=w0, _src=_src: e.dma_start(
                    out=out_hbm[dc * 128:(dc + 1) * 128, w0:w0 + WIN], in_=_src(dc)),
                    waits=[t5], dma=True)
                tow[dc % 2] = tst
                tamid[dc] = t4
            fsave[w_i] = (tamid, None, fpP, w0)
        if w_i >= 1:
            for mt in range(16):
                bki = (2, 3)[mt % 2]
                bk = pb[bki]
                last = None
                for dc in range(NDC):
                    wts = [wr1_toks[dc], txn2w[dc]]
                    if dc == 0:
                        wts.append(bankfree[bki])
                    last = p.on("T", lambda e, mt=mt, dc=dc, bk=bk: e.matmul(
                        bk[:, 0:WIN], wr1b[:, dc * 2048 + mt * 128: dc * 2048 + (mt + 1) * 128],
                        xn2w[:, dc * WIN:(dc + 1) * WIN],
                        start=(dc == 0), stop=(dc == NDC - 1)),
                        waits=wts, inc=(sT, 1) if dc == NDC - 1 else None)
                th1[mt] = p.on("S", lambda e, mt=mt, bk=bk: e.activation(
                    h1w[:, mt * WIN:(mt + 1) * WIN], bk[:, 0:WIN], AF.Relu,
                    bias=vcol("br1", mt)), waits=[last], inc=(sS, 1))
                bankfree[bki] = th1[mt]
            wr1done_prev = th1[15]
        if w_i < NW:
            tamid = fsave[w_i][0]
            tprev = None
            for dc in range(NDC):
                tsq2 = p.on("V", lambda e, dc=dc: e.tensor_tensor(
                    sqw, amidw[:, dc * WIN:(dc + 1) * WIN], amidw[:, dc * WIN:(dc + 1) * WIN], ALU.mult),
                    waits=[tamid[dc], tprev], inc=(sV, 1))
                w = [tamid[dc]]
                if dc == 0:
                    w += [bankfree[0], bankfree[1]]
                p.on("T", lambda e, dc=dc: e.matmul(
                    pb[0][0:1, 0:WIN], ones_col_f[:, :], amidw[:, dc * WIN:(dc + 1) * WIN],
                    start=(dc == 0), stop=(dc == NDC - 1)), waits=w)
                tprev = p.on("T", lambda e, dc=dc: e.matmul(
                    pb[1][0:1, 0:WIN], ones_col_f[:, :], sqw,
                    start=(dc == 0), stop=(dc == NDC - 1)), waits=[tsq2], inc=(sT, 1))
            tv = p.on("V", lambda e: e.tensor_scalar_mul(m2row, pb[0][0:1, 0:WIN], 1.0 / D),
                      waits=[tprev], inc=(sV, 1))
            tv = p.on("V", lambda e: e.tensor_scalar_mul(vtmp, pb[1][0:1, 0:WIN], 1.0 / D),
                      inc=(sV, 1))
            tv = p.on("V", lambda e: e.tensor_tensor(r2wrow, m2row, m2row, ALU.mult),
                      inc=(sV, 1))
            tv = p.on("V", lambda e: e.tensor_tensor(vtmp, vtmp, r2wrow, ALU.subtract),
                      inc=(sV, 1))
            tv = p.on("V", lambda e: e.tensor_scalar_add(vtmp, vtmp, EPS), inc=(sV, 1))
            p.on("S", lambda e: e.activation(vtmp, vtmp, AF.Ln),
                 waits=[tv], inc=(sS, 1))
            tsqr = p.on("S", lambda e: e.activation(r2wrow, vtmp, AF.Exp, scale=-0.5),
                        inc=(sS, 1))
            tm_a = p.on("T", lambda e: e.matmul(pb[0][:, 0:WIN], onesPP[0:1, 0:128], m2row,
                                                start=True, stop=True), waits=[tv], inc=(sT, 1))
            tm2r = p.on("V", lambda e: e.tensor_copy(m2rep, pb[0][:, 0:WIN]), waits=[tm_a], inc=(sV, 1))
            bankfree[0] = tm2r
            tm_b = p.on("T", lambda e: e.matmul(pb[1][:, 0:WIN], onesPP[0:1, 0:128], r2wrow,
                                                start=True, stop=True), waits=[tsqr], inc=(sT, 1))
            tr2w = p.on("V", lambda e: e.tensor_copy(r2wrep, pb[1][:, 0:WIN]), waits=[tm_b], inc=(sV, 1))
            bankfree[1] = tr2w
            txn2w = {}
            for dc in range(NDC):
                tb = tmp4 if dc % 2 == 0 else sqw
                w_x = [tm2r, tr2w, txn2w.get(dc - 2)]
                if dc == 0:
                    w_x.append(wr1done_prev)
                p.on("V", lambda e, dc=dc, tb=tb: e.tensor_tensor(
                    tb, amidw[:, dc * WIN:(dc + 1) * WIN], m2rep, ALU.subtract),
                    waits=w_x, inc=(sV, 1))
                t2 = p.on("V", lambda e, dc=dc, tb=tb: e.tensor_tensor(tb, tb, r2wrep, ALU.mult), inc=(sV, 1))
                txn2w[dc] = p.on("S", lambda e, dc=dc, tb=tb: e.activation(
                    xn2w[:, dc * WIN:(dc + 1) * WIN], tb, AF.Identity,
                    bias=vcol("b2", dc), scale=vcol("g2", dc)), waits=[t2], inc=(sS, 1))
                txn2w_last = txn2w[dc]
        if w_i >= 1:
            _, _, fpPp, w0p = fsave[w_i - 1]
            for mt in range(NDC):
                bki = fpPp[mt % 2]
                bk = pb[bki]
                last = None
                for hc in range(16):
                    wts = [wr2_toks[hc], th1[hc]]
                    if hc == 0:
                        wts += [bankfree[bki], tow[mt % 2]]
                    last = p.on("T", lambda e, mt=mt, hc=hc, bk=bk: e.matmul(
                        bk[:, 0:WIN], wr2b[:, hc * 1024 + mt * 128: hc * 1024 + (mt + 1) * 128],
                        h1w[:, hc * WIN:(hc + 1) * WIN],
                        start=(hc == 0), stop=(hc == 15)),
                        waits=wts, inc=(sT, 1) if hc == 15 else None)
                t1 = p.on("V", lambda e, mt=mt, bk=bk: e.tensor_scalar(
                    outw[mt % 2], bk[:, 0:WIN], vcol("br2", mt), None, ALU.add),
                    waits=[last, tow[mt % 2]], inc=(sV, 1))
                bankfree[bki] = t1
                if not getattr(build_program, "_skip_iffn_dma", False):
                    tow[mt % 2] = p.on("G", lambda e, mt=mt, w0p=w0p: e.dma_start(
                        out=out_hbm[mt * 128:(mt + 1) * 128, w0p:w0p + WIN], in_=outw[mt % 2],
                        accum_op=ALU.add), waits=[t1], dma=True)
                else:
                    tow[mt % 2] = t1

    tap("amidw", amidw, [p.tok(sV)])
    tap("xn2w", xn2w, [p.tok(sS)])
    tap("h1w", h1w, [p.tok(sS)])
    tap("wr1b0", wr1b[:, 0:2048], [wr1_toks[0]])
    tap("wr2b0", wr2b[:, 0:1024], [wr2_toks[0]])
    tap("wr2bf", wr2b, [wr2_toks[-1]])
    tap("wr1bf", wr1b, [wr1_toks[-1]])

    p.barrier("p4", dma_sems=p.dma_pool)

    m_rep5 = carve(0, [128, SEQ], F32)
    r_rep5 = carve(8, [128, SEQ], F32)
    xnb = carve(16, [128, NDC * SEQ], BF16)
    xscr = carve(48, [128, SEQ], F32)
    mwbf = carve(56, [128, 8192], BF16)
    Gb = carve(72, [128, 2048], BF16)
    wqc = carve(76, [128, 2048], BF16)
    wkc = carve(80, [128, 2048], BF16)
    wvc = carve(84, [128, 2048], BF16)
    qkb = carve(88, [128, 2 * SEQ], BF16)
    kkb = carve(96, [128, 2 * SEQ], BF16)
    vktm = carve(104, [128, NJT * 288], BF16)
    EABm = carve(114, [128, NJT * 512], BF16)
    outmA = carve(130, [65, SEQ], F32)
    outmB = carve(138, [65, SEQ], F32)
    outc = carve(146, [128, 2 * SEQ], BF16)
    mrep16 = carve(154, [16, SEQ], F32)
    rrep16 = carve(162, [16, SEQ], F32)
    logsb = carve(170, [16, SEQ], F32)
    mscr = carve(178, [128, SEQ], F32)
    grep = carve(154, [128, 256], F32)
    bvcrep = carve(155, [128, 256], F32)
    mow = [carve(170, [128, WIN], F32), carve(172, [128, WIN], F32)]

    tmr5 = replicate(m_rep5, statrows[0:1, :], 0, [], 0)
    trr5 = replicate(r_rep5, statrows[64:65, :], 64, [], 1)

    twr = p.on("Y", lambda e: e.dma_start(
        out=wrf[:].rearrange("p (c n) -> p c n", c=8), in_=mwr.rearrange("(c p) n -> p c n", p=128)),
        dma=True)

    tlog = None
    txnb = {}
    prevx = None
    for c in range(NDC):
        txl = p.on("Y", lambda e, c=c: e.dma_start(
            out=xscr, in_=xT[c * 128:(c + 1) * 128, :]), waits=[prevx, tlog], dma=True)
        for cc in range(4):
            w = [txl, twr]
            if c == 0:
                w.append(bankfree[4 + cc])
            tlog = p.on("T", lambda e, c=c, cc=cc: e.matmul(
                pb[4 + cc][0:16, 0:512], wrf[:, c * 16:(c + 1) * 16],
                xscr[:, cc * 512:(cc + 1) * 512],
                start=(c == 0), stop=(c == NDC - 1)), waits=w, inc=(sT, 1))
        t1 = p.on("V", lambda e, c=c: e.tensor_tensor(mscr, xscr, m_rep5, ALU.subtract),
                  waits=[txl, tmr5, trr5], inc=(sV, 1))
        prevx = t1
        txnb[c] = p.on("V", lambda e, c=c: e.tensor_tensor(
            xnb[:, c * SEQ:(c + 1) * SEQ], mscr, r_rep5, ALU.mult), inc=(sV, 1))
    tXNB = txnb[NDC - 1]

    tcs = None
    for c in range(NDC):
        w = [twr, tCONST]
        if c == 0:
            w.append(bankfree[0])
        tcs = p.on("T", lambda e, c=c: e.matmul(
            pb[0][0:16, 0:1], wrf[:, c * 16:(c + 1) * 16], ones_col_f[:, :],
            start=(c == 0), stop=(c == NDC - 1)), waits=w, inc=(sT, 1))
    tcsv = p.on("V", lambda e: e.tensor_copy(gat[0:16, 32:33], pb[0][0:16, 0:1]), waits=[tcs], inc=(sV, 1))
    bankfree[0] = tcsv

    trm16 = None
    for cc in range(4):
        tm_a = p.on("T", lambda e, cc=cc: e.matmul(
            pb[2][0:16, 0:512], onesPP[0:1, 0:16], statrows[0:1, cc * 512:(cc + 1) * 512],
            start=True, stop=True), waits=[trm16, bankfree[2]], inc=(sT, 1))
        tmv = p.on("V", lambda e, cc=cc: e.tensor_copy(mrep16[:, cc * 512:(cc + 1) * 512], pb[2][0:16, 0:512]),
                   waits=[tm_a], inc=(sV, 1))
        bankfree[2] = tmv
        tm_b = p.on("T", lambda e, cc=cc: e.matmul(
            pb[3][0:16, 0:512], onesPP[64:65, 0:16], statrows[64:65, cc * 512:(cc + 1) * 512],
            start=True, stop=True), waits=[tmv, bankfree[3]], inc=(sT, 1))
        trm16 = p.on("V", lambda e, cc=cc: e.tensor_copy(rrep16[:, cc * 512:(cc + 1) * 512], pb[3][0:16, 0:512]),
                     waits=[tm_b], inc=(sV, 1))
        bankfree[3] = trm16

    tlg = None
    for cc in range(4):
        p.on("V", lambda e, cc=cc: e.tensor_scalar(
            logsb[:, cc * 512:(cc + 1) * 512], mrep16[:, cc * 512:(cc + 1) * 512],
            gat[0:16, 32:33], None, ALU.mult), waits=[tlog, tcsv, trm16], inc=(sV, 1))
        p.on("V", lambda e, cc=cc: e.tensor_tensor(
            logsb[:, cc * 512:(cc + 1) * 512], pb[4 + cc][0:16, 0:512],
            logsb[:, cc * 512:(cc + 1) * 512], ALU.subtract), inc=(sV, 1))
        tlg = p.on("V", lambda e, cc=cc: e.tensor_tensor(
            logsb[:, cc * 512:(cc + 1) * 512], logsb[:, cc * 512:(cc + 1) * 512],
            rrep16[:, cc * 512:(cc + 1) * 512], ALU.mult), inc=(sV, 1))
        bankfree[4 + cc] = tlg
    tmean = p.on("V", lambda e: e.reduce_sum(gat[0:16, 33:34], logsb, axis=AX.X), waits=[tlg], inc=(sV, 1))
    tmean = p.on("V", lambda e: e.tensor_scalar(
        gat[0:16, 34:35], gat[0:16, 33:34], 1.0 / SEQ, vcol("mbr", 0)[0:16, :], ALU.mult, ALU.add),
        inc=(sV, 1))
    tdm = p.on("G", lambda e: e.dma_start(out=gat[0:1, 0:16], in_=gat[0:16, 34:35]),
               waits=[tmean], dma=True)
    t1 = p.on("V", lambda e: e.reduce_max(gat[0:1, 16:17], gat[0:1, 0:16], axis=AX.X),
              waits=[tdm], inc=(sV, 1))
    t2 = p.on("V", lambda e: e.tensor_tensor(gat[0:1, 0:16], gat[0:1, 0:16],
                                             gat[0:1, 16:17].broadcast_to([1, 16]), ALU.subtract),
              inc=(sV, 1))
    t3 = p.on("S", lambda e: e.activation(gat[0:1, 0:16], gat[0:1, 0:16], AF.Exp), waits=[t2], inc=(sS, 1))
    t4 = p.on("V", lambda e: e.reduce_sum(gat[0:1, 16:17], gat[0:1, 0:16], axis=AX.X),
              waits=[t3], inc=(sV, 1))
    t5 = p.on("V", lambda e: e.reciprocal(gat[0:1, 17:18], gat[0:1, 16:17]), inc=(sV, 1))
    tdist = p.on("V", lambda e: e.tensor_tensor(gat[0:1, 0:16], gat[0:1, 0:16],
                                                gat[0:1, 17:18].broadcast_to([1, 16]), ALU.mult),
                 inc=(sV, 1))
    p.on("V", lambda e: e.max(gat[0:1, 18:26], gat[0:1, 0:16]), waits=[tdist], inc=(sV, 1))
    ttk = p.on("V", lambda e: e.max_index(mi8[0:1, :], gat[0:1, 18:26], gat[0:1, 0:16]),
               inc=(sV, 1))
    tsp = p.on("V", lambda e: e.memset(gat[0:1, 48:64], 0.0), inc=(sV, 1))
    toh = {}
    for k in range(4):
        toh[k] = p.on("V", lambda e, k=k: e.tensor_tensor(
            gat[0:1, 64 + 16 * k:80 + 16 * k], gat[0:1, 0:16],
            gat[0:1, 18 + k:19 + k].broadcast_to([1, 16]), ALU.is_equal), waits=[ttk], inc=(sV, 1))
        p.on("V", lambda e, k=k: e.tensor_tensor(
            gat[0:1, 128 + 16 * k:144 + 16 * k], gat[0:1, 64 + 16 * k:80 + 16 * k],
            gat[0:1, 18 + k:19 + k].broadcast_to([1, 16]), ALU.mult), inc=(sV, 1))
        tsp = p.on("V", lambda e, k=k: e.tensor_tensor(
            gat[0:1, 48:64], gat[0:1, 48:64], gat[0:1, 128 + 16 * k:144 + 16 * k], ALU.add),
            inc=(sV, 1))
    tge = p.on("S", lambda e: e.activation(gat[0:1, 192:208], gat[0:1, 48:64], AF.Exp),
               waits=[tsp], inc=(sS, 1))
    p.on("V", lambda e: e.reduce_sum(gat[0:1, 16:17], gat[0:1, 192:208], axis=AX.X),
         waits=[tge], inc=(sV, 1))
    p.on("V", lambda e: e.reciprocal(gat[0:1, 17:18], gat[0:1, 16:17]), inc=(sV, 1))
    tga = p.on("V", lambda e: e.tensor_tensor(gat[0:1, 192:208], gat[0:1, 192:208],
                                              gat[0:1, 17:18].broadcast_to([1, 16]), ALU.mult),
               inc=(sV, 1))
    tgk = None
    for k in range(4):
        p.on("V", lambda e, k=k: e.tensor_tensor(
            gat[0:1, 128 + 16 * k:144 + 16 * k], gat[0:1, 192:208],
            gat[0:1, 64 + 16 * k:80 + 16 * k], ALU.mult), waits=[tga], inc=(sV, 1))
        tgk = p.on("V", lambda e, k=k: e.reduce_sum(
            gat[0:1, 26 + k:27 + k], gat[0:1, 128 + 16 * k:144 + 16 * k], axis=AX.X), inc=(sV, 1))
    tgc = p.on("G", lambda e: e.dma_start(out=gat[0:4, 40:41], in_=gat[0:1, 26:30]),
               waits=[tgk], dma=True)
    tge2 = {}
    for t2_ in range(2):
        tm_a = p.on("T", lambda e, t2_=t2_: e.matmul(
            pb[0][:, 0:1], E64f[:, t2_ * 128:(t2_ + 1) * 128], gat[0:4, 40:41],
            start=True, stop=True), waits=[tgc, bankfree[0]], inc=(sT, 1))
        tge2[t2_] = p.on("S", lambda e, t2_=t2_: e.copy(gexp[:, t2_:t2_ + 1], pb[0][:, 0:1]),
                         waits=[tm_a], inc=(sS, 1))
        bankfree[0] = tge2[t2_]
    tgq = p.on("S", lambda e: e.mul(gexpQ[:], gexp[:], ISQ8), waits=[tge2[1]], inc=(sS, 1))
    th4 = []
    for k in range(4):
        t1 = p.on("V", lambda e, k=k: e.tensor_copy(gatb[0:1, :], gat[0:1, 64 + 16 * k:80 + 16 * k]),
                  waits=[toh[k], (th4[-1] if th4 else None)], inc=(sV, 1))
        th4.append(p.on("G", lambda e, k=k: e.dma_start(out=H4T[k:k + 1, :], in_=gatb[0:1, :]),
                        waits=[t1], dma=True))
    tohc = [p.on("G", lambda e, k=k: e.dma_start(
        out=gat[0:16, 41 + k:42 + k], in_=gat[0:1, 64 + 16 * k:80 + 16 * k]),
        waits=[toh[k]], dma=True) for k in range(4)]
    tbc_last = None
    for wi in range(3):
        for k in range(4):
            tm_a = p.on("T", lambda e, wi=wi, k=k: e.matmul(
                pb[1][0:64, 0:1], bh3t[:, wi * 64:(wi + 1) * 64], gat[0:16, 41 + k:42 + k],
                start=True, stop=True), waits=[t_bh3, tohc[k], tbc_last, bankfree[1]],
                inc=(sT, 1))
            tbc_last = p.on("S", lambda e, wi=wi, k=k: e.copy(
                bcpp[64 * (k % 2):64 * (k % 2) + 64, 2 * wi + k // 2:2 * wi + k // 2 + 1],
                pb[1][0:64, 0:1]), waits=[tm_a], inc=(sS, 1))
            bankfree[1] = tbc_last
    tBC = tbc_last

    tm_a = p.on("T", lambda e: e.matmul(pb[0][0:16, 0:256], H4T[:, :], Ecol4b[:, :],
                                        start=True, stop=True), waits=th4 + [tCONST, bankfree[0]],
                inc=(sT, 1))
    th4e = p.on("V", lambda e: e.tensor_copy(H4E[:, :], pb[0][0:16, 0:256]), waits=[tm_a], inc=(sV, 1))
    bankfree[0] = th4e
    tGB = None
    for c in range(NDC):
        tm_a = p.on("T", lambda e, c=c: e.matmul(
            pb[0][:, 0:256], RowSelb[:, c * 128:(c + 1) * 128], H4E[:, :],
            start=True, stop=True), waits=[th4e, bankfree[0]], inc=(sT, 1))
        tGB = p.on("V", lambda e, c=c: e.tensor_tensor(
            Gb[:, c * 256:(c + 1) * 256], pb[0][:, 0:256], M64b[:, :], ALU.mult),
            waits=[tm_a], inc=(sV, 1))
        bankfree[0] = tGB

    wslot_free[0] = wslot_free[1] = None

    def stream_w5(hbm, nchunks, pre=None):
        toks = []
        for c in range(nchunks):
            s = c % 2
            w = [wslot_free[s]]
            if pre is not None and c < 2:
                w.append(pre)
            tl = p.on("Y", lambda e, c=c, s=s, hbm=hbm: e.dma_start(
                out=mscr[:, s * 1024:(s + 1) * 1024], in_=hbm[c * 128:(c + 1) * 128, :]),
                waits=w, dma=True)
            tc_ = p.on("S", lambda e, c=c, s=s: e.copy(
                mwbf[:, c * 1024:(c + 1) * 1024], mscr[:, s * 1024:(s + 1) * 1024]),
                waits=[tl], inc=(sS, 1))
            wslot_free[s] = tc_
            toks.append(tc_)
        return toks

    def compact_w(wtoks, dst, pre):
        last_ev = pre
        for it in range(NDC):
            last = None
            for oc in range(NDC):
                w = [wtoks[oc], tGB]
                if oc == 0:
                    w += [last_ev, bankfree[2]]
                last = p.on("T", lambda e, it=it, oc=oc: e.matmul(
                    pb[2][:, 0:256], mwbf[:, oc * 1024 + it * 128: oc * 1024 + (it + 1) * 128],
                    Gb[:, oc * 256:(oc + 1) * 256],
                    start=(oc == 0), stop=(oc == NDC - 1)),
                    waits=w, inc=(sT, 1) if oc == NDC - 1 else None)
            last_ev = p.on("V", lambda e, it=it, dst=dst: e.tensor_copy(
                dst[:, it * 256:(it + 1) * 256], pb[2][:, 0:256]), waits=[last], inc=(sV, 1))
            bankfree[2] = last_ev
        return last_ev

    tq_toks = stream_w5(mwqT, 8, pre=tXNB)
    tWQC = compact_w(tq_toks, wqc, None)
    tk_toks = stream_w5(mwkT, 8, pre=tWQC)
    tWKC = compact_w(tk_toks, wkc, None)
    tv_toks = stream_w5(mwvT, 8, pre=tWKC)
    tWVC = compact_w(tv_toks, wvc, None)

    def cproj(wc, twc, dst, bcol, gtile):
        toks = []
        for half in range(2):
            for cc in range(4):
                bki = 4 + ((half * 4 + cc) % 2)
                bk = pb[bki]
                last = None
                for dc in range(NDC):
                    w = [twc, tXNB]
                    if dc == 0:
                        w.append(bankfree[bki])
                    last = p.on("T", lambda e, half=half, cc=cc, dc=dc, bk=bk, wc=wc: e.matmul(
                        bk[:, 0:512], wc[:, dc * 256 + half * 128: dc * 256 + (half + 1) * 128],
                        xnb[:, dc * SEQ + cc * 512: dc * SEQ + (cc + 1) * 512],
                        start=(dc == 0), stop=(dc == NDC - 1)),
                        waits=w, inc=(sT, 1) if dc == NDC - 1 else None)
                te = p.on("V", lambda e, half=half, cc=cc, bk=bk, dst=dst, bcol=bcol, gtile=gtile: e.tensor_scalar(
                    dst[:, half * SEQ + cc * 512: half * SEQ + (cc + 1) * 512], bk[:, 0:512],
                    bcpp[:, bcol + half:bcol + half + 1], gtile[:, half:half + 1], ALU.add, ALU.mult),
                    waits=[last, tBC, tgq], inc=(sV, 1))
                bankfree[bki] = te
                toks.append(te)
        return toks[-1]

    tQK = cproj(wqc, tWQC, qkb, 0, gexpQ)
    tKK = cproj(wkc, tWKC, kkb, 2, gexp)

    tgrow = []
    for k in range(4):
        tgrow.append(p.on("V", lambda e, k=k: e.tensor_copy(
            scr_row[0:1, k * 64:(k + 1) * 64],
            gat[0:1, 26 + k:27 + k].broadcast_to([1, 64])), waits=[tgk], inc=(sV, 1)))
    tbvc = [p.on("G", lambda e, half=half: e.dma_start(
        out=gat[0:1, 224 + half * 128:224 + (half + 1) * 128],
        in_=bcpp[:, 4 + half:5 + half]), waits=[tBC], dma=True) for half in range(2)]
    tm_a = p.on("T", lambda e: e.matmul(pb[0][:, 0:256], onesPP[0:1, 0:128], scr_row[0:1, 0:256],
                                        start=True, stop=True), waits=tgrow + [bankfree[0]], inc=(sT, 1))
    tgrep = p.on("V", lambda e: e.tensor_copy(grep[:], pb[0][:, 0:256]), waits=[tm_a], inc=(sV, 1))
    bankfree[0] = tgrep
    tm_b = p.on("T", lambda e: e.matmul(pb[1][:, 0:256], onesPP[0:1, 0:128], gat[0:1, 224:480],
                                        start=True, stop=True), waits=tbvc + [bankfree[1]], inc=(sT, 1))
    tbvcrep = p.on("V", lambda e: e.tensor_copy(bvcrep[:], pb[1][:, 0:256]), waits=[tm_b], inc=(sV, 1))
    bankfree[1] = tbvcrep

    tms = p.on("V", lambda e: e.memset(vktm[:, :], 0.0), inc=(sV, 1))
    tones = p.on("V", lambda e: e.tensor_copy(
        vktm[:, :].rearrange("p (a b) -> p a b", b=72)[:, :, 64:65],
        ones_col_b[:, 0:1].broadcast_to([128, 64]).rearrange("p (a b) -> p a b", b=1)),
        waits=[tms], inc=(sV, 1))
    tvk_last = None
    for jt in range(NJT):
        last = None
        for dc in range(NDC):
            w = [tWVC, tXNB]
            if dc == 0:
                w.append(bankfree[2])
            last = p.on("T", lambda e, jt=jt, dc=dc: e.matmul(
                pb[2][:, 0:256], xnb[:, dc * SEQ + jt * 128: dc * SEQ + (jt + 1) * 128],
                wvc[:, dc * 256:(dc + 1) * 256],
                start=(dc == 0), stop=(dc == NDC - 1)),
                waits=w, inc=(sT, 1) if dc == NDC - 1 else None)
        t1 = p.on("V", lambda e, jt=jt: e.tensor_tensor(xscr[:, 0:256], pb[2][:, 0:256], bvcrep[:], ALU.add),
                  waits=[last, tbvcrep], inc=(sV, 1))
        bankfree[2] = t1
        p.on("V", lambda e, jt=jt: e.tensor_tensor(xscr[:, 256:512], xscr[:, 0:256], grep[:], ALU.mult),
             waits=[tgrep, tones], inc=(sV, 1))
        for k in range(4):
            tvk_last = p.on("V", lambda e, jt=jt, k=k: e.tensor_copy(
                vktm[:, jt * 288 + k * 72: jt * 288 + k * 72 + 64],
                xscr[:, 256 + k * 64:256 + (k + 1) * 64]), inc=(sV, 1))
    tVK = tvk_last

    tprev = None
    for src, base, tdep in ((qkb, 0, tQK), (kkb, 2, tKK)):
        for kp in range(2):
            tq = p.on("V", lambda e, kp=kp, src=src: e.tensor_tensor(
                mscr, src[:, kp * SEQ:(kp + 1) * SEQ], src[:, kp * SEQ:(kp + 1) * SEQ], ALU.mult),
                waits=[tdep, tprev, tVK], inc=(sV, 1))
            tqb = p.on("V", lambda e: e.tensor_copy(EABm[:, 0:SEQ], mscr), inc=(sV, 1))
            for cc in range(4):
                tm_a = p.on("T", lambda e, cc=cc: e.matmul(
                    pb[0][0:1, 0:512], ones_col_b[:, :], EABm[:, cc * 512:(cc + 1) * 512],
                    start=True, stop=True), waits=[tqb, bankfree[0]], inc=(sT, 1))
                tprev = p.on("V", lambda e, kp=kp, cc=cc, base=base: e.reduce_max(
                    gat[0:1, 240 + (base + kp) * 4 + cc:240 + (base + kp) * 4 + cc + 1],
                    pb[0][0:1, 0:512], axis=AX.X), waits=[tm_a], inc=(sV, 1))
                bankfree[0] = tprev
    for kp in range(4):
        tprev = p.on("V", lambda e, kp=kp: e.reduce_max(
            gat[0:1, 230 + kp:231 + kp], gat[0:1, 240 + kp * 4:244 + kp * 4], axis=AX.X),
            waits=[tprev], inc=(sV, 1))
    t_ = p.on("V", lambda e: e.tensor_tensor(gat[0:1, 234:236], gat[0:1, 230:232], gat[0:1, 232:234], ALU.mult),
              waits=[tprev], inc=(sV, 1))
    t_ = p.on("S", lambda e: e.activation(gat[0:1, 236:238], gat[0:1, 234:236], AF.Sqrt),
              waits=[t_], inc=(sS, 1))
    t_ = p.on("V", lambda e: e.tensor_scalar_mul(gat[0:1, 234:236], gat[0:1, 236:238], -1.05),
              waits=[t_], inc=(sV, 1))
    tm_a = p.on("T", lambda e: e.matmul(pb[0][:, 0:2], onesPP[0:1, 0:128], gat[0:1, 234:236],
                                        start=True, stop=True), waits=[t_, bankfree[0]], inc=(sT, 1))
    tMNEG = p.on("V", lambda e: e.tensor_copy(mnegM[:], pb[0][:, 0:2]), waits=[tm_a], inc=(sV, 1))
    bankfree[0] = tMNEG

    tok_pvm = {}
    tok_sbandm = {}
    wo_toks = stream_w5(mwo, 2, pre=tMNEG)
    tmo = [None, None]
    tOUTC_kp = {}

    def emit_wo(ch):
        for mt in range(NDC):
            for cc in range(4):
                par = (mt * 4 + cc) % 2
                bki = 4 + par
                bk = pb[bki]
                w = [wo_toks[ch], tOUTC_kp[ch], tmo[par], bankfree[bki]]
                last = p.on("T", lambda e, mt=mt, cc=cc, ch=ch, bk=bk: e.matmul(
                    bk[:, 0:512], mwbf[:, ch * 1024 + mt * 128: ch * 1024 + (mt + 1) * 128],
                    outc[:, ch * SEQ + cc * 512: ch * SEQ + (cc + 1) * 512],
                    start=True, stop=True),
                    waits=w, inc=(sT, 1))
                if ch == 0:
                    t1 = p.on("V", lambda e, mt=mt, cc=cc, par=par, bk=bk: e.tensor_scalar(
                        mow[par], bk[:, 0:512], vcol("mbo", mt), None, ALU.add),
                        waits=[last, tmo[par]], inc=(sV, 1))
                else:
                    t1 = p.on("V", lambda e, mt=mt, cc=cc, par=par, bk=bk: e.tensor_copy(
                        mow[par], bk[:, 0:512]),
                        waits=[last, tmo[par]], inc=(sV, 1))
                bankfree[bki] = t1
                tmo[par] = p.on("G", lambda e, mt=mt, cc=cc, par=par: e.dma_start(
                    out=out_hbm[mt * 128:(mt + 1) * 128, cc * 512:(cc + 1) * 512],
                    in_=mow[par], accum_op=ALU.add), waits=[t1], dma=True)
    for kp in range(2):
        tok_pvm_prev = tok_pvm
        tok_expm = {}
        tok_pvm = {}
        for iq in range(NIQ):
            def s_and_expm(j, iq=iq, kp=kp, tok_pvm_prev=tok_pvm_prev):
                w = [tQK, tKK, tMNEG, tok_sbandm.get(0)]
                p.on("T", lambda e, j=j, iq=iq, kp=kp: e.matmul(
                    pb[0][:, 0:IQ], kkb[0:64, kp * SEQ + j * 128: kp * SEQ + (j + 1) * 128],
                    qkb[0:64, kp * SEQ + iq * IQ: kp * SEQ + (iq + 1) * IQ],
                    start=True, stop=True, tile_position=(0, 0)), waits=w)
                tmm = p.on("T", lambda e, j=j, iq=iq, kp=kp: e.matmul(
                    pb[1][:, 0:IQ], kkb[64:128, kp * SEQ + j * 128: kp * SEQ + (j + 1) * 128],
                    qkb[64:128, kp * SEQ + iq * IQ: kp * SEQ + (iq + 1) * IQ],
                    start=True, stop=True, tile_position=(64, 0)), inc=(sT, 1))
                if iq == 0:
                    prevpv = tok_pvm_prev.get((NIQ - 1, j))
                else:
                    prevpv = tok_pvm.get((iq - 1, j))
                te = p.on("S", lambda e, j=j, kp=kp: e.activation(
                    EABm[:, j * 512:(j + 1) * 512].rearrange("p (a b) -> p a b", b=IQ),
                    psS[:, 0:1024].rearrange("p (a b) -> p a b", b=512)[:, :, 0:IQ],
                    AF.Exp, bias=mnegM[:, kp:kp + 1]), waits=[tmm, prevpv], inc=(sS, 1))
                tok_expm[(iq, j)] = te
                tok_sbandm[0] = te

            s_and_expm(0)
            s_and_expm(1)
            for j in range(NJT):
                if j + 2 <= NJT - 1:
                    s_and_expm(j + 2)
                tpvm = None
                for hh in range(2):
                    bki = 2 + hh
                    bk = pb[bki]
                    w = [tok_expm[(iq, j)], tVK]
                    if j == 0:
                        w.append(bankfree[bki])
                    tpvm = p.on("T", lambda e, j=j, hh=hh, kp=kp, bk=bk: e.matmul(
                        bk[0:65, 0:IQ],
                        vktm[:, j * 288 + (2 * kp + hh) * 72: j * 288 + (2 * kp + hh) * 72 + 65],
                        EABm[:, j * 512 + hh * IQ: j * 512 + (hh + 1) * IQ],
                        start=(j == 0), stop=(j == NJT - 1)), waits=w, inc=(sT, 1))
                tok_pvm[(iq, j)] = tpvm
            tlast = tok_pvm[(iq, NJT - 1)]
            for hh in range(2):
                outm = outmA if hh == 0 else outmB
                bankfree[2 + hh] = p.on("V", lambda e, hh=hh, iq=iq, outm=outm: e.tensor_copy(
                    outm[:, iq * IQ:(iq + 1) * IQ], pb[2 + hh][0:65, 0:IQ]),
                    waits=[tlast], inc=(sV, 1))
        for hh in range(2):
            k = 2 * kp + hh
            outm = outmA if hh == 0 else outmB
            p.on("S", lambda e, outm=outm: e.activation(outm[64:65, :], outm[64:65, :], AF.Ln),
                 waits=[bankfree[2 + hh]], inc=(sS, 1))
            tiv = p.on("S", lambda e, outm=outm: e.activation(
                outm[64:65, :], outm[64:65, :], AF.Exp, scale=-1.0), inc=(sS, 1))
            for cc in range(4):
                tm_a = p.on("T", lambda e, cc=cc, outm=outm: e.matmul(
                    pb[6][0:64, 0:512], onesPP[64:65, 0:64], outm[64:65, cc * 512:(cc + 1) * 512],
                    start=True, stop=True), waits=[tiv, bankfree[6]], inc=(sT, 1))
                bankfree[6] = p.on("V", lambda e, cc=cc, k=k, outm=outm: e.tensor_tensor(
                    outc[64 * (k % 2):64 * (k % 2) + 64,
                         (k // 2) * SEQ + cc * 512:(k // 2) * SEQ + (cc + 1) * 512],
                    outm[0:64, cc * 512:(cc + 1) * 512], pb[6][0:64, 0:512], ALU.mult),
                    waits=[tm_a], inc=(sV, 1))
        tOUTC_kp[kp] = bankfree[6]
    tOUTC = bankfree[6]
    emit_wo(0)
    emit_wo(1)

    tap("dist", gat[0:1, 0:64], [tdist, ttk, tgk])
    tap("gall", gat[0:1, 192:208], [tga])
    tap("outc0", outc[:, 0:512], [tOUTC])
    tap("qkb0", qkb[:, 0:512], [tQK])
    tap("xnb0", xnb[:, 0:2048], [tXNB])

    p.on("G", None, waits=p.all_dma_toks())
    p.emit()
    return nc



def _consts_blob(Ws):
    c = np.zeros((128, 2048), dtype=np.float32)
    c[:, 0:128] = 1.0
    ist = np.zeros((128, 64), dtype=np.float32)
    for q in range(128):
        ist[q, q % 64] = 1.0
    c[:, 128:192] = ist
    m64 = np.zeros((128, 256), dtype=np.float32)
    for q in range(128):
        for k in range(4):
            m64[q, k * 64 + (q % 64)] = 1.0
    c[:, 192:448] = m64
    Ws = np.asarray(Ws, dtype=np.float32).reshape(-1)
    w1e = np.zeros((128, 64), dtype=np.float32)
    for mt in range(4):
        for q in range(128):
            w1e[q, mt * 16 + 4 * mt + q // 32] = Ws[q % 32]
    c[:, 448:512] = w1e
    w2e = np.zeros((128, 32), dtype=np.float32)
    for dc in range(4):
        for q in range(128):
            w2e[q, dc * 8 + 2 * dc + q // 64] = Ws[32 + q % 64]
    c[:, 512:544] = w2e
    e8 = np.zeros((8, 512), dtype=np.float32)
    for dc in range(4):
        for q in range(128):
            e8[2 * dc + q // 64, dc * 128 + q] = 1.0
    c[0:8, 544:1056] = e8
    esel = np.zeros((16, 128), dtype=np.float32)
    for h in range(16):
        esel[h, h * 8:(h + 1) * 8] = 1.0
    c[0:16, 1056:1184] = esel
    ecol4 = np.zeros((4, 256), dtype=np.float32)
    for k in range(4):
        ecol4[k, k * 64:(k + 1) * 64] = 1.0
    c[0:4, 1184:1440] = ecol4
    e64 = np.zeros((4, 256), dtype=np.float32)
    for t in range(2):
        for q in range(128):
            e64[2 * t + q // 64, t * 128 + q] = 1.0
    c[0:4, 1440:1696] = e64
    rowsel = np.zeros((16, 1024), dtype=np.float32)
    for col in range(1024):
        cc, q = divmod(col, 128)
        rowsel[2 * cc + q // 64, col] = 1.0
    c[32:48, 544:1568] = rowsel
    return c


def _pack_vecs(ins):
    cols = np.zeros((128, 128), dtype=np.float32)
    off = 0

    def add(arr1d, n):
        nonlocal off
        a = np.zeros((n * 128,), dtype=np.float32)
        v = np.asarray(arr1d, dtype=np.float32).reshape(-1)
        a[:v.size] = v
        cols[:, off:off + n] = a.reshape(n, 128).T
        off += n

    add(ins["L0_attn_bq"], 8)
    add(ins["L0_attn_bk"], 8)
    add(ins["L0_attn_bv"], 4)
    add(ins["L0_attn_bqv"], 4)
    add(ins["L0_attn_bf"], 8)
    add(ins["L0_attn_br1"], 16)
    add(ins["L0_attn_br2"], 8)
    add(ins["L0_attn_g1"], 8)
    add(ins["L0_attn_b1"], 8)
    add(ins["L0_attn_g2"], 8)
    add(ins["L0_attn_b2"], 8)
    add(np.full(16, float(np.asarray(ins["L0_attn_bs"]).reshape(-1)[0])), 1)
    add(ins["L0_ffn_br"], 1)
    add(ins["L0_ffn_bo"], 8)
    return cols


def kernel(**ins):
    x = np.asarray(ins["L0_attn_x"], dtype=np.float32)
    B = x.shape[0]
    consts = _consts_blob(ins["L0_attn_Ws"])
    vecs = _pack_vecs(ins)

    def T(a):
        return np.ascontiguousarray(np.asarray(a, dtype=np.float32).T)

    shared = {
        "wqT": T(ins["L0_attn_Wq"]), "wkT": T(ins["L0_attn_Wk"]),
        "wvT": T(ins["L0_attn_Wv"]), "wqvT": T(ins["L0_attn_Wqv"]),
        "wfT": T(ins["L0_attn_Wf"]),
        "wr1T": T(ins["L0_attn_Wr1"]), "wr2T": T(ins["L0_attn_Wr2"]),
        "mwqT": T(ins["L0_ffn_Wq"]), "mwkT": T(ins["L0_ffn_Wk"]), "mwvT": T(ins["L0_ffn_Wv"]),
        "mwo": np.ascontiguousarray(np.asarray(ins["L0_ffn_Wo"], dtype=np.float32)),
        "mwr": np.ascontiguousarray(np.asarray(ins["L0_ffn_Wr"], dtype=np.float32)),
        "vecs": vecs, "consts": consts,
        "bh3": np.ascontiguousarray(np.stack(
            [np.asarray(ins["L0_ffn_bq"], dtype=np.float32).reshape(16, 64),
             np.asarray(ins["L0_ffn_bk"], dtype=np.float32).reshape(16, 64),
             np.asarray(ins["L0_ffn_bv"], dtype=np.float32).reshape(16, 64)], axis=0
        ).transpose(1, 0, 2).reshape(16, 192)),
    }
    in_maps = []
    for b in range(B):
        m = dict(shared)
        m["xT"] = np.ascontiguousarray(x[b].T)
        in_maps.append(m)

    nc = build_program(taps=tuple(_TAPS), maxphase=getattr(kernel, "_MAXPHASE", 5), hpn=getattr(kernel, "_HPN", 8))
    res = run_bass_kernel_spmd(nc, in_maps, core_ids=list(range(B)))
    outs = np.stack([np.asarray(r["out"]) for r in res.results], axis=0)
    if _TAPS:
        kernel.tap_results = [{k: np.asarray(v) for k, v in r.items() if k.startswith("tap_")}
                              for r in res.results]
    return np.ascontiguousarray(outs.transpose(0, 2, 1))


# version 17
# speedup vs baseline: 1.2260x; 1.2260x over previous
import itertools
import numpy as np
from contextlib import ExitStack

import concourse.bass as bass
import concourse.mybir as mybir
from concourse.bass_utils import run_bass_kernel_spmd

F32 = mybir.dt.float32
BF16 = mybir.dt.bfloat16
U32 = mybir.dt.uint32
AF = mybir.ActivationFunctionType
ALU = mybir.AluOpType
AX = mybir.AxisListType

D = 1024
SEQ = 2048
NDC = 8
NJT = 16
IQ = 256
NIQ = SEQ // IQ
EPS = 1e-5
DFF = 2048
ISQ8 = 0.125
ISQ32 = float(1.0 / np.sqrt(32.0))
WIN = 512
NW = SEQ // WIN
ARENA_KB = 194

_TAPS = []
_MAXPHASE = 5
_HPN = 8


class Prog:
    ENGINES = ("Y", "G", "T", "V", "S")

    def __init__(self, nc):
        self.nc = nc
        self.steps = {e: [] for e in self.ENGINES}
        self.semc = {}
        self.chain_sems = {}
        self.es = ExitStack()
        self._uid = 0

    def _nm(self, p):
        self._uid += 1
        return "%s%d" % (p, self._uid)

    def sem(self, name=None):
        h = self.es.enter_context(self.nc.semaphore(name=name or self._nm("sm")))
        self.semc[h] = 0
        return h

    def sb(self, name, shape, dt):
        return self.es.enter_context(self.nc.sbuf_tensor(name, shape, dt))

    def psum(self, name, shape):
        return self.es.enter_context(self.nc.psum_tensor(name, shape, F32))

    def init_dma_pool(self, n=12):
        self.dma_pools = {q: [self.sem("dma%s%d" % (q, i)) for i in range(n if q != "S" else 4)]
                          for q in ("Y", "G", "S")}
        self.dma_prev = {q: [None] * len(self.dma_pools[q]) for q in ("Y", "G", "S")}
        self.dma_rr = {q: 0 for q in ("Y", "G", "S")}
        self.dma_pool = self.dma_pools["Y"] + self.dma_pools["G"] + self.dma_pools["S"]

    def all_dma_toks(self):
        return [self.tok(s) for s in self.dma_pool]

    def on(self, eng, fn, waits=(), inc=None, dma=False):
        w = []
        if fn is not None and eng in self.chain_sems:
            cs = self.chain_sems[eng]
            w.append((cs, self.semc[cs]))
            if inc is None and not dma:
                inc = (cs, 1)
        def _flat(ts):
            for t in ts:
                if t is None:
                    continue
                if isinstance(t, list):
                    _flat(t)
                else:
                    w.append(t)
        _flat(waits)
        if dma:
            pool = self.dma_pools[eng]
            i = self.dma_rr[eng] % len(pool)
            self.dma_rr[eng] += 1
            if self.dma_prev[eng][i] is not None:
                w.append(self.dma_prev[eng][i])
            inc = (pool[i], 16)
        self.steps[eng].append((w, fn, inc))
        if inc is not None:
            s, k = inc
            self.semc[s] += k
            tok = (s, self.semc[s])
            if dma:
                self.dma_prev[eng][i] = tok
            return tok
        return None

    def tok(self, sem):
        return (sem, self.semc[sem])

    def barrier(self, name, dma_sems=()):
        bar = self.sem("bar_" + name)
        dw = [self.tok(s) for s in dma_sems]
        for e in self.ENGINES:
            self.on(e, lambda eng: eng.nop(), waits=dw, inc=(bar, 1))
        t = self.tok(bar)
        for e in self.ENGINES:
            self.on(e, None, waits=[t])
        return t

    def emit(self):
        nc = self.nc
        with nc.Block() as block:
            def run(eng_obj, name):
                seen = {}
                for w, fn, inc in self.steps[name]:
                    for s, v in w:
                        if seen.get(s, -1) < v:
                            eng_obj.wait_ge(s, v)
                            seen[s] = v
                    if fn is not None:
                        ins = fn(eng_obj)
                        if inc is not None:
                            if ins is None:
                                eng_obj.nop().then_inc(*inc)
                            else:
                                ins.then_inc(*inc)
                    elif inc is not None:
                        eng_obj.nop().then_inc(*inc)

            @block.sync
            def _(e):
                run(e, "Y")

            @block.gpsimd
            def _(e):
                run(e, "G")

            @block.tensor
            def _(e):
                run(e, "T")

            @block.vector
            def _(e):
                run(e, "V")

            @block.scalar
            def _(e):
                run(e, "S")


def build_program(taps=(), maxphase=5, hpn=8):
    nc = bass.Bass("TRN2", target_bir_lowering=False)
    p = Prog(nc)

    def din(name, shape, dt=F32):
        return nc.dram_tensor(name, shape, dt, kind="ExternalInput").ap()

    xT = din("xT", [D, SEQ])
    wqT = din("wqT", [D, D]); wkT = din("wkT", [D, D])
    wvT = din("wvT", [D, 512]); wqvT = din("wqvT", [D, 512])
    wfT = din("wfT", [D, D])
    wr1T = din("wr1T", [D, DFF]); wr2T = din("wr2T", [DFF, D])
    mwqT = din("mwqT", [D, D]); mwkT = din("mwkT", [D, D]); mwvT = din("mwvT", [D, D])
    mwo = din("mwo", [256, D]); mwr = din("mwr", [D, 16])
    vecs = din("vecs", [128, 128])
    consts = din("consts", [128, 2048])
    bh3 = din("bh3", [16, 192])
    out_hbm = nc.dram_tensor("out", [D, SEQ], F32, kind="ExternalOutput").ap()

    taps = set(taps)

    p.init_dma_pool(16)
    sG = p.sem("sG")
    sT = p.sem("sT"); sV = p.sem("sV"); sS = p.sem("sS")
    p.chain_sems = {"V": sV, "S": sS, "G": sG}

    psS = p.psum("psS", [128, 1024])
    pb = [psS[:, 0:512], psS[:, 512:1024]] + [p.psum("pb%d" % i, [128, 512]) for i in range(2, 8)]
    bankfree = {i: None for i in range(8)}

    arena = p.sb("arena", [128, ARENA_KB * 512], BF16)

    _audit = []

    def carve(kb, shape, dt):
        cols = shape[-1] * (2 if dt == F32 else 1)
        start = int(kb * 512)
        _audit.append((float(kb), float(kb) + cols / 512.0, tuple(shape), "f32" if dt == F32 else "bf16"))
        assert start + cols <= ARENA_KB * 512, ("carve OOB", kb, shape)
        v = arena[0:shape[0], start:start + cols]
        if dt == F32:
            v = v.bitcast(F32)
        return v
    build_program.audit = _audit

    vecs_t = p.sb("vecs_t", [128, 128], F32)
    onesPP = p.sb("onesPP", [128, 128], F32)
    ones_col_f = p.sb("ones_col_f", [128, 1], F32)
    ones_col_b = p.sb("ones_col_b", [128, 1], BF16)
    E8b = p.sb("E8b", [72, 512], BF16)
    ISTb = p.sb("ISTb", [128, 64], BF16)
    Eselb = p.sb("Eselb", [16, 128], BF16)
    Ecol4b = p.sb("Ecol4b", [4, 256], BF16)
    RowSelb = p.sb("RowSelb", [16, 1024], BF16)
    M64b = p.sb("M64b", [128, 256], BF16)
    E64f = p.sb("E64f", [4, 256], F32)
    w1b = p.sb("w1b", [128, 64], BF16)
    w2b = p.sb("w2b", [128, 32], BF16)
    negM = p.sb("negM", [128, 8], F32)
    nrm = p.sb("nrm", [1, 96], F32)
    bqs = p.sb("bqs", [128, 8], F32)
    bqvs = p.sb("bqvs", [128, 4], F32)
    scr_row = p.sb("scr_row", [33, 512], F32)
    ones8b = p.sb("ones8b", [72, 8], BF16)
    H4T = p.sb("H4T", [4, 16], BF16)
    H4E = p.sb("H4E", [16, 256], BF16)
    gat = p.sb("gat", [16, 512], F32)
    gatb = p.sb("gatb", [4, 16], BF16)
    mi8 = p.sb("mi8", [1, 8], U32)
    gexp = p.sb("gexp", [128, 2], F32)
    gexpQ = p.sb("gexpQ", [128, 2], F32)
    bcpp = p.sb("bcpp", [128, 6], F32)
    bh3t = p.sb("bh3t", [16, 192], F32)
    mnegM = p.sb("mnegM", [128, 2], F32)
    wrf = p.sb("wrf", [128, 16 * NDC], F32)

    _sizes = [("bq", 8), ("bk", 8), ("bv", 4), ("bqv", 4), ("bf", 8), ("br1", 16),
              ("br2", 8), ("g1", 8), ("b1", 8), ("g2", 8), ("b2", 8), ("bs", 1),
              ("mbr", 1), ("mbo", 8)]
    VC = {}
    _off = 0
    for _n, _k in _sizes:
        VC[_n] = (_off, _k)
        _off += _k

    def vcol(name, i=0):
        o, n = VC[name]
        assert i < n
        return vecs_t[:, o + i:o + i + 1]

    def tap(name, ap, waits):
        if name not in taps:
            return
        if not isinstance(ap, bass.AP):
            ap = ap[:, :]
        t = nc.dram_tensor("tap_" + name, [ap.shape[0], ap.shape[1]], ap.dtype,
                           kind="ExternalOutput").ap()
        p.on("G", lambda e, t=t, ap=ap: e.dma_start(out=t[:, :], in_=ap), waits=waits, dma=True)

    cstage = carve(0, [128, 2048], F32)
    t_vec = p.on("Y", lambda e: e.dma_start(out=vecs_t[:], in_=vecs[:, :]), dma=True)
    t_cst = p.on("Y", lambda e: e.dma_start(out=cstage, in_=consts[:, :]), dma=True)
    t_bh3 = p.on("Y", lambda e: e.dma_start(out=bh3t[:], in_=bh3[:, :]), dma=True)

    def vc_cast(dst, src):
        return p.on("V", lambda e, d=dst, s=src: e.tensor_copy(d, s), waits=[t_cst], inc=(sV, 1))

    vc_cast(onesPP[:, :], cstage[:, 0:128])
    vc_cast(ones_col_f[:, :], cstage[:, 0:1])
    vc_cast(ones_col_b[:, :], cstage[:, 0:1])
    vc_cast(ones8b[64:72, :], cstage[0:8, 0:8])
    vc_cast(ISTb[:, :], cstage[:, 128:192])
    vc_cast(M64b[:, :], cstage[:, 192:448])
    vc_cast(w1b[:, :], cstage[:, 448:512])
    vc_cast(w2b[:, :], cstage[:, 512:544])
    vc_cast(E8b[64:72, :], cstage[0:8, 544:1056])
    vc_cast(E8b[32:40, :], cstage[0:8, 544:1056])
    vc_cast(Eselb[:, :], cstage[0:16, 1056:1184])
    vc_cast(Ecol4b[:, :], cstage[0:4, 1184:1440])
    vc_cast(E64f[:, :], cstage[0:4, 1440:1696])
    tCONST = vc_cast(RowSelb[:, :], cstage[32:48, 544:1568])
    tb1 = p.on("G", lambda e: e.tensor_scalar_mul(bqs[:], vecs_t[:, VC["bq"][0]:VC["bq"][0] + 8], ISQ8),
               waits=[t_vec], inc=(sG, 1))
    tb2 = p.on("G", lambda e: e.tensor_scalar_mul(bqvs[:], vecs_t[:, VC["bqv"][0]:VC["bqv"][0] + 4], ISQ32),
               waits=[t_vec], inc=(sG, 1))

    xf = carve(8, [128, NDC * SEQ], F32)
    x2s = carve(72, [128, SEQ], F32)
    xs1 = carve(80, [128, SEQ], F32)
    xs2 = carve(88, [128, SEQ], F32)
    tmpr = carve(96, [1, SEQ], F32)
    tmp2r = carve(104, [1, SEQ], F32)
    r2row = carve(112, [1, SEQ], F32)
    r2rep = carve(120, [128, SEQ], F32)
    m_rep = carve(128, [128, SEQ], F32)
    r_rep = carve(136, [128, SEQ], F32)
    xn2b = carve(144, [128, NDC * SEQ], BF16)
    statrows = carve(186, [65, SEQ], F32)

    tk_x = [p.on("Y", lambda e, c=c: e.dma_start(out=xf[:, c * SEQ:(c + 1) * SEQ],
                                                 in_=xT[c * 128:(c + 1) * 128, :]), dma=True)
            for c in range(NDC)]

    prev_sq = None
    for c in range(NDC):
        t_sq = p.on("V", lambda e, c=c: e.tensor_tensor(
            x2s, xf[:, c * SEQ:(c + 1) * SEQ], xf[:, c * SEQ:(c + 1) * SEQ], ALU.mult),
            waits=[tk_x[c], prev_sq], inc=(sV, 1))
        for cc in range(4):
            p.on("T", lambda e, c=c, cc=cc: e.matmul(
                pb[cc][0:1, 0:512], ones_col_f[:, :],
                xf[:, c * SEQ + cc * 512: c * SEQ + (cc + 1) * 512],
                start=(c == 0), stop=(c == NDC - 1)), waits=[tk_x[c], tCONST])
            prev_sq = p.on("T", lambda e, c=c, cc=cc: e.matmul(
                pb[4 + cc][0:1, 0:512], ones_col_f[:, :], x2s[:, cc * 512:(cc + 1) * 512],
                start=(c == 0), stop=(c == NDC - 1)), waits=[t_sq], inc=(sT, 1))
    tSTAT = prev_sq

    tv = None
    for cc in range(4):
        p.on("V", lambda e, cc=cc: e.tensor_scalar_mul(
            statrows[0:1, cc * 512:(cc + 1) * 512], pb[cc][0:1, 0:512], 1.0 / D),
            waits=[tSTAT], inc=(sV, 1))
        tv = p.on("V", lambda e, cc=cc: e.tensor_scalar_mul(
            tmpr[0:1, cc * 512:(cc + 1) * 512], pb[4 + cc][0:1, 0:512], 1.0 / D),
            waits=[tSTAT], inc=(sV, 1))
    tv = p.on("V", lambda e: e.tensor_tensor(tmp2r, statrows[0:1, :], statrows[0:1, :], ALU.mult),
              waits=[tv], inc=(sV, 1))
    tv = p.on("V", lambda e: e.tensor_tensor(tmpr, tmpr, tmp2r, ALU.subtract), inc=(sV, 1))
    tv = p.on("V", lambda e: e.tensor_scalar_add(tmpr, tmpr, EPS), inc=(sV, 1))
    p.on("S", lambda e: e.activation(tmpr, tmpr, AF.Ln), waits=[tv], inc=(sS, 1))
    tsq_ = p.on("S", lambda e: e.activation(statrows[64:65, :], tmpr, AF.Exp, scale=-0.5),
                inc=(sS, 1))
    tv = p.on("V", lambda e: e.tensor_tensor(tmpr, statrows[64:65, :], statrows[64:65, :], ALU.mult),
              waits=[tsq_], inc=(sV, 1))
    tv = p.on("V", lambda e: e.tensor_scalar(tmpr, tmpr, -EPS, 1.0 + EPS, ALU.mult, ALU.add), inc=(sV, 1))
    p.on("S", lambda e: e.activation(tmpr, tmpr, AF.Ln), waits=[tv], inc=(sS, 1))
    tr2 = p.on("S", lambda e: e.activation(r2row, tmpr, AF.Exp, scale=-0.5), inc=(sS, 1))

    def replicate(dst, src_row, base_part, waits, bank, width=SEQ):
        last = None
        ncc = width // 512
        for cc in range(ncc):
            w = list(waits) + [bankfree[bank], last]
            tm = p.on("T", lambda e, cc=cc, dst=dst, src_row=src_row, base_part=base_part, bank=bank: e.matmul(
                pb[bank][:, 0:512], onesPP[base_part:base_part + 1, 0:128],
                src_row[0:1, cc * 512:(cc + 1) * 512], start=True, stop=True),
                waits=w, inc=(sT, 1))
            last = p.on("V", lambda e, cc=cc, dst=dst, bank=bank: e.tensor_copy(
                dst[:, cc * 512:(cc + 1) * 512], pb[bank][:, 0:512]), waits=[tm], inc=(sV, 1))
        bankfree[bank] = last
        return last

    tmr = replicate(m_rep, statrows[0:1, :], 0, [tSTAT, p.tok(sV)], 0)
    trr = replicate(r_rep, statrows[64:65, :], 64, [tsq_], 1)
    tr2r = replicate(r2rep, r2row, 0, [tr2], 2)

    tXN2 = None
    for c in range(NDC):
        ta = p.on("V", lambda e, c=c: e.tensor_tensor(
            xs1, xf[:, c * SEQ:(c + 1) * SEQ], m_rep, ALU.subtract),
            waits=[tk_x[c], tmr, trr, tr2r, tXN2], inc=(sV, 1))
        p.on("V", lambda e, c=c: e.tensor_tensor(xs2, xs1, r_rep, ALU.mult), inc=(sV, 1))
        tc2 = p.on("V", lambda e, c=c: e.tensor_tensor(xs1, xs2, r2rep, ALU.mult), inc=(sV, 1))
        tXN2 = p.on("S", lambda e, c=c: e.activation(
            xn2b[:, c * SEQ:(c + 1) * SEQ], xs1, AF.Identity,
            bias=vcol("b1", c), scale=vcol("g1", c)), waits=[tc2, t_vec], inc=(sS, 1))
    tap("ISTb", ISTb, [tCONST])
    tap("E8b", E8b[64:72, :], [tCONST])
    tap("xf0", xf[:, 0:2048], [tk_x[0], tXN2])
    tap("ones_col", ones_col_f[:, :], [tCONST])
    tap("xn2b0", xn2b[:, 0:2048], [tXN2])
    tap("statm", statrows[0:1, :], [tsq_])
    tap("statr", statrows[64:65, :], [tsq_])

    p.barrier("p1", dma_sems=p.dma_pool)

    Qb = carve(0, [128, NDC * SEQ], BF16)
    Kb = carve(32, [128, NDC * SEQ], BF16)
    Vtm = carve(64, [128, NJT * 512], BF16)
    wstage = carve(80, [128, 2048], F32)
    wbf = carve(88, [128, 8192], BF16)
    wbf2 = carve(104, [128, 8192], BF16)
    qvb = carve(120, [128, 4 * SEQ], BF16)
    bvrow = carve(136, [1, 512], F32)
    bvrep = carve(138, [128, 512], F32)
    sTt = carve(140, [16, SEQ], BF16)

    wslot_free = [None, None]

    def stream_w(hbm, nchunks, width, dst, stage, pre=None, eng="G"):
        toks = []
        sem = {"G": sG, "S": sS, "V": sV}[eng]
        for c in range(nchunks):
            s = c % 2
            w = [wslot_free[s]]
            if pre is not None and c < 2:
                w.append(pre)
            tl = p.on("Y", lambda e, c=c, s=s, hbm=hbm, width=width, stage=stage: e.dma_start(
                out=stage[:, s * 1024:s * 1024 + width], in_=hbm[c * 128:(c + 1) * 128, :]),
                waits=w, dma=True)
            if eng == "S":
                tc_ = p.on("S", lambda e, c=c, s=s, dst=dst, width=width, stage=stage: e.copy(
                    dst[:, c * width:(c + 1) * width], stage[:, s * 1024:s * 1024 + width]),
                    waits=[tl], inc=(sem, 1))
            else:
                tc_ = p.on(eng, lambda e, c=c, s=s, dst=dst, width=width, stage=stage: e.tensor_copy(
                    dst[:, c * width:(c + 1) * width], stage[:, s * 1024:s * 1024 + width]),
                    waits=[tl], inc=(sem, 1))
            wslot_free[s] = tc_
            toks.append(tc_)
        return toks

    def fm_proj(wtoks, wtile, wwidth, nmt, evac):
        out_toks = []
        bank_i = 0
        for mt in range(nmt):
            for cc in range(4):
                bki = 4 + (bank_i % 4)
                bk = pb[bki]
                last = None
                for dc in range(NDC):
                    w = [wtoks[dc], tXN2]
                    if dc == 0:
                        w.append(bankfree[bki])
                    last = p.on("T", lambda e, mt=mt, cc=cc, dc=dc, bk=bk, wtile=wtile, wwidth=wwidth: e.matmul(
                        bk[:, 0:512],
                        wtile[:, dc * wwidth + mt * 128: dc * wwidth + (mt + 1) * 128],
                        xn2b[:, dc * SEQ + cc * 512: dc * SEQ + (cc + 1) * 512],
                        start=(dc == 0), stop=(dc == NDC - 1)),
                        waits=w, inc=(sT, 1) if dc == NDC - 1 else None)
                te = evac(mt, cc, bk, last)
                bankfree[bki] = te
                out_toks.append(te)
                bank_i += 1
        return out_toks

    wq_toks = stream_w(wqT, 8, 1024, wbf, wstage, eng="S")
    tQ = fm_proj(wq_toks, wbf, 1024, 8, lambda mt, cc, bk, tk: p.on(
        "V", lambda e, mt=mt, cc=cc, bk=bk: e.tensor_scalar(
            Qb[:, mt * SEQ + cc * 512: mt * SEQ + (cc + 1) * 512], bk[:, 0:512],
            ISQ8, bqs[:, mt:mt + 1], ALU.mult, ALU.add),
        waits=[tk, tb1], inc=(sV, 1)))

    wk_toks = stream_w(wkT, 8, 1024, wbf2, wstage)
    tK = fm_proj(wk_toks, wbf2, 1024, 8, lambda mt, cc, bk, tk: p.on(
        "V", lambda e, mt=mt, cc=cc, bk=bk: e.tensor_scalar(
            Kb[:, mt * SEQ + cc * 512: mt * SEQ + (cc + 1) * 512], bk[:, 0:512],
            vcol("bk", mt), None, ALU.add),
        waits=[tk], inc=(sV, 1)))

    tbv = [p.on("G", lambda e, c=c: e.dma_start(
        out=bvrow[0:1, c * 128:(c + 1) * 128],
        in_=vecs_t[:, VC["bv"][0] + c:VC["bv"][0] + c + 1]), waits=[t_vec], dma=True)
        for c in range(4)]
    tm_ = p.on("T", lambda e: e.matmul(pb[0][:, 0:512], onesPP[0:1, 0:128], bvrow[0:1, :],
                                       start=True, stop=True), waits=tbv + [bankfree[0]], inc=(sT, 1))
    tbvrep = p.on("V", lambda e: e.tensor_copy(bvrep, pb[0][:, 0:512]), waits=[tm_], inc=(sV, 1))
    bankfree[0] = tbvrep

    wv_toks = stream_w(wvT, 8, 512, wbf, wstage, pre=tQ[-1], eng="S")
    tV = None
    for jt in range(NJT):
        bki = 2 + (jt % 2)
        bk = pb[bki]
        last = None
        for dc in range(NDC):
            w = [wv_toks[dc], tXN2]
            if dc == 0:
                w.append(bankfree[bki])
            last = p.on("T", lambda e, jt=jt, dc=dc, bk=bk: e.matmul(
                bk[:, 0:512],
                xn2b[:, dc * SEQ + jt * 128: dc * SEQ + (jt + 1) * 128],
                wbf[:, dc * 512:(dc + 1) * 512],
                start=(dc == 0), stop=(dc == NDC - 1)),
                waits=w, inc=(sT, 1) if dc == NDC - 1 else None)
        tV = p.on("V", lambda e, jt=jt, bk=bk: e.tensor_tensor(
            Vtm[:, jt * 512:(jt + 1) * 512], bk[:, 0:512], bvrep, ALU.add),
            waits=[last, tbvrep], inc=(sV, 1))
        bankfree[bki] = tV

    wqv_toks = stream_w(wqvT, 8, 512, wbf2, wstage, pre=tK[-1])
    tQV = fm_proj(wqv_toks, wbf2, 512, 4, lambda mt, cc, bk, tk: p.on(
        "V", lambda e, mt=mt, cc=cc, bk=bk: e.tensor_scalar(
            qvb[:, mt * SEQ + cc * 512: mt * SEQ + (cc + 1) * 512], bk[:, 0:512],
            ISQ32, bqvs[:, mt:mt + 1], ALU.mult, ALU.add),
        waits=[tk, tb2], inc=(sV, 1)))

    tS_T = None
    for cc2 in range(4):
        last = None
        for mt in range(4):
            w = [tQV[-1], tCONST]
            if mt == 0:
                w += [bankfree[1], tS_T]
            last = p.on("T", lambda e, cc2=cc2, mt=mt: e.matmul(
                pb[1][0:16, 0:512], w1b[:, mt * 16:(mt + 1) * 16],
                qvb[:, mt * SEQ + cc2 * 512: mt * SEQ + (cc2 + 1) * 512],
                start=(mt == 0), stop=(mt == 3)),
                waits=w, inc=(sT, 1) if mt == 3 else None)
        tS_T = p.on("V", lambda e, cc2=cc2: e.tensor_scalar(
            sTt[:, cc2 * 512:(cc2 + 1) * 512], pb[1][0:16, 0:512],
            vcol("bs", 0)[0:16, :], None, ALU.add), waits=[last], inc=(sV, 1))
        bankfree[1] = tS_T

    sqs = wbf[:, 0:2048]
    tprev = None
    for src, base, tdep in ((Qb, 0, tQ[-1]), (Kb, 32, tK[-1])):
        for hp in range(8):
            tq = p.on("V", lambda e, hp=hp, src=src: e.tensor_tensor(
                sqs, src[:, hp * SEQ:(hp + 1) * SEQ], src[:, hp * SEQ:(hp + 1) * SEQ], ALU.mult),
                waits=[tdep, tV, tprev], inc=(sV, 1))
            for cc in range(4):
                tm2 = p.on("T", lambda e, cc=cc: e.matmul(
                    pb[0][0:1, 0:512], ones_col_b[:, :], sqs[:, cc * 512:(cc + 1) * 512],
                    start=True, stop=True), waits=[tq, bankfree[0]], inc=(sT, 1))
                tprev = p.on("V", lambda e, hp=hp, cc=cc, base=base: e.reduce_max(
                    nrm[0:1, base + hp * 4 + cc: base + hp * 4 + cc + 1],
                    pb[0][0:1, 0:512], axis=AX.X), waits=[tm2], inc=(sV, 1))
                bankfree[0] = tprev
    tlastn = tprev
    for hp in range(8):
        p.on("V", lambda e, hp=hp: e.reduce_max(
            nrm[0:1, 64 + hp:64 + hp + 1], nrm[0:1, hp * 4:hp * 4 + 4], axis=AX.X),
            waits=[tlastn], inc=(sV, 1))
        tprev = p.on("V", lambda e, hp=hp: e.reduce_max(
            nrm[0:1, 72 + hp:72 + hp + 1], nrm[0:1, 32 + hp * 4:32 + hp * 4 + 4], axis=AX.X),
            inc=(sV, 1))
    t_ = p.on("V", lambda e: e.tensor_tensor(nrm[0:1, 80:88], nrm[0:1, 64:72], nrm[0:1, 72:80], ALU.mult),
              waits=[tprev], inc=(sV, 1))
    t_ = p.on("S", lambda e: e.activation(nrm[0:1, 88:96], nrm[0:1, 80:88], AF.Sqrt),
              waits=[t_], inc=(sS, 1))
    t_ = p.on("V", lambda e: e.tensor_scalar_mul(nrm[0:1, 80:88], nrm[0:1, 88:96], -1.05),
              waits=[t_], inc=(sV, 1))
    tm2 = p.on("T", lambda e: e.matmul(pb[0][:, 0:8], onesPP[0:1, 0:128], nrm[0:1, 80:88],
                                       start=True, stop=True), waits=[t_, bankfree[0]], inc=(sT, 1))
    tNEGM = p.on("V", lambda e: e.tensor_copy(negM[:], pb[0][:, 0:8]), waits=[tm2], inc=(sV, 1))
    bankfree[0] = tNEGM

    tap("Qb0", Qb[:, 0:2048], [tQ[-1]])
    tap("Kb0", Kb[:, 0:2048], [tK[-1]])
    tap("Vtm0", Vtm[:, 0:512], [tV])
    tap("sTt", sTt, [tS_T])
    tap("negM", negM, [tNEGM])

    p.barrier("p2", dma_sems=p.dma_pool)

    if maxphase <= 2:
        p.on("G", None, waits=p.all_dma_toks())
        p.emit()
        return nc

    EAB = carve(80, [128, NJT * 512], BF16)
    outTA = carve(96, [128, 4 * SEQ], BF16)
    outTB = carve(112, [128, 4 * SEQ], BF16)
    prodb = [carve(132, [128, 512], BF16), carve(133, [128, 512], BF16)]
    def carveR(row, kb, shape, dt):
        cols = shape[-1] * (2 if dt == F32 else 1)
        start = int(kb * 512)
        v = arena[row:row + shape[0], start:start + cols]
        return v.bitcast(F32) if dt == F32 else v
    _rowp = {0: 64, 1: 32}
    cS_m = {(hh, par): carveR(_rowp[par], 134 + 2 * hh, [8, 512], F32)
            for hh in (0, 1) for par in (0, 1)}
    ivd_m = {(0, 0): carveR(64, 144, [8, 512], F32), (1, 0): carveR(64, 146, [8, 512], F32),
             (0, 1): carveR(32, 128, [8, 512], F32), (1, 1): carveR(32, 130, [8, 512], F32)}
    ivr_m = {(0, 0): carveR(64, 148, [8, 512], F32), (1, 0): carveR(64, 150, [8, 512], F32),
             (0, 1): carveR(32, 140, [8, 512], F32), (1, 1): carveR(32, 142, [8, 512], F32)}
    expcB_m = {(0, 0): carveR(64, 184, [8, 512], BF16), (1, 0): carveR(64, 185, [8, 512], BF16),
               (0, 1): carveR(64, 130, [8, 512], BF16), (1, 1): carveR(64, 131, [8, 512], BF16)}
    compfs_m = {(0, 0): carveR(64, 138, [8, 512], BF16), (1, 0): carveR(64, 139, [8, 512], BF16),
                (0, 1): carveR(32, 138, [8, 512], BF16), (1, 1): carveR(32, 139, [8, 512], BF16),
                (0, 2): carveR(64, 128, [8, 512], BF16), (1, 2): carveR(64, 129, [8, 512], BF16),
                (0, 3): carveR(32, 184, [8, 512], BF16), (1, 3): carveR(32, 185, [8, 512], BF16)}
    denAB = carve(144, [33, SEQ], F32)
    finall = carve(152, [128, NDC * SEQ], BF16)

    wfb = carve(0, [128, 8192], BF16)
    wstage4 = carve(16, [128, 2048], F32)
    wf_toks = None

    tok_pv = {}
    tok_sband = {}
    tFIN = None
    gate_out = {}
    gate_den = {}
    epi = {"bkr": {7: None, 0: None, 3: None, 1: None}, "bkfinr": {4: None, 5: None},
           "crepfree": {0: [None, None], 1: [None, None]},
           "prodfree": {0: None, 1: None}, "foldfree": {0: None, 1: None}, "tfin": None}

    def emit_epilogue(hp, evdc, denv):
        epi["foldfree"][0] = denv
        epi["foldfree"][1] = evdc[0]

        def comp_chain(hh, cc2, bank):
            head = 2 * hp + hh
            par = cc2 % 2
            outT = outTA if hh == 0 else outTB
            sl = slice(cc2 * 512, (cc2 + 1) * 512)
            ivd = ivd_m[(hh, par)]
            ivr = ivr_m[(hh, par)]
            cS = cS_m[(hh, par)]
            expcB = expcB_m[(hh, par)]
            compfs = compfs_m[(hh, cc2)]
            st = {}
            bk = pb[bank]

            def s0():
                last = None
                for dc in range(4):
                    w = [evdc[dc]]
                    if dc == 0:
                        w.append(epi["bkr"][bank])
                        if bank == 3:
                            w.append(evdc[1])
                    last = p.on("T", lambda e, dc=dc: e.matmul(
                        bk[0:8, 0:512], w2b[:, dc * 8:(dc + 1) * 8],
                        outT[:, dc * SEQ + sl.start: dc * SEQ + sl.stop],
                        start=(dc == 0), stop=(dc == 3)),
                        waits=w, inc=(sT, 1) if dc == 3 else None)
                st["c"] = last

            def s1():
                st["t1"] = p.on("V", lambda e: e.tensor_copy(cS, bk[0:8, 0:512]),
                                waits=[st["c"]], inc=(sV, 1))

            def s2():
                st["tsr"] = p.on("T", lambda e: e.matmul(
                    bk[32:40, 0:512], Eselb[:, head * 8:(head + 1) * 8], sTt[:, sl],
                    start=True, stop=True), waits=[tS_T, st["t1"]], inc=(sT, 1))
                st["tir"] = p.on("T", lambda e: e.matmul(
                    bk[64:72, 0:512], onesPP[32 * hh:32 * hh + 1, 0:8],
                    denAB[32 * hh:32 * hh + 1, sl],
                    start=True, stop=True), waits=[denv], inc=(sT, 1))
                gate_den.setdefault(cc2, [None, None])[hh] = st["tir"]

            def s2b():
                p.on("S", lambda e: e.activation(ivd, bk[64:72, 0:512], AF.Ln),
                     waits=[st["tir"]], inc=(sS, 1))
                st["trec"] = p.on("S", lambda e: e.activation(ivd, ivd, AF.Exp, scale=-1.0),
                                  inc=(sS, 1))

            def s3():
                p.on("V", lambda e: e.tensor_tensor(cS, cS, ivd, ALU.mult),
                     waits=[st["trec"]], inc=(sV, 1))
                st["t3"] = p.on("V", lambda e: e.tensor_tensor(
                    cS, cS, bk[32:40, 0:512], ALU.add), waits=[st["tsr"]], inc=(sV, 1))

            def s4():
                st["t4"] = p.on("S", lambda e: e.activation(cS, cS, AF.Exp),
                                waits=[st["t3"]], inc=(sS, 1))
                st["t4b"] = p.on("S", lambda e: e.copy(expcB, cS),
                                 waits=[st["t4"]], inc=(sS, 1))

            def s5():
                st["t5"] = p.on("T", lambda e: e.matmul(
                    bk[32:40, 0:512], ones8b[64:72, :], expcB,
                    start=True, stop=True), waits=[st["t4b"], st["t3"]], inc=(sT, 1))

            def s6():
                p.on("S", lambda e: e.activation(ivr, bk[32:40, 0:512], AF.Ln),
                     waits=[st["t5"]], inc=(sS, 1))
                st["trecr"] = p.on("S", lambda e: e.activation(ivr, ivr, AF.Exp, scale=-1.0),
                                   inc=(sS, 1))

            def s7():
                st["tc1"] = p.on("G", lambda e: e.tensor_tensor(
                    cS, cS, ivd, ALU.mult),
                    waits=[st["t4b"]], inc=(sG, 1))

            def s8():
                st["tcf"] = p.on("G", lambda e: e.tensor_tensor(
                    compfs[:, :], cS, ivr, ALU.mult),
                    waits=[st["trecr"], st["tc1"]], inc=(sG, 1))
                epi["bkr"][bank] = st["tcf"]

            return [s0, s1, s2, s2b, s3, s4, s5, s6, s7, s8], st

        def fin_chain(hh, cc2, st):
            outT = outTA if hh == 0 else outTB
            sl = slice(cc2 * 512, (cc2 + 1) * 512)
            compfs = compfs_m[(hh, cc2)]
            e8r = _rowp[cc2 % 2]
            rbank = 4 + hh
            steps = []
            tprs = []

            def mk_rep_prod(dc):
                def f():
                    w = [st["tcf"], epi["bkfinr"][rbank]]
                    trep = p.on("T", lambda e: e.matmul(
                        pb[rbank][:, 0:512], E8b[e8r:e8r + 8, dc * 128:(dc + 1) * 128], compfs[:, :],
                        start=True, stop=True), waits=w, inc=(sT, 1))
                    pbuf = prodb[hh]
                    tpr = p.on("V", lambda e: e.tensor_tensor(
                        pbuf[:, :], outT[:, dc * SEQ + sl.start: dc * SEQ + sl.stop],
                        pb[rbank][:, 0:512], ALU.mult),
                        waits=[trep, epi["prodfree"][hh]], inc=(sV, 1))
                    tprs.append(tpr)
                    epi["bkfinr"][rbank] = tpr
                    fb = 6 if hh == 0 else 2
                    w2 = [tpr]
                    if dc == 0:
                        w2.append(epi["foldfree"][hh])
                    tff = p.on("T", lambda e, dc=dc, fb=fb: e.matmul(
                        pb[fb][0:64, 0:512], ISTb[:, :], pbuf,
                        start=(dc == 0), stop=(dc == 3)), waits=w2, inc=(sT, 1))
                    epi["prodfree"][hh] = tff
                    st["tff"] = tff
                return f

            for dc in range(4):
                steps.append(mk_rep_prod(dc))

            def s_ev():
                fb = 6 if hh == 0 else 2
                tfin = p.on("V", lambda e, fb=fb: e.tensor_copy(
                    finall[64 * hh:64 * (hh + 1), hp * SEQ + sl.start: hp * SEQ + sl.stop],
                    pb[fb][0:64, 0:512]), waits=[st["tff"]], inc=(sV, 1))
                epi["foldfree"][hh] = tfin
                epi["tfin"] = tfin
                if hh == 1:
                    gate_out[cc2] = tprs[-1]
            steps.append(s_ev)
            return steps

        def zrun(chains):
            for ops in itertools.zip_longest(*chains):
                for op in ops:
                    if op:
                        op()
        metas = {}
        for wave in (0, 1):
            comps = []
            for cc2 in (2 * wave, 2 * wave + 1):
                ca, sta = comp_chain(0, cc2, (7, 3)[cc2 % 2])
                cb, stb = comp_chain(1, cc2, (0, 1)[cc2 % 2])
                comps += [ca, cb]
                metas[cc2] = (sta, stb)
            if wave == 1:
                comps += [fin_chain(0, 0, metas[0][0]), fin_chain(1, 0, metas[0][1])]
            zrun(comps)
        for cc2 in range(1, 4):
            sta, stb = metas[cc2]
            zrun([fin_chain(0, cc2, sta), fin_chain(1, cc2, stb)])
        return epi["tfin"]

    for hp in range(hpn):
        tok_pv_prevhp = tok_pv
        tok_exp = {}
        tok_pv = {}
        pending_evac = None
        for iq in range(NIQ):
            def s_and_exp(j, iq=iq, hp=hp, tok_pv_prevhp=tok_pv_prevhp):
                w = [tQ[-1], tK[-1], tNEGM, tok_sband.get(0), bankfree[0], bankfree[1]]
                p.on("T", lambda e, j=j, iq=iq, hp=hp: e.matmul(
                    pb[0][:, 0:IQ],
                    Kb[0:64, hp * SEQ + j * 128: hp * SEQ + (j + 1) * 128],
                    Qb[0:64, hp * SEQ + iq * IQ: hp * SEQ + (iq + 1) * IQ],
                    start=True, stop=True, tile_position=(0, 0)), waits=w)
                tmm = p.on("T", lambda e, j=j, iq=iq, hp=hp: e.matmul(
                    pb[1][:, 0:IQ],
                    Kb[64:128, hp * SEQ + j * 128: hp * SEQ + (j + 1) * 128],
                    Qb[64:128, hp * SEQ + iq * IQ: hp * SEQ + (iq + 1) * IQ],
                    start=True, stop=True, tile_position=(64, 0)), inc=(sT, 1))
                if iq == 0:
                    prevpv = tok_pv_prevhp.get((NIQ - 1, j))
                else:
                    prevpv = tok_pv.get((iq - 1, j))
                te = p.on("S", lambda e, j=j, hp=hp: e.activation(
                    EAB[:, j * 512:(j + 1) * 512].rearrange("p (a b) -> p a b", b=IQ),
                    psS[:, 0:1024].rearrange("p (a b) -> p a b", b=512)[:, :, 0:IQ],
                    AF.Exp, bias=negM[:, hp:hp + 1]), waits=[tmm, prevpv], inc=(sS, 1))
                tok_exp[(iq, j)] = te
                tok_sband[0] = te

            s_and_exp(0)
            s_and_exp(1)
            if pending_evac is not None:
                pending_evac()
                pending_evac = None
            for j in range(NJT):
                if j + 2 <= NJT - 1:
                    s_and_exp(j + 2)
                for dc in range(4):
                    bki = 2 + dc
                    bk = pb[bki]
                    w = [tok_exp[(iq, j)], tV]
                    if j == 0:
                        w.append(bankfree[bki])
                    p.on("T", lambda e, j=j, dc=dc, bk=bk: e.matmul(
                        bk[:, 0:2 * IQ],
                        Vtm[:, j * 512 + dc * 128: j * 512 + (dc + 1) * 128],
                        EAB[:, j * 512: (j + 1) * 512],
                        start=(j == 0), stop=(j == NJT - 1)), waits=w)
                w = [tok_exp[(iq, j)]]
                if j == 0:
                    w.append(bankfree[6])
                tok_pv[(iq, j)] = p.on("T", lambda e, j=j: e.matmul(
                    pb[6][0:1, 0:2 * IQ], ones_col_b[:, :], EAB[:, j * 512: (j + 1) * 512],
                    start=(j == 0), stop=(j == NJT - 1)), waits=w, inc=(sT, 1))
            tlast = tok_pv[(iq, NJT - 1)]
            go = gate_out.get(iq // 2)
            gd = gate_den.get(iq // 2)
            w = [tlast] + (gd if gd else [])
            p.on("V", lambda e, iq=iq: e.tensor_copy(
                denAB[0:1, iq * IQ:(iq + 1) * IQ], pb[6][0:1, 0:IQ]), waits=w, inc=(sV, 1))
            bankfree[6] = p.on("V", lambda e, iq=iq: e.tensor_copy(
                denAB[32:33, iq * IQ:(iq + 1) * IQ], pb[6][0:1, IQ:2 * IQ]),
                inc=(sV, 1))
            def mk_evac(iq=iq, tlast=tlast, go=go):
                def f():
                    for dc in range(4):
                        bk = pb[2 + dc]
                        ta = p.on("S", lambda e, dc=dc, iq=iq, bk=bk: e.copy(
                            outTA[:, dc * SEQ + iq * IQ: dc * SEQ + (iq + 1) * IQ], bk[:, 0:IQ]),
                            waits=[tlast, go], inc=(sS, 1))
                        tb = p.on("V", lambda e, dc=dc, iq=iq, bk=bk: e.tensor_copy(
                            outTB[:, dc * SEQ + iq * IQ: dc * SEQ + (iq + 1) * IQ], bk[:, IQ:2 * IQ]),
                            waits=[tlast, go], inc=(sV, 1))
                        bankfree[2 + dc] = [ta, tb]
                return f
            pending_evac = mk_evac()
        pending_evac()
        pending_evac = None
        tFIN = emit_epilogue(hp, [bankfree[2 + dc] for dc in range(4)], bankfree[6])
        bankfree[0] = epi["bkr"][0]
        bankfree[1] = epi["bkr"][1]
        bankfree[2] = epi["foldfree"][1]
        bankfree[3] = epi["bkr"][3]
        bankfree[4] = epi["bkfinr"][4]
        bankfree[5] = epi["bkfinr"][5]
        bankfree[6] = epi["foldfree"][0]
        if hp == 5 and maxphase >= 4:
            wslot_free[0] = wslot_free[1] = None
            wf_toks = stream_w(wfT, 8, 1024, wfb, wstage4, pre=p.tok(sT), eng="S")

    tap("finall0", finall[:, 0:2048], [tFIN])
    tap("outTB2", outTB[:, 2 * SEQ:3 * SEQ], [p.tok(sV)])
    tap("outTBf", outTB, [p.tok(sV)])
    tap("outTAf", outTA, [p.tok(sV)])
    tap("EABlast", EAB[:, 0:1024], [p.tok(sV)])
    tap("denA", denAB[0:1, :], [p.tok(sV)])
    tap("denB", denAB[32:33, :], [p.tok(sV)])
    tap("outTA0", outTA[:, 0:2048], [p.tok(sV)])

    p.barrier("p3", dma_sems=p.dma_pool)

    if maxphase <= 3:
        p.on("G", None, waits=p.all_dma_toks())
        p.emit()
        return nc

    mrepw = carve(24, [128, WIN], F32)
    rrepw = carve(26, [128, WIN], F32)
    xw = [carve(28, [128, WIN], F32), carve(30, [128, WIN], F32)]
    wr1b = carve(32, [128, 16384], BF16)
    wr2b = carve(64, [128, 16384], BF16)
    amidw = carve(96, [128, NDC * WIN], F32)
    h1w = carve(112, [128, 16 * WIN], BF16)
    xn2w = carve(128, [128, NDC * WIN], BF16)
    outw = [carve(136, [128, WIN], F32), carve(138, [128, WIN], F32)]
    sqw = carve(140, [128, WIN], F32)
    m2row = carve(142, [1, WIN], F32)
    vtmp = carve(144, [1, WIN], F32)
    r2wrow = carve(146, [1, WIN], F32)
    m2rep = carve(148, [128, WIN], F32)
    r2wrep = carve(150, [128, WIN], F32)
    tmp4 = carve(184, [128, WIN], F32)

    if wf_toks is None:
        wslot_free[0] = wslot_free[1] = None
        wf_toks = stream_w(wfT, 8, 1024, wfb, wstage4, eng="S")
    wslot_free[0] = wslot_free[1] = None
    wr1_toks = []
    for c in range(8):
        tl = p.on("Y", lambda e, c=c: e.dma_start(
            out=wstage4[:, 0:2048], in_=wr1T[c * 128:(c + 1) * 128, :]),
            waits=[wslot_free[0], wslot_free[1]], dma=True)
        tc_ = p.on("S", lambda e, c=c: e.copy(
            wr1b[:, c * 2048:(c + 1) * 2048], wstage4[:, 0:2048]), waits=[tl], inc=(sS, 1))
        wslot_free[0] = wslot_free[1] = tc_
        wr1_toks.append(tc_)
    wr2_toks = stream_w(wr2T, 16, 1024, wr2b, wstage4)

    tow = [None, None]
    th1 = {}
    txn2w_last = None
    wr1done_prev = None
    fsave = {}
    for w_i in range(NW + 1):
        if w_i < NW:
            w0 = w_i * WIN
            fpP = (4, 5) if w_i % 2 == 0 else (6, 7)
            tmrw = replicate(mrepw, statrows[0:1, w0:w0 + WIN], 0, [], 0, width=WIN)
            trrw = replicate(rrepw, statrows[64:65, w0:w0 + WIN], 64, [], 1, width=WIN)
            tamid = {}
            for dc in range(NDC):
                bki = fpP[dc % 2]
                bk = pb[bki]
                last = None
                for fc in range(NDC):
                    wts = [wf_toks[fc], tFIN]
                    if fc == 0:
                        wts.append(bankfree[bki])
                    last = p.on("T", lambda e, dc=dc, fc=fc, w0=w0, bk=bk: e.matmul(
                        bk[:, 0:WIN], wfb[:, fc * 1024 + dc * 128: fc * 1024 + (dc + 1) * 128],
                        finall[:, fc * SEQ + w0: fc * SEQ + w0 + WIN],
                        start=(fc == 0), stop=(fc == NDC - 1)),
                        waits=wts, inc=(sT, 1) if fc == NDC - 1 else None)
                txl = p.on("S", lambda e, dc=dc, w0=w0: e.dma_start(
                    out=xw[dc % 2], in_=xT[dc * 128:(dc + 1) * 128, w0:w0 + WIN]),
                    waits=[tow[dc % 2]], dma=True)
                t1 = p.on("G", lambda e, dc=dc: e.tensor_tensor(
                    tmp4, xw[dc % 2], mrepw, ALU.subtract),
                    waits=[txl, tmrw, trrw, txn2w_last], inc=(sG, 1))
                t2g = p.on("G", lambda e, dc=dc: e.tensor_tensor(tmp4, tmp4, rrepw, ALU.mult),
                           inc=(sG, 1))
                t3 = p.on("V", lambda e, dc=dc, bk=bk: e.tensor_scalar(
                    sqw, bk[:, 0:WIN], vcol("bf", dc), None, ALU.add),
                    waits=[last] + ([txn2w_last] if dc == 0 else []), inc=(sV, 1))
                bankfree[bki] = t3
                t4 = p.on("V", lambda e, dc=dc: e.tensor_tensor(
                    amidw[:, dc * WIN:(dc + 1) * WIN], tmp4, sqw, ALU.add),
                    waits=[t2g], inc=(sV, 1))
                t5 = p.on("V", lambda e, dc=dc: e.tensor_tensor(
                    outw[dc % 2], xw[dc % 2], amidw[:, dc * WIN:(dc + 1) * WIN], ALU.add),
                    waits=[tow[dc % 2]], inc=(sV, 1))
                _src = (lambda dc: amidw[:, dc * WIN:(dc + 1) * WIN]) if getattr(build_program, "_plain_amid_only", False) else (lambda dc: outw[dc % 2])
                tst = p.on("G", lambda e, dc=dc, w0=w0, _src=_src: e.dma_start(
                    out=out_hbm[dc * 128:(dc + 1) * 128, w0:w0 + WIN], in_=_src(dc)),
                    waits=[t5], dma=True)
                tow[dc % 2] = tst
                tamid[dc] = t4
            fsave[w_i] = (tamid, None, fpP, w0)
        if w_i >= 1:
            for mt in range(16):
                bki = (2, 3)[mt % 2]
                bk = pb[bki]
                last = None
                for dc in range(NDC):
                    wts = [wr1_toks[dc], txn2w[dc]]
                    if dc == 0:
                        wts.append(bankfree[bki])
                    last = p.on("T", lambda e, mt=mt, dc=dc, bk=bk: e.matmul(
                        bk[:, 0:WIN], wr1b[:, dc * 2048 + mt * 128: dc * 2048 + (mt + 1) * 128],
                        xn2w[:, dc * WIN:(dc + 1) * WIN],
                        start=(dc == 0), stop=(dc == NDC - 1)),
                        waits=wts, inc=(sT, 1) if dc == NDC - 1 else None)
                th1[mt] = p.on("S", lambda e, mt=mt, bk=bk: e.activation(
                    h1w[:, mt * WIN:(mt + 1) * WIN], bk[:, 0:WIN], AF.Relu,
                    bias=vcol("br1", mt)), waits=[last], inc=(sS, 1))
                bankfree[bki] = th1[mt]
            wr1done_prev = th1[15]
        if w_i < NW:
            tamid = fsave[w_i][0]
            tprev = None
            for dc in range(NDC):
                tsq2 = p.on("V", lambda e, dc=dc: e.tensor_tensor(
                    sqw, amidw[:, dc * WIN:(dc + 1) * WIN], amidw[:, dc * WIN:(dc + 1) * WIN], ALU.mult),
                    waits=[tamid[dc], tprev], inc=(sV, 1))
                w = [tamid[dc]]
                if dc == 0:
                    w += [bankfree[0], bankfree[1]]
                p.on("T", lambda e, dc=dc: e.matmul(
                    pb[0][0:1, 0:WIN], ones_col_f[:, :], amidw[:, dc * WIN:(dc + 1) * WIN],
                    start=(dc == 0), stop=(dc == NDC - 1)), waits=w)
                tprev = p.on("T", lambda e, dc=dc: e.matmul(
                    pb[1][0:1, 0:WIN], ones_col_f[:, :], sqw,
                    start=(dc == 0), stop=(dc == NDC - 1)), waits=[tsq2], inc=(sT, 1))
            tv = p.on("V", lambda e: e.tensor_scalar_mul(m2row, pb[0][0:1, 0:WIN], 1.0 / D),
                      waits=[tprev], inc=(sV, 1))
            tv = p.on("V", lambda e: e.tensor_scalar_mul(vtmp, pb[1][0:1, 0:WIN], 1.0 / D),
                      inc=(sV, 1))
            tv = p.on("V", lambda e: e.tensor_tensor(r2wrow, m2row, m2row, ALU.mult),
                      inc=(sV, 1))
            tv = p.on("V", lambda e: e.tensor_tensor(vtmp, vtmp, r2wrow, ALU.subtract),
                      inc=(sV, 1))
            tv = p.on("V", lambda e: e.tensor_scalar_add(vtmp, vtmp, EPS), inc=(sV, 1))
            p.on("S", lambda e: e.activation(vtmp, vtmp, AF.Ln),
                 waits=[tv], inc=(sS, 1))
            tsqr = p.on("S", lambda e: e.activation(r2wrow, vtmp, AF.Exp, scale=-0.5),
                        inc=(sS, 1))
            tm_a = p.on("T", lambda e: e.matmul(pb[0][:, 0:WIN], onesPP[0:1, 0:128], m2row,
                                                start=True, stop=True), waits=[tv], inc=(sT, 1))
            tm2r = p.on("V", lambda e: e.tensor_copy(m2rep, pb[0][:, 0:WIN]), waits=[tm_a], inc=(sV, 1))
            bankfree[0] = tm2r
            tm_b = p.on("T", lambda e: e.matmul(pb[1][:, 0:WIN], onesPP[0:1, 0:128], r2wrow,
                                                start=True, stop=True), waits=[tsqr], inc=(sT, 1))
            tr2w = p.on("V", lambda e: e.tensor_copy(r2wrep, pb[1][:, 0:WIN]), waits=[tm_b], inc=(sV, 1))
            bankfree[1] = tr2w
            txn2w = {}
            for dc in range(NDC):
                tb = tmp4 if dc % 2 == 0 else sqw
                w_x = [tm2r, tr2w, txn2w.get(dc - 2)]
                if dc == 0:
                    w_x.append(wr1done_prev)
                p.on("V", lambda e, dc=dc, tb=tb: e.tensor_tensor(
                    tb, amidw[:, dc * WIN:(dc + 1) * WIN], m2rep, ALU.subtract),
                    waits=w_x, inc=(sV, 1))
                t2 = p.on("V", lambda e, dc=dc, tb=tb: e.tensor_tensor(tb, tb, r2wrep, ALU.mult), inc=(sV, 1))
                txn2w[dc] = p.on("S", lambda e, dc=dc, tb=tb: e.activation(
                    xn2w[:, dc * WIN:(dc + 1) * WIN], tb, AF.Identity,
                    bias=vcol("b2", dc), scale=vcol("g2", dc)), waits=[t2], inc=(sS, 1))
                txn2w_last = txn2w[dc]
        if w_i >= 1:
            _, _, fpPp, w0p = fsave[w_i - 1]
            for mt in range(NDC):
                bki = fpPp[mt % 2]
                bk = pb[bki]
                last = None
                for hc in range(16):
                    wts = [wr2_toks[hc], th1[hc]]
                    if hc == 0:
                        wts += [bankfree[bki], tow[mt % 2]]
                    last = p.on("T", lambda e, mt=mt, hc=hc, bk=bk: e.matmul(
                        bk[:, 0:WIN], wr2b[:, hc * 1024 + mt * 128: hc * 1024 + (mt + 1) * 128],
                        h1w[:, hc * WIN:(hc + 1) * WIN],
                        start=(hc == 0), stop=(hc == 15)),
                        waits=wts, inc=(sT, 1) if hc == 15 else None)
                t1 = p.on("V", lambda e, mt=mt, bk=bk: e.tensor_scalar(
                    outw[mt % 2], bk[:, 0:WIN], vcol("br2", mt), None, ALU.add),
                    waits=[last, tow[mt % 2]], inc=(sV, 1))
                bankfree[bki] = t1
                if not getattr(build_program, "_skip_iffn_dma", False):
                    tow[mt % 2] = p.on("G", lambda e, mt=mt, w0p=w0p: e.dma_start(
                        out=out_hbm[mt * 128:(mt + 1) * 128, w0p:w0p + WIN], in_=outw[mt % 2],
                        accum_op=ALU.add), waits=[t1], dma=True)
                else:
                    tow[mt % 2] = t1

    tap("amidw", amidw, [p.tok(sV)])
    tap("xn2w", xn2w, [p.tok(sS)])
    tap("h1w", h1w, [p.tok(sS)])
    tap("wr1b0", wr1b[:, 0:2048], [wr1_toks[0]])
    tap("wr2b0", wr2b[:, 0:1024], [wr2_toks[0]])
    tap("wr2bf", wr2b, [wr2_toks[-1]])
    tap("wr1bf", wr1b, [wr1_toks[-1]])

    p.barrier("p4", dma_sems=p.dma_pool)

    m_rep5 = carve(0, [128, SEQ], F32)
    r_rep5 = carve(8, [128, SEQ], F32)
    xnb = carve(16, [128, NDC * SEQ], BF16)
    xscr = carve(48, [128, SEQ], F32)
    mwbf = carve(56, [128, 8192], BF16)
    Gb = carve(72, [128, 2048], BF16)
    wqc = carve(76, [128, 2048], BF16)
    wkc = carve(80, [128, 2048], BF16)
    wvc = carve(84, [128, 2048], BF16)
    qkb = carve(88, [128, 2 * SEQ], BF16)
    kkb = carve(96, [128, 2 * SEQ], BF16)
    vktm = carve(104, [128, NJT * 288], BF16)
    EABm = carve(114, [128, NJT * 512], BF16)
    outmA = carve(130, [65, SEQ], F32)
    outmB = carve(138, [65, SEQ], F32)
    outc = carve(146, [128, 2 * SEQ], BF16)
    mrep16 = carve(154, [16, SEQ], F32)
    rrep16 = carve(162, [16, SEQ], F32)
    logsb = carve(170, [16, SEQ], F32)
    mscr = carve(178, [128, SEQ], F32)
    grep = carve(154, [128, 256], F32)
    bvcrep = carve(155, [128, 256], F32)
    mow = [carve(130, [128, WIN], F32), carve(132, [128, WIN], F32)]

    tmr5 = replicate(m_rep5, statrows[0:1, :], 0, [], 0)
    trr5 = replicate(r_rep5, statrows[64:65, :], 64, [], 1)

    twr = p.on("Y", lambda e: e.dma_start(
        out=wrf[:].rearrange("p (c n) -> p c n", c=8), in_=mwr.rearrange("(c p) n -> p c n", p=128)),
        dma=True)

    tlog = None
    txnb = {}
    prevx = None
    for c in range(NDC):
        txl = p.on("Y", lambda e, c=c: e.dma_start(
            out=xscr, in_=xT[c * 128:(c + 1) * 128, :]), waits=[prevx, tlog], dma=True)
        for cc in range(4):
            w = [txl, twr]
            if c == 0:
                w.append(bankfree[4 + cc])
            tlog = p.on("T", lambda e, c=c, cc=cc: e.matmul(
                pb[4 + cc][0:16, 0:512], wrf[:, c * 16:(c + 1) * 16],
                xscr[:, cc * 512:(cc + 1) * 512],
                start=(c == 0), stop=(c == NDC - 1)), waits=w, inc=(sT, 1))
        t1 = p.on("V", lambda e, c=c: e.tensor_tensor(mscr, xscr, m_rep5, ALU.subtract),
                  waits=[txl, tmr5, trr5], inc=(sV, 1))
        prevx = t1
        txnb[c] = p.on("V", lambda e, c=c: e.tensor_tensor(
            xnb[:, c * SEQ:(c + 1) * SEQ], mscr, r_rep5, ALU.mult), inc=(sV, 1))
    tXNB = txnb[NDC - 1]

    tcs = None
    for c in range(NDC):
        w = [twr, tCONST]
        if c == 0:
            w.append(bankfree[0])
        tcs = p.on("T", lambda e, c=c: e.matmul(
            pb[0][0:16, 0:1], wrf[:, c * 16:(c + 1) * 16], ones_col_f[:, :],
            start=(c == 0), stop=(c == NDC - 1)), waits=w, inc=(sT, 1))
    tcsv = p.on("V", lambda e: e.tensor_copy(gat[0:16, 32:33], pb[0][0:16, 0:1]), waits=[tcs], inc=(sV, 1))
    bankfree[0] = tcsv

    trm16 = None
    for cc in range(4):
        tm_a = p.on("T", lambda e, cc=cc: e.matmul(
            pb[2][0:16, 0:512], onesPP[0:1, 0:16], statrows[0:1, cc * 512:(cc + 1) * 512],
            start=True, stop=True), waits=[trm16, bankfree[2]], inc=(sT, 1))
        tmv = p.on("V", lambda e, cc=cc: e.tensor_copy(mrep16[:, cc * 512:(cc + 1) * 512], pb[2][0:16, 0:512]),
                   waits=[tm_a], inc=(sV, 1))
        bankfree[2] = tmv
        tm_b = p.on("T", lambda e, cc=cc: e.matmul(
            pb[3][0:16, 0:512], onesPP[64:65, 0:16], statrows[64:65, cc * 512:(cc + 1) * 512],
            start=True, stop=True), waits=[tmv, bankfree[3]], inc=(sT, 1))
        trm16 = p.on("V", lambda e, cc=cc: e.tensor_copy(rrep16[:, cc * 512:(cc + 1) * 512], pb[3][0:16, 0:512]),
                     waits=[tm_b], inc=(sV, 1))
        bankfree[3] = trm16

    tlg = None
    for cc in range(4):
        p.on("V", lambda e, cc=cc: e.tensor_scalar(
            logsb[:, cc * 512:(cc + 1) * 512], mrep16[:, cc * 512:(cc + 1) * 512],
            gat[0:16, 32:33], None, ALU.mult), waits=[tlog, tcsv, trm16], inc=(sV, 1))
        p.on("V", lambda e, cc=cc: e.tensor_tensor(
            logsb[:, cc * 512:(cc + 1) * 512], pb[4 + cc][0:16, 0:512],
            logsb[:, cc * 512:(cc + 1) * 512], ALU.subtract), inc=(sV, 1))
        tlg = p.on("V", lambda e, cc=cc: e.tensor_tensor(
            logsb[:, cc * 512:(cc + 1) * 512], logsb[:, cc * 512:(cc + 1) * 512],
            rrep16[:, cc * 512:(cc + 1) * 512], ALU.mult), inc=(sV, 1))
        bankfree[4 + cc] = tlg
    tmean = p.on("V", lambda e: e.reduce_sum(gat[0:16, 33:34], logsb, axis=AX.X), waits=[tlg], inc=(sV, 1))
    tmean = p.on("V", lambda e: e.tensor_scalar(
        gat[0:16, 34:35], gat[0:16, 33:34], 1.0 / SEQ, vcol("mbr", 0)[0:16, :], ALU.mult, ALU.add),
        inc=(sV, 1))
    tdm = p.on("G", lambda e: e.dma_start(out=gat[0:1, 0:16], in_=gat[0:16, 34:35]),
               waits=[tmean], dma=True)
    t1 = p.on("V", lambda e: e.reduce_max(gat[0:1, 16:17], gat[0:1, 0:16], axis=AX.X),
              waits=[tdm], inc=(sV, 1))
    t2 = p.on("V", lambda e: e.tensor_tensor(gat[0:1, 0:16], gat[0:1, 0:16],
                                             gat[0:1, 16:17].broadcast_to([1, 16]), ALU.subtract),
              inc=(sV, 1))
    t3 = p.on("S", lambda e: e.activation(gat[0:1, 0:16], gat[0:1, 0:16], AF.Exp), waits=[t2], inc=(sS, 1))
    t4 = p.on("V", lambda e: e.reduce_sum(gat[0:1, 16:17], gat[0:1, 0:16], axis=AX.X),
              waits=[t3], inc=(sV, 1))
    t5 = p.on("V", lambda e: e.reciprocal(gat[0:1, 17:18], gat[0:1, 16:17]), inc=(sV, 1))
    tdist = p.on("V", lambda e: e.tensor_tensor(gat[0:1, 0:16], gat[0:1, 0:16],
                                                gat[0:1, 17:18].broadcast_to([1, 16]), ALU.mult),
                 inc=(sV, 1))
    p.on("V", lambda e: e.max(gat[0:1, 18:26], gat[0:1, 0:16]), waits=[tdist], inc=(sV, 1))
    ttk = p.on("V", lambda e: e.max_index(mi8[0:1, :], gat[0:1, 18:26], gat[0:1, 0:16]),
               inc=(sV, 1))
    tsp = p.on("V", lambda e: e.memset(gat[0:1, 48:64], 0.0), inc=(sV, 1))
    toh = {}
    for k in range(4):
        toh[k] = p.on("V", lambda e, k=k: e.tensor_tensor(
            gat[0:1, 64 + 16 * k:80 + 16 * k], gat[0:1, 0:16],
            gat[0:1, 18 + k:19 + k].broadcast_to([1, 16]), ALU.is_equal), waits=[ttk], inc=(sV, 1))
        p.on("V", lambda e, k=k: e.tensor_tensor(
            gat[0:1, 128 + 16 * k:144 + 16 * k], gat[0:1, 64 + 16 * k:80 + 16 * k],
            gat[0:1, 18 + k:19 + k].broadcast_to([1, 16]), ALU.mult), inc=(sV, 1))
        tsp = p.on("V", lambda e, k=k: e.tensor_tensor(
            gat[0:1, 48:64], gat[0:1, 48:64], gat[0:1, 128 + 16 * k:144 + 16 * k], ALU.add),
            inc=(sV, 1))
    tge = p.on("S", lambda e: e.activation(gat[0:1, 192:208], gat[0:1, 48:64], AF.Exp),
               waits=[tsp], inc=(sS, 1))
    p.on("V", lambda e: e.reduce_sum(gat[0:1, 16:17], gat[0:1, 192:208], axis=AX.X),
         waits=[tge], inc=(sV, 1))
    p.on("V", lambda e: e.reciprocal(gat[0:1, 17:18], gat[0:1, 16:17]), inc=(sV, 1))
    tga = p.on("V", lambda e: e.tensor_tensor(gat[0:1, 192:208], gat[0:1, 192:208],
                                              gat[0:1, 17:18].broadcast_to([1, 16]), ALU.mult),
               inc=(sV, 1))
    tgk = None
    for k in range(4):
        p.on("V", lambda e, k=k: e.tensor_tensor(
            gat[0:1, 128 + 16 * k:144 + 16 * k], gat[0:1, 192:208],
            gat[0:1, 64 + 16 * k:80 + 16 * k], ALU.mult), waits=[tga], inc=(sV, 1))
        tgk = p.on("V", lambda e, k=k: e.reduce_sum(
            gat[0:1, 26 + k:27 + k], gat[0:1, 128 + 16 * k:144 + 16 * k], axis=AX.X), inc=(sV, 1))
    tgc = p.on("G", lambda e: e.dma_start(out=gat[0:4, 40:41], in_=gat[0:1, 26:30]),
               waits=[tgk], dma=True)
    tge2 = {}
    for t2_ in range(2):
        tm_a = p.on("T", lambda e, t2_=t2_: e.matmul(
            pb[0][:, 0:1], E64f[:, t2_ * 128:(t2_ + 1) * 128], gat[0:4, 40:41],
            start=True, stop=True), waits=[tgc, bankfree[0]], inc=(sT, 1))
        tge2[t2_] = p.on("S", lambda e, t2_=t2_: e.copy(gexp[:, t2_:t2_ + 1], pb[0][:, 0:1]),
                         waits=[tm_a], inc=(sS, 1))
        bankfree[0] = tge2[t2_]
    tgq = p.on("S", lambda e: e.mul(gexpQ[:], gexp[:], ISQ8), waits=[tge2[1]], inc=(sS, 1))
    th4 = []
    for k in range(4):
        t1 = p.on("V", lambda e, k=k: e.tensor_copy(gatb[0:1, :], gat[0:1, 64 + 16 * k:80 + 16 * k]),
                  waits=[toh[k], (th4[-1] if th4 else None)], inc=(sV, 1))
        th4.append(p.on("G", lambda e, k=k: e.dma_start(out=H4T[k:k + 1, :], in_=gatb[0:1, :]),
                        waits=[t1], dma=True))
    tohc = [p.on("G", lambda e, k=k: e.dma_start(
        out=gat[0:16, 41 + k:42 + k], in_=gat[0:1, 64 + 16 * k:80 + 16 * k]),
        waits=[toh[k]], dma=True) for k in range(4)]
    tbc_last = None
    for wi in range(3):
        for k in range(4):
            tm_a = p.on("T", lambda e, wi=wi, k=k: e.matmul(
                pb[1][0:64, 0:1], bh3t[:, wi * 64:(wi + 1) * 64], gat[0:16, 41 + k:42 + k],
                start=True, stop=True), waits=[t_bh3, tohc[k], tbc_last, bankfree[1]],
                inc=(sT, 1))
            tbc_last = p.on("S", lambda e, wi=wi, k=k: e.copy(
                bcpp[64 * (k % 2):64 * (k % 2) + 64, 2 * wi + k // 2:2 * wi + k // 2 + 1],
                pb[1][0:64, 0:1]), waits=[tm_a], inc=(sS, 1))
            bankfree[1] = tbc_last
    tBC = tbc_last

    tm_a = p.on("T", lambda e: e.matmul(pb[0][0:16, 0:256], H4T[:, :], Ecol4b[:, :],
                                        start=True, stop=True), waits=th4 + [tCONST, bankfree[0]],
                inc=(sT, 1))
    th4e = p.on("V", lambda e: e.tensor_copy(H4E[:, :], pb[0][0:16, 0:256]), waits=[tm_a], inc=(sV, 1))
    bankfree[0] = th4e
    tGB = None
    for c in range(NDC):
        tm_a = p.on("T", lambda e, c=c: e.matmul(
            pb[0][:, 0:256], RowSelb[:, c * 128:(c + 1) * 128], H4E[:, :],
            start=True, stop=True), waits=[th4e, bankfree[0]], inc=(sT, 1))
        tGB = p.on("V", lambda e, c=c: e.tensor_tensor(
            Gb[:, c * 256:(c + 1) * 256], pb[0][:, 0:256], M64b[:, :], ALU.mult),
            waits=[tm_a], inc=(sV, 1))
        bankfree[0] = tGB

    wslot_free[0] = wslot_free[1] = None

    def stream_w5(hbm, nchunks, pre=None):
        toks = []
        for c in range(nchunks):
            s = c % 2
            w = [wslot_free[s]]
            if pre is not None and c < 2:
                w.append(pre)
            tl = p.on("Y", lambda e, c=c, s=s, hbm=hbm: e.dma_start(
                out=mscr[:, s * 1024:(s + 1) * 1024], in_=hbm[c * 128:(c + 1) * 128, :]),
                waits=w, dma=True)
            tc_ = p.on("S", lambda e, c=c, s=s: e.copy(
                mwbf[:, c * 1024:(c + 1) * 1024], mscr[:, s * 1024:(s + 1) * 1024]),
                waits=[tl], inc=(sS, 1))
            wslot_free[s] = tc_
            toks.append(tc_)
        return toks

    def compact_w(wtoks, dst, pre):
        last_ev = pre
        for it in range(NDC):
            last = None
            for oc in range(NDC):
                w = [wtoks[oc], tGB]
                if oc == 0:
                    w += [last_ev, bankfree[2]]
                last = p.on("T", lambda e, it=it, oc=oc: e.matmul(
                    pb[2][:, 0:256], mwbf[:, oc * 1024 + it * 128: oc * 1024 + (it + 1) * 128],
                    Gb[:, oc * 256:(oc + 1) * 256],
                    start=(oc == 0), stop=(oc == NDC - 1)),
                    waits=w, inc=(sT, 1) if oc == NDC - 1 else None)
            last_ev = p.on("V", lambda e, it=it, dst=dst: e.tensor_copy(
                dst[:, it * 256:(it + 1) * 256], pb[2][:, 0:256]), waits=[last], inc=(sV, 1))
            bankfree[2] = last_ev
        return last_ev

    tq_toks = stream_w5(mwqT, 8, pre=tXNB)
    tWQC = compact_w(tq_toks, wqc, None)
    tk_toks = stream_w5(mwkT, 8, pre=tWQC)
    tWKC = compact_w(tk_toks, wkc, None)
    tv_toks = stream_w5(mwvT, 8, pre=tWKC)
    tWVC = compact_w(tv_toks, wvc, None)

    def cproj(wc, twc, dst, bcol, gtile):
        toks = []
        for half in range(2):
            for cc in range(4):
                bki = 4 + ((half * 4 + cc) % 2)
                bk = pb[bki]
                last = None
                for dc in range(NDC):
                    w = [twc, tXNB]
                    if dc == 0:
                        w.append(bankfree[bki])
                    last = p.on("T", lambda e, half=half, cc=cc, dc=dc, bk=bk, wc=wc: e.matmul(
                        bk[:, 0:512], wc[:, dc * 256 + half * 128: dc * 256 + (half + 1) * 128],
                        xnb[:, dc * SEQ + cc * 512: dc * SEQ + (cc + 1) * 512],
                        start=(dc == 0), stop=(dc == NDC - 1)),
                        waits=w, inc=(sT, 1) if dc == NDC - 1 else None)
                te = p.on("V", lambda e, half=half, cc=cc, bk=bk, dst=dst, bcol=bcol, gtile=gtile: e.tensor_scalar(
                    dst[:, half * SEQ + cc * 512: half * SEQ + (cc + 1) * 512], bk[:, 0:512],
                    bcpp[:, bcol + half:bcol + half + 1], gtile[:, half:half + 1], ALU.add, ALU.mult),
                    waits=[last, tBC, tgq], inc=(sV, 1))
                bankfree[bki] = te
                toks.append(te)
        return toks[-1]

    tQK = cproj(wqc, tWQC, qkb, 0, gexpQ)
    tKK = cproj(wkc, tWKC, kkb, 2, gexp)

    tgrow = []
    for k in range(4):
        tgrow.append(p.on("V", lambda e, k=k: e.tensor_copy(
            scr_row[0:1, k * 64:(k + 1) * 64],
            gat[0:1, 26 + k:27 + k].broadcast_to([1, 64])), waits=[tgk], inc=(sV, 1)))
    tbvc = [p.on("G", lambda e, half=half: e.dma_start(
        out=gat[0:1, 224 + half * 128:224 + (half + 1) * 128],
        in_=bcpp[:, 4 + half:5 + half]), waits=[tBC], dma=True) for half in range(2)]
    tm_a = p.on("T", lambda e: e.matmul(pb[0][:, 0:256], onesPP[0:1, 0:128], scr_row[0:1, 0:256],
                                        start=True, stop=True), waits=tgrow + [bankfree[0]], inc=(sT, 1))
    tgrep = p.on("V", lambda e: e.tensor_copy(grep[:], pb[0][:, 0:256]), waits=[tm_a], inc=(sV, 1))
    bankfree[0] = tgrep
    tm_b = p.on("T", lambda e: e.matmul(pb[1][:, 0:256], onesPP[0:1, 0:128], gat[0:1, 224:480],
                                        start=True, stop=True), waits=tbvc + [bankfree[1]], inc=(sT, 1))
    tbvcrep = p.on("V", lambda e: e.tensor_copy(bvcrep[:], pb[1][:, 0:256]), waits=[tm_b], inc=(sV, 1))
    bankfree[1] = tbvcrep

    tms = p.on("V", lambda e: e.memset(vktm[:, :], 0.0), inc=(sV, 1))
    tones = p.on("V", lambda e: e.tensor_copy(
        vktm[:, :].rearrange("p (a b) -> p a b", b=72)[:, :, 64:65],
        ones_col_b[:, 0:1].broadcast_to([128, 64]).rearrange("p (a b) -> p a b", b=1)),
        waits=[tms], inc=(sV, 1))
    tvk_last = None
    for jt in range(NJT):
        last = None
        for dc in range(NDC):
            w = [tWVC, tXNB]
            if dc == 0:
                w.append(bankfree[2])
            last = p.on("T", lambda e, jt=jt, dc=dc: e.matmul(
                pb[2][:, 0:256], xnb[:, dc * SEQ + jt * 128: dc * SEQ + (jt + 1) * 128],
                wvc[:, dc * 256:(dc + 1) * 256],
                start=(dc == 0), stop=(dc == NDC - 1)),
                waits=w, inc=(sT, 1) if dc == NDC - 1 else None)
        t1 = p.on("V", lambda e, jt=jt: e.tensor_tensor(xscr[:, 0:256], pb[2][:, 0:256], bvcrep[:], ALU.add),
                  waits=[last, tbvcrep], inc=(sV, 1))
        bankfree[2] = t1
        p.on("V", lambda e, jt=jt: e.tensor_tensor(xscr[:, 256:512], xscr[:, 0:256], grep[:], ALU.mult),
             waits=[tgrep, tones], inc=(sV, 1))
        for k in range(4):
            tvk_last = p.on("V", lambda e, jt=jt, k=k: e.tensor_copy(
                vktm[:, jt * 288 + k * 72: jt * 288 + k * 72 + 64],
                xscr[:, 256 + k * 64:256 + (k + 1) * 64]), inc=(sV, 1))
    tVK = tvk_last

    tprev = None
    for src, base, tdep in ((qkb, 0, tQK), (kkb, 2, tKK)):
        for kp in range(2):
            tq = p.on("V", lambda e, kp=kp, src=src: e.tensor_tensor(
                mscr, src[:, kp * SEQ:(kp + 1) * SEQ], src[:, kp * SEQ:(kp + 1) * SEQ], ALU.mult),
                waits=[tdep, tprev, tVK], inc=(sV, 1))
            tqb = p.on("V", lambda e: e.tensor_copy(EABm[:, 0:SEQ], mscr), inc=(sV, 1))
            for cc in range(4):
                tm_a = p.on("T", lambda e, cc=cc: e.matmul(
                    pb[0][0:1, 0:512], ones_col_b[:, :], EABm[:, cc * 512:(cc + 1) * 512],
                    start=True, stop=True), waits=[tqb, bankfree[0]], inc=(sT, 1))
                tprev = p.on("V", lambda e, kp=kp, cc=cc, base=base: e.reduce_max(
                    gat[0:1, 240 + (base + kp) * 4 + cc:240 + (base + kp) * 4 + cc + 1],
                    pb[0][0:1, 0:512], axis=AX.X), waits=[tm_a], inc=(sV, 1))
                bankfree[0] = tprev
    for kp in range(4):
        tprev = p.on("V", lambda e, kp=kp: e.reduce_max(
            gat[0:1, 230 + kp:231 + kp], gat[0:1, 240 + kp * 4:244 + kp * 4], axis=AX.X),
            waits=[tprev], inc=(sV, 1))
    t_ = p.on("V", lambda e: e.tensor_tensor(gat[0:1, 234:236], gat[0:1, 230:232], gat[0:1, 232:234], ALU.mult),
              waits=[tprev], inc=(sV, 1))
    t_ = p.on("S", lambda e: e.activation(gat[0:1, 236:238], gat[0:1, 234:236], AF.Sqrt),
              waits=[t_], inc=(sS, 1))
    t_ = p.on("V", lambda e: e.tensor_scalar_mul(gat[0:1, 234:236], gat[0:1, 236:238], -1.05),
              waits=[t_], inc=(sV, 1))
    tm_a = p.on("T", lambda e: e.matmul(pb[0][:, 0:2], onesPP[0:1, 0:128], gat[0:1, 234:236],
                                        start=True, stop=True), waits=[t_, bankfree[0]], inc=(sT, 1))
    tMNEG = p.on("V", lambda e: e.tensor_copy(mnegM[:], pb[0][:, 0:2]), waits=[tm_a], inc=(sV, 1))
    bankfree[0] = tMNEG

    tok_pvm = {}
    tok_sbandm = {}
    for kp in range(2):
        tok_pvm_prev = tok_pvm
        tok_expm = {}
        tok_pvm = {}
        for iq in range(NIQ):
            def s_and_expm(j, iq=iq, kp=kp, tok_pvm_prev=tok_pvm_prev):
                w = [tQK, tKK, tMNEG, tok_sbandm.get(0)]
                p.on("T", lambda e, j=j, iq=iq, kp=kp: e.matmul(
                    pb[0][:, 0:IQ], kkb[0:64, kp * SEQ + j * 128: kp * SEQ + (j + 1) * 128],
                    qkb[0:64, kp * SEQ + iq * IQ: kp * SEQ + (iq + 1) * IQ],
                    start=True, stop=True, tile_position=(0, 0)), waits=w)
                tmm = p.on("T", lambda e, j=j, iq=iq, kp=kp: e.matmul(
                    pb[1][:, 0:IQ], kkb[64:128, kp * SEQ + j * 128: kp * SEQ + (j + 1) * 128],
                    qkb[64:128, kp * SEQ + iq * IQ: kp * SEQ + (iq + 1) * IQ],
                    start=True, stop=True, tile_position=(64, 0)), inc=(sT, 1))
                if iq == 0:
                    prevpv = tok_pvm_prev.get((NIQ - 1, j))
                else:
                    prevpv = tok_pvm.get((iq - 1, j))
                te = p.on("S", lambda e, j=j, kp=kp: e.activation(
                    EABm[:, j * 512:(j + 1) * 512].rearrange("p (a b) -> p a b", b=IQ),
                    psS[:, 0:1024].rearrange("p (a b) -> p a b", b=512)[:, :, 0:IQ],
                    AF.Exp, bias=mnegM[:, kp:kp + 1]), waits=[tmm, prevpv], inc=(sS, 1))
                tok_expm[(iq, j)] = te
                tok_sbandm[0] = te

            s_and_expm(0)
            s_and_expm(1)
            for j in range(NJT):
                if j + 2 <= NJT - 1:
                    s_and_expm(j + 2)
                tpvm = None
                for hh in range(2):
                    bki = 2 + hh
                    bk = pb[bki]
                    w = [tok_expm[(iq, j)], tVK]
                    if j == 0:
                        w.append(bankfree[bki])
                    tpvm = p.on("T", lambda e, j=j, hh=hh, kp=kp, bk=bk: e.matmul(
                        bk[0:65, 0:IQ],
                        vktm[:, j * 288 + (2 * kp + hh) * 72: j * 288 + (2 * kp + hh) * 72 + 65],
                        EABm[:, j * 512 + hh * IQ: j * 512 + (hh + 1) * IQ],
                        start=(j == 0), stop=(j == NJT - 1)), waits=w, inc=(sT, 1))
                tok_pvm[(iq, j)] = tpvm
            tlast = tok_pvm[(iq, NJT - 1)]
            for hh in range(2):
                outm = outmA if hh == 0 else outmB
                bankfree[2 + hh] = p.on("V", lambda e, hh=hh, iq=iq, outm=outm: e.tensor_copy(
                    outm[:, iq * IQ:(iq + 1) * IQ], pb[2 + hh][0:65, 0:IQ]),
                    waits=[tlast], inc=(sV, 1))
        for hh in range(2):
            k = 2 * kp + hh
            outm = outmA if hh == 0 else outmB
            p.on("S", lambda e, outm=outm: e.activation(outm[64:65, :], outm[64:65, :], AF.Ln),
                 waits=[bankfree[2 + hh]], inc=(sS, 1))
            tiv = p.on("S", lambda e, outm=outm: e.activation(
                outm[64:65, :], outm[64:65, :], AF.Exp, scale=-1.0), inc=(sS, 1))
            for cc in range(4):
                tm_a = p.on("T", lambda e, cc=cc, outm=outm: e.matmul(
                    pb[6][0:64, 0:512], onesPP[64:65, 0:64], outm[64:65, cc * 512:(cc + 1) * 512],
                    start=True, stop=True), waits=[tiv, bankfree[6]], inc=(sT, 1))
                bankfree[6] = p.on("V", lambda e, cc=cc, k=k, outm=outm: e.tensor_tensor(
                    outc[64 * (k % 2):64 * (k % 2) + 64,
                         (k // 2) * SEQ + cc * 512:(k // 2) * SEQ + (cc + 1) * 512],
                    outm[0:64, cc * 512:(cc + 1) * 512], pb[6][0:64, 0:512], ALU.mult),
                    waits=[tm_a], inc=(sV, 1))
    tOUTC = bankfree[6]

    wo_toks = stream_w5(mwo, 2, pre=tMNEG)
    tmo = [None, None]
    for mt in range(NDC):
        for cc in range(4):
            par = (mt * 4 + cc) % 2
            bki = 4 + par
            bk = pb[bki]
            last = None
            for ch in range(2):
                w = [wo_toks[ch], tOUTC]
                if ch == 0:
                    w += [tmo[par], bankfree[bki]]
                last = p.on("T", lambda e, mt=mt, cc=cc, ch=ch, bk=bk: e.matmul(
                    bk[:, 0:512], mwbf[:, ch * 1024 + mt * 128: ch * 1024 + (mt + 1) * 128],
                    outc[:, ch * SEQ + cc * 512: ch * SEQ + (cc + 1) * 512],
                    start=(ch == 0), stop=(ch == 1)),
                    waits=w, inc=(sT, 1) if ch == 1 else None)
            t1 = p.on("V", lambda e, mt=mt, cc=cc, par=par, bk=bk: e.tensor_scalar(
                mow[par], bk[:, 0:512], vcol("mbo", mt), None, ALU.add),
                waits=[last, tmo[par]], inc=(sV, 1))
            bankfree[bki] = t1
            tmo[par] = p.on("G", lambda e, mt=mt, cc=cc, par=par: e.dma_start(
                out=out_hbm[mt * 128:(mt + 1) * 128, cc * 512:(cc + 1) * 512],
                in_=mow[par], accum_op=ALU.add), waits=[t1], dma=True)

    tap("dist", gat[0:1, 0:64], [tdist, ttk, tgk])
    tap("gall", gat[0:1, 192:208], [tga])
    tap("outc0", outc[:, 0:512], [tOUTC])
    tap("qkb0", qkb[:, 0:512], [tQK])
    tap("xnb0", xnb[:, 0:2048], [tXNB])

    p.on("G", None, waits=p.all_dma_toks())
    p.emit()
    return nc



def _consts_blob(Ws):
    c = np.zeros((128, 2048), dtype=np.float32)
    c[:, 0:128] = 1.0
    ist = np.zeros((128, 64), dtype=np.float32)
    for q in range(128):
        ist[q, q % 64] = 1.0
    c[:, 128:192] = ist
    m64 = np.zeros((128, 256), dtype=np.float32)
    for q in range(128):
        for k in range(4):
            m64[q, k * 64 + (q % 64)] = 1.0
    c[:, 192:448] = m64
    Ws = np.asarray(Ws, dtype=np.float32).reshape(-1)
    w1e = np.zeros((128, 64), dtype=np.float32)
    for mt in range(4):
        for q in range(128):
            w1e[q, mt * 16 + 4 * mt + q // 32] = Ws[q % 32]
    c[:, 448:512] = w1e
    w2e = np.zeros((128, 32), dtype=np.float32)
    for dc in range(4):
        for q in range(128):
            w2e[q, dc * 8 + 2 * dc + q // 64] = Ws[32 + q % 64]
    c[:, 512:544] = w2e
    e8 = np.zeros((8, 512), dtype=np.float32)
    for dc in range(4):
        for q in range(128):
            e8[2 * dc + q // 64, dc * 128 + q] = 1.0
    c[0:8, 544:1056] = e8
    esel = np.zeros((16, 128), dtype=np.float32)
    for h in range(16):
        esel[h, h * 8:(h + 1) * 8] = 1.0
    c[0:16, 1056:1184] = esel
    ecol4 = np.zeros((4, 256), dtype=np.float32)
    for k in range(4):
        ecol4[k, k * 64:(k + 1) * 64] = 1.0
    c[0:4, 1184:1440] = ecol4
    e64 = np.zeros((4, 256), dtype=np.float32)
    for t in range(2):
        for q in range(128):
            e64[2 * t + q // 64, t * 128 + q] = 1.0
    c[0:4, 1440:1696] = e64
    rowsel = np.zeros((16, 1024), dtype=np.float32)
    for col in range(1024):
        cc, q = divmod(col, 128)
        rowsel[2 * cc + q // 64, col] = 1.0
    c[32:48, 544:1568] = rowsel
    return c


def _pack_vecs(ins):
    cols = np.zeros((128, 128), dtype=np.float32)
    off = 0

    def add(arr1d, n):
        nonlocal off
        a = np.zeros((n * 128,), dtype=np.float32)
        v = np.asarray(arr1d, dtype=np.float32).reshape(-1)
        a[:v.size] = v
        cols[:, off:off + n] = a.reshape(n, 128).T
        off += n

    add(ins["L0_attn_bq"], 8)
    add(ins["L0_attn_bk"], 8)
    add(ins["L0_attn_bv"], 4)
    add(ins["L0_attn_bqv"], 4)
    add(ins["L0_attn_bf"], 8)
    add(ins["L0_attn_br1"], 16)
    add(ins["L0_attn_br2"], 8)
    add(ins["L0_attn_g1"], 8)
    add(ins["L0_attn_b1"], 8)
    add(ins["L0_attn_g2"], 8)
    add(ins["L0_attn_b2"], 8)
    add(np.full(16, float(np.asarray(ins["L0_attn_bs"]).reshape(-1)[0])), 1)
    add(ins["L0_ffn_br"], 1)
    add(ins["L0_ffn_bo"], 8)
    return cols


def kernel(**ins):
    x = np.asarray(ins["L0_attn_x"], dtype=np.float32)
    B = x.shape[0]
    consts = _consts_blob(ins["L0_attn_Ws"])
    vecs = _pack_vecs(ins)

    def T(a):
        return np.ascontiguousarray(np.asarray(a, dtype=np.float32).T)

    shared = {
        "wqT": T(ins["L0_attn_Wq"]), "wkT": T(ins["L0_attn_Wk"]),
        "wvT": T(ins["L0_attn_Wv"]), "wqvT": T(ins["L0_attn_Wqv"]),
        "wfT": T(ins["L0_attn_Wf"]),
        "wr1T": T(ins["L0_attn_Wr1"]), "wr2T": T(ins["L0_attn_Wr2"]),
        "mwqT": T(ins["L0_ffn_Wq"]), "mwkT": T(ins["L0_ffn_Wk"]), "mwvT": T(ins["L0_ffn_Wv"]),
        "mwo": np.ascontiguousarray(np.asarray(ins["L0_ffn_Wo"], dtype=np.float32)),
        "mwr": np.ascontiguousarray(np.asarray(ins["L0_ffn_Wr"], dtype=np.float32)),
        "vecs": vecs, "consts": consts,
        "bh3": np.ascontiguousarray(np.stack(
            [np.asarray(ins["L0_ffn_bq"], dtype=np.float32).reshape(16, 64),
             np.asarray(ins["L0_ffn_bk"], dtype=np.float32).reshape(16, 64),
             np.asarray(ins["L0_ffn_bv"], dtype=np.float32).reshape(16, 64)], axis=0
        ).transpose(1, 0, 2).reshape(16, 192)),
    }
    in_maps = []
    for b in range(B):
        m = dict(shared)
        m["xT"] = np.ascontiguousarray(x[b].T)
        in_maps.append(m)

    nc = build_program(taps=tuple(_TAPS), maxphase=getattr(kernel, "_MAXPHASE", 5), hpn=getattr(kernel, "_HPN", 8))
    res = run_bass_kernel_spmd(nc, in_maps, core_ids=list(range(B)))
    outs = np.stack([np.asarray(r["out"]) for r in res.results], axis=0)
    if _TAPS:
        kernel.tap_results = [{k: np.asarray(v) for k, v in r.items() if k.startswith("tap_")}
                              for r in res.results]
    return np.ascontiguousarray(outs.transpose(0, 2, 1))


# version 18
# speedup vs baseline: 1.2355x; 1.0078x over previous
import itertools
import numpy as np
from contextlib import ExitStack

import concourse.bass as bass
import concourse.mybir as mybir
from concourse.bass_utils import run_bass_kernel_spmd

F32 = mybir.dt.float32
BF16 = mybir.dt.bfloat16
U32 = mybir.dt.uint32
AF = mybir.ActivationFunctionType
ALU = mybir.AluOpType
AX = mybir.AxisListType

D = 1024
SEQ = 2048
NDC = 8
NJT = 16
IQ = 256
NIQ = SEQ // IQ
EPS = 1e-5
DFF = 2048
ISQ8 = 0.125
ISQ32 = float(1.0 / np.sqrt(32.0))
WIN = 512
NW = SEQ // WIN
ARENA_KB = 194

_TAPS = []
_MAXPHASE = 5
_HPN = 8


class Prog:
    ENGINES = ("Y", "G", "T", "V", "S")

    def __init__(self, nc):
        self.nc = nc
        self.steps = {e: [] for e in self.ENGINES}
        self.semc = {}
        self.chain_sems = {}
        self.es = ExitStack()
        self._uid = 0

    def _nm(self, p):
        self._uid += 1
        return "%s%d" % (p, self._uid)

    def sem(self, name=None):
        h = self.es.enter_context(self.nc.semaphore(name=name or self._nm("sm")))
        self.semc[h] = 0
        return h

    def sb(self, name, shape, dt):
        return self.es.enter_context(self.nc.sbuf_tensor(name, shape, dt))

    def psum(self, name, shape):
        return self.es.enter_context(self.nc.psum_tensor(name, shape, F32))

    def init_dma_pool(self, n=12):
        self.dma_pools = {q: [self.sem("dma%s%d" % (q, i)) for i in range(n if q != "S" else 4)]
                          for q in ("Y", "G", "S")}
        self.dma_prev = {q: [None] * len(self.dma_pools[q]) for q in ("Y", "G", "S")}
        self.dma_rr = {q: 0 for q in ("Y", "G", "S")}
        self.dma_pool = self.dma_pools["Y"] + self.dma_pools["G"] + self.dma_pools["S"]

    def all_dma_toks(self):
        return [self.tok(s) for s in self.dma_pool]

    def on(self, eng, fn, waits=(), inc=None, dma=False):
        w = []
        if fn is not None and eng in self.chain_sems:
            cs = self.chain_sems[eng]
            w.append((cs, self.semc[cs]))
            if inc is None and not dma:
                inc = (cs, 1)
        def _flat(ts):
            for t in ts:
                if t is None:
                    continue
                if isinstance(t, list):
                    _flat(t)
                else:
                    w.append(t)
        _flat(waits)
        if dma:
            pool = self.dma_pools[eng]
            i = self.dma_rr[eng] % len(pool)
            self.dma_rr[eng] += 1
            if self.dma_prev[eng][i] is not None:
                w.append(self.dma_prev[eng][i])
            inc = (pool[i], 16)
        self.steps[eng].append((w, fn, inc))
        if inc is not None:
            s, k = inc
            self.semc[s] += k
            tok = (s, self.semc[s])
            if dma:
                self.dma_prev[eng][i] = tok
            return tok
        return None

    def tok(self, sem):
        return (sem, self.semc[sem])

    def barrier(self, name, dma_sems=()):
        bar = self.sem("bar_" + name)
        dw = [self.tok(s) for s in dma_sems]
        for e in self.ENGINES:
            self.on(e, lambda eng: eng.nop(), waits=dw, inc=(bar, 1))
        t = self.tok(bar)
        for e in self.ENGINES:
            self.on(e, None, waits=[t])
        return t

    def emit(self):
        nc = self.nc
        with nc.Block() as block:
            def run(eng_obj, name):
                seen = {}
                for w, fn, inc in self.steps[name]:
                    for s, v in w:
                        if seen.get(s, -1) < v:
                            eng_obj.wait_ge(s, v)
                            seen[s] = v
                    if fn is not None:
                        ins = fn(eng_obj)
                        if inc is not None:
                            if ins is None:
                                eng_obj.nop().then_inc(*inc)
                            else:
                                ins.then_inc(*inc)
                    elif inc is not None:
                        eng_obj.nop().then_inc(*inc)

            @block.sync
            def _(e):
                run(e, "Y")

            @block.gpsimd
            def _(e):
                run(e, "G")

            @block.tensor
            def _(e):
                run(e, "T")

            @block.vector
            def _(e):
                run(e, "V")

            @block.scalar
            def _(e):
                run(e, "S")


def build_program(taps=(), maxphase=5, hpn=8):
    nc = bass.Bass("TRN2", target_bir_lowering=False)
    p = Prog(nc)

    def din(name, shape, dt=F32):
        return nc.dram_tensor(name, shape, dt, kind="ExternalInput").ap()

    xT = din("xT", [D, SEQ])
    wqT = din("wqT", [D, D]); wkT = din("wkT", [D, D])
    wvT = din("wvT", [D, 512]); wqvT = din("wqvT", [D, 512])
    wfT = din("wfT", [D, D])
    wr1T = din("wr1T", [D, DFF]); wr2T = din("wr2T", [DFF, D])
    mwqT = din("mwqT", [D, D]); mwkT = din("mwkT", [D, D]); mwvT = din("mwvT", [D, D])
    mwo = din("mwo", [256, D]); mwr = din("mwr", [D, 16])
    vecs = din("vecs", [128, 128])
    consts = din("consts", [128, 2048])
    bh3 = din("bh3", [16, 192])
    out_hbm = nc.dram_tensor("out", [D, SEQ], F32, kind="ExternalOutput").ap()

    taps = set(taps)

    p.init_dma_pool(16)
    sG = p.sem("sG")
    sT = p.sem("sT"); sV = p.sem("sV"); sS = p.sem("sS")
    p.chain_sems = {"V": sV, "S": sS, "G": sG}

    psS = p.psum("psS", [128, 1024])
    pb = [psS[:, 0:512], psS[:, 512:1024]] + [p.psum("pb%d" % i, [128, 512]) for i in range(2, 8)]
    bankfree = {i: None for i in range(8)}

    arena = p.sb("arena", [128, ARENA_KB * 512], BF16)

    _audit = []

    def carve(kb, shape, dt):
        cols = shape[-1] * (2 if dt == F32 else 1)
        start = int(kb * 512)
        _audit.append((float(kb), float(kb) + cols / 512.0, tuple(shape), "f32" if dt == F32 else "bf16"))
        assert start + cols <= ARENA_KB * 512, ("carve OOB", kb, shape)
        v = arena[0:shape[0], start:start + cols]
        if dt == F32:
            v = v.bitcast(F32)
        return v
    build_program.audit = _audit

    vecs_t = p.sb("vecs_t", [128, 128], F32)
    onesPP = p.sb("onesPP", [128, 128], F32)
    ones_col_f = p.sb("ones_col_f", [128, 1], F32)
    ones_col_b = p.sb("ones_col_b", [128, 1], BF16)
    E8b = p.sb("E8b", [72, 512], BF16)
    ISTb = p.sb("ISTb", [128, 64], BF16)
    Eselb = p.sb("Eselb", [16, 128], BF16)
    Ecol4b = p.sb("Ecol4b", [4, 256], BF16)
    RowSelb = p.sb("RowSelb", [16, 1024], BF16)
    M64b = p.sb("M64b", [128, 256], BF16)
    E64f = p.sb("E64f", [4, 256], F32)
    w1b = p.sb("w1b", [128, 64], BF16)
    w2b = p.sb("w2b", [128, 32], BF16)
    negM = p.sb("negM", [128, 8], F32)
    nrm = p.sb("nrm", [1, 96], F32)
    bqs = p.sb("bqs", [128, 8], F32)
    bqvs = p.sb("bqvs", [128, 4], F32)
    scr_row = p.sb("scr_row", [33, 512], F32)
    ones8b = p.sb("ones8b", [72, 8], BF16)
    H4T = p.sb("H4T", [4, 16], BF16)
    H4E = p.sb("H4E", [16, 256], BF16)
    gat = p.sb("gat", [16, 512], F32)
    gatb = p.sb("gatb", [4, 16], BF16)
    mi8 = p.sb("mi8", [1, 8], U32)
    gexp = p.sb("gexp", [128, 2], F32)
    gexpQ = p.sb("gexpQ", [128, 2], F32)
    bcpp = p.sb("bcpp", [128, 6], F32)
    bh3t = p.sb("bh3t", [16, 192], F32)
    mnegM = p.sb("mnegM", [128, 2], F32)
    wrf = p.sb("wrf", [128, 16 * NDC], F32)

    _sizes = [("bq", 8), ("bk", 8), ("bv", 4), ("bqv", 4), ("bf", 8), ("br1", 16),
              ("br2", 8), ("g1", 8), ("b1", 8), ("g2", 8), ("b2", 8), ("bs", 1),
              ("mbr", 1), ("mbo", 8)]
    VC = {}
    _off = 0
    for _n, _k in _sizes:
        VC[_n] = (_off, _k)
        _off += _k

    def vcol(name, i=0):
        o, n = VC[name]
        assert i < n
        return vecs_t[:, o + i:o + i + 1]

    def tap(name, ap, waits):
        if name not in taps:
            return
        if not isinstance(ap, bass.AP):
            ap = ap[:, :]
        t = nc.dram_tensor("tap_" + name, [ap.shape[0], ap.shape[1]], ap.dtype,
                           kind="ExternalOutput").ap()
        p.on("G", lambda e, t=t, ap=ap: e.dma_start(out=t[:, :], in_=ap), waits=waits, dma=True)

    cstage = carve(0, [128, 2048], F32)
    t_vec = p.on("Y", lambda e: e.dma_start(out=vecs_t[:], in_=vecs[:, :]), dma=True)
    t_cst = p.on("Y", lambda e: e.dma_start(out=cstage, in_=consts[:, :]), dma=True)
    t_bh3 = p.on("Y", lambda e: e.dma_start(out=bh3t[:], in_=bh3[:, :]), dma=True)

    def vc_cast(dst, src):
        return p.on("V", lambda e, d=dst, s=src: e.tensor_copy(d, s), waits=[t_cst], inc=(sV, 1))

    vc_cast(onesPP[:, :], cstage[:, 0:128])
    vc_cast(ones_col_f[:, :], cstage[:, 0:1])
    vc_cast(ones_col_b[:, :], cstage[:, 0:1])
    vc_cast(ones8b[64:72, :], cstage[0:8, 0:8])
    vc_cast(ISTb[:, :], cstage[:, 128:192])
    vc_cast(M64b[:, :], cstage[:, 192:448])
    vc_cast(w1b[:, :], cstage[:, 448:512])
    vc_cast(w2b[:, :], cstage[:, 512:544])
    vc_cast(E8b[64:72, :], cstage[0:8, 544:1056])
    vc_cast(E8b[32:40, :], cstage[0:8, 544:1056])
    vc_cast(Eselb[:, :], cstage[0:16, 1056:1184])
    vc_cast(Ecol4b[:, :], cstage[0:4, 1184:1440])
    vc_cast(E64f[:, :], cstage[0:4, 1440:1696])
    tCONST = vc_cast(RowSelb[:, :], cstage[32:48, 544:1568])
    tb1 = p.on("G", lambda e: e.tensor_scalar_mul(bqs[:], vecs_t[:, VC["bq"][0]:VC["bq"][0] + 8], ISQ8),
               waits=[t_vec], inc=(sG, 1))
    tb2 = p.on("G", lambda e: e.tensor_scalar_mul(bqvs[:], vecs_t[:, VC["bqv"][0]:VC["bqv"][0] + 4], ISQ32),
               waits=[t_vec], inc=(sG, 1))

    xf = carve(8, [128, NDC * SEQ], F32)
    x2s = carve(72, [128, SEQ], F32)
    xs1 = carve(80, [128, SEQ], F32)
    xs2 = carve(88, [128, SEQ], F32)
    tmpr = carve(96, [1, SEQ], F32)
    tmp2r = carve(104, [1, SEQ], F32)
    r2row = carve(112, [1, SEQ], F32)
    r2rep = carve(120, [128, SEQ], F32)
    m_rep = carve(128, [128, SEQ], F32)
    r_rep = carve(136, [128, SEQ], F32)
    xn2b = carve(144, [128, NDC * SEQ], BF16)
    statrows = carve(186, [65, SEQ], F32)

    tk_x = [p.on("Y", lambda e, c=c: e.dma_start(out=xf[:, c * SEQ:(c + 1) * SEQ],
                                                 in_=xT[c * 128:(c + 1) * 128, :]), dma=True)
            for c in range(NDC)]

    prev_sq = None
    for c in range(NDC):
        t_sq = p.on("V", lambda e, c=c: e.tensor_tensor(
            x2s, xf[:, c * SEQ:(c + 1) * SEQ], xf[:, c * SEQ:(c + 1) * SEQ], ALU.mult),
            waits=[tk_x[c], prev_sq], inc=(sV, 1))
        for cc in range(4):
            p.on("T", lambda e, c=c, cc=cc: e.matmul(
                pb[cc][0:1, 0:512], ones_col_f[:, :],
                xf[:, c * SEQ + cc * 512: c * SEQ + (cc + 1) * 512],
                start=(c == 0), stop=(c == NDC - 1)), waits=[tk_x[c], tCONST])
            prev_sq = p.on("T", lambda e, c=c, cc=cc: e.matmul(
                pb[4 + cc][0:1, 0:512], ones_col_f[:, :], x2s[:, cc * 512:(cc + 1) * 512],
                start=(c == 0), stop=(c == NDC - 1)), waits=[t_sq], inc=(sT, 1))
    tSTAT = prev_sq

    tv = None
    for cc in range(4):
        p.on("V", lambda e, cc=cc: e.tensor_scalar_mul(
            statrows[0:1, cc * 512:(cc + 1) * 512], pb[cc][0:1, 0:512], 1.0 / D),
            waits=[tSTAT], inc=(sV, 1))
        tv = p.on("V", lambda e, cc=cc: e.tensor_scalar_mul(
            tmpr[0:1, cc * 512:(cc + 1) * 512], pb[4 + cc][0:1, 0:512], 1.0 / D),
            waits=[tSTAT], inc=(sV, 1))
    tv = p.on("V", lambda e: e.tensor_tensor(tmp2r, statrows[0:1, :], statrows[0:1, :], ALU.mult),
              waits=[tv], inc=(sV, 1))
    tv = p.on("V", lambda e: e.tensor_tensor(tmpr, tmpr, tmp2r, ALU.subtract), inc=(sV, 1))
    tv = p.on("V", lambda e: e.tensor_scalar_add(tmpr, tmpr, EPS), inc=(sV, 1))
    p.on("S", lambda e: e.activation(tmpr, tmpr, AF.Ln), waits=[tv], inc=(sS, 1))
    tsq_ = p.on("S", lambda e: e.activation(statrows[64:65, :], tmpr, AF.Exp, scale=-0.5),
                inc=(sS, 1))
    tv = p.on("V", lambda e: e.tensor_tensor(tmpr, statrows[64:65, :], statrows[64:65, :], ALU.mult),
              waits=[tsq_], inc=(sV, 1))
    tv = p.on("V", lambda e: e.tensor_scalar(tmpr, tmpr, -EPS, 1.0 + EPS, ALU.mult, ALU.add), inc=(sV, 1))
    p.on("S", lambda e: e.activation(tmpr, tmpr, AF.Ln), waits=[tv], inc=(sS, 1))
    tr2 = p.on("S", lambda e: e.activation(r2row, tmpr, AF.Exp, scale=-0.5), inc=(sS, 1))

    def replicate(dst, src_row, base_part, waits, bank, width=SEQ):
        last = None
        ncc = width // 512
        for cc in range(ncc):
            w = list(waits) + [bankfree[bank], last]
            tm = p.on("T", lambda e, cc=cc, dst=dst, src_row=src_row, base_part=base_part, bank=bank: e.matmul(
                pb[bank][:, 0:512], onesPP[base_part:base_part + 1, 0:128],
                src_row[0:1, cc * 512:(cc + 1) * 512], start=True, stop=True),
                waits=w, inc=(sT, 1))
            last = p.on("V", lambda e, cc=cc, dst=dst, bank=bank: e.tensor_copy(
                dst[:, cc * 512:(cc + 1) * 512], pb[bank][:, 0:512]), waits=[tm], inc=(sV, 1))
        bankfree[bank] = last
        return last

    tmr = replicate(m_rep, statrows[0:1, :], 0, [tSTAT, p.tok(sV)], 0)
    trr = replicate(r_rep, statrows[64:65, :], 64, [tsq_], 1)
    tr2r = replicate(r2rep, r2row, 0, [tr2], 2)

    tXN2 = None
    for c in range(NDC):
        ta = p.on("V", lambda e, c=c: e.tensor_tensor(
            xs1, xf[:, c * SEQ:(c + 1) * SEQ], m_rep, ALU.subtract),
            waits=[tk_x[c], tmr, trr, tr2r, tXN2], inc=(sV, 1))
        p.on("V", lambda e, c=c: e.tensor_tensor(xs2, xs1, r_rep, ALU.mult), inc=(sV, 1))
        tc2 = p.on("V", lambda e, c=c: e.tensor_tensor(xs1, xs2, r2rep, ALU.mult), inc=(sV, 1))
        tXN2 = p.on("S", lambda e, c=c: e.activation(
            xn2b[:, c * SEQ:(c + 1) * SEQ], xs1, AF.Identity,
            bias=vcol("b1", c), scale=vcol("g1", c)), waits=[tc2, t_vec], inc=(sS, 1))
    tap("ISTb", ISTb, [tCONST])
    tap("E8b", E8b[64:72, :], [tCONST])
    tap("xf0", xf[:, 0:2048], [tk_x[0], tXN2])
    tap("ones_col", ones_col_f[:, :], [tCONST])
    tap("xn2b0", xn2b[:, 0:2048], [tXN2])
    tap("statm", statrows[0:1, :], [tsq_])
    tap("statr", statrows[64:65, :], [tsq_])

    p.barrier("p1", dma_sems=p.dma_pool)

    Qb = carve(0, [128, NDC * SEQ], BF16)
    Kb = carve(32, [128, NDC * SEQ], BF16)
    Vtm = carve(64, [128, NJT * 512], BF16)
    wstage = carve(80, [128, 2048], F32)
    wbf = carve(88, [128, 8192], BF16)
    wbf2 = carve(104, [128, 8192], BF16)
    qvb = carve(120, [128, 4 * SEQ], BF16)
    bvrow = carve(136, [1, 512], F32)
    bvrep = carve(138, [128, 512], F32)
    sTt = carve(140, [16, SEQ], BF16)

    wslot_free = [None, None]

    def stream_w(hbm, nchunks, width, dst, stage, pre=None, eng="G"):
        toks = []
        sem = {"G": sG, "S": sS, "V": sV}[eng]
        for c in range(nchunks):
            s = c % 2
            w = [wslot_free[s]]
            if pre is not None and c < 2:
                w.append(pre)
            tl = p.on("Y", lambda e, c=c, s=s, hbm=hbm, width=width, stage=stage: e.dma_start(
                out=stage[:, s * 1024:s * 1024 + width], in_=hbm[c * 128:(c + 1) * 128, :]),
                waits=w, dma=True)
            if eng == "S":
                tc_ = p.on("S", lambda e, c=c, s=s, dst=dst, width=width, stage=stage: e.copy(
                    dst[:, c * width:(c + 1) * width], stage[:, s * 1024:s * 1024 + width]),
                    waits=[tl], inc=(sem, 1))
            else:
                tc_ = p.on(eng, lambda e, c=c, s=s, dst=dst, width=width, stage=stage: e.tensor_copy(
                    dst[:, c * width:(c + 1) * width], stage[:, s * 1024:s * 1024 + width]),
                    waits=[tl], inc=(sem, 1))
            wslot_free[s] = tc_
            toks.append(tc_)
        return toks

    def fm_proj(wtoks, wtile, wwidth, nmt, evac):
        out_toks = []
        bank_i = 0
        for mt in range(nmt):
            for cc in range(4):
                bki = 4 + (bank_i % 4)
                bk = pb[bki]
                last = None
                for dc in range(NDC):
                    w = [wtoks[dc], tXN2]
                    if dc == 0:
                        w.append(bankfree[bki])
                    last = p.on("T", lambda e, mt=mt, cc=cc, dc=dc, bk=bk, wtile=wtile, wwidth=wwidth: e.matmul(
                        bk[:, 0:512],
                        wtile[:, dc * wwidth + mt * 128: dc * wwidth + (mt + 1) * 128],
                        xn2b[:, dc * SEQ + cc * 512: dc * SEQ + (cc + 1) * 512],
                        start=(dc == 0), stop=(dc == NDC - 1)),
                        waits=w, inc=(sT, 1) if dc == NDC - 1 else None)
                te = evac(mt, cc, bk, last)
                bankfree[bki] = te
                out_toks.append(te)
                bank_i += 1
        return out_toks

    wq_toks = stream_w(wqT, 8, 1024, wbf, wstage, eng="S")
    tQ = fm_proj(wq_toks, wbf, 1024, 8, lambda mt, cc, bk, tk: p.on(
        "V", lambda e, mt=mt, cc=cc, bk=bk: e.tensor_scalar(
            Qb[:, mt * SEQ + cc * 512: mt * SEQ + (cc + 1) * 512], bk[:, 0:512],
            ISQ8, bqs[:, mt:mt + 1], ALU.mult, ALU.add),
        waits=[tk, tb1], inc=(sV, 1)))

    wk_toks = stream_w(wkT, 8, 1024, wbf2, wstage)
    tK = fm_proj(wk_toks, wbf2, 1024, 8, lambda mt, cc, bk, tk: p.on(
        "V", lambda e, mt=mt, cc=cc, bk=bk: e.tensor_scalar(
            Kb[:, mt * SEQ + cc * 512: mt * SEQ + (cc + 1) * 512], bk[:, 0:512],
            vcol("bk", mt), None, ALU.add),
        waits=[tk], inc=(sV, 1)))

    tbv = [p.on("G", lambda e, c=c: e.dma_start(
        out=bvrow[0:1, c * 128:(c + 1) * 128],
        in_=vecs_t[:, VC["bv"][0] + c:VC["bv"][0] + c + 1]), waits=[t_vec], dma=True)
        for c in range(4)]
    tm_ = p.on("T", lambda e: e.matmul(pb[0][:, 0:512], onesPP[0:1, 0:128], bvrow[0:1, :],
                                       start=True, stop=True), waits=tbv + [bankfree[0]], inc=(sT, 1))
    tbvrep = p.on("V", lambda e: e.tensor_copy(bvrep, pb[0][:, 0:512]), waits=[tm_], inc=(sV, 1))
    bankfree[0] = tbvrep

    wv_toks = stream_w(wvT, 8, 512, wbf, wstage, pre=tQ[-1], eng="S")
    tV = None
    for jt in range(NJT):
        bki = 2 + (jt % 2)
        bk = pb[bki]
        last = None
        for dc in range(NDC):
            w = [wv_toks[dc], tXN2]
            if dc == 0:
                w.append(bankfree[bki])
            last = p.on("T", lambda e, jt=jt, dc=dc, bk=bk: e.matmul(
                bk[:, 0:512],
                xn2b[:, dc * SEQ + jt * 128: dc * SEQ + (jt + 1) * 128],
                wbf[:, dc * 512:(dc + 1) * 512],
                start=(dc == 0), stop=(dc == NDC - 1)),
                waits=w, inc=(sT, 1) if dc == NDC - 1 else None)
        tV = p.on("V", lambda e, jt=jt, bk=bk: e.tensor_tensor(
            Vtm[:, jt * 512:(jt + 1) * 512], bk[:, 0:512], bvrep, ALU.add),
            waits=[last, tbvrep], inc=(sV, 1))
        bankfree[bki] = tV

    wqv_toks = stream_w(wqvT, 8, 512, wbf2, wstage, pre=tK[-1])
    tQV = fm_proj(wqv_toks, wbf2, 512, 4, lambda mt, cc, bk, tk: p.on(
        "V", lambda e, mt=mt, cc=cc, bk=bk: e.tensor_scalar(
            qvb[:, mt * SEQ + cc * 512: mt * SEQ + (cc + 1) * 512], bk[:, 0:512],
            ISQ32, bqvs[:, mt:mt + 1], ALU.mult, ALU.add),
        waits=[tk, tb2], inc=(sV, 1)))

    tS_T = None
    for cc2 in range(4):
        last = None
        for mt in range(4):
            w = [tQV[-1], tCONST]
            if mt == 0:
                w += [bankfree[1], tS_T]
            last = p.on("T", lambda e, cc2=cc2, mt=mt: e.matmul(
                pb[1][0:16, 0:512], w1b[:, mt * 16:(mt + 1) * 16],
                qvb[:, mt * SEQ + cc2 * 512: mt * SEQ + (cc2 + 1) * 512],
                start=(mt == 0), stop=(mt == 3)),
                waits=w, inc=(sT, 1) if mt == 3 else None)
        tS_T = p.on("V", lambda e, cc2=cc2: e.tensor_scalar(
            sTt[:, cc2 * 512:(cc2 + 1) * 512], pb[1][0:16, 0:512],
            vcol("bs", 0)[0:16, :], None, ALU.add), waits=[last], inc=(sV, 1))
        bankfree[1] = tS_T

    sqs = wbf[:, 0:2048]
    tprev = None
    for src, base, tdep in ((Qb, 0, tQ[-1]), (Kb, 32, tK[-1])):
        for hp in range(8):
            tq = p.on("V", lambda e, hp=hp, src=src: e.tensor_tensor(
                sqs, src[:, hp * SEQ:(hp + 1) * SEQ], src[:, hp * SEQ:(hp + 1) * SEQ], ALU.mult),
                waits=[tdep, tV, tprev], inc=(sV, 1))
            for cc in range(4):
                tm2 = p.on("T", lambda e, cc=cc: e.matmul(
                    pb[0][0:1, 0:512], ones_col_b[:, :], sqs[:, cc * 512:(cc + 1) * 512],
                    start=True, stop=True), waits=[tq, bankfree[0]], inc=(sT, 1))
                tprev = p.on("V", lambda e, hp=hp, cc=cc, base=base: e.reduce_max(
                    nrm[0:1, base + hp * 4 + cc: base + hp * 4 + cc + 1],
                    pb[0][0:1, 0:512], axis=AX.X), waits=[tm2], inc=(sV, 1))
                bankfree[0] = tprev
    tlastn = tprev
    for hp in range(8):
        p.on("V", lambda e, hp=hp: e.reduce_max(
            nrm[0:1, 64 + hp:64 + hp + 1], nrm[0:1, hp * 4:hp * 4 + 4], axis=AX.X),
            waits=[tlastn], inc=(sV, 1))
        tprev = p.on("V", lambda e, hp=hp: e.reduce_max(
            nrm[0:1, 72 + hp:72 + hp + 1], nrm[0:1, 32 + hp * 4:32 + hp * 4 + 4], axis=AX.X),
            inc=(sV, 1))
    t_ = p.on("V", lambda e: e.tensor_tensor(nrm[0:1, 80:88], nrm[0:1, 64:72], nrm[0:1, 72:80], ALU.mult),
              waits=[tprev], inc=(sV, 1))
    t_ = p.on("S", lambda e: e.activation(nrm[0:1, 88:96], nrm[0:1, 80:88], AF.Sqrt),
              waits=[t_], inc=(sS, 1))
    t_ = p.on("V", lambda e: e.tensor_scalar_mul(nrm[0:1, 80:88], nrm[0:1, 88:96], -1.05),
              waits=[t_], inc=(sV, 1))
    tm2 = p.on("T", lambda e: e.matmul(pb[0][:, 0:8], onesPP[0:1, 0:128], nrm[0:1, 80:88],
                                       start=True, stop=True), waits=[t_, bankfree[0]], inc=(sT, 1))
    tNEGM = p.on("V", lambda e: e.tensor_copy(negM[:], pb[0][:, 0:8]), waits=[tm2], inc=(sV, 1))
    bankfree[0] = tNEGM

    tap("Qb0", Qb[:, 0:2048], [tQ[-1]])
    tap("Kb0", Kb[:, 0:2048], [tK[-1]])
    tap("Vtm0", Vtm[:, 0:512], [tV])
    tap("sTt", sTt, [tS_T])
    tap("negM", negM, [tNEGM])

    p.barrier("p2", dma_sems=p.dma_pool)

    if maxphase <= 2:
        p.on("G", None, waits=p.all_dma_toks())
        p.emit()
        return nc

    EAB = carve(80, [128, NJT * 512], BF16)
    outTA = carve(96, [128, 4 * SEQ], BF16)
    outTB = carve(112, [128, 4 * SEQ], BF16)
    prodb = [carve(132, [128, 512], BF16), carve(133, [128, 512], BF16)]
    def carveR(row, kb, shape, dt):
        cols = shape[-1] * (2 if dt == F32 else 1)
        start = int(kb * 512)
        v = arena[row:row + shape[0], start:start + cols]
        return v.bitcast(F32) if dt == F32 else v
    _rowp = {0: 64, 1: 32}
    cS_m = {(hh, par): carveR(_rowp[par], 134 + 2 * hh, [8, 512], F32)
            for hh in (0, 1) for par in (0, 1)}
    ivd_m = {(0, 0): carveR(64, 144, [8, 512], F32), (1, 0): carveR(64, 146, [8, 512], F32),
             (0, 1): carveR(32, 128, [8, 512], F32), (1, 1): carveR(32, 130, [8, 512], F32)}
    ivr_m = {(0, 0): carveR(64, 148, [8, 512], F32), (1, 0): carveR(64, 150, [8, 512], F32),
             (0, 1): carveR(32, 140, [8, 512], F32), (1, 1): carveR(32, 142, [8, 512], F32)}
    expcB_m = {(0, 0): carveR(64, 184, [8, 512], BF16), (1, 0): carveR(64, 185, [8, 512], BF16),
               (0, 1): carveR(64, 130, [8, 512], BF16), (1, 1): carveR(64, 131, [8, 512], BF16)}
    compfs_m = {(0, 0): carveR(64, 138, [8, 512], BF16), (1, 0): carveR(64, 139, [8, 512], BF16),
                (0, 1): carveR(32, 138, [8, 512], BF16), (1, 1): carveR(32, 139, [8, 512], BF16),
                (0, 2): carveR(64, 128, [8, 512], BF16), (1, 2): carveR(64, 129, [8, 512], BF16),
                (0, 3): carveR(32, 184, [8, 512], BF16), (1, 3): carveR(32, 185, [8, 512], BF16)}
    denAB = carve(144, [33, SEQ], F32)
    finall = carve(152, [128, NDC * SEQ], BF16)

    wfb = carve(0, [128, 8192], BF16)
    wstage4 = carve(16, [128, 2048], F32)
    wf_toks = None

    tok_pv = {}
    tok_sband = {}
    tFIN = None
    gate_out = {}
    gate_den = {}
    epi = {"bkr": {7: None, 0: None, 3: None, 1: None}, "bkfinr": {4: None, 5: None},
           "crepfree": {0: [None, None], 1: [None, None]},
           "prodfree": {0: None, 1: None}, "foldfree": {0: None, 1: None}, "tfin": None}

    def emit_epilogue(hp, evdc, denv):
        epi["foldfree"][0] = denv
        epi["foldfree"][1] = evdc[0]

        def comp_chain(hh, cc2, bank):
            head = 2 * hp + hh
            par = cc2 % 2
            outT = outTA if hh == 0 else outTB
            sl = slice(cc2 * 512, (cc2 + 1) * 512)
            ivd = ivd_m[(hh, par)]
            ivr = ivr_m[(hh, par)]
            cS = cS_m[(hh, par)]
            expcB = expcB_m[(hh, par)]
            compfs = compfs_m[(hh, cc2)]
            st = {}
            bk = pb[bank]

            def s0():
                last = None
                for dc in range(4):
                    w = [evdc[dc]]
                    if dc == 0:
                        w.append(epi["bkr"][bank])
                        if bank == 3:
                            w.append(evdc[1])
                    last = p.on("T", lambda e, dc=dc: e.matmul(
                        bk[0:8, 0:512], w2b[:, dc * 8:(dc + 1) * 8],
                        outT[:, dc * SEQ + sl.start: dc * SEQ + sl.stop],
                        start=(dc == 0), stop=(dc == 3)),
                        waits=w, inc=(sT, 1) if dc == 3 else None)
                st["c"] = last

            def s1():
                st["t1"] = p.on("V", lambda e: e.tensor_copy(cS, bk[0:8, 0:512]),
                                waits=[st["c"]], inc=(sV, 1))

            def s2():
                st["tsr"] = p.on("T", lambda e: e.matmul(
                    bk[32:40, 0:512], Eselb[:, head * 8:(head + 1) * 8], sTt[:, sl],
                    start=True, stop=True), waits=[tS_T, st["t1"]], inc=(sT, 1))
                st["tir"] = p.on("T", lambda e: e.matmul(
                    bk[64:72, 0:512], onesPP[32 * hh:32 * hh + 1, 0:8],
                    denAB[32 * hh:32 * hh + 1, sl],
                    start=True, stop=True), waits=[denv], inc=(sT, 1))
                gate_den.setdefault(cc2, [None, None])[hh] = st["tir"]

            def s2b():
                p.on("S", lambda e: e.activation(ivd, bk[64:72, 0:512], AF.Ln),
                     waits=[st["tir"]], inc=(sS, 1))
                st["trec"] = p.on("S", lambda e: e.activation(ivd, ivd, AF.Exp, scale=-1.0),
                                  inc=(sS, 1))

            def s3():
                p.on("V", lambda e: e.tensor_tensor(cS, cS, ivd, ALU.mult),
                     waits=[st["trec"]], inc=(sV, 1))
                st["t3"] = p.on("V", lambda e: e.tensor_tensor(
                    cS, cS, bk[32:40, 0:512], ALU.add), waits=[st["tsr"]], inc=(sV, 1))

            def s4():
                st["t4"] = p.on("S", lambda e: e.activation(cS, cS, AF.Exp),
                                waits=[st["t3"]], inc=(sS, 1))
                st["t4b"] = p.on("S", lambda e: e.copy(expcB, cS),
                                 waits=[st["t4"]], inc=(sS, 1))

            def s5():
                st["t5"] = p.on("T", lambda e: e.matmul(
                    bk[32:40, 0:512], ones8b[64:72, :], expcB,
                    start=True, stop=True), waits=[st["t4b"], st["t3"]], inc=(sT, 1))

            def s6():
                p.on("S", lambda e: e.activation(ivr, bk[32:40, 0:512], AF.Ln),
                     waits=[st["t5"]], inc=(sS, 1))
                st["trecr"] = p.on("S", lambda e: e.activation(ivr, ivr, AF.Exp, scale=-1.0),
                                   inc=(sS, 1))

            def s7():
                st["tc1"] = p.on("G", lambda e: e.tensor_tensor(
                    cS, cS, ivd, ALU.mult),
                    waits=[st["t4b"]], inc=(sG, 1))

            def s8():
                st["tcf"] = p.on("G", lambda e: e.tensor_tensor(
                    compfs[:, :], cS, ivr, ALU.mult),
                    waits=[st["trecr"], st["tc1"]], inc=(sG, 1))
                epi["bkr"][bank] = st["tcf"]

            return [s0, s1, s2, s2b, s3, s4, s5, s6, s7, s8], st

        def fin_chain(hh, cc2, st):
            outT = outTA if hh == 0 else outTB
            sl = slice(cc2 * 512, (cc2 + 1) * 512)
            compfs = compfs_m[(hh, cc2)]
            e8r = _rowp[cc2 % 2]
            rbank = 4 + hh
            steps = []
            tprs = []

            def mk_rep_prod(dc):
                def f():
                    w = [st["tcf"], epi["bkfinr"][rbank]]
                    trep = p.on("T", lambda e: e.matmul(
                        pb[rbank][:, 0:512], E8b[e8r:e8r + 8, dc * 128:(dc + 1) * 128], compfs[:, :],
                        start=True, stop=True), waits=w, inc=(sT, 1))
                    pbuf = prodb[hh]
                    tpr = p.on("V", lambda e: e.tensor_tensor(
                        pbuf[:, :], outT[:, dc * SEQ + sl.start: dc * SEQ + sl.stop],
                        pb[rbank][:, 0:512], ALU.mult),
                        waits=[trep, epi["prodfree"][hh]], inc=(sV, 1))
                    tprs.append(tpr)
                    epi["bkfinr"][rbank] = tpr
                    fb = 6 if hh == 0 else 2
                    w2 = [tpr]
                    if dc == 0:
                        w2.append(epi["foldfree"][hh])
                    tff = p.on("T", lambda e, dc=dc, fb=fb: e.matmul(
                        pb[fb][0:64, 0:512], ISTb[:, :], pbuf,
                        start=(dc == 0), stop=(dc == 3)), waits=w2, inc=(sT, 1))
                    epi["prodfree"][hh] = tff
                    st["tff"] = tff
                return f

            for dc in range(4):
                steps.append(mk_rep_prod(dc))

            def s_ev():
                fb = 6 if hh == 0 else 2
                tfin = p.on("V", lambda e, fb=fb: e.tensor_copy(
                    finall[64 * hh:64 * (hh + 1), hp * SEQ + sl.start: hp * SEQ + sl.stop],
                    pb[fb][0:64, 0:512]), waits=[st["tff"]], inc=(sV, 1))
                epi["foldfree"][hh] = tfin
                epi["tfin"] = tfin
                if hh == 1:
                    gate_out[cc2] = tprs[-1]
            steps.append(s_ev)
            return steps

        def zrun(chains):
            for ops in itertools.zip_longest(*chains):
                for op in ops:
                    if op:
                        op()
        metas = {}
        for wave in (0, 1):
            comps = []
            for cc2 in (2 * wave, 2 * wave + 1):
                ca, sta = comp_chain(0, cc2, (7, 3)[cc2 % 2])
                cb, stb = comp_chain(1, cc2, (0, 1)[cc2 % 2])
                comps += [ca, cb]
                metas[cc2] = (sta, stb)
            if wave == 1:
                comps += [fin_chain(0, 0, metas[0][0]), fin_chain(1, 0, metas[0][1])]
            zrun(comps)
        for cc2 in range(1, 4):
            sta, stb = metas[cc2]
            zrun([fin_chain(0, cc2, sta), fin_chain(1, cc2, stb)])
        return epi["tfin"]

    for hp in range(hpn):
        tok_pv_prevhp = tok_pv
        tok_exp = {}
        tok_pv = {}
        pending_evac = None
        for iq in range(NIQ):
            def s_and_exp(j, iq=iq, hp=hp, tok_pv_prevhp=tok_pv_prevhp):
                w = [tQ[-1], tK[-1], tNEGM, tok_sband.get(0), bankfree[0], bankfree[1]]
                p.on("T", lambda e, j=j, iq=iq, hp=hp: e.matmul(
                    pb[0][:, 0:IQ],
                    Kb[0:64, hp * SEQ + j * 128: hp * SEQ + (j + 1) * 128],
                    Qb[0:64, hp * SEQ + iq * IQ: hp * SEQ + (iq + 1) * IQ],
                    start=True, stop=True, tile_position=(0, 0)), waits=w)
                tmm = p.on("T", lambda e, j=j, iq=iq, hp=hp: e.matmul(
                    pb[1][:, 0:IQ],
                    Kb[64:128, hp * SEQ + j * 128: hp * SEQ + (j + 1) * 128],
                    Qb[64:128, hp * SEQ + iq * IQ: hp * SEQ + (iq + 1) * IQ],
                    start=True, stop=True, tile_position=(64, 0)), inc=(sT, 1))
                if iq == 0:
                    prevpv = tok_pv_prevhp.get((NIQ - 1, j))
                else:
                    prevpv = tok_pv.get((iq - 1, j))
                te = p.on("S", lambda e, j=j, hp=hp: e.activation(
                    EAB[:, j * 512:(j + 1) * 512].rearrange("p (a b) -> p a b", b=IQ),
                    psS[:, 0:1024].rearrange("p (a b) -> p a b", b=512)[:, :, 0:IQ],
                    AF.Exp, bias=negM[:, hp:hp + 1]), waits=[tmm, prevpv], inc=(sS, 1))
                tok_exp[(iq, j)] = te
                tok_sband[0] = te

            s_and_exp(0)
            s_and_exp(1)
            if pending_evac is not None:
                pending_evac()
                pending_evac = None
            for j in range(NJT):
                if j + 2 <= NJT - 1:
                    s_and_exp(j + 2)
                for dc in range(4):
                    bki = 2 + dc
                    bk = pb[bki]
                    w = [tok_exp[(iq, j)], tV]
                    if j == 0:
                        w.append(bankfree[bki])
                    p.on("T", lambda e, j=j, dc=dc, bk=bk: e.matmul(
                        bk[:, 0:2 * IQ],
                        Vtm[:, j * 512 + dc * 128: j * 512 + (dc + 1) * 128],
                        EAB[:, j * 512: (j + 1) * 512],
                        start=(j == 0), stop=(j == NJT - 1)), waits=w)
                w = [tok_exp[(iq, j)]]
                if j == 0:
                    w.append(bankfree[6])
                tok_pv[(iq, j)] = p.on("T", lambda e, j=j: e.matmul(
                    pb[6][0:1, 0:2 * IQ], ones_col_b[:, :], EAB[:, j * 512: (j + 1) * 512],
                    start=(j == 0), stop=(j == NJT - 1)), waits=w, inc=(sT, 1))
            tlast = tok_pv[(iq, NJT - 1)]
            go = gate_out.get(iq // 2)
            gd = gate_den.get(iq // 2)
            w = [tlast] + (gd if gd else [])
            p.on("V", lambda e, iq=iq: e.tensor_copy(
                denAB[0:1, iq * IQ:(iq + 1) * IQ], pb[6][0:1, 0:IQ]), waits=w, inc=(sV, 1))
            bankfree[6] = p.on("V", lambda e, iq=iq: e.tensor_copy(
                denAB[32:33, iq * IQ:(iq + 1) * IQ], pb[6][0:1, IQ:2 * IQ]),
                inc=(sV, 1))
            def mk_evac(iq=iq, tlast=tlast, go=go):
                def f():
                    for dc in range(4):
                        bk = pb[2 + dc]
                        ta = p.on("S", lambda e, dc=dc, iq=iq, bk=bk: e.copy(
                            outTA[:, dc * SEQ + iq * IQ: dc * SEQ + (iq + 1) * IQ], bk[:, 0:IQ]),
                            waits=[tlast, go], inc=(sS, 1))
                        tb = p.on("V", lambda e, dc=dc, iq=iq, bk=bk: e.tensor_copy(
                            outTB[:, dc * SEQ + iq * IQ: dc * SEQ + (iq + 1) * IQ], bk[:, IQ:2 * IQ]),
                            waits=[tlast, go], inc=(sV, 1))
                        bankfree[2 + dc] = [ta, tb]
                return f
            pending_evac = mk_evac()
        pending_evac()
        pending_evac = None
        tFIN = emit_epilogue(hp, [bankfree[2 + dc] for dc in range(4)], bankfree[6])
        bankfree[0] = epi["bkr"][0]
        bankfree[1] = epi["bkr"][1]
        bankfree[2] = epi["foldfree"][1]
        bankfree[3] = epi["bkr"][3]
        bankfree[4] = epi["bkfinr"][4]
        bankfree[5] = epi["bkfinr"][5]
        bankfree[6] = epi["foldfree"][0]
        if hp == 5 and maxphase >= 4:
            wslot_free[0] = wslot_free[1] = None
            wf_toks = stream_w(wfT, 8, 1024, wfb, wstage4, pre=p.tok(sT), eng="S")

    tap("finall0", finall[:, 0:2048], [tFIN])
    tap("outTB2", outTB[:, 2 * SEQ:3 * SEQ], [p.tok(sV)])
    tap("outTBf", outTB, [p.tok(sV)])
    tap("outTAf", outTA, [p.tok(sV)])
    tap("EABlast", EAB[:, 0:1024], [p.tok(sV)])
    tap("denA", denAB[0:1, :], [p.tok(sV)])
    tap("denB", denAB[32:33, :], [p.tok(sV)])
    tap("outTA0", outTA[:, 0:2048], [p.tok(sV)])

    p.barrier("p3", dma_sems=p.dma_pool)

    if maxphase <= 3:
        p.on("G", None, waits=p.all_dma_toks())
        p.emit()
        return nc

    mrepw = carve(24, [128, WIN], F32)
    rrepw = carve(26, [128, WIN], F32)
    xw = [carve(28, [128, WIN], F32), carve(30, [128, WIN], F32)]
    wr1b = carve(32, [128, 16384], BF16)
    wr2b = carve(64, [128, 16384], BF16)
    amidw = carve(96, [128, NDC * WIN], F32)
    h1w = carve(112, [128, 16 * WIN], BF16)
    xn2w = carve(128, [128, NDC * WIN], BF16)
    outw = [carve(136, [128, WIN], F32), carve(138, [128, WIN], F32)]
    sqw = carve(140, [128, WIN], F32)
    m2row = carve(142, [1, WIN], F32)
    vtmp = carve(144, [1, WIN], F32)
    r2wrow = carve(146, [1, WIN], F32)
    m2rep = carve(148, [128, WIN], F32)
    r2wrep = carve(150, [128, WIN], F32)
    tmp4 = carve(184, [128, WIN], F32)

    if wf_toks is None:
        wslot_free[0] = wslot_free[1] = None
        wf_toks = stream_w(wfT, 8, 1024, wfb, wstage4, eng="S")
    wslot_free[0] = wslot_free[1] = None
    wr1_toks = []
    for c in range(8):
        tl = p.on("Y", lambda e, c=c: e.dma_start(
            out=wstage4[:, 0:2048], in_=wr1T[c * 128:(c + 1) * 128, :]),
            waits=[wslot_free[0], wslot_free[1]], dma=True)
        tc_ = p.on("S", lambda e, c=c: e.copy(
            wr1b[:, c * 2048:(c + 1) * 2048], wstage4[:, 0:2048]), waits=[tl], inc=(sS, 1))
        wslot_free[0] = wslot_free[1] = tc_
        wr1_toks.append(tc_)
    wr2_toks = stream_w(wr2T, 16, 1024, wr2b, wstage4)

    tow = [None, None]
    th1 = {}
    txn2w_last = None
    wr1done_prev = None
    fsave = {}
    for w_i in range(NW + 1):
        if w_i < NW:
            w0 = w_i * WIN
            fpP = (4, 5) if w_i % 2 == 0 else (6, 7)
            tmrw = replicate(mrepw, statrows[0:1, w0:w0 + WIN], 0, [], 0, width=WIN)
            trrw = replicate(rrepw, statrows[64:65, w0:w0 + WIN], 64, [], 1, width=WIN)
            tamid = {}
            for dc in range(NDC):
                bki = fpP[dc % 2]
                bk = pb[bki]
                last = None
                for fc in range(NDC):
                    wts = [wf_toks[fc], tFIN]
                    if fc == 0:
                        wts.append(bankfree[bki])
                    last = p.on("T", lambda e, dc=dc, fc=fc, w0=w0, bk=bk: e.matmul(
                        bk[:, 0:WIN], wfb[:, fc * 1024 + dc * 128: fc * 1024 + (dc + 1) * 128],
                        finall[:, fc * SEQ + w0: fc * SEQ + w0 + WIN],
                        start=(fc == 0), stop=(fc == NDC - 1)),
                        waits=wts, inc=(sT, 1) if fc == NDC - 1 else None)
                txl = p.on("S", lambda e, dc=dc, w0=w0: e.dma_start(
                    out=xw[dc % 2], in_=xT[dc * 128:(dc + 1) * 128, w0:w0 + WIN]),
                    waits=[tow[dc % 2]], dma=True)
                t1 = p.on("G", lambda e, dc=dc: e.tensor_tensor(
                    tmp4, xw[dc % 2], mrepw, ALU.subtract),
                    waits=[txl, tmrw, trrw, txn2w_last], inc=(sG, 1))
                t2g = p.on("G", lambda e, dc=dc: e.tensor_tensor(tmp4, tmp4, rrepw, ALU.mult),
                           inc=(sG, 1))
                t3 = p.on("V", lambda e, dc=dc, bk=bk: e.tensor_scalar(
                    sqw, bk[:, 0:WIN], vcol("bf", dc), None, ALU.add),
                    waits=[last] + ([txn2w_last] if dc == 0 else []), inc=(sV, 1))
                bankfree[bki] = t3
                t4 = p.on("V", lambda e, dc=dc: e.tensor_tensor(
                    amidw[:, dc * WIN:(dc + 1) * WIN], tmp4, sqw, ALU.add),
                    waits=[t2g], inc=(sV, 1))
                t5 = p.on("V", lambda e, dc=dc: e.tensor_tensor(
                    outw[dc % 2], xw[dc % 2], amidw[:, dc * WIN:(dc + 1) * WIN], ALU.add),
                    waits=[tow[dc % 2]], inc=(sV, 1))
                _src = (lambda dc: amidw[:, dc * WIN:(dc + 1) * WIN]) if getattr(build_program, "_plain_amid_only", False) else (lambda dc: outw[dc % 2])
                tst = p.on("G", lambda e, dc=dc, w0=w0, _src=_src: e.dma_start(
                    out=out_hbm[dc * 128:(dc + 1) * 128, w0:w0 + WIN], in_=_src(dc)),
                    waits=[t5], dma=True)
                tow[dc % 2] = tst
                tamid[dc] = t4
            fsave[w_i] = (tamid, None, fpP, w0)
        if w_i >= 1:
            for mt in range(16):
                bki = (2, 3)[mt % 2]
                bk = pb[bki]
                last = None
                for dc in range(NDC):
                    wts = [wr1_toks[dc], txn2w[dc]]
                    if dc == 0:
                        wts.append(bankfree[bki])
                    last = p.on("T", lambda e, mt=mt, dc=dc, bk=bk: e.matmul(
                        bk[:, 0:WIN], wr1b[:, dc * 2048 + mt * 128: dc * 2048 + (mt + 1) * 128],
                        xn2w[:, dc * WIN:(dc + 1) * WIN],
                        start=(dc == 0), stop=(dc == NDC - 1)),
                        waits=wts, inc=(sT, 1) if dc == NDC - 1 else None)
                th1[mt] = p.on("S", lambda e, mt=mt, bk=bk: e.activation(
                    h1w[:, mt * WIN:(mt + 1) * WIN], bk[:, 0:WIN], AF.Relu,
                    bias=vcol("br1", mt)), waits=[last], inc=(sS, 1))
                bankfree[bki] = th1[mt]
            wr1done_prev = th1[15]
        if w_i < NW:
            tamid = fsave[w_i][0]
            tprev = None
            for dc in range(NDC):
                tsq2 = p.on("V", lambda e, dc=dc: e.tensor_tensor(
                    sqw, amidw[:, dc * WIN:(dc + 1) * WIN], amidw[:, dc * WIN:(dc + 1) * WIN], ALU.mult),
                    waits=[tamid[dc], tprev], inc=(sV, 1))
                w = [tamid[dc]]
                if dc == 0:
                    w += [bankfree[0], bankfree[1]]
                p.on("T", lambda e, dc=dc: e.matmul(
                    pb[0][0:1, 0:WIN], ones_col_f[:, :], amidw[:, dc * WIN:(dc + 1) * WIN],
                    start=(dc == 0), stop=(dc == NDC - 1)), waits=w)
                tprev = p.on("T", lambda e, dc=dc: e.matmul(
                    pb[1][0:1, 0:WIN], ones_col_f[:, :], sqw,
                    start=(dc == 0), stop=(dc == NDC - 1)), waits=[tsq2], inc=(sT, 1))
            tv = p.on("V", lambda e: e.tensor_scalar_mul(m2row, pb[0][0:1, 0:WIN], 1.0 / D),
                      waits=[tprev], inc=(sV, 1))
            tv = p.on("V", lambda e: e.tensor_scalar_mul(vtmp, pb[1][0:1, 0:WIN], 1.0 / D),
                      inc=(sV, 1))
            tv = p.on("V", lambda e: e.tensor_tensor(r2wrow, m2row, m2row, ALU.mult),
                      inc=(sV, 1))
            tv = p.on("V", lambda e: e.tensor_tensor(vtmp, vtmp, r2wrow, ALU.subtract),
                      inc=(sV, 1))
            tv = p.on("V", lambda e: e.tensor_scalar_add(vtmp, vtmp, EPS), inc=(sV, 1))
            p.on("S", lambda e: e.activation(vtmp, vtmp, AF.Ln),
                 waits=[tv], inc=(sS, 1))
            tsqr = p.on("S", lambda e: e.activation(r2wrow, vtmp, AF.Exp, scale=-0.5),
                        inc=(sS, 1))
            tm_a = p.on("T", lambda e: e.matmul(pb[0][:, 0:WIN], onesPP[0:1, 0:128], m2row,
                                                start=True, stop=True), waits=[tv], inc=(sT, 1))
            tm2r = p.on("V", lambda e: e.tensor_copy(m2rep, pb[0][:, 0:WIN]), waits=[tm_a], inc=(sV, 1))
            bankfree[0] = tm2r
            tm_b = p.on("T", lambda e: e.matmul(pb[1][:, 0:WIN], onesPP[0:1, 0:128], r2wrow,
                                                start=True, stop=True), waits=[tsqr], inc=(sT, 1))
            tr2w = p.on("V", lambda e: e.tensor_copy(r2wrep, pb[1][:, 0:WIN]), waits=[tm_b], inc=(sV, 1))
            bankfree[1] = tr2w
            txn2w = {}
            for dc in range(NDC):
                tb = tmp4 if dc % 2 == 0 else sqw
                w_x = [tm2r, tr2w, txn2w.get(dc - 2)]
                if dc == 0:
                    w_x.append(wr1done_prev)
                p.on("V", lambda e, dc=dc, tb=tb: e.tensor_tensor(
                    tb, amidw[:, dc * WIN:(dc + 1) * WIN], m2rep, ALU.subtract),
                    waits=w_x, inc=(sV, 1))
                t2 = p.on("V", lambda e, dc=dc, tb=tb: e.tensor_tensor(tb, tb, r2wrep, ALU.mult), inc=(sV, 1))
                txn2w[dc] = p.on("S", lambda e, dc=dc, tb=tb: e.activation(
                    xn2w[:, dc * WIN:(dc + 1) * WIN], tb, AF.Identity,
                    bias=vcol("b2", dc), scale=vcol("g2", dc)), waits=[t2], inc=(sS, 1))
                txn2w_last = txn2w[dc]
        if w_i >= 1:
            _, _, fpPp, w0p = fsave[w_i - 1]
            for mt in range(NDC):
                bki = fpPp[mt % 2]
                bk = pb[bki]
                last = None
                for hc in range(16):
                    wts = [wr2_toks[hc], th1[hc]]
                    if hc == 0:
                        wts += [bankfree[bki], tow[mt % 2]]
                    last = p.on("T", lambda e, mt=mt, hc=hc, bk=bk: e.matmul(
                        bk[:, 0:WIN], wr2b[:, hc * 1024 + mt * 128: hc * 1024 + (mt + 1) * 128],
                        h1w[:, hc * WIN:(hc + 1) * WIN],
                        start=(hc == 0), stop=(hc == 15)),
                        waits=wts, inc=(sT, 1) if hc == 15 else None)
                t1 = p.on("V", lambda e, mt=mt, bk=bk: e.tensor_scalar(
                    outw[mt % 2], bk[:, 0:WIN], vcol("br2", mt), None, ALU.add),
                    waits=[last, tow[mt % 2]], inc=(sV, 1))
                bankfree[bki] = t1
                if not getattr(build_program, "_skip_iffn_dma", False):
                    tow[mt % 2] = p.on("G", lambda e, mt=mt, w0p=w0p: e.dma_start(
                        out=out_hbm[mt * 128:(mt + 1) * 128, w0p:w0p + WIN], in_=outw[mt % 2],
                        accum_op=ALU.add), waits=[t1], dma=True)
                else:
                    tow[mt % 2] = t1

    tap("amidw", amidw, [p.tok(sV)])
    tap("xn2w", xn2w, [p.tok(sS)])
    tap("h1w", h1w, [p.tok(sS)])
    tap("wr1b0", wr1b[:, 0:2048], [wr1_toks[0]])
    tap("wr2b0", wr2b[:, 0:1024], [wr2_toks[0]])
    tap("wr2bf", wr2b, [wr2_toks[-1]])
    tap("wr1bf", wr1b, [wr1_toks[-1]])

    p.barrier("p4", dma_sems=p.dma_pool)

    m_rep5 = carve(0, [128, SEQ], F32)
    r_rep5 = carve(8, [128, SEQ], F32)
    xnb = carve(16, [128, NDC * SEQ], BF16)
    xscr = carve(48, [128, SEQ], F32)
    mwbf = carve(56, [128, 8192], BF16)
    Gb = carve(72, [128, 2048], BF16)
    wqc = carve(76, [128, 2048], BF16)
    wkc = carve(80, [128, 2048], BF16)
    wvc = carve(84, [128, 2048], BF16)
    qkb = carve(88, [128, 2 * SEQ], BF16)
    kkb = carve(96, [128, 2 * SEQ], BF16)
    vktm = carve(104, [128, NJT * 288], BF16)
    EABm = carve(114, [128, NJT * 512], BF16)
    outmA = carve(130, [65, SEQ], F32)
    outmB = carve(138, [65, SEQ], F32)
    outc = carve(146, [128, 2 * SEQ], BF16)
    mrep16 = carve(154, [16, SEQ], F32)
    rrep16 = carve(162, [16, SEQ], F32)
    logsb = carve(170, [16, SEQ], F32)
    mscr = carve(178, [128, SEQ], F32)
    grep = carve(154, [128, 256], F32)
    bvcrep = carve(155, [128, 256], F32)
    mow = [carve(130, [128, WIN], F32), carve(132, [128, WIN], F32)]

    tmr5 = replicate(m_rep5, statrows[0:1, :], 0, [], 0)
    trr5 = replicate(r_rep5, statrows[64:65, :], 64, [], 1)

    twr = p.on("Y", lambda e: e.dma_start(
        out=wrf[:].rearrange("p (c n) -> p c n", c=8), in_=mwr.rearrange("(c p) n -> p c n", p=128)),
        dma=True)

    tlog = None
    txnb = {}
    prevx = None
    for c in range(NDC):
        txl = p.on("Y", lambda e, c=c: e.dma_start(
            out=xscr, in_=xT[c * 128:(c + 1) * 128, :]), waits=[prevx, tlog], dma=True)
        for cc in range(4):
            w = [txl, twr]
            if c == 0:
                w.append(bankfree[4 + cc])
            tlog = p.on("T", lambda e, c=c, cc=cc: e.matmul(
                pb[4 + cc][0:16, 0:512], wrf[:, c * 16:(c + 1) * 16],
                xscr[:, cc * 512:(cc + 1) * 512],
                start=(c == 0), stop=(c == NDC - 1)), waits=w, inc=(sT, 1))
        t1 = p.on("V", lambda e, c=c: e.tensor_tensor(mscr, xscr, m_rep5, ALU.subtract),
                  waits=[txl, tmr5, trr5], inc=(sV, 1))
        prevx = t1
        txnb[c] = p.on("V", lambda e, c=c: e.tensor_tensor(
            xnb[:, c * SEQ:(c + 1) * SEQ], mscr, r_rep5, ALU.mult), inc=(sV, 1))
    tXNB = txnb[NDC - 1]

    tcs = None
    for c in range(NDC):
        w = [twr, tCONST]
        if c == 0:
            w.append(bankfree[0])
        tcs = p.on("T", lambda e, c=c: e.matmul(
            pb[0][0:16, 0:1], wrf[:, c * 16:(c + 1) * 16], ones_col_f[:, :],
            start=(c == 0), stop=(c == NDC - 1)), waits=w, inc=(sT, 1))
    tcsv = p.on("V", lambda e: e.tensor_copy(gat[0:16, 32:33], pb[0][0:16, 0:1]), waits=[tcs], inc=(sV, 1))
    bankfree[0] = tcsv

    trm16 = None
    for cc in range(4):
        tm_a = p.on("T", lambda e, cc=cc: e.matmul(
            pb[2][0:16, 0:512], onesPP[0:1, 0:16], statrows[0:1, cc * 512:(cc + 1) * 512],
            start=True, stop=True), waits=[trm16, bankfree[2]], inc=(sT, 1))
        tmv = p.on("V", lambda e, cc=cc: e.tensor_copy(mrep16[:, cc * 512:(cc + 1) * 512], pb[2][0:16, 0:512]),
                   waits=[tm_a], inc=(sV, 1))
        bankfree[2] = tmv
        tm_b = p.on("T", lambda e, cc=cc: e.matmul(
            pb[3][0:16, 0:512], onesPP[64:65, 0:16], statrows[64:65, cc * 512:(cc + 1) * 512],
            start=True, stop=True), waits=[tmv, bankfree[3]], inc=(sT, 1))
        trm16 = p.on("V", lambda e, cc=cc: e.tensor_copy(rrep16[:, cc * 512:(cc + 1) * 512], pb[3][0:16, 0:512]),
                     waits=[tm_b], inc=(sV, 1))
        bankfree[3] = trm16

    tlg = None
    for cc in range(4):
        p.on("V", lambda e, cc=cc: e.tensor_scalar(
            logsb[:, cc * 512:(cc + 1) * 512], mrep16[:, cc * 512:(cc + 1) * 512],
            gat[0:16, 32:33], None, ALU.mult), waits=[tlog, tcsv, trm16], inc=(sV, 1))
        p.on("V", lambda e, cc=cc: e.tensor_tensor(
            logsb[:, cc * 512:(cc + 1) * 512], pb[4 + cc][0:16, 0:512],
            logsb[:, cc * 512:(cc + 1) * 512], ALU.subtract), inc=(sV, 1))
        tlg = p.on("V", lambda e, cc=cc: e.tensor_tensor(
            logsb[:, cc * 512:(cc + 1) * 512], logsb[:, cc * 512:(cc + 1) * 512],
            rrep16[:, cc * 512:(cc + 1) * 512], ALU.mult), inc=(sV, 1))
        bankfree[4 + cc] = tlg
    tmean = p.on("V", lambda e: e.reduce_sum(gat[0:16, 33:34], logsb, axis=AX.X), waits=[tlg], inc=(sV, 1))
    tmean = p.on("V", lambda e: e.tensor_scalar(
        gat[0:16, 34:35], gat[0:16, 33:34], 1.0 / SEQ, vcol("mbr", 0)[0:16, :], ALU.mult, ALU.add),
        inc=(sV, 1))
    tdm = p.on("G", lambda e: e.dma_start(out=gat[0:1, 0:16], in_=gat[0:16, 34:35]),
               waits=[tmean], dma=True)
    t1 = p.on("V", lambda e: e.reduce_max(gat[0:1, 16:17], gat[0:1, 0:16], axis=AX.X),
              waits=[tdm], inc=(sV, 1))
    t2 = p.on("V", lambda e: e.tensor_tensor(gat[0:1, 0:16], gat[0:1, 0:16],
                                             gat[0:1, 16:17].broadcast_to([1, 16]), ALU.subtract),
              inc=(sV, 1))
    t3 = p.on("S", lambda e: e.activation(gat[0:1, 0:16], gat[0:1, 0:16], AF.Exp), waits=[t2], inc=(sS, 1))
    t4 = p.on("V", lambda e: e.reduce_sum(gat[0:1, 16:17], gat[0:1, 0:16], axis=AX.X),
              waits=[t3], inc=(sV, 1))
    t5 = p.on("V", lambda e: e.reciprocal(gat[0:1, 17:18], gat[0:1, 16:17]), inc=(sV, 1))
    tdist = p.on("V", lambda e: e.tensor_tensor(gat[0:1, 0:16], gat[0:1, 0:16],
                                                gat[0:1, 17:18].broadcast_to([1, 16]), ALU.mult),
                 inc=(sV, 1))
    p.on("V", lambda e: e.max(gat[0:1, 18:26], gat[0:1, 0:16]), waits=[tdist], inc=(sV, 1))
    ttk = p.on("V", lambda e: e.max_index(mi8[0:1, :], gat[0:1, 18:26], gat[0:1, 0:16]),
               inc=(sV, 1))
    tsp = p.on("V", lambda e: e.memset(gat[0:1, 48:64], 0.0), inc=(sV, 1))
    toh = {}
    for k in range(4):
        toh[k] = p.on("V", lambda e, k=k: e.tensor_tensor(
            gat[0:1, 64 + 16 * k:80 + 16 * k], gat[0:1, 0:16],
            gat[0:1, 18 + k:19 + k].broadcast_to([1, 16]), ALU.is_equal), waits=[ttk], inc=(sV, 1))
        p.on("V", lambda e, k=k: e.tensor_tensor(
            gat[0:1, 128 + 16 * k:144 + 16 * k], gat[0:1, 64 + 16 * k:80 + 16 * k],
            gat[0:1, 18 + k:19 + k].broadcast_to([1, 16]), ALU.mult), inc=(sV, 1))
        tsp = p.on("V", lambda e, k=k: e.tensor_tensor(
            gat[0:1, 48:64], gat[0:1, 48:64], gat[0:1, 128 + 16 * k:144 + 16 * k], ALU.add),
            inc=(sV, 1))
    tge = p.on("S", lambda e: e.activation(gat[0:1, 192:208], gat[0:1, 48:64], AF.Exp),
               waits=[tsp], inc=(sS, 1))
    p.on("V", lambda e: e.reduce_sum(gat[0:1, 16:17], gat[0:1, 192:208], axis=AX.X),
         waits=[tge], inc=(sV, 1))
    p.on("V", lambda e: e.reciprocal(gat[0:1, 17:18], gat[0:1, 16:17]), inc=(sV, 1))
    tga = p.on("V", lambda e: e.tensor_tensor(gat[0:1, 192:208], gat[0:1, 192:208],
                                              gat[0:1, 17:18].broadcast_to([1, 16]), ALU.mult),
               inc=(sV, 1))
    tgk = None
    for k in range(4):
        p.on("V", lambda e, k=k: e.tensor_tensor(
            gat[0:1, 128 + 16 * k:144 + 16 * k], gat[0:1, 192:208],
            gat[0:1, 64 + 16 * k:80 + 16 * k], ALU.mult), waits=[tga], inc=(sV, 1))
        tgk = p.on("V", lambda e, k=k: e.reduce_sum(
            gat[0:1, 26 + k:27 + k], gat[0:1, 128 + 16 * k:144 + 16 * k], axis=AX.X), inc=(sV, 1))
    tgc = p.on("G", lambda e: e.dma_start(out=gat[0:4, 40:41], in_=gat[0:1, 26:30]),
               waits=[tgk], dma=True)
    tge2 = {}
    for t2_ in range(2):
        tm_a = p.on("T", lambda e, t2_=t2_: e.matmul(
            pb[0][:, 0:1], E64f[:, t2_ * 128:(t2_ + 1) * 128], gat[0:4, 40:41],
            start=True, stop=True), waits=[tgc, bankfree[0]], inc=(sT, 1))
        tge2[t2_] = p.on("S", lambda e, t2_=t2_: e.copy(gexp[:, t2_:t2_ + 1], pb[0][:, 0:1]),
                         waits=[tm_a], inc=(sS, 1))
        bankfree[0] = tge2[t2_]
    tgq = p.on("S", lambda e: e.mul(gexpQ[:], gexp[:], ISQ8), waits=[tge2[1]], inc=(sS, 1))
    th4 = []
    for k in range(4):
        t1 = p.on("V", lambda e, k=k: e.tensor_copy(gatb[0:1, :], gat[0:1, 64 + 16 * k:80 + 16 * k]),
                  waits=[toh[k], (th4[-1] if th4 else None)], inc=(sV, 1))
        th4.append(p.on("G", lambda e, k=k: e.dma_start(out=H4T[k:k + 1, :], in_=gatb[0:1, :]),
                        waits=[t1], dma=True))
    tohc = [p.on("G", lambda e, k=k: e.dma_start(
        out=gat[0:16, 41 + k:42 + k], in_=gat[0:1, 64 + 16 * k:80 + 16 * k]),
        waits=[toh[k]], dma=True) for k in range(4)]
    tbc_last = None
    for wi in range(3):
        for k in range(4):
            tm_a = p.on("T", lambda e, wi=wi, k=k: e.matmul(
                pb[1][0:64, 0:1], bh3t[:, wi * 64:(wi + 1) * 64], gat[0:16, 41 + k:42 + k],
                start=True, stop=True), waits=[t_bh3, tohc[k], tbc_last, bankfree[1]],
                inc=(sT, 1))
            tbc_last = p.on("S", lambda e, wi=wi, k=k: e.copy(
                bcpp[64 * (k % 2):64 * (k % 2) + 64, 2 * wi + k // 2:2 * wi + k // 2 + 1],
                pb[1][0:64, 0:1]), waits=[tm_a], inc=(sS, 1))
            bankfree[1] = tbc_last
    tBC = tbc_last

    tm_a = p.on("T", lambda e: e.matmul(pb[0][0:16, 0:256], H4T[:, :], Ecol4b[:, :],
                                        start=True, stop=True), waits=th4 + [tCONST, bankfree[0]],
                inc=(sT, 1))
    th4e = p.on("V", lambda e: e.tensor_copy(H4E[:, :], pb[0][0:16, 0:256]), waits=[tm_a], inc=(sV, 1))
    bankfree[0] = th4e
    tGB = None
    for c in range(NDC):
        tm_a = p.on("T", lambda e, c=c: e.matmul(
            pb[0][:, 0:256], RowSelb[:, c * 128:(c + 1) * 128], H4E[:, :],
            start=True, stop=True), waits=[th4e, bankfree[0]], inc=(sT, 1))
        tGB = p.on("V", lambda e, c=c: e.tensor_tensor(
            Gb[:, c * 256:(c + 1) * 256], pb[0][:, 0:256], M64b[:, :], ALU.mult),
            waits=[tm_a], inc=(sV, 1))
        bankfree[0] = tGB

    wslot_free[0] = wslot_free[1] = None

    def stream_w5(hbm, nchunks, pre=None):
        toks = []
        for c in range(nchunks):
            s = c % 2
            w = [wslot_free[s]]
            if pre is not None and c < 2:
                w.append(pre)
            tl = p.on("Y", lambda e, c=c, s=s, hbm=hbm: e.dma_start(
                out=mscr[:, s * 1024:(s + 1) * 1024], in_=hbm[c * 128:(c + 1) * 128, :]),
                waits=w, dma=True)
            tc_ = p.on("S", lambda e, c=c, s=s: e.copy(
                mwbf[:, c * 1024:(c + 1) * 1024], mscr[:, s * 1024:(s + 1) * 1024]),
                waits=[tl], inc=(sS, 1))
            wslot_free[s] = tc_
            toks.append(tc_)
        return toks

    def compact_w(wtoks, dst, pre):
        last_ev = pre
        for it in range(NDC):
            last = None
            for oc in range(NDC):
                w = [wtoks[oc], tGB]
                if oc == 0:
                    w += [last_ev, bankfree[2]]
                last = p.on("T", lambda e, it=it, oc=oc: e.matmul(
                    pb[2][:, 0:256], mwbf[:, oc * 1024 + it * 128: oc * 1024 + (it + 1) * 128],
                    Gb[:, oc * 256:(oc + 1) * 256],
                    start=(oc == 0), stop=(oc == NDC - 1)),
                    waits=w, inc=(sT, 1) if oc == NDC - 1 else None)
            last_ev = p.on("V", lambda e, it=it, dst=dst: e.tensor_copy(
                dst[:, it * 256:(it + 1) * 256], pb[2][:, 0:256]), waits=[last], inc=(sV, 1))
            bankfree[2] = last_ev
        return last_ev

    tq_toks = stream_w5(mwqT, 8, pre=tXNB)
    tWQC = compact_w(tq_toks, wqc, None)
    tk_toks = stream_w5(mwkT, 8, pre=tWQC)
    tWKC = compact_w(tk_toks, wkc, None)
    tv_toks = stream_w5(mwvT, 8, pre=tWKC)
    tWVC = compact_w(tv_toks, wvc, None)

    def cproj(wc, twc, dst, bcol, gtile):
        toks = []
        for half in range(2):
            for cc in range(4):
                bki = 4 + ((half * 4 + cc) % 2)
                bk = pb[bki]
                last = None
                for dc in range(NDC):
                    w = [twc, tXNB]
                    if dc == 0:
                        w.append(bankfree[bki])
                    last = p.on("T", lambda e, half=half, cc=cc, dc=dc, bk=bk, wc=wc: e.matmul(
                        bk[:, 0:512], wc[:, dc * 256 + half * 128: dc * 256 + (half + 1) * 128],
                        xnb[:, dc * SEQ + cc * 512: dc * SEQ + (cc + 1) * 512],
                        start=(dc == 0), stop=(dc == NDC - 1)),
                        waits=w, inc=(sT, 1) if dc == NDC - 1 else None)
                te = p.on("V", lambda e, half=half, cc=cc, bk=bk, dst=dst, bcol=bcol, gtile=gtile: e.tensor_scalar(
                    dst[:, half * SEQ + cc * 512: half * SEQ + (cc + 1) * 512], bk[:, 0:512],
                    bcpp[:, bcol + half:bcol + half + 1], gtile[:, half:half + 1], ALU.add, ALU.mult),
                    waits=[last, tBC, tgq], inc=(sV, 1))
                bankfree[bki] = te
                toks.append(te)
        return toks[-1]

    tQK = cproj(wqc, tWQC, qkb, 0, gexpQ)
    tKK = cproj(wkc, tWKC, kkb, 2, gexp)

    tgrow = []
    for k in range(4):
        tgrow.append(p.on("V", lambda e, k=k: e.tensor_copy(
            scr_row[0:1, k * 64:(k + 1) * 64],
            gat[0:1, 26 + k:27 + k].broadcast_to([1, 64])), waits=[tgk], inc=(sV, 1)))
    tbvc = [p.on("G", lambda e, half=half: e.dma_start(
        out=gat[0:1, 224 + half * 128:224 + (half + 1) * 128],
        in_=bcpp[:, 4 + half:5 + half]), waits=[tBC], dma=True) for half in range(2)]
    tm_a = p.on("T", lambda e: e.matmul(pb[0][:, 0:256], onesPP[0:1, 0:128], scr_row[0:1, 0:256],
                                        start=True, stop=True), waits=tgrow + [bankfree[0]], inc=(sT, 1))
    tgrep = p.on("V", lambda e: e.tensor_copy(grep[:], pb[0][:, 0:256]), waits=[tm_a], inc=(sV, 1))
    bankfree[0] = tgrep
    tm_b = p.on("T", lambda e: e.matmul(pb[1][:, 0:256], onesPP[0:1, 0:128], gat[0:1, 224:480],
                                        start=True, stop=True), waits=tbvc + [bankfree[1]], inc=(sT, 1))
    tbvcrep = p.on("V", lambda e: e.tensor_copy(bvcrep[:], pb[1][:, 0:256]), waits=[tm_b], inc=(sV, 1))
    bankfree[1] = tbvcrep

    tms = p.on("V", lambda e: e.memset(vktm[:, :], 0.0), inc=(sV, 1))
    tones = p.on("V", lambda e: e.tensor_copy(
        vktm[:, :].rearrange("p (a b) -> p a b", b=72)[:, :, 64:65],
        ones_col_b[:, 0:1].broadcast_to([128, 64]).rearrange("p (a b) -> p a b", b=1)),
        waits=[tms], inc=(sV, 1))
    tvk_last = None
    for jt in range(NJT):
        last = None
        for dc in range(NDC):
            w = [tWVC, tXNB]
            if dc == 0:
                w.append(bankfree[2])
            last = p.on("T", lambda e, jt=jt, dc=dc: e.matmul(
                pb[2][:, 0:256], xnb[:, dc * SEQ + jt * 128: dc * SEQ + (jt + 1) * 128],
                wvc[:, dc * 256:(dc + 1) * 256],
                start=(dc == 0), stop=(dc == NDC - 1)),
                waits=w, inc=(sT, 1) if dc == NDC - 1 else None)
        t1 = p.on("V", lambda e, jt=jt: e.tensor_tensor(xscr[:, 0:256], pb[2][:, 0:256], bvcrep[:], ALU.add),
                  waits=[last, tbvcrep], inc=(sV, 1))
        bankfree[2] = t1
        p.on("V", lambda e, jt=jt: e.tensor_tensor(xscr[:, 256:512], xscr[:, 0:256], grep[:], ALU.mult),
             waits=[tgrep, tones], inc=(sV, 1))
        for k in range(4):
            tvk_last = p.on("V", lambda e, jt=jt, k=k: e.tensor_copy(
                vktm[:, jt * 288 + k * 72: jt * 288 + k * 72 + 64],
                xscr[:, 256 + k * 64:256 + (k + 1) * 64]), inc=(sV, 1))
    tVK = tvk_last

    tprev = None
    for src, base, tdep in ((qkb, 0, tQK), (kkb, 2, tKK)):
        for kp in range(2):
            tq = p.on("V", lambda e, kp=kp, src=src: e.tensor_tensor(
                mscr, src[:, kp * SEQ:(kp + 1) * SEQ], src[:, kp * SEQ:(kp + 1) * SEQ], ALU.mult),
                waits=[tdep, tprev, tVK], inc=(sV, 1))
            tqb = p.on("V", lambda e: e.tensor_copy(EABm[:, 0:SEQ], mscr), inc=(sV, 1))
            for cc in range(4):
                tm_a = p.on("T", lambda e, cc=cc: e.matmul(
                    pb[0][0:1, 0:512], ones_col_b[:, :], EABm[:, cc * 512:(cc + 1) * 512],
                    start=True, stop=True), waits=[tqb, bankfree[0]], inc=(sT, 1))
                tprev = p.on("V", lambda e, kp=kp, cc=cc, base=base: e.reduce_max(
                    gat[0:1, 240 + (base + kp) * 4 + cc:240 + (base + kp) * 4 + cc + 1],
                    pb[0][0:1, 0:512], axis=AX.X), waits=[tm_a], inc=(sV, 1))
                bankfree[0] = tprev
    for kp in range(4):
        tprev = p.on("V", lambda e, kp=kp: e.reduce_max(
            gat[0:1, 230 + kp:231 + kp], gat[0:1, 240 + kp * 4:244 + kp * 4], axis=AX.X),
            waits=[tprev], inc=(sV, 1))
    t_ = p.on("V", lambda e: e.tensor_tensor(gat[0:1, 234:236], gat[0:1, 230:232], gat[0:1, 232:234], ALU.mult),
              waits=[tprev], inc=(sV, 1))
    t_ = p.on("S", lambda e: e.activation(gat[0:1, 236:238], gat[0:1, 234:236], AF.Sqrt),
              waits=[t_], inc=(sS, 1))
    t_ = p.on("V", lambda e: e.tensor_scalar_mul(gat[0:1, 234:236], gat[0:1, 236:238], -1.05),
              waits=[t_], inc=(sV, 1))
    tm_a = p.on("T", lambda e: e.matmul(pb[0][:, 0:2], onesPP[0:1, 0:128], gat[0:1, 234:236],
                                        start=True, stop=True), waits=[t_, bankfree[0]], inc=(sT, 1))
    tMNEG = p.on("V", lambda e: e.tensor_copy(mnegM[:], pb[0][:, 0:2]), waits=[tm_a], inc=(sV, 1))
    bankfree[0] = tMNEG

    tok_pvm = {}
    tok_sbandm = {}
    for kp in range(2):
        tok_pvm_prev = tok_pvm
        tok_expm = {}
        tok_pvm = {}
        pending_evacm = None
        for iq in range(NIQ):
            def s_and_expm(j, iq=iq, kp=kp, tok_pvm_prev=tok_pvm_prev):
                w = [tQK, tKK, tMNEG, tok_sbandm.get(0)]
                p.on("T", lambda e, j=j, iq=iq, kp=kp: e.matmul(
                    pb[0][:, 0:IQ], kkb[0:64, kp * SEQ + j * 128: kp * SEQ + (j + 1) * 128],
                    qkb[0:64, kp * SEQ + iq * IQ: kp * SEQ + (iq + 1) * IQ],
                    start=True, stop=True, tile_position=(0, 0)), waits=w)
                tmm = p.on("T", lambda e, j=j, iq=iq, kp=kp: e.matmul(
                    pb[1][:, 0:IQ], kkb[64:128, kp * SEQ + j * 128: kp * SEQ + (j + 1) * 128],
                    qkb[64:128, kp * SEQ + iq * IQ: kp * SEQ + (iq + 1) * IQ],
                    start=True, stop=True, tile_position=(64, 0)), inc=(sT, 1))
                if iq == 0:
                    prevpv = tok_pvm_prev.get((NIQ - 1, j))
                else:
                    prevpv = tok_pvm.get((iq - 1, j))
                te = p.on("S", lambda e, j=j, kp=kp: e.activation(
                    EABm[:, j * 512:(j + 1) * 512].rearrange("p (a b) -> p a b", b=IQ),
                    psS[:, 0:1024].rearrange("p (a b) -> p a b", b=512)[:, :, 0:IQ],
                    AF.Exp, bias=mnegM[:, kp:kp + 1]), waits=[tmm, prevpv], inc=(sS, 1))
                tok_expm[(iq, j)] = te
                tok_sbandm[0] = te

            s_and_expm(0)
            s_and_expm(1)
            if pending_evacm is not None:
                pending_evacm()
                pending_evacm = None
            for j in range(NJT):
                if j + 2 <= NJT - 1:
                    s_and_expm(j + 2)
                tpvm = None
                for hh in range(2):
                    bki = 2 + hh
                    bk = pb[bki]
                    w = [tok_expm[(iq, j)], tVK]
                    if j == 0:
                        w.append(bankfree[bki])
                    tpvm = p.on("T", lambda e, j=j, hh=hh, kp=kp, bk=bk: e.matmul(
                        bk[0:65, 0:IQ],
                        vktm[:, j * 288 + (2 * kp + hh) * 72: j * 288 + (2 * kp + hh) * 72 + 65],
                        EABm[:, j * 512 + hh * IQ: j * 512 + (hh + 1) * IQ],
                        start=(j == 0), stop=(j == NJT - 1)), waits=w, inc=(sT, 1))
                tok_pvm[(iq, j)] = tpvm
            tlast = tok_pvm[(iq, NJT - 1)]
            def mk_evacm(iq=iq, tlast=tlast):
                def f():
                    for hh in range(2):
                        outm = outmA if hh == 0 else outmB
                        eng, sem = ("V", sV) if hh == 0 else ("S", sS)
                        if hh == 0:
                            bankfree[2 + hh] = p.on("V", lambda e, hh=hh, iq=iq, outm=outm: e.tensor_copy(
                                outm[:, iq * IQ:(iq + 1) * IQ], pb[2 + hh][0:65, 0:IQ]),
                                waits=[tlast], inc=(sV, 1))
                        else:
                            bankfree[2 + hh] = p.on("S", lambda e, hh=hh, iq=iq, outm=outm: e.copy(
                                outm[:, iq * IQ:(iq + 1) * IQ], pb[2 + hh][0:65, 0:IQ]),
                                waits=[tlast], inc=(sS, 1))
                return f
            pending_evacm = mk_evacm()
        pending_evacm()
        pending_evacm = None
        for hh in range(2):
            k = 2 * kp + hh
            outm = outmA if hh == 0 else outmB
            p.on("S", lambda e, outm=outm: e.activation(outm[64:65, :], outm[64:65, :], AF.Ln),
                 waits=[bankfree[2 + hh]], inc=(sS, 1))
            tiv = p.on("S", lambda e, outm=outm: e.activation(
                outm[64:65, :], outm[64:65, :], AF.Exp, scale=-1.0), inc=(sS, 1))
            for cc in range(4):
                tm_a = p.on("T", lambda e, cc=cc, outm=outm: e.matmul(
                    pb[6][0:64, 0:512], onesPP[64:65, 0:64], outm[64:65, cc * 512:(cc + 1) * 512],
                    start=True, stop=True), waits=[tiv, bankfree[6]], inc=(sT, 1))
                bankfree[6] = p.on("V", lambda e, cc=cc, k=k, outm=outm: e.tensor_tensor(
                    outc[64 * (k % 2):64 * (k % 2) + 64,
                         (k // 2) * SEQ + cc * 512:(k // 2) * SEQ + (cc + 1) * 512],
                    outm[0:64, cc * 512:(cc + 1) * 512], pb[6][0:64, 0:512], ALU.mult),
                    waits=[tm_a], inc=(sV, 1))
    tOUTC = bankfree[6]

    wo_toks = stream_w5(mwo, 2, pre=tMNEG)
    tmo = [None, None]
    for mt in range(NDC):
        for cc in range(4):
            par = (mt * 4 + cc) % 2
            bki = 4 + par
            bk = pb[bki]
            last = None
            for ch in range(2):
                w = [wo_toks[ch], tOUTC]
                if ch == 0:
                    w += [tmo[par], bankfree[bki]]
                last = p.on("T", lambda e, mt=mt, cc=cc, ch=ch, bk=bk: e.matmul(
                    bk[:, 0:512], mwbf[:, ch * 1024 + mt * 128: ch * 1024 + (mt + 1) * 128],
                    outc[:, ch * SEQ + cc * 512: ch * SEQ + (cc + 1) * 512],
                    start=(ch == 0), stop=(ch == 1)),
                    waits=w, inc=(sT, 1) if ch == 1 else None)
            t1 = p.on("V", lambda e, mt=mt, cc=cc, par=par, bk=bk: e.tensor_scalar(
                mow[par], bk[:, 0:512], vcol("mbo", mt), None, ALU.add),
                waits=[last, tmo[par]], inc=(sV, 1))
            bankfree[bki] = t1
            tmo[par] = p.on("G", lambda e, mt=mt, cc=cc, par=par: e.dma_start(
                out=out_hbm[mt * 128:(mt + 1) * 128, cc * 512:(cc + 1) * 512],
                in_=mow[par], accum_op=ALU.add), waits=[t1], dma=True)

    tap("dist", gat[0:1, 0:64], [tdist, ttk, tgk])
    tap("gall", gat[0:1, 192:208], [tga])
    tap("outc0", outc[:, 0:512], [tOUTC])
    tap("qkb0", qkb[:, 0:512], [tQK])
    tap("xnb0", xnb[:, 0:2048], [tXNB])

    p.on("G", None, waits=p.all_dma_toks())
    p.emit()
    return nc



def _consts_blob(Ws):
    c = np.zeros((128, 2048), dtype=np.float32)
    c[:, 0:128] = 1.0
    ist = np.zeros((128, 64), dtype=np.float32)
    for q in range(128):
        ist[q, q % 64] = 1.0
    c[:, 128:192] = ist
    m64 = np.zeros((128, 256), dtype=np.float32)
    for q in range(128):
        for k in range(4):
            m64[q, k * 64 + (q % 64)] = 1.0
    c[:, 192:448] = m64
    Ws = np.asarray(Ws, dtype=np.float32).reshape(-1)
    w1e = np.zeros((128, 64), dtype=np.float32)
    for mt in range(4):
        for q in range(128):
            w1e[q, mt * 16 + 4 * mt + q // 32] = Ws[q % 32]
    c[:, 448:512] = w1e
    w2e = np.zeros((128, 32), dtype=np.float32)
    for dc in range(4):
        for q in range(128):
            w2e[q, dc * 8 + 2 * dc + q // 64] = Ws[32 + q % 64]
    c[:, 512:544] = w2e
    e8 = np.zeros((8, 512), dtype=np.float32)
    for dc in range(4):
        for q in range(128):
            e8[2 * dc + q // 64, dc * 128 + q] = 1.0
    c[0:8, 544:1056] = e8
    esel = np.zeros((16, 128), dtype=np.float32)
    for h in range(16):
        esel[h, h * 8:(h + 1) * 8] = 1.0
    c[0:16, 1056:1184] = esel
    ecol4 = np.zeros((4, 256), dtype=np.float32)
    for k in range(4):
        ecol4[k, k * 64:(k + 1) * 64] = 1.0
    c[0:4, 1184:1440] = ecol4
    e64 = np.zeros((4, 256), dtype=np.float32)
    for t in range(2):
        for q in range(128):
            e64[2 * t + q // 64, t * 128 + q] = 1.0
    c[0:4, 1440:1696] = e64
    rowsel = np.zeros((16, 1024), dtype=np.float32)
    for col in range(1024):
        cc, q = divmod(col, 128)
        rowsel[2 * cc + q // 64, col] = 1.0
    c[32:48, 544:1568] = rowsel
    return c


def _pack_vecs(ins):
    cols = np.zeros((128, 128), dtype=np.float32)
    off = 0

    def add(arr1d, n):
        nonlocal off
        a = np.zeros((n * 128,), dtype=np.float32)
        v = np.asarray(arr1d, dtype=np.float32).reshape(-1)
        a[:v.size] = v
        cols[:, off:off + n] = a.reshape(n, 128).T
        off += n

    add(ins["L0_attn_bq"], 8)
    add(ins["L0_attn_bk"], 8)
    add(ins["L0_attn_bv"], 4)
    add(ins["L0_attn_bqv"], 4)
    add(ins["L0_attn_bf"], 8)
    add(ins["L0_attn_br1"], 16)
    add(ins["L0_attn_br2"], 8)
    add(ins["L0_attn_g1"], 8)
    add(ins["L0_attn_b1"], 8)
    add(ins["L0_attn_g2"], 8)
    add(ins["L0_attn_b2"], 8)
    add(np.full(16, float(np.asarray(ins["L0_attn_bs"]).reshape(-1)[0])), 1)
    add(ins["L0_ffn_br"], 1)
    add(ins["L0_ffn_bo"], 8)
    return cols


def kernel(**ins):
    x = np.asarray(ins["L0_attn_x"], dtype=np.float32)
    B = x.shape[0]
    consts = _consts_blob(ins["L0_attn_Ws"])
    vecs = _pack_vecs(ins)

    def T(a):
        return np.ascontiguousarray(np.asarray(a, dtype=np.float32).T)

    shared = {
        "wqT": T(ins["L0_attn_Wq"]), "wkT": T(ins["L0_attn_Wk"]),
        "wvT": T(ins["L0_attn_Wv"]), "wqvT": T(ins["L0_attn_Wqv"]),
        "wfT": T(ins["L0_attn_Wf"]),
        "wr1T": T(ins["L0_attn_Wr1"]), "wr2T": T(ins["L0_attn_Wr2"]),
        "mwqT": T(ins["L0_ffn_Wq"]), "mwkT": T(ins["L0_ffn_Wk"]), "mwvT": T(ins["L0_ffn_Wv"]),
        "mwo": np.ascontiguousarray(np.asarray(ins["L0_ffn_Wo"], dtype=np.float32)),
        "mwr": np.ascontiguousarray(np.asarray(ins["L0_ffn_Wr"], dtype=np.float32)),
        "vecs": vecs, "consts": consts,
        "bh3": np.ascontiguousarray(np.stack(
            [np.asarray(ins["L0_ffn_bq"], dtype=np.float32).reshape(16, 64),
             np.asarray(ins["L0_ffn_bk"], dtype=np.float32).reshape(16, 64),
             np.asarray(ins["L0_ffn_bv"], dtype=np.float32).reshape(16, 64)], axis=0
        ).transpose(1, 0, 2).reshape(16, 192)),
    }
    in_maps = []
    for b in range(B):
        m = dict(shared)
        m["xT"] = np.ascontiguousarray(x[b].T)
        in_maps.append(m)

    nc = build_program(taps=tuple(_TAPS), maxphase=getattr(kernel, "_MAXPHASE", 5), hpn=getattr(kernel, "_HPN", 8))
    res = run_bass_kernel_spmd(nc, in_maps, core_ids=list(range(B)))
    outs = np.stack([np.asarray(r["out"]) for r in res.results], axis=0)
    if _TAPS:
        kernel.tap_results = [{k: np.asarray(v) for k, v in r.items() if k.startswith("tap_")}
                              for r in res.results]
    return np.ascontiguousarray(outs.transpose(0, 2, 1))


# version 19
# speedup vs baseline: 1.2406x; 1.0041x over previous
import itertools
import numpy as np
from contextlib import ExitStack

import concourse.bass as bass
import concourse.mybir as mybir
from concourse.bass_utils import run_bass_kernel_spmd

F32 = mybir.dt.float32
BF16 = mybir.dt.bfloat16
U32 = mybir.dt.uint32
AF = mybir.ActivationFunctionType
ALU = mybir.AluOpType
AX = mybir.AxisListType

D = 1024
SEQ = 2048
NDC = 8
NJT = 16
IQ = 256
NIQ = SEQ // IQ
EPS = 1e-5
DFF = 2048
ISQ8 = 0.125
ISQ32 = float(1.0 / np.sqrt(32.0))
WIN = 512
NW = SEQ // WIN
ARENA_KB = 194

_TAPS = []
_MAXPHASE = 5
_HPN = 8


class Prog:
    ENGINES = ("Y", "G", "T", "V", "S")

    def __init__(self, nc):
        self.nc = nc
        self.steps = {e: [] for e in self.ENGINES}
        self.semc = {}
        self.chain_sems = {}
        self.es = ExitStack()
        self._uid = 0

    def _nm(self, p):
        self._uid += 1
        return "%s%d" % (p, self._uid)

    def sem(self, name=None):
        h = self.es.enter_context(self.nc.semaphore(name=name or self._nm("sm")))
        self.semc[h] = 0
        return h

    def sb(self, name, shape, dt):
        return self.es.enter_context(self.nc.sbuf_tensor(name, shape, dt))

    def psum(self, name, shape):
        return self.es.enter_context(self.nc.psum_tensor(name, shape, F32))

    def init_dma_pool(self, n=12):
        self.dma_pools = {q: [self.sem("dma%s%d" % (q, i)) for i in range(n if q != "S" else 4)]
                          for q in ("Y", "G", "S")}
        self.dma_prev = {q: [None] * len(self.dma_pools[q]) for q in ("Y", "G", "S")}
        self.dma_rr = {q: 0 for q in ("Y", "G", "S")}
        self.dma_pool = self.dma_pools["Y"] + self.dma_pools["G"] + self.dma_pools["S"]

    def all_dma_toks(self):
        return [self.tok(s) for s in self.dma_pool]

    def on(self, eng, fn, waits=(), inc=None, dma=False):
        w = []
        if fn is not None and eng in self.chain_sems:
            cs = self.chain_sems[eng]
            w.append((cs, self.semc[cs]))
            if inc is None and not dma:
                inc = (cs, 1)
        def _flat(ts):
            for t in ts:
                if t is None:
                    continue
                if isinstance(t, list):
                    _flat(t)
                else:
                    w.append(t)
        _flat(waits)
        if dma:
            pool = self.dma_pools[eng]
            i = self.dma_rr[eng] % len(pool)
            self.dma_rr[eng] += 1
            if self.dma_prev[eng][i] is not None:
                w.append(self.dma_prev[eng][i])
            inc = (pool[i], 16)
        self.steps[eng].append((w, fn, inc))
        if inc is not None:
            s, k = inc
            self.semc[s] += k
            tok = (s, self.semc[s])
            if dma:
                self.dma_prev[eng][i] = tok
            return tok
        return None

    def tok(self, sem):
        return (sem, self.semc[sem])

    def barrier(self, name, dma_sems=()):
        bar = self.sem("bar_" + name)
        dw = [self.tok(s) for s in dma_sems]
        for e in self.ENGINES:
            self.on(e, lambda eng: eng.nop(), waits=dw, inc=(bar, 1))
        t = self.tok(bar)
        for e in self.ENGINES:
            self.on(e, None, waits=[t])
        return t

    def emit(self):
        nc = self.nc
        with nc.Block() as block:
            def run(eng_obj, name):
                seen = {}
                for w, fn, inc in self.steps[name]:
                    for s, v in w:
                        if seen.get(s, -1) < v:
                            eng_obj.wait_ge(s, v)
                            seen[s] = v
                    if fn is not None:
                        ins = fn(eng_obj)
                        if inc is not None:
                            if ins is None:
                                eng_obj.nop().then_inc(*inc)
                            else:
                                ins.then_inc(*inc)
                    elif inc is not None:
                        eng_obj.nop().then_inc(*inc)

            @block.sync
            def _(e):
                run(e, "Y")

            @block.gpsimd
            def _(e):
                run(e, "G")

            @block.tensor
            def _(e):
                run(e, "T")

            @block.vector
            def _(e):
                run(e, "V")

            @block.scalar
            def _(e):
                run(e, "S")


def build_program(taps=(), maxphase=5, hpn=8):
    nc = bass.Bass("TRN2", target_bir_lowering=False)
    p = Prog(nc)

    def din(name, shape, dt=F32):
        return nc.dram_tensor(name, shape, dt, kind="ExternalInput").ap()

    xT = din("xT", [D, SEQ])
    wqT = din("wqT", [D, D]); wkT = din("wkT", [D, D])
    wvT = din("wvT", [D, 512]); wqvT = din("wqvT", [D, 512])
    wfT = din("wfT", [D, D])
    wr1T = din("wr1T", [D, DFF]); wr2T = din("wr2T", [DFF, D])
    mwqT = din("mwqT", [D, D]); mwkT = din("mwkT", [D, D]); mwvT = din("mwvT", [D, D])
    mwo = din("mwo", [256, D]); mwr = din("mwr", [D, 16])
    vecs = din("vecs", [128, 128])
    consts = din("consts", [128, 2048])
    bh3 = din("bh3", [16, 192])
    out_hbm = nc.dram_tensor("out", [D, SEQ], F32, kind="ExternalOutput").ap()

    taps = set(taps)

    p.init_dma_pool(16)
    sG = p.sem("sG")
    sT = p.sem("sT"); sV = p.sem("sV"); sS = p.sem("sS")
    p.chain_sems = {"V": sV, "S": sS, "G": sG}

    psS = p.psum("psS", [128, 1024])
    pb = [psS[:, 0:512], psS[:, 512:1024]] + [p.psum("pb%d" % i, [128, 512]) for i in range(2, 8)]
    bankfree = {i: None for i in range(8)}

    arena = p.sb("arena", [128, ARENA_KB * 512], BF16)

    _audit = []

    def carve(kb, shape, dt):
        cols = shape[-1] * (2 if dt == F32 else 1)
        start = int(kb * 512)
        _audit.append((float(kb), float(kb) + cols / 512.0, tuple(shape), "f32" if dt == F32 else "bf16"))
        assert start + cols <= ARENA_KB * 512, ("carve OOB", kb, shape)
        v = arena[0:shape[0], start:start + cols]
        if dt == F32:
            v = v.bitcast(F32)
        return v
    build_program.audit = _audit

    vecs_t = p.sb("vecs_t", [128, 128], F32)
    onesPP = p.sb("onesPP", [128, 128], F32)
    ones_col_f = p.sb("ones_col_f", [128, 1], F32)
    ones_col_b = p.sb("ones_col_b", [128, 1], BF16)
    E8b = p.sb("E8b", [72, 512], BF16)
    ISTb = p.sb("ISTb", [128, 64], BF16)
    Eselb = p.sb("Eselb", [16, 128], BF16)
    Ecol4b = p.sb("Ecol4b", [4, 256], BF16)
    RowSelb = p.sb("RowSelb", [16, 1024], BF16)
    M64b = p.sb("M64b", [128, 256], BF16)
    E64f = p.sb("E64f", [4, 256], F32)
    w1b = p.sb("w1b", [128, 64], BF16)
    w2b = p.sb("w2b", [128, 32], BF16)
    negM = p.sb("negM", [128, 8], F32)
    nrm = p.sb("nrm", [1, 96], F32)
    bqs = p.sb("bqs", [128, 8], F32)
    bqvs = p.sb("bqvs", [128, 4], F32)
    scr_row = p.sb("scr_row", [33, 512], F32)
    ones8b = p.sb("ones8b", [72, 8], BF16)
    H4T = p.sb("H4T", [4, 16], BF16)
    H4E = p.sb("H4E", [16, 256], BF16)
    gat = p.sb("gat", [16, 512], F32)
    gatb = p.sb("gatb", [4, 16], BF16)
    mi8 = p.sb("mi8", [1, 8], U32)
    gexp = p.sb("gexp", [128, 2], F32)
    gexpQ = p.sb("gexpQ", [128, 2], F32)
    bcpp = p.sb("bcpp", [128, 6], F32)
    bh3t = p.sb("bh3t", [16, 192], F32)
    mnegM = p.sb("mnegM", [128, 2], F32)
    wrf = p.sb("wrf", [128, 16 * NDC], F32)

    _sizes = [("bq", 8), ("bk", 8), ("bv", 4), ("bqv", 4), ("bf", 8), ("br1", 16),
              ("br2", 8), ("g1", 8), ("b1", 8), ("g2", 8), ("b2", 8), ("bs", 1),
              ("mbr", 1), ("mbo", 8)]
    VC = {}
    _off = 0
    for _n, _k in _sizes:
        VC[_n] = (_off, _k)
        _off += _k

    def vcol(name, i=0):
        o, n = VC[name]
        assert i < n
        return vecs_t[:, o + i:o + i + 1]

    def tap(name, ap, waits):
        if name not in taps:
            return
        if not isinstance(ap, bass.AP):
            ap = ap[:, :]
        t = nc.dram_tensor("tap_" + name, [ap.shape[0], ap.shape[1]], ap.dtype,
                           kind="ExternalOutput").ap()
        p.on("G", lambda e, t=t, ap=ap: e.dma_start(out=t[:, :], in_=ap), waits=waits, dma=True)

    cstage = carve(0, [128, 2048], F32)
    t_vec = p.on("Y", lambda e: e.dma_start(out=vecs_t[:], in_=vecs[:, :]), dma=True)
    t_cst = p.on("Y", lambda e: e.dma_start(out=cstage, in_=consts[:, :]), dma=True)
    t_bh3 = p.on("Y", lambda e: e.dma_start(out=bh3t[:], in_=bh3[:, :]), dma=True)

    def vc_cast(dst, src):
        return p.on("V", lambda e, d=dst, s=src: e.tensor_copy(d, s), waits=[t_cst], inc=(sV, 1))

    vc_cast(onesPP[:, :], cstage[:, 0:128])
    vc_cast(ones_col_f[:, :], cstage[:, 0:1])
    vc_cast(ones_col_b[:, :], cstage[:, 0:1])
    vc_cast(ones8b[64:72, :], cstage[0:8, 0:8])
    vc_cast(ISTb[:, :], cstage[:, 128:192])
    vc_cast(M64b[:, :], cstage[:, 192:448])
    vc_cast(w1b[:, :], cstage[:, 448:512])
    vc_cast(w2b[:, :], cstage[:, 512:544])
    vc_cast(E8b[64:72, :], cstage[0:8, 544:1056])
    vc_cast(E8b[32:40, :], cstage[0:8, 544:1056])
    vc_cast(Eselb[:, :], cstage[0:16, 1056:1184])
    vc_cast(Ecol4b[:, :], cstage[0:4, 1184:1440])
    vc_cast(E64f[:, :], cstage[0:4, 1440:1696])
    tCONST = vc_cast(RowSelb[:, :], cstage[32:48, 544:1568])
    tb1 = p.on("G", lambda e: e.tensor_scalar_mul(bqs[:], vecs_t[:, VC["bq"][0]:VC["bq"][0] + 8], ISQ8),
               waits=[t_vec], inc=(sG, 1))
    tb2 = p.on("G", lambda e: e.tensor_scalar_mul(bqvs[:], vecs_t[:, VC["bqv"][0]:VC["bqv"][0] + 4], ISQ32),
               waits=[t_vec], inc=(sG, 1))

    xf = carve(8, [128, NDC * SEQ], F32)
    x2s = carve(72, [128, SEQ], F32)
    xs1 = carve(80, [128, SEQ], F32)
    xs2 = carve(88, [128, SEQ], F32)
    tmpr = carve(96, [1, SEQ], F32)
    tmp2r = carve(104, [1, SEQ], F32)
    r2row = carve(112, [1, SEQ], F32)
    r2rep = carve(120, [128, SEQ], F32)
    m_rep = carve(128, [128, SEQ], F32)
    r_rep = carve(136, [128, SEQ], F32)
    xn2b = carve(144, [128, NDC * SEQ], BF16)
    statrows = carve(186, [65, SEQ], F32)

    tk_x = [p.on("Y", lambda e, c=c: e.dma_start(out=xf[:, c * SEQ:(c + 1) * SEQ],
                                                 in_=xT[c * 128:(c + 1) * 128, :]), dma=True)
            for c in range(NDC)]

    prev_sq = None
    for c in range(NDC):
        t_sq = p.on("V", lambda e, c=c: e.tensor_tensor(
            x2s, xf[:, c * SEQ:(c + 1) * SEQ], xf[:, c * SEQ:(c + 1) * SEQ], ALU.mult),
            waits=[tk_x[c], prev_sq], inc=(sV, 1))
        for cc in range(4):
            p.on("T", lambda e, c=c, cc=cc: e.matmul(
                pb[cc][0:1, 0:512], ones_col_f[:, :],
                xf[:, c * SEQ + cc * 512: c * SEQ + (cc + 1) * 512],
                start=(c == 0), stop=(c == NDC - 1)), waits=[tk_x[c], tCONST])
            prev_sq = p.on("T", lambda e, c=c, cc=cc: e.matmul(
                pb[4 + cc][0:1, 0:512], ones_col_f[:, :], x2s[:, cc * 512:(cc + 1) * 512],
                start=(c == 0), stop=(c == NDC - 1)), waits=[t_sq], inc=(sT, 1))
    tSTAT = prev_sq

    tv = None
    for cc in range(4):
        p.on("V", lambda e, cc=cc: e.tensor_scalar_mul(
            statrows[0:1, cc * 512:(cc + 1) * 512], pb[cc][0:1, 0:512], 1.0 / D),
            waits=[tSTAT], inc=(sV, 1))
        tv = p.on("V", lambda e, cc=cc: e.tensor_scalar_mul(
            tmpr[0:1, cc * 512:(cc + 1) * 512], pb[4 + cc][0:1, 0:512], 1.0 / D),
            waits=[tSTAT], inc=(sV, 1))
    tv = p.on("V", lambda e: e.tensor_tensor(tmp2r, statrows[0:1, :], statrows[0:1, :], ALU.mult),
              waits=[tv], inc=(sV, 1))
    tv = p.on("V", lambda e: e.tensor_tensor(tmpr, tmpr, tmp2r, ALU.subtract), inc=(sV, 1))
    tv = p.on("V", lambda e: e.tensor_scalar_add(tmpr, tmpr, EPS), inc=(sV, 1))
    p.on("S", lambda e: e.activation(tmpr, tmpr, AF.Ln), waits=[tv], inc=(sS, 1))
    tsq_ = p.on("S", lambda e: e.activation(statrows[64:65, :], tmpr, AF.Exp, scale=-0.5),
                inc=(sS, 1))
    tv = p.on("V", lambda e: e.tensor_tensor(tmpr, statrows[64:65, :], statrows[64:65, :], ALU.mult),
              waits=[tsq_], inc=(sV, 1))
    tv = p.on("V", lambda e: e.tensor_scalar(tmpr, tmpr, -EPS, 1.0 + EPS, ALU.mult, ALU.add), inc=(sV, 1))
    p.on("S", lambda e: e.activation(tmpr, tmpr, AF.Ln), waits=[tv], inc=(sS, 1))
    tr2 = p.on("S", lambda e: e.activation(r2row, tmpr, AF.Exp, scale=-0.5), inc=(sS, 1))

    def replicate(dst, src_row, base_part, waits, bank, width=SEQ):
        last = None
        ncc = width // 512
        for cc in range(ncc):
            w = list(waits) + [bankfree[bank], last]
            tm = p.on("T", lambda e, cc=cc, dst=dst, src_row=src_row, base_part=base_part, bank=bank: e.matmul(
                pb[bank][:, 0:512], onesPP[base_part:base_part + 1, 0:128],
                src_row[0:1, cc * 512:(cc + 1) * 512], start=True, stop=True),
                waits=w, inc=(sT, 1))
            last = p.on("V", lambda e, cc=cc, dst=dst, bank=bank: e.tensor_copy(
                dst[:, cc * 512:(cc + 1) * 512], pb[bank][:, 0:512]), waits=[tm], inc=(sV, 1))
        bankfree[bank] = last
        return last

    tmr = replicate(m_rep, statrows[0:1, :], 0, [tSTAT, p.tok(sV)], 0)
    trr = replicate(r_rep, statrows[64:65, :], 64, [tsq_], 1)
    tr2r = replicate(r2rep, r2row, 0, [tr2], 2)

    tXN2 = None
    for c in range(NDC):
        ta = p.on("V", lambda e, c=c: e.tensor_tensor(
            xs1, xf[:, c * SEQ:(c + 1) * SEQ], m_rep, ALU.subtract),
            waits=[tk_x[c], tmr, trr, tr2r, tXN2], inc=(sV, 1))
        p.on("V", lambda e, c=c: e.tensor_tensor(xs2, xs1, r_rep, ALU.mult), inc=(sV, 1))
        tc2 = p.on("V", lambda e, c=c: e.tensor_tensor(xs1, xs2, r2rep, ALU.mult), inc=(sV, 1))
        tXN2 = p.on("S", lambda e, c=c: e.activation(
            xn2b[:, c * SEQ:(c + 1) * SEQ], xs1, AF.Identity,
            bias=vcol("b1", c), scale=vcol("g1", c)), waits=[tc2, t_vec], inc=(sS, 1))
    tap("ISTb", ISTb, [tCONST])
    tap("E8b", E8b[64:72, :], [tCONST])
    tap("xf0", xf[:, 0:2048], [tk_x[0], tXN2])
    tap("ones_col", ones_col_f[:, :], [tCONST])
    tap("xn2b0", xn2b[:, 0:2048], [tXN2])
    tap("statm", statrows[0:1, :], [tsq_])
    tap("statr", statrows[64:65, :], [tsq_])

    p.barrier("p1", dma_sems=p.dma_pool)

    Qb = carve(0, [128, NDC * SEQ], BF16)
    Kb = carve(32, [128, NDC * SEQ], BF16)
    Vtm = carve(64, [128, NJT * 512], BF16)
    wstage = carve(80, [128, 2048], F32)
    wbf = carve(88, [128, 8192], BF16)
    wbf2 = carve(104, [128, 8192], BF16)
    qvb = carve(120, [128, 4 * SEQ], BF16)
    bvrow = carve(136, [1, 512], F32)
    bvrep = carve(138, [128, 512], F32)
    sTt = carve(140, [16, SEQ], BF16)

    wslot_free = [None, None]

    def stream_w(hbm, nchunks, width, dst, stage, pre=None, eng="G"):
        toks = []
        sem = {"G": sG, "S": sS, "V": sV}[eng]
        for c in range(nchunks):
            s = c % 2
            w = [wslot_free[s]]
            if pre is not None and c < 2:
                w.append(pre)
            tl = p.on("Y", lambda e, c=c, s=s, hbm=hbm, width=width, stage=stage: e.dma_start(
                out=stage[:, s * 1024:s * 1024 + width], in_=hbm[c * 128:(c + 1) * 128, :]),
                waits=w, dma=True)
            if eng == "S":
                tc_ = p.on("S", lambda e, c=c, s=s, dst=dst, width=width, stage=stage: e.copy(
                    dst[:, c * width:(c + 1) * width], stage[:, s * 1024:s * 1024 + width]),
                    waits=[tl], inc=(sem, 1))
            else:
                tc_ = p.on(eng, lambda e, c=c, s=s, dst=dst, width=width, stage=stage: e.tensor_copy(
                    dst[:, c * width:(c + 1) * width], stage[:, s * 1024:s * 1024 + width]),
                    waits=[tl], inc=(sem, 1))
            wslot_free[s] = tc_
            toks.append(tc_)
        return toks

    def fm_proj(wtoks, wtile, wwidth, nmt, evac):
        out_toks = []
        bank_i = 0
        for mt in range(nmt):
            for cc in range(4):
                bki = 4 + (bank_i % 4)
                bk = pb[bki]
                last = None
                for dc in range(NDC):
                    w = [wtoks[dc], tXN2]
                    if dc == 0:
                        w.append(bankfree[bki])
                    last = p.on("T", lambda e, mt=mt, cc=cc, dc=dc, bk=bk, wtile=wtile, wwidth=wwidth: e.matmul(
                        bk[:, 0:512],
                        wtile[:, dc * wwidth + mt * 128: dc * wwidth + (mt + 1) * 128],
                        xn2b[:, dc * SEQ + cc * 512: dc * SEQ + (cc + 1) * 512],
                        start=(dc == 0), stop=(dc == NDC - 1)),
                        waits=w, inc=(sT, 1) if dc == NDC - 1 else None)
                te = evac(mt, cc, bk, last)
                bankfree[bki] = te
                out_toks.append(te)
                bank_i += 1
        return out_toks

    wq_toks = stream_w(wqT, 8, 1024, wbf, wstage, eng="S")
    tQ = fm_proj(wq_toks, wbf, 1024, 8, lambda mt, cc, bk, tk: p.on(
        "V", lambda e, mt=mt, cc=cc, bk=bk: e.tensor_scalar(
            Qb[:, mt * SEQ + cc * 512: mt * SEQ + (cc + 1) * 512], bk[:, 0:512],
            ISQ8, bqs[:, mt:mt + 1], ALU.mult, ALU.add),
        waits=[tk, tb1], inc=(sV, 1)))

    wk_toks = stream_w(wkT, 8, 1024, wbf2, wstage)
    tK = fm_proj(wk_toks, wbf2, 1024, 8, lambda mt, cc, bk, tk: p.on(
        "V", lambda e, mt=mt, cc=cc, bk=bk: e.tensor_scalar(
            Kb[:, mt * SEQ + cc * 512: mt * SEQ + (cc + 1) * 512], bk[:, 0:512],
            vcol("bk", mt), None, ALU.add),
        waits=[tk], inc=(sV, 1)))

    tbv = [p.on("G", lambda e, c=c: e.dma_start(
        out=bvrow[0:1, c * 128:(c + 1) * 128],
        in_=vecs_t[:, VC["bv"][0] + c:VC["bv"][0] + c + 1]), waits=[t_vec], dma=True)
        for c in range(4)]
    tm_ = p.on("T", lambda e: e.matmul(pb[0][:, 0:512], onesPP[0:1, 0:128], bvrow[0:1, :],
                                       start=True, stop=True), waits=tbv + [bankfree[0]], inc=(sT, 1))
    tbvrep = p.on("V", lambda e: e.tensor_copy(bvrep, pb[0][:, 0:512]), waits=[tm_], inc=(sV, 1))
    bankfree[0] = tbvrep

    wv_toks = stream_w(wvT, 8, 512, wbf, wstage, pre=tQ[-1], eng="S")
    tV = None
    for jt in range(NJT):
        bki = 2 + (jt % 2)
        bk = pb[bki]
        last = None
        for dc in range(NDC):
            w = [wv_toks[dc], tXN2]
            if dc == 0:
                w.append(bankfree[bki])
            last = p.on("T", lambda e, jt=jt, dc=dc, bk=bk: e.matmul(
                bk[:, 0:512],
                xn2b[:, dc * SEQ + jt * 128: dc * SEQ + (jt + 1) * 128],
                wbf[:, dc * 512:(dc + 1) * 512],
                start=(dc == 0), stop=(dc == NDC - 1)),
                waits=w, inc=(sT, 1) if dc == NDC - 1 else None)
        tV = p.on("V", lambda e, jt=jt, bk=bk: e.tensor_tensor(
            Vtm[:, jt * 512:(jt + 1) * 512], bk[:, 0:512], bvrep, ALU.add),
            waits=[last, tbvrep], inc=(sV, 1))
        bankfree[bki] = tV

    wqv_toks = stream_w(wqvT, 8, 512, wbf2, wstage, pre=tK[-1])
    tQV = fm_proj(wqv_toks, wbf2, 512, 4, lambda mt, cc, bk, tk: p.on(
        "V", lambda e, mt=mt, cc=cc, bk=bk: e.tensor_scalar(
            qvb[:, mt * SEQ + cc * 512: mt * SEQ + (cc + 1) * 512], bk[:, 0:512],
            ISQ32, bqvs[:, mt:mt + 1], ALU.mult, ALU.add),
        waits=[tk, tb2], inc=(sV, 1)))

    tS_T = None
    for cc2 in range(4):
        last = None
        for mt in range(4):
            w = [tQV[-1], tCONST]
            if mt == 0:
                w += [bankfree[1], tS_T]
            last = p.on("T", lambda e, cc2=cc2, mt=mt: e.matmul(
                pb[1][0:16, 0:512], w1b[:, mt * 16:(mt + 1) * 16],
                qvb[:, mt * SEQ + cc2 * 512: mt * SEQ + (cc2 + 1) * 512],
                start=(mt == 0), stop=(mt == 3)),
                waits=w, inc=(sT, 1) if mt == 3 else None)
        tS_T = p.on("V", lambda e, cc2=cc2: e.tensor_scalar(
            sTt[:, cc2 * 512:(cc2 + 1) * 512], pb[1][0:16, 0:512],
            vcol("bs", 0)[0:16, :], None, ALU.add), waits=[last], inc=(sV, 1))
        bankfree[1] = tS_T

    sqs = wbf[:, 0:2048]
    tprev = None
    for src, base, tdep in ((Qb, 0, tQ[-1]), (Kb, 32, tK[-1])):
        for hp in range(8):
            tq = p.on("V", lambda e, hp=hp, src=src: e.tensor_tensor(
                sqs, src[:, hp * SEQ:(hp + 1) * SEQ], src[:, hp * SEQ:(hp + 1) * SEQ], ALU.mult),
                waits=[tdep, tV, tprev], inc=(sV, 1))
            for cc in range(4):
                tm2 = p.on("T", lambda e, cc=cc: e.matmul(
                    pb[0][0:1, 0:512], ones_col_b[:, :], sqs[:, cc * 512:(cc + 1) * 512],
                    start=True, stop=True), waits=[tq, bankfree[0]], inc=(sT, 1))
                tprev = p.on("V", lambda e, hp=hp, cc=cc, base=base: e.reduce_max(
                    nrm[0:1, base + hp * 4 + cc: base + hp * 4 + cc + 1],
                    pb[0][0:1, 0:512], axis=AX.X), waits=[tm2], inc=(sV, 1))
                bankfree[0] = tprev
    tlastn = tprev
    for hp in range(8):
        p.on("V", lambda e, hp=hp: e.reduce_max(
            nrm[0:1, 64 + hp:64 + hp + 1], nrm[0:1, hp * 4:hp * 4 + 4], axis=AX.X),
            waits=[tlastn], inc=(sV, 1))
        tprev = p.on("V", lambda e, hp=hp: e.reduce_max(
            nrm[0:1, 72 + hp:72 + hp + 1], nrm[0:1, 32 + hp * 4:32 + hp * 4 + 4], axis=AX.X),
            inc=(sV, 1))
    t_ = p.on("V", lambda e: e.tensor_tensor(nrm[0:1, 80:88], nrm[0:1, 64:72], nrm[0:1, 72:80], ALU.mult),
              waits=[tprev], inc=(sV, 1))
    t_ = p.on("S", lambda e: e.activation(nrm[0:1, 88:96], nrm[0:1, 80:88], AF.Sqrt),
              waits=[t_], inc=(sS, 1))
    t_ = p.on("V", lambda e: e.tensor_scalar_mul(nrm[0:1, 80:88], nrm[0:1, 88:96], -1.05),
              waits=[t_], inc=(sV, 1))
    tm2 = p.on("T", lambda e: e.matmul(pb[0][:, 0:8], onesPP[0:1, 0:128], nrm[0:1, 80:88],
                                       start=True, stop=True), waits=[t_, bankfree[0]], inc=(sT, 1))
    tNEGM = p.on("V", lambda e: e.tensor_copy(negM[:], pb[0][:, 0:8]), waits=[tm2], inc=(sV, 1))
    bankfree[0] = tNEGM

    tap("Qb0", Qb[:, 0:2048], [tQ[-1]])
    tap("Kb0", Kb[:, 0:2048], [tK[-1]])
    tap("Vtm0", Vtm[:, 0:512], [tV])
    tap("sTt", sTt, [tS_T])
    tap("negM", negM, [tNEGM])

    p.barrier("p2", dma_sems=p.dma_pool)

    if maxphase <= 2:
        p.on("G", None, waits=p.all_dma_toks())
        p.emit()
        return nc

    EAB = carve(80, [128, NJT * 512], BF16)
    outTA = carve(96, [128, 4 * SEQ], BF16)
    outTB = carve(112, [128, 4 * SEQ], BF16)
    prodb = [carve(132, [128, 512], BF16), carve(133, [128, 512], BF16)]
    def carveR(row, kb, shape, dt):
        cols = shape[-1] * (2 if dt == F32 else 1)
        start = int(kb * 512)
        v = arena[row:row + shape[0], start:start + cols]
        return v.bitcast(F32) if dt == F32 else v
    _rowp = {0: 64, 1: 32}
    cS_m = {(hh, par): carveR(_rowp[par], 134 + 2 * hh, [8, 512], F32)
            for hh in (0, 1) for par in (0, 1)}
    ivd_m = {(0, 0): carveR(64, 144, [8, 512], F32), (1, 0): carveR(64, 146, [8, 512], F32),
             (0, 1): carveR(32, 128, [8, 512], F32), (1, 1): carveR(32, 130, [8, 512], F32)}
    ivr_m = {(0, 0): carveR(64, 148, [8, 512], F32), (1, 0): carveR(64, 150, [8, 512], F32),
             (0, 1): carveR(32, 140, [8, 512], F32), (1, 1): carveR(32, 142, [8, 512], F32)}
    expcB_m = {(0, 0): carveR(64, 184, [8, 512], BF16), (1, 0): carveR(64, 185, [8, 512], BF16),
               (0, 1): carveR(64, 130, [8, 512], BF16), (1, 1): carveR(64, 131, [8, 512], BF16)}
    compfs_m = {(0, 0): carveR(64, 138, [8, 512], BF16), (1, 0): carveR(64, 139, [8, 512], BF16),
                (0, 1): carveR(32, 138, [8, 512], BF16), (1, 1): carveR(32, 139, [8, 512], BF16),
                (0, 2): carveR(64, 128, [8, 512], BF16), (1, 2): carveR(64, 129, [8, 512], BF16),
                (0, 3): carveR(32, 184, [8, 512], BF16), (1, 3): carveR(32, 185, [8, 512], BF16)}
    denAB = carve(144, [33, SEQ], F32)
    finall = carve(152, [128, NDC * SEQ], BF16)

    wfb = carve(0, [128, 8192], BF16)
    wstage4 = carve(16, [128, 2048], F32)
    wf_toks = None

    tok_pv = {}
    tok_sband = {}
    tFIN = None
    gate_out = {}
    gate_den = {}
    epi = {"bkr": {7: None, 0: None, 3: None, 1: None}, "bkfinr": {4: None, 5: None},
           "crepfree": {0: [None, None], 1: [None, None]},
           "prodfree": {0: None, 1: None}, "foldfree": {0: None, 1: None}, "tfin": None}

    def emit_epilogue(hp, evdc, denv):
        epi["foldfree"][0] = denv
        epi["foldfree"][1] = evdc[0]

        def comp_chain(hh, cc2, bank):
            head = 2 * hp + hh
            par = cc2 % 2
            outT = outTA if hh == 0 else outTB
            sl = slice(cc2 * 512, (cc2 + 1) * 512)
            ivd = ivd_m[(hh, par)]
            ivr = ivr_m[(hh, par)]
            cS = cS_m[(hh, par)]
            expcB = expcB_m[(hh, par)]
            compfs = compfs_m[(hh, cc2)]
            st = {}
            bk = pb[bank]

            def s0():
                last = None
                for dc in range(4):
                    w = [evdc[dc]]
                    if dc == 0:
                        w.append(epi["bkr"][bank])
                        if bank == 3:
                            w.append(evdc[1])
                    last = p.on("T", lambda e, dc=dc: e.matmul(
                        bk[0:8, 0:512], w2b[:, dc * 8:(dc + 1) * 8],
                        outT[:, dc * SEQ + sl.start: dc * SEQ + sl.stop],
                        start=(dc == 0), stop=(dc == 3)),
                        waits=w, inc=(sT, 1) if dc == 3 else None)
                st["c"] = last

            def s1():
                st["t1"] = p.on("V", lambda e: e.tensor_copy(cS, bk[0:8, 0:512]),
                                waits=[st["c"]], inc=(sV, 1))

            def s2():
                st["tsr"] = p.on("T", lambda e: e.matmul(
                    bk[32:40, 0:512], Eselb[:, head * 8:(head + 1) * 8], sTt[:, sl],
                    start=True, stop=True), waits=[tS_T, st["t1"]], inc=(sT, 1))
                st["tir"] = p.on("T", lambda e: e.matmul(
                    bk[64:72, 0:512], onesPP[32 * hh:32 * hh + 1, 0:8],
                    denAB[32 * hh:32 * hh + 1, sl],
                    start=True, stop=True), waits=[denv], inc=(sT, 1))
                gate_den.setdefault(cc2, [None, None])[hh] = st["tir"]

            def s2b():
                p.on("S", lambda e: e.activation(ivd, bk[64:72, 0:512], AF.Ln),
                     waits=[st["tir"]], inc=(sS, 1))
                st["trec"] = p.on("S", lambda e: e.activation(ivd, ivd, AF.Exp, scale=-1.0),
                                  inc=(sS, 1))

            def s3():
                p.on("V", lambda e: e.tensor_tensor(cS, cS, ivd, ALU.mult),
                     waits=[st["trec"]], inc=(sV, 1))
                st["t3"] = p.on("V", lambda e: e.tensor_tensor(
                    cS, cS, bk[32:40, 0:512], ALU.add), waits=[st["tsr"]], inc=(sV, 1))

            def s4():
                st["t4"] = p.on("S", lambda e: e.activation(cS, cS, AF.Exp),
                                waits=[st["t3"]], inc=(sS, 1))
                st["t4b"] = p.on("S", lambda e: e.copy(expcB, cS),
                                 waits=[st["t4"]], inc=(sS, 1))

            def s5():
                st["t5"] = p.on("T", lambda e: e.matmul(
                    bk[32:40, 0:512], ones8b[64:72, :], expcB,
                    start=True, stop=True), waits=[st["t4b"], st["t3"]], inc=(sT, 1))

            def s6():
                p.on("S", lambda e: e.activation(ivr, bk[32:40, 0:512], AF.Ln),
                     waits=[st["t5"]], inc=(sS, 1))
                st["trecr"] = p.on("S", lambda e: e.activation(ivr, ivr, AF.Exp, scale=-1.0),
                                   inc=(sS, 1))

            def s7():
                st["tc1"] = p.on("G", lambda e: e.tensor_tensor(
                    cS, cS, ivd, ALU.mult),
                    waits=[st["t4b"]], inc=(sG, 1))

            def s8():
                st["tcf"] = p.on("G", lambda e: e.tensor_tensor(
                    compfs[:, :], cS, ivr, ALU.mult),
                    waits=[st["trecr"], st["tc1"]], inc=(sG, 1))
                epi["bkr"][bank] = st["tcf"]

            return [s0, s1, s2, s2b, s3, s4, s5, s6, s7, s8], st

        def fin_chain(hh, cc2, st):
            outT = outTA if hh == 0 else outTB
            sl = slice(cc2 * 512, (cc2 + 1) * 512)
            compfs = compfs_m[(hh, cc2)]
            e8r = _rowp[cc2 % 2]
            rbank = 4 + hh
            steps = []
            tprs = []

            def mk_rep_prod(dc):
                def f():
                    w = [st["tcf"], epi["bkfinr"][rbank]]
                    trep = p.on("T", lambda e: e.matmul(
                        pb[rbank][:, 0:512], E8b[e8r:e8r + 8, dc * 128:(dc + 1) * 128], compfs[:, :],
                        start=True, stop=True), waits=w, inc=(sT, 1))
                    pbuf = prodb[hh]
                    tpr = p.on("V", lambda e: e.tensor_tensor(
                        pbuf[:, :], outT[:, dc * SEQ + sl.start: dc * SEQ + sl.stop],
                        pb[rbank][:, 0:512], ALU.mult),
                        waits=[trep, epi["prodfree"][hh]], inc=(sV, 1))
                    tprs.append(tpr)
                    epi["bkfinr"][rbank] = tpr
                    fb = 6 if hh == 0 else 2
                    w2 = [tpr]
                    if dc == 0:
                        w2.append(epi["foldfree"][hh])
                    tff = p.on("T", lambda e, dc=dc, fb=fb: e.matmul(
                        pb[fb][0:64, 0:512], ISTb[:, :], pbuf,
                        start=(dc == 0), stop=(dc == 3)), waits=w2, inc=(sT, 1))
                    epi["prodfree"][hh] = tff
                    st["tff"] = tff
                return f

            for dc in range(4):
                steps.append(mk_rep_prod(dc))

            def s_ev():
                fb = 6 if hh == 0 else 2
                tfin = p.on("V", lambda e, fb=fb: e.tensor_copy(
                    finall[64 * hh:64 * (hh + 1), hp * SEQ + sl.start: hp * SEQ + sl.stop],
                    pb[fb][0:64, 0:512]), waits=[st["tff"]], inc=(sV, 1))
                epi["foldfree"][hh] = tfin
                epi["tfin"] = tfin
                if hh == 1:
                    gate_out[cc2] = tprs[-1]
            steps.append(s_ev)
            return steps

        def zrun(chains):
            for ops in itertools.zip_longest(*chains):
                for op in ops:
                    if op:
                        op()
        metas = {}
        for wave in (0, 1):
            comps = []
            for cc2 in (2 * wave, 2 * wave + 1):
                ca, sta = comp_chain(0, cc2, (7, 3)[cc2 % 2])
                cb, stb = comp_chain(1, cc2, (0, 1)[cc2 % 2])
                comps += [ca, cb]
                metas[cc2] = (sta, stb)
            if wave == 1:
                comps += [fin_chain(0, 0, metas[0][0]), fin_chain(1, 0, metas[0][1])]
            zrun(comps)
        for cc2 in range(1, 4):
            sta, stb = metas[cc2]
            zrun([fin_chain(0, cc2, sta), fin_chain(1, cc2, stb)])
        return epi["tfin"]

    for hp in range(hpn):
        tok_pv_prevhp = tok_pv
        tok_exp = {}
        tok_pv = {}
        pending_evac = None
        for iq in range(NIQ):
            def s_and_exp(j, iq=iq, hp=hp, tok_pv_prevhp=tok_pv_prevhp):
                w = [tQ[-1], tK[-1], tNEGM, tok_sband.get(0), bankfree[0], bankfree[1]]
                p.on("T", lambda e, j=j, iq=iq, hp=hp: e.matmul(
                    pb[0][:, 0:IQ],
                    Kb[0:64, hp * SEQ + j * 128: hp * SEQ + (j + 1) * 128],
                    Qb[0:64, hp * SEQ + iq * IQ: hp * SEQ + (iq + 1) * IQ],
                    start=True, stop=True, tile_position=(0, 0)), waits=w)
                tmm = p.on("T", lambda e, j=j, iq=iq, hp=hp: e.matmul(
                    pb[1][:, 0:IQ],
                    Kb[64:128, hp * SEQ + j * 128: hp * SEQ + (j + 1) * 128],
                    Qb[64:128, hp * SEQ + iq * IQ: hp * SEQ + (iq + 1) * IQ],
                    start=True, stop=True, tile_position=(64, 0)), inc=(sT, 1))
                if iq == 0:
                    prevpv = tok_pv_prevhp.get((NIQ - 1, j))
                else:
                    prevpv = tok_pv.get((iq - 1, j))
                te = p.on("S", lambda e, j=j, hp=hp: e.activation(
                    EAB[:, j * 512:(j + 1) * 512].rearrange("p (a b) -> p a b", b=IQ),
                    psS[:, 0:1024].rearrange("p (a b) -> p a b", b=512)[:, :, 0:IQ],
                    AF.Exp, bias=negM[:, hp:hp + 1]), waits=[tmm, prevpv], inc=(sS, 1))
                tok_exp[(iq, j)] = te
                tok_sband[0] = te

            s_and_exp(0)
            s_and_exp(1)
            if pending_evac is not None:
                pending_evac()
                pending_evac = None
            for j in range(NJT):
                if j + 2 <= NJT - 1:
                    s_and_exp(j + 2)
                for dc in range(4):
                    bki = 2 + dc
                    bk = pb[bki]
                    w = [tok_exp[(iq, j)], tV]
                    if j == 0:
                        w.append(bankfree[bki])
                    p.on("T", lambda e, j=j, dc=dc, bk=bk: e.matmul(
                        bk[:, 0:2 * IQ],
                        Vtm[:, j * 512 + dc * 128: j * 512 + (dc + 1) * 128],
                        EAB[:, j * 512: (j + 1) * 512],
                        start=(j == 0), stop=(j == NJT - 1)), waits=w)
                w = [tok_exp[(iq, j)]]
                if j == 0:
                    w.append(bankfree[6])
                tok_pv[(iq, j)] = p.on("T", lambda e, j=j: e.matmul(
                    pb[6][0:1, 0:2 * IQ], ones_col_b[:, :], EAB[:, j * 512: (j + 1) * 512],
                    start=(j == 0), stop=(j == NJT - 1)), waits=w, inc=(sT, 1))
            tlast = tok_pv[(iq, NJT - 1)]
            go = gate_out.get(iq // 2)
            gd = gate_den.get(iq // 2)
            w = [tlast] + (gd if gd else [])
            p.on("V", lambda e, iq=iq: e.tensor_copy(
                denAB[0:1, iq * IQ:(iq + 1) * IQ], pb[6][0:1, 0:IQ]), waits=w, inc=(sV, 1))
            bankfree[6] = p.on("V", lambda e, iq=iq: e.tensor_copy(
                denAB[32:33, iq * IQ:(iq + 1) * IQ], pb[6][0:1, IQ:2 * IQ]),
                inc=(sV, 1))
            def mk_evac(iq=iq, tlast=tlast, go=go):
                def f():
                    for dc in range(4):
                        bk = pb[2 + dc]
                        ta = p.on("S", lambda e, dc=dc, iq=iq, bk=bk: e.copy(
                            outTA[:, dc * SEQ + iq * IQ: dc * SEQ + (iq + 1) * IQ], bk[:, 0:IQ]),
                            waits=[tlast, go], inc=(sS, 1))
                        tb = p.on("V", lambda e, dc=dc, iq=iq, bk=bk: e.tensor_copy(
                            outTB[:, dc * SEQ + iq * IQ: dc * SEQ + (iq + 1) * IQ], bk[:, IQ:2 * IQ]),
                            waits=[tlast, go], inc=(sV, 1))
                        bankfree[2 + dc] = [ta, tb]
                return f
            pending_evac = mk_evac()
        pending_evac()
        pending_evac = None
        tFIN = emit_epilogue(hp, [bankfree[2 + dc] for dc in range(4)], bankfree[6])
        bankfree[0] = epi["bkr"][0]
        bankfree[1] = epi["bkr"][1]
        bankfree[2] = epi["foldfree"][1]
        bankfree[3] = epi["bkr"][3]
        bankfree[4] = epi["bkfinr"][4]
        bankfree[5] = epi["bkfinr"][5]
        bankfree[6] = epi["foldfree"][0]
        if hp == 5 and maxphase >= 4:
            wslot_free[0] = wslot_free[1] = None
            wf_toks = stream_w(wfT, 8, 1024, wfb, wstage4, pre=p.tok(sT), eng="S")

    tap("finall0", finall[:, 0:2048], [tFIN])
    tap("outTB2", outTB[:, 2 * SEQ:3 * SEQ], [p.tok(sV)])
    tap("outTBf", outTB, [p.tok(sV)])
    tap("outTAf", outTA, [p.tok(sV)])
    tap("EABlast", EAB[:, 0:1024], [p.tok(sV)])
    tap("denA", denAB[0:1, :], [p.tok(sV)])
    tap("denB", denAB[32:33, :], [p.tok(sV)])
    tap("outTA0", outTA[:, 0:2048], [p.tok(sV)])

    p.barrier("p3", dma_sems=p.dma_pool)

    if maxphase <= 3:
        p.on("G", None, waits=p.all_dma_toks())
        p.emit()
        return nc

    mrepw = carve(24, [128, WIN], F32)
    rrepw = carve(26, [128, WIN], F32)
    xw = [carve(28, [128, WIN], F32), carve(30, [128, WIN], F32)]
    wr1b = carve(32, [128, 16384], BF16)
    wr2b = carve(64, [128, 16384], BF16)
    amidw = carve(96, [128, NDC * WIN], F32)
    h1w = carve(112, [128, 16 * WIN], BF16)
    xn2w = carve(128, [128, NDC * WIN], BF16)
    outw = [carve(136, [128, WIN], F32), carve(138, [128, WIN], F32)]
    sqw = carve(140, [128, WIN], F32)
    m2row = carve(142, [1, WIN], F32)
    vtmp = carve(144, [1, WIN], F32)
    r2wrow = carve(146, [1, WIN], F32)
    m2rep = carve(148, [128, WIN], F32)
    r2wrep = carve(150, [128, WIN], F32)
    tmp4 = carve(184, [128, WIN], F32)

    if wf_toks is None:
        wslot_free[0] = wslot_free[1] = None
        wf_toks = stream_w(wfT, 8, 1024, wfb, wstage4, eng="S")
    wslot_free[0] = wslot_free[1] = None
    wr1_toks = []
    for c in range(8):
        tl = p.on("Y", lambda e, c=c: e.dma_start(
            out=wstage4[:, 0:2048], in_=wr1T[c * 128:(c + 1) * 128, :]),
            waits=[wslot_free[0], wslot_free[1]], dma=True)
        tc_ = p.on("S", lambda e, c=c: e.copy(
            wr1b[:, c * 2048:(c + 1) * 2048], wstage4[:, 0:2048]), waits=[tl], inc=(sS, 1))
        wslot_free[0] = wslot_free[1] = tc_
        wr1_toks.append(tc_)
    wr2_toks = None

    tow = [None, None]
    th1 = {}
    txn2w_last = None
    wr1done_prev = None
    fsave = {}
    for w_i in range(NW + 1):
        if w_i < NW:
            w0 = w_i * WIN
            fpP = (4, 5) if w_i % 2 == 0 else (6, 7)
            tmrw = replicate(mrepw, statrows[0:1, w0:w0 + WIN], 0, [], 0, width=WIN)
            trrw = replicate(rrepw, statrows[64:65, w0:w0 + WIN], 64, [], 1, width=WIN)
            tamid = {}
            for dc in range(NDC):
                bki = fpP[dc % 2]
                bk = pb[bki]
                last = None
                for fc in range(NDC):
                    wts = [wf_toks[fc], tFIN]
                    if fc == 0:
                        wts.append(bankfree[bki])
                    last = p.on("T", lambda e, dc=dc, fc=fc, w0=w0, bk=bk: e.matmul(
                        bk[:, 0:WIN], wfb[:, fc * 1024 + dc * 128: fc * 1024 + (dc + 1) * 128],
                        finall[:, fc * SEQ + w0: fc * SEQ + w0 + WIN],
                        start=(fc == 0), stop=(fc == NDC - 1)),
                        waits=wts, inc=(sT, 1) if fc == NDC - 1 else None)
                txl = p.on("S", lambda e, dc=dc, w0=w0: e.dma_start(
                    out=xw[dc % 2], in_=xT[dc * 128:(dc + 1) * 128, w0:w0 + WIN]),
                    waits=[tow[dc % 2]], dma=True)
                t1 = p.on("G", lambda e, dc=dc: e.tensor_tensor(
                    tmp4, xw[dc % 2], mrepw, ALU.subtract),
                    waits=[txl, tmrw, trrw, txn2w_last], inc=(sG, 1))
                t2g = p.on("G", lambda e, dc=dc: e.tensor_tensor(tmp4, tmp4, rrepw, ALU.mult),
                           inc=(sG, 1))
                t3 = p.on("V", lambda e, dc=dc, bk=bk: e.tensor_scalar(
                    sqw, bk[:, 0:WIN], vcol("bf", dc), None, ALU.add),
                    waits=[last] + ([txn2w_last] if dc == 0 else []), inc=(sV, 1))
                bankfree[bki] = t3
                t4 = p.on("V", lambda e, dc=dc: e.tensor_tensor(
                    amidw[:, dc * WIN:(dc + 1) * WIN], tmp4, sqw, ALU.add),
                    waits=[t2g], inc=(sV, 1))
                t5 = p.on("V", lambda e, dc=dc: e.tensor_tensor(
                    outw[dc % 2], xw[dc % 2], amidw[:, dc * WIN:(dc + 1) * WIN], ALU.add),
                    waits=[tow[dc % 2]], inc=(sV, 1))
                _src = (lambda dc: amidw[:, dc * WIN:(dc + 1) * WIN]) if getattr(build_program, "_plain_amid_only", False) else (lambda dc: outw[dc % 2])
                tst = p.on("G", lambda e, dc=dc, w0=w0, _src=_src: e.dma_start(
                    out=out_hbm[dc * 128:(dc + 1) * 128, w0:w0 + WIN], in_=_src(dc)),
                    waits=[t5], dma=True)
                tow[dc % 2] = tst
                tamid[dc] = t4
            fsave[w_i] = (tamid, None, fpP, w0)
        if w_i == 1:
            wr2_toks = stream_w(wr2T, 16, 1024, wr2b, wstage4, eng="S")
        if w_i >= 1:
            for mt in range(16):
                bki = (2, 3)[mt % 2]
                bk = pb[bki]
                last = None
                for dc in range(NDC):
                    wts = [wr1_toks[dc], txn2w[dc]]
                    if dc == 0:
                        wts.append(bankfree[bki])
                    last = p.on("T", lambda e, mt=mt, dc=dc, bk=bk: e.matmul(
                        bk[:, 0:WIN], wr1b[:, dc * 2048 + mt * 128: dc * 2048 + (mt + 1) * 128],
                        xn2w[:, dc * WIN:(dc + 1) * WIN],
                        start=(dc == 0), stop=(dc == NDC - 1)),
                        waits=wts, inc=(sT, 1) if dc == NDC - 1 else None)
                th1[mt] = p.on("S", lambda e, mt=mt, bk=bk: e.activation(
                    h1w[:, mt * WIN:(mt + 1) * WIN], bk[:, 0:WIN], AF.Relu,
                    bias=vcol("br1", mt)), waits=[last], inc=(sS, 1))
                bankfree[bki] = th1[mt]
            wr1done_prev = th1[15]
        if w_i < NW:
            tamid = fsave[w_i][0]
            tprev = None
            for dc in range(NDC):
                tsq2 = p.on("V", lambda e, dc=dc: e.tensor_tensor(
                    sqw, amidw[:, dc * WIN:(dc + 1) * WIN], amidw[:, dc * WIN:(dc + 1) * WIN], ALU.mult),
                    waits=[tamid[dc], tprev], inc=(sV, 1))
                w = [tamid[dc]]
                if dc == 0:
                    w += [bankfree[0], bankfree[1]]
                p.on("T", lambda e, dc=dc: e.matmul(
                    pb[0][0:1, 0:WIN], ones_col_f[:, :], amidw[:, dc * WIN:(dc + 1) * WIN],
                    start=(dc == 0), stop=(dc == NDC - 1)), waits=w)
                tprev = p.on("T", lambda e, dc=dc: e.matmul(
                    pb[1][0:1, 0:WIN], ones_col_f[:, :], sqw,
                    start=(dc == 0), stop=(dc == NDC - 1)), waits=[tsq2], inc=(sT, 1))
            tv = p.on("V", lambda e: e.tensor_scalar_mul(m2row, pb[0][0:1, 0:WIN], 1.0 / D),
                      waits=[tprev], inc=(sV, 1))
            tv = p.on("V", lambda e: e.tensor_scalar_mul(vtmp, pb[1][0:1, 0:WIN], 1.0 / D),
                      inc=(sV, 1))
            tv = p.on("V", lambda e: e.tensor_tensor(r2wrow, m2row, m2row, ALU.mult),
                      inc=(sV, 1))
            tv = p.on("V", lambda e: e.tensor_tensor(vtmp, vtmp, r2wrow, ALU.subtract),
                      inc=(sV, 1))
            tv = p.on("V", lambda e: e.tensor_scalar_add(vtmp, vtmp, EPS), inc=(sV, 1))
            p.on("S", lambda e: e.activation(vtmp, vtmp, AF.Ln),
                 waits=[tv], inc=(sS, 1))
            tsqr = p.on("S", lambda e: e.activation(r2wrow, vtmp, AF.Exp, scale=-0.5),
                        inc=(sS, 1))
            tm_a = p.on("T", lambda e: e.matmul(pb[0][:, 0:WIN], onesPP[0:1, 0:128], m2row,
                                                start=True, stop=True), waits=[tv], inc=(sT, 1))
            tm2r = p.on("V", lambda e: e.tensor_copy(m2rep, pb[0][:, 0:WIN]), waits=[tm_a], inc=(sV, 1))
            bankfree[0] = tm2r
            tm_b = p.on("T", lambda e: e.matmul(pb[1][:, 0:WIN], onesPP[0:1, 0:128], r2wrow,
                                                start=True, stop=True), waits=[tsqr], inc=(sT, 1))
            tr2w = p.on("V", lambda e: e.tensor_copy(r2wrep, pb[1][:, 0:WIN]), waits=[tm_b], inc=(sV, 1))
            bankfree[1] = tr2w
            txn2w = {}
            for dc in range(NDC):
                tb = tmp4 if dc % 2 == 0 else sqw
                w_x = [tm2r, tr2w, txn2w.get(dc - 2)]
                if dc == 0:
                    w_x.append(wr1done_prev)
                p.on("V", lambda e, dc=dc, tb=tb: e.tensor_tensor(
                    tb, amidw[:, dc * WIN:(dc + 1) * WIN], m2rep, ALU.subtract),
                    waits=w_x, inc=(sV, 1))
                t2 = p.on("V", lambda e, dc=dc, tb=tb: e.tensor_tensor(tb, tb, r2wrep, ALU.mult), inc=(sV, 1))
                txn2w[dc] = p.on("S", lambda e, dc=dc, tb=tb: e.activation(
                    xn2w[:, dc * WIN:(dc + 1) * WIN], tb, AF.Identity,
                    bias=vcol("b2", dc), scale=vcol("g2", dc)), waits=[t2], inc=(sS, 1))
                txn2w_last = txn2w[dc]
        if w_i >= 1:
            _, _, fpPp, w0p = fsave[w_i - 1]
            for mt in range(NDC):
                bki = fpPp[mt % 2]
                bk = pb[bki]
                last = None
                for hc in range(16):
                    wts = [wr2_toks[hc], th1[hc]]
                    if hc == 0:
                        wts += [bankfree[bki], tow[mt % 2]]
                    last = p.on("T", lambda e, mt=mt, hc=hc, bk=bk: e.matmul(
                        bk[:, 0:WIN], wr2b[:, hc * 1024 + mt * 128: hc * 1024 + (mt + 1) * 128],
                        h1w[:, hc * WIN:(hc + 1) * WIN],
                        start=(hc == 0), stop=(hc == 15)),
                        waits=wts, inc=(sT, 1) if hc == 15 else None)
                t1 = p.on("V", lambda e, mt=mt, bk=bk: e.tensor_scalar(
                    outw[mt % 2], bk[:, 0:WIN], vcol("br2", mt), None, ALU.add),
                    waits=[last, tow[mt % 2]], inc=(sV, 1))
                bankfree[bki] = t1
                if not getattr(build_program, "_skip_iffn_dma", False):
                    tow[mt % 2] = p.on("G", lambda e, mt=mt, w0p=w0p: e.dma_start(
                        out=out_hbm[mt * 128:(mt + 1) * 128, w0p:w0p + WIN], in_=outw[mt % 2],
                        accum_op=ALU.add), waits=[t1], dma=True)
                else:
                    tow[mt % 2] = t1

    tap("amidw", amidw, [p.tok(sV)])
    tap("xn2w", xn2w, [p.tok(sS)])
    tap("h1w", h1w, [p.tok(sS)])
    tap("wr1b0", wr1b[:, 0:2048], [wr1_toks[0]])
    tap("wr2b0", wr2b[:, 0:1024], [wr2_toks[0]])
    tap("wr2bf", wr2b, [wr2_toks[-1]])
    tap("wr1bf", wr1b, [wr1_toks[-1]])

    p.barrier("p4", dma_sems=p.dma_pool)

    m_rep5 = carve(0, [128, SEQ], F32)
    r_rep5 = carve(8, [128, SEQ], F32)
    xnb = carve(16, [128, NDC * SEQ], BF16)
    xscr = carve(48, [128, SEQ], F32)
    mwbf = carve(56, [128, 8192], BF16)
    Gb = carve(72, [128, 2048], BF16)
    wqc = carve(76, [128, 2048], BF16)
    wkc = carve(80, [128, 2048], BF16)
    wvc = carve(84, [128, 2048], BF16)
    qkb = carve(88, [128, 2 * SEQ], BF16)
    kkb = carve(96, [128, 2 * SEQ], BF16)
    vktm = carve(104, [128, NJT * 288], BF16)
    EABm = carve(114, [128, NJT * 512], BF16)
    outmA = carve(130, [65, SEQ], F32)
    outmB = carve(138, [65, SEQ], F32)
    outc = carve(146, [128, 2 * SEQ], BF16)
    mrep16 = carve(154, [16, SEQ], F32)
    rrep16 = carve(162, [16, SEQ], F32)
    logsb = carve(170, [16, SEQ], F32)
    mscr = carve(178, [128, SEQ], F32)
    grep = carve(154, [128, 256], F32)
    bvcrep = carve(155, [128, 256], F32)
    mow = [carve(130, [128, WIN], F32), carve(132, [128, WIN], F32)]

    tmr5 = replicate(m_rep5, statrows[0:1, :], 0, [], 0)
    trr5 = replicate(r_rep5, statrows[64:65, :], 64, [], 1)

    twr = p.on("Y", lambda e: e.dma_start(
        out=wrf[:].rearrange("p (c n) -> p c n", c=8), in_=mwr.rearrange("(c p) n -> p c n", p=128)),
        dma=True)

    tlog = None
    txnb = {}
    prevx = None
    for c in range(NDC):
        txl = p.on("Y", lambda e, c=c: e.dma_start(
            out=xscr, in_=xT[c * 128:(c + 1) * 128, :]), waits=[prevx, tlog], dma=True)
        for cc in range(4):
            w = [txl, twr]
            if c == 0:
                w.append(bankfree[4 + cc])
            tlog = p.on("T", lambda e, c=c, cc=cc: e.matmul(
                pb[4 + cc][0:16, 0:512], wrf[:, c * 16:(c + 1) * 16],
                xscr[:, cc * 512:(cc + 1) * 512],
                start=(c == 0), stop=(c == NDC - 1)), waits=w, inc=(sT, 1))
        t1 = p.on("V", lambda e, c=c: e.tensor_tensor(mscr, xscr, m_rep5, ALU.subtract),
                  waits=[txl, tmr5, trr5], inc=(sV, 1))
        prevx = t1
        txnb[c] = p.on("V", lambda e, c=c: e.tensor_tensor(
            xnb[:, c * SEQ:(c + 1) * SEQ], mscr, r_rep5, ALU.mult), inc=(sV, 1))
    tXNB = txnb[NDC - 1]

    tcs = None
    for c in range(NDC):
        w = [twr, tCONST]
        if c == 0:
            w.append(bankfree[0])
        tcs = p.on("T", lambda e, c=c: e.matmul(
            pb[0][0:16, 0:1], wrf[:, c * 16:(c + 1) * 16], ones_col_f[:, :],
            start=(c == 0), stop=(c == NDC - 1)), waits=w, inc=(sT, 1))
    tcsv = p.on("V", lambda e: e.tensor_copy(gat[0:16, 32:33], pb[0][0:16, 0:1]), waits=[tcs], inc=(sV, 1))
    bankfree[0] = tcsv

    trm16 = None
    for cc in range(4):
        tm_a = p.on("T", lambda e, cc=cc: e.matmul(
            pb[2][0:16, 0:512], onesPP[0:1, 0:16], statrows[0:1, cc * 512:(cc + 1) * 512],
            start=True, stop=True), waits=[trm16, bankfree[2]], inc=(sT, 1))
        tmv = p.on("V", lambda e, cc=cc: e.tensor_copy(mrep16[:, cc * 512:(cc + 1) * 512], pb[2][0:16, 0:512]),
                   waits=[tm_a], inc=(sV, 1))
        bankfree[2] = tmv
        tm_b = p.on("T", lambda e, cc=cc: e.matmul(
            pb[3][0:16, 0:512], onesPP[64:65, 0:16], statrows[64:65, cc * 512:(cc + 1) * 512],
            start=True, stop=True), waits=[tmv, bankfree[3]], inc=(sT, 1))
        trm16 = p.on("V", lambda e, cc=cc: e.tensor_copy(rrep16[:, cc * 512:(cc + 1) * 512], pb[3][0:16, 0:512]),
                     waits=[tm_b], inc=(sV, 1))
        bankfree[3] = trm16

    tlg = None
    for cc in range(4):
        p.on("V", lambda e, cc=cc: e.tensor_scalar(
            logsb[:, cc * 512:(cc + 1) * 512], mrep16[:, cc * 512:(cc + 1) * 512],
            gat[0:16, 32:33], None, ALU.mult), waits=[tlog, tcsv, trm16], inc=(sV, 1))
        p.on("V", lambda e, cc=cc: e.tensor_tensor(
            logsb[:, cc * 512:(cc + 1) * 512], pb[4 + cc][0:16, 0:512],
            logsb[:, cc * 512:(cc + 1) * 512], ALU.subtract), inc=(sV, 1))
        tlg = p.on("V", lambda e, cc=cc: e.tensor_tensor(
            logsb[:, cc * 512:(cc + 1) * 512], logsb[:, cc * 512:(cc + 1) * 512],
            rrep16[:, cc * 512:(cc + 1) * 512], ALU.mult), inc=(sV, 1))
        bankfree[4 + cc] = tlg
    tmean = p.on("V", lambda e: e.reduce_sum(gat[0:16, 33:34], logsb, axis=AX.X), waits=[tlg], inc=(sV, 1))
    tmean = p.on("V", lambda e: e.tensor_scalar(
        gat[0:16, 34:35], gat[0:16, 33:34], 1.0 / SEQ, vcol("mbr", 0)[0:16, :], ALU.mult, ALU.add),
        inc=(sV, 1))
    tdm = p.on("G", lambda e: e.dma_start(out=gat[0:1, 0:16], in_=gat[0:16, 34:35]),
               waits=[tmean], dma=True)
    t1 = p.on("V", lambda e: e.reduce_max(gat[0:1, 16:17], gat[0:1, 0:16], axis=AX.X),
              waits=[tdm], inc=(sV, 1))
    t2 = p.on("V", lambda e: e.tensor_tensor(gat[0:1, 0:16], gat[0:1, 0:16],
                                             gat[0:1, 16:17].broadcast_to([1, 16]), ALU.subtract),
              inc=(sV, 1))
    t3 = p.on("S", lambda e: e.activation(gat[0:1, 0:16], gat[0:1, 0:16], AF.Exp), waits=[t2], inc=(sS, 1))
    t4 = p.on("V", lambda e: e.reduce_sum(gat[0:1, 16:17], gat[0:1, 0:16], axis=AX.X),
              waits=[t3], inc=(sV, 1))
    t5 = p.on("V", lambda e: e.reciprocal(gat[0:1, 17:18], gat[0:1, 16:17]), inc=(sV, 1))
    tdist = p.on("V", lambda e: e.tensor_tensor(gat[0:1, 0:16], gat[0:1, 0:16],
                                                gat[0:1, 17:18].broadcast_to([1, 16]), ALU.mult),
                 inc=(sV, 1))
    p.on("V", lambda e: e.max(gat[0:1, 18:26], gat[0:1, 0:16]), waits=[tdist], inc=(sV, 1))
    ttk = p.on("V", lambda e: e.max_index(mi8[0:1, :], gat[0:1, 18:26], gat[0:1, 0:16]),
               inc=(sV, 1))
    tsp = p.on("V", lambda e: e.memset(gat[0:1, 48:64], 0.0), inc=(sV, 1))
    toh = {}
    for k in range(4):
        toh[k] = p.on("V", lambda e, k=k: e.tensor_tensor(
            gat[0:1, 64 + 16 * k:80 + 16 * k], gat[0:1, 0:16],
            gat[0:1, 18 + k:19 + k].broadcast_to([1, 16]), ALU.is_equal), waits=[ttk], inc=(sV, 1))
        p.on("V", lambda e, k=k: e.tensor_tensor(
            gat[0:1, 128 + 16 * k:144 + 16 * k], gat[0:1, 64 + 16 * k:80 + 16 * k],
            gat[0:1, 18 + k:19 + k].broadcast_to([1, 16]), ALU.mult), inc=(sV, 1))
        tsp = p.on("V", lambda e, k=k: e.tensor_tensor(
            gat[0:1, 48:64], gat[0:1, 48:64], gat[0:1, 128 + 16 * k:144 + 16 * k], ALU.add),
            inc=(sV, 1))
    tge = p.on("S", lambda e: e.activation(gat[0:1, 192:208], gat[0:1, 48:64], AF.Exp),
               waits=[tsp], inc=(sS, 1))
    p.on("V", lambda e: e.reduce_sum(gat[0:1, 16:17], gat[0:1, 192:208], axis=AX.X),
         waits=[tge], inc=(sV, 1))
    p.on("V", lambda e: e.reciprocal(gat[0:1, 17:18], gat[0:1, 16:17]), inc=(sV, 1))
    tga = p.on("V", lambda e: e.tensor_tensor(gat[0:1, 192:208], gat[0:1, 192:208],
                                              gat[0:1, 17:18].broadcast_to([1, 16]), ALU.mult),
               inc=(sV, 1))
    tgk = None
    for k in range(4):
        p.on("V", lambda e, k=k: e.tensor_tensor(
            gat[0:1, 128 + 16 * k:144 + 16 * k], gat[0:1, 192:208],
            gat[0:1, 64 + 16 * k:80 + 16 * k], ALU.mult), waits=[tga], inc=(sV, 1))
        tgk = p.on("V", lambda e, k=k: e.reduce_sum(
            gat[0:1, 26 + k:27 + k], gat[0:1, 128 + 16 * k:144 + 16 * k], axis=AX.X), inc=(sV, 1))
    tgc = p.on("G", lambda e: e.dma_start(out=gat[0:4, 40:41], in_=gat[0:1, 26:30]),
               waits=[tgk], dma=True)
    tge2 = {}
    for t2_ in range(2):
        tm_a = p.on("T", lambda e, t2_=t2_: e.matmul(
            pb[0][:, 0:1], E64f[:, t2_ * 128:(t2_ + 1) * 128], gat[0:4, 40:41],
            start=True, stop=True), waits=[tgc, bankfree[0]], inc=(sT, 1))
        tge2[t2_] = p.on("S", lambda e, t2_=t2_: e.copy(gexp[:, t2_:t2_ + 1], pb[0][:, 0:1]),
                         waits=[tm_a], inc=(sS, 1))
        bankfree[0] = tge2[t2_]
    tgq = p.on("S", lambda e: e.mul(gexpQ[:], gexp[:], ISQ8), waits=[tge2[1]], inc=(sS, 1))
    th4 = []
    for k in range(4):
        t1 = p.on("V", lambda e, k=k: e.tensor_copy(gatb[0:1, :], gat[0:1, 64 + 16 * k:80 + 16 * k]),
                  waits=[toh[k], (th4[-1] if th4 else None)], inc=(sV, 1))
        th4.append(p.on("G", lambda e, k=k: e.dma_start(out=H4T[k:k + 1, :], in_=gatb[0:1, :]),
                        waits=[t1], dma=True))
    tohc = [p.on("G", lambda e, k=k: e.dma_start(
        out=gat[0:16, 41 + k:42 + k], in_=gat[0:1, 64 + 16 * k:80 + 16 * k]),
        waits=[toh[k]], dma=True) for k in range(4)]
    tbc_last = None
    for wi in range(3):
        for k in range(4):
            tm_a = p.on("T", lambda e, wi=wi, k=k: e.matmul(
                pb[1][0:64, 0:1], bh3t[:, wi * 64:(wi + 1) * 64], gat[0:16, 41 + k:42 + k],
                start=True, stop=True), waits=[t_bh3, tohc[k], tbc_last, bankfree[1]],
                inc=(sT, 1))
            tbc_last = p.on("S", lambda e, wi=wi, k=k: e.copy(
                bcpp[64 * (k % 2):64 * (k % 2) + 64, 2 * wi + k // 2:2 * wi + k // 2 + 1],
                pb[1][0:64, 0:1]), waits=[tm_a], inc=(sS, 1))
            bankfree[1] = tbc_last
    tBC = tbc_last

    tm_a = p.on("T", lambda e: e.matmul(pb[0][0:16, 0:256], H4T[:, :], Ecol4b[:, :],
                                        start=True, stop=True), waits=th4 + [tCONST, bankfree[0]],
                inc=(sT, 1))
    th4e = p.on("V", lambda e: e.tensor_copy(H4E[:, :], pb[0][0:16, 0:256]), waits=[tm_a], inc=(sV, 1))
    bankfree[0] = th4e
    tGB = None
    for c in range(NDC):
        tm_a = p.on("T", lambda e, c=c: e.matmul(
            pb[0][:, 0:256], RowSelb[:, c * 128:(c + 1) * 128], H4E[:, :],
            start=True, stop=True), waits=[th4e, bankfree[0]], inc=(sT, 1))
        tGB = p.on("V", lambda e, c=c: e.tensor_tensor(
            Gb[:, c * 256:(c + 1) * 256], pb[0][:, 0:256], M64b[:, :], ALU.mult),
            waits=[tm_a], inc=(sV, 1))
        bankfree[0] = tGB

    wslot_free[0] = wslot_free[1] = None

    def stream_w5(hbm, nchunks, pre=None):
        toks = []
        for c in range(nchunks):
            s = c % 2
            w = [wslot_free[s]]
            if pre is not None and c < 2:
                w.append(pre)
            tl = p.on("Y", lambda e, c=c, s=s, hbm=hbm: e.dma_start(
                out=mscr[:, s * 1024:(s + 1) * 1024], in_=hbm[c * 128:(c + 1) * 128, :]),
                waits=w, dma=True)
            tc_ = p.on("S", lambda e, c=c, s=s: e.copy(
                mwbf[:, c * 1024:(c + 1) * 1024], mscr[:, s * 1024:(s + 1) * 1024]),
                waits=[tl], inc=(sS, 1))
            wslot_free[s] = tc_
            toks.append(tc_)
        return toks

    def compact_w(wtoks, dst, pre):
        last_ev = pre
        for it in range(NDC):
            last = None
            for oc in range(NDC):
                w = [wtoks[oc], tGB]
                if oc == 0:
                    w += [last_ev, bankfree[2]]
                last = p.on("T", lambda e, it=it, oc=oc: e.matmul(
                    pb[2][:, 0:256], mwbf[:, oc * 1024 + it * 128: oc * 1024 + (it + 1) * 128],
                    Gb[:, oc * 256:(oc + 1) * 256],
                    start=(oc == 0), stop=(oc == NDC - 1)),
                    waits=w, inc=(sT, 1) if oc == NDC - 1 else None)
            last_ev = p.on("V", lambda e, it=it, dst=dst: e.tensor_copy(
                dst[:, it * 256:(it + 1) * 256], pb[2][:, 0:256]), waits=[last], inc=(sV, 1))
            bankfree[2] = last_ev
        return last_ev

    tq_toks = stream_w5(mwqT, 8, pre=tXNB)
    tWQC = compact_w(tq_toks, wqc, None)
    tk_toks = stream_w5(mwkT, 8, pre=tWQC)
    tWKC = compact_w(tk_toks, wkc, None)
    tv_toks = stream_w5(mwvT, 8, pre=tWKC)
    tWVC = compact_w(tv_toks, wvc, None)

    def cproj(wc, twc, dst, bcol, gtile):
        toks = []
        for half in range(2):
            for cc in range(4):
                bki = 4 + ((half * 4 + cc) % 2)
                bk = pb[bki]
                last = None
                for dc in range(NDC):
                    w = [twc, tXNB]
                    if dc == 0:
                        w.append(bankfree[bki])
                    last = p.on("T", lambda e, half=half, cc=cc, dc=dc, bk=bk, wc=wc: e.matmul(
                        bk[:, 0:512], wc[:, dc * 256 + half * 128: dc * 256 + (half + 1) * 128],
                        xnb[:, dc * SEQ + cc * 512: dc * SEQ + (cc + 1) * 512],
                        start=(dc == 0), stop=(dc == NDC - 1)),
                        waits=w, inc=(sT, 1) if dc == NDC - 1 else None)
                te = p.on("V", lambda e, half=half, cc=cc, bk=bk, dst=dst, bcol=bcol, gtile=gtile: e.tensor_scalar(
                    dst[:, half * SEQ + cc * 512: half * SEQ + (cc + 1) * 512], bk[:, 0:512],
                    bcpp[:, bcol + half:bcol + half + 1], gtile[:, half:half + 1], ALU.add, ALU.mult),
                    waits=[last, tBC, tgq], inc=(sV, 1))
                bankfree[bki] = te
                toks.append(te)
        return toks[-1]

    tQK = cproj(wqc, tWQC, qkb, 0, gexpQ)
    tKK = cproj(wkc, tWKC, kkb, 2, gexp)

    tgrow = []
    for k in range(4):
        tgrow.append(p.on("V", lambda e, k=k: e.tensor_copy(
            scr_row[0:1, k * 64:(k + 1) * 64],
            gat[0:1, 26 + k:27 + k].broadcast_to([1, 64])), waits=[tgk], inc=(sV, 1)))
    tbvc = [p.on("G", lambda e, half=half: e.dma_start(
        out=gat[0:1, 224 + half * 128:224 + (half + 1) * 128],
        in_=bcpp[:, 4 + half:5 + half]), waits=[tBC], dma=True) for half in range(2)]
    tm_a = p.on("T", lambda e: e.matmul(pb[0][:, 0:256], onesPP[0:1, 0:128], scr_row[0:1, 0:256],
                                        start=True, stop=True), waits=tgrow + [bankfree[0]], inc=(sT, 1))
    tgrep = p.on("V", lambda e: e.tensor_copy(grep[:], pb[0][:, 0:256]), waits=[tm_a], inc=(sV, 1))
    bankfree[0] = tgrep
    tm_b = p.on("T", lambda e: e.matmul(pb[1][:, 0:256], onesPP[0:1, 0:128], gat[0:1, 224:480],
                                        start=True, stop=True), waits=tbvc + [bankfree[1]], inc=(sT, 1))
    tbvcrep = p.on("V", lambda e: e.tensor_copy(bvcrep[:], pb[1][:, 0:256]), waits=[tm_b], inc=(sV, 1))
    bankfree[1] = tbvcrep

    tms = p.on("V", lambda e: e.memset(vktm[:, :], 0.0), inc=(sV, 1))
    tones = p.on("V", lambda e: e.tensor_copy(
        vktm[:, :].rearrange("p (a b) -> p a b", b=72)[:, :, 64:65],
        ones_col_b[:, 0:1].broadcast_to([128, 64]).rearrange("p (a b) -> p a b", b=1)),
        waits=[tms], inc=(sV, 1))
    tvk_last = None
    for jt in range(NJT):
        last = None
        for dc in range(NDC):
            w = [tWVC, tXNB]
            if dc == 0:
                w.append(bankfree[2])
            last = p.on("T", lambda e, jt=jt, dc=dc: e.matmul(
                pb[2][:, 0:256], xnb[:, dc * SEQ + jt * 128: dc * SEQ + (jt + 1) * 128],
                wvc[:, dc * 256:(dc + 1) * 256],
                start=(dc == 0), stop=(dc == NDC - 1)),
                waits=w, inc=(sT, 1) if dc == NDC - 1 else None)
        t1 = p.on("V", lambda e, jt=jt: e.tensor_tensor(xscr[:, 0:256], pb[2][:, 0:256], bvcrep[:], ALU.add),
                  waits=[last, tbvcrep], inc=(sV, 1))
        bankfree[2] = t1
        p.on("V", lambda e, jt=jt: e.tensor_tensor(xscr[:, 256:512], xscr[:, 0:256], grep[:], ALU.mult),
             waits=[tgrep, tones], inc=(sV, 1))
        for k in range(4):
            tvk_last = p.on("V", lambda e, jt=jt, k=k: e.tensor_copy(
                vktm[:, jt * 288 + k * 72: jt * 288 + k * 72 + 64],
                xscr[:, 256 + k * 64:256 + (k + 1) * 64]), inc=(sV, 1))
    tVK = tvk_last

    tprev = None
    for src, base, tdep in ((qkb, 0, tQK), (kkb, 2, tKK)):
        for kp in range(2):
            tq = p.on("V", lambda e, kp=kp, src=src: e.tensor_tensor(
                mscr, src[:, kp * SEQ:(kp + 1) * SEQ], src[:, kp * SEQ:(kp + 1) * SEQ], ALU.mult),
                waits=[tdep, tprev, tVK], inc=(sV, 1))
            tqb = p.on("V", lambda e: e.tensor_copy(EABm[:, 0:SEQ], mscr), inc=(sV, 1))
            for cc in range(4):
                tm_a = p.on("T", lambda e, cc=cc: e.matmul(
                    pb[0][0:1, 0:512], ones_col_b[:, :], EABm[:, cc * 512:(cc + 1) * 512],
                    start=True, stop=True), waits=[tqb, bankfree[0]], inc=(sT, 1))
                tprev = p.on("V", lambda e, kp=kp, cc=cc, base=base: e.reduce_max(
                    gat[0:1, 240 + (base + kp) * 4 + cc:240 + (base + kp) * 4 + cc + 1],
                    pb[0][0:1, 0:512], axis=AX.X), waits=[tm_a], inc=(sV, 1))
                bankfree[0] = tprev
    for kp in range(4):
        tprev = p.on("V", lambda e, kp=kp: e.reduce_max(
            gat[0:1, 230 + kp:231 + kp], gat[0:1, 240 + kp * 4:244 + kp * 4], axis=AX.X),
            waits=[tprev], inc=(sV, 1))
    t_ = p.on("V", lambda e: e.tensor_tensor(gat[0:1, 234:236], gat[0:1, 230:232], gat[0:1, 232:234], ALU.mult),
              waits=[tprev], inc=(sV, 1))
    t_ = p.on("S", lambda e: e.activation(gat[0:1, 236:238], gat[0:1, 234:236], AF.Sqrt),
              waits=[t_], inc=(sS, 1))
    t_ = p.on("V", lambda e: e.tensor_scalar_mul(gat[0:1, 234:236], gat[0:1, 236:238], -1.05),
              waits=[t_], inc=(sV, 1))
    tm_a = p.on("T", lambda e: e.matmul(pb[0][:, 0:2], onesPP[0:1, 0:128], gat[0:1, 234:236],
                                        start=True, stop=True), waits=[t_, bankfree[0]], inc=(sT, 1))
    tMNEG = p.on("V", lambda e: e.tensor_copy(mnegM[:], pb[0][:, 0:2]), waits=[tm_a], inc=(sV, 1))
    bankfree[0] = tMNEG

    tok_pvm = {}
    tok_sbandm = {}
    for kp in range(2):
        tok_pvm_prev = tok_pvm
        tok_expm = {}
        tok_pvm = {}
        pending_evacm = None
        for iq in range(NIQ):
            def s_and_expm(j, iq=iq, kp=kp, tok_pvm_prev=tok_pvm_prev):
                w = [tQK, tKK, tMNEG, tok_sbandm.get(0)]
                p.on("T", lambda e, j=j, iq=iq, kp=kp: e.matmul(
                    pb[0][:, 0:IQ], kkb[0:64, kp * SEQ + j * 128: kp * SEQ + (j + 1) * 128],
                    qkb[0:64, kp * SEQ + iq * IQ: kp * SEQ + (iq + 1) * IQ],
                    start=True, stop=True, tile_position=(0, 0)), waits=w)
                tmm = p.on("T", lambda e, j=j, iq=iq, kp=kp: e.matmul(
                    pb[1][:, 0:IQ], kkb[64:128, kp * SEQ + j * 128: kp * SEQ + (j + 1) * 128],
                    qkb[64:128, kp * SEQ + iq * IQ: kp * SEQ + (iq + 1) * IQ],
                    start=True, stop=True, tile_position=(64, 0)), inc=(sT, 1))
                if iq == 0:
                    prevpv = tok_pvm_prev.get((NIQ - 1, j))
                else:
                    prevpv = tok_pvm.get((iq - 1, j))
                te = p.on("S", lambda e, j=j, kp=kp: e.activation(
                    EABm[:, j * 512:(j + 1) * 512].rearrange("p (a b) -> p a b", b=IQ),
                    psS[:, 0:1024].rearrange("p (a b) -> p a b", b=512)[:, :, 0:IQ],
                    AF.Exp, bias=mnegM[:, kp:kp + 1]), waits=[tmm, prevpv], inc=(sS, 1))
                tok_expm[(iq, j)] = te
                tok_sbandm[0] = te

            s_and_expm(0)
            s_and_expm(1)
            if pending_evacm is not None:
                pending_evacm()
                pending_evacm = None
            for j in range(NJT):
                if j + 2 <= NJT - 1:
                    s_and_expm(j + 2)
                tpvm = None
                for hh in range(2):
                    bki = 2 + hh
                    bk = pb[bki]
                    w = [tok_expm[(iq, j)], tVK]
                    if j == 0:
                        w.append(bankfree[bki])
                    tpvm = p.on("T", lambda e, j=j, hh=hh, kp=kp, bk=bk: e.matmul(
                        bk[0:65, 0:IQ],
                        vktm[:, j * 288 + (2 * kp + hh) * 72: j * 288 + (2 * kp + hh) * 72 + 65],
                        EABm[:, j * 512 + hh * IQ: j * 512 + (hh + 1) * IQ],
                        start=(j == 0), stop=(j == NJT - 1)), waits=w, inc=(sT, 1))
                tok_pvm[(iq, j)] = tpvm
            tlast = tok_pvm[(iq, NJT - 1)]
            def mk_evacm(iq=iq, tlast=tlast):
                def f():
                    for hh in range(2):
                        outm = outmA if hh == 0 else outmB
                        eng, sem = ("V", sV) if hh == 0 else ("S", sS)
                        if hh == 0:
                            bankfree[2 + hh] = p.on("V", lambda e, hh=hh, iq=iq, outm=outm: e.tensor_copy(
                                outm[:, iq * IQ:(iq + 1) * IQ], pb[2 + hh][0:65, 0:IQ]),
                                waits=[tlast], inc=(sV, 1))
                        else:
                            bankfree[2 + hh] = p.on("S", lambda e, hh=hh, iq=iq, outm=outm: e.copy(
                                outm[:, iq * IQ:(iq + 1) * IQ], pb[2 + hh][0:65, 0:IQ]),
                                waits=[tlast], inc=(sS, 1))
                return f
            pending_evacm = mk_evacm()
        pending_evacm()
        pending_evacm = None
        for hh in range(2):
            k = 2 * kp + hh
            outm = outmA if hh == 0 else outmB
            p.on("S", lambda e, outm=outm: e.activation(outm[64:65, :], outm[64:65, :], AF.Ln),
                 waits=[bankfree[2 + hh]], inc=(sS, 1))
            tiv = p.on("S", lambda e, outm=outm: e.activation(
                outm[64:65, :], outm[64:65, :], AF.Exp, scale=-1.0), inc=(sS, 1))
            for cc in range(4):
                tm_a = p.on("T", lambda e, cc=cc, outm=outm: e.matmul(
                    pb[6][0:64, 0:512], onesPP[64:65, 0:64], outm[64:65, cc * 512:(cc + 1) * 512],
                    start=True, stop=True), waits=[tiv, bankfree[6]], inc=(sT, 1))
                bankfree[6] = p.on("V", lambda e, cc=cc, k=k, outm=outm: e.tensor_tensor(
                    outc[64 * (k % 2):64 * (k % 2) + 64,
                         (k // 2) * SEQ + cc * 512:(k // 2) * SEQ + (cc + 1) * 512],
                    outm[0:64, cc * 512:(cc + 1) * 512], pb[6][0:64, 0:512], ALU.mult),
                    waits=[tm_a], inc=(sV, 1))
    tOUTC = bankfree[6]

    wo_toks = stream_w5(mwo, 2, pre=tMNEG)
    tmo = [None, None]
    for mt in range(NDC):
        for cc in range(4):
            par = (mt * 4 + cc) % 2
            bki = 4 + par
            bk = pb[bki]
            last = None
            for ch in range(2):
                w = [wo_toks[ch], tOUTC]
                if ch == 0:
                    w += [tmo[par], bankfree[bki]]
                last = p.on("T", lambda e, mt=mt, cc=cc, ch=ch, bk=bk: e.matmul(
                    bk[:, 0:512], mwbf[:, ch * 1024 + mt * 128: ch * 1024 + (mt + 1) * 128],
                    outc[:, ch * SEQ + cc * 512: ch * SEQ + (cc + 1) * 512],
                    start=(ch == 0), stop=(ch == 1)),
                    waits=w, inc=(sT, 1) if ch == 1 else None)
            t1 = p.on("V", lambda e, mt=mt, cc=cc, par=par, bk=bk: e.tensor_scalar(
                mow[par], bk[:, 0:512], vcol("mbo", mt), None, ALU.add),
                waits=[last, tmo[par]], inc=(sV, 1))
            bankfree[bki] = t1
            tmo[par] = p.on("G", lambda e, mt=mt, cc=cc, par=par: e.dma_start(
                out=out_hbm[mt * 128:(mt + 1) * 128, cc * 512:(cc + 1) * 512],
                in_=mow[par], accum_op=ALU.add), waits=[t1], dma=True)

    tap("dist", gat[0:1, 0:64], [tdist, ttk, tgk])
    tap("gall", gat[0:1, 192:208], [tga])
    tap("outc0", outc[:, 0:512], [tOUTC])
    tap("qkb0", qkb[:, 0:512], [tQK])
    tap("xnb0", xnb[:, 0:2048], [tXNB])

    p.on("G", None, waits=p.all_dma_toks())
    p.emit()
    return nc



def _consts_blob(Ws):
    c = np.zeros((128, 2048), dtype=np.float32)
    c[:, 0:128] = 1.0
    ist = np.zeros((128, 64), dtype=np.float32)
    for q in range(128):
        ist[q, q % 64] = 1.0
    c[:, 128:192] = ist
    m64 = np.zeros((128, 256), dtype=np.float32)
    for q in range(128):
        for k in range(4):
            m64[q, k * 64 + (q % 64)] = 1.0
    c[:, 192:448] = m64
    Ws = np.asarray(Ws, dtype=np.float32).reshape(-1)
    w1e = np.zeros((128, 64), dtype=np.float32)
    for mt in range(4):
        for q in range(128):
            w1e[q, mt * 16 + 4 * mt + q // 32] = Ws[q % 32]
    c[:, 448:512] = w1e
    w2e = np.zeros((128, 32), dtype=np.float32)
    for dc in range(4):
        for q in range(128):
            w2e[q, dc * 8 + 2 * dc + q // 64] = Ws[32 + q % 64]
    c[:, 512:544] = w2e
    e8 = np.zeros((8, 512), dtype=np.float32)
    for dc in range(4):
        for q in range(128):
            e8[2 * dc + q // 64, dc * 128 + q] = 1.0
    c[0:8, 544:1056] = e8
    esel = np.zeros((16, 128), dtype=np.float32)
    for h in range(16):
        esel[h, h * 8:(h + 1) * 8] = 1.0
    c[0:16, 1056:1184] = esel
    ecol4 = np.zeros((4, 256), dtype=np.float32)
    for k in range(4):
        ecol4[k, k * 64:(k + 1) * 64] = 1.0
    c[0:4, 1184:1440] = ecol4
    e64 = np.zeros((4, 256), dtype=np.float32)
    for t in range(2):
        for q in range(128):
            e64[2 * t + q // 64, t * 128 + q] = 1.0
    c[0:4, 1440:1696] = e64
    rowsel = np.zeros((16, 1024), dtype=np.float32)
    for col in range(1024):
        cc, q = divmod(col, 128)
        rowsel[2 * cc + q // 64, col] = 1.0
    c[32:48, 544:1568] = rowsel
    return c


def _pack_vecs(ins):
    cols = np.zeros((128, 128), dtype=np.float32)
    off = 0

    def add(arr1d, n):
        nonlocal off
        a = np.zeros((n * 128,), dtype=np.float32)
        v = np.asarray(arr1d, dtype=np.float32).reshape(-1)
        a[:v.size] = v
        cols[:, off:off + n] = a.reshape(n, 128).T
        off += n

    add(ins["L0_attn_bq"], 8)
    add(ins["L0_attn_bk"], 8)
    add(ins["L0_attn_bv"], 4)
    add(ins["L0_attn_bqv"], 4)
    add(ins["L0_attn_bf"], 8)
    add(ins["L0_attn_br1"], 16)
    add(ins["L0_attn_br2"], 8)
    add(ins["L0_attn_g1"], 8)
    add(ins["L0_attn_b1"], 8)
    add(ins["L0_attn_g2"], 8)
    add(ins["L0_attn_b2"], 8)
    add(np.full(16, float(np.asarray(ins["L0_attn_bs"]).reshape(-1)[0])), 1)
    add(ins["L0_ffn_br"], 1)
    add(ins["L0_ffn_bo"], 8)
    return cols


def kernel(**ins):
    x = np.asarray(ins["L0_attn_x"], dtype=np.float32)
    B = x.shape[0]
    consts = _consts_blob(ins["L0_attn_Ws"])
    vecs = _pack_vecs(ins)

    def T(a):
        return np.ascontiguousarray(np.asarray(a, dtype=np.float32).T)

    shared = {
        "wqT": T(ins["L0_attn_Wq"]), "wkT": T(ins["L0_attn_Wk"]),
        "wvT": T(ins["L0_attn_Wv"]), "wqvT": T(ins["L0_attn_Wqv"]),
        "wfT": T(ins["L0_attn_Wf"]),
        "wr1T": T(ins["L0_attn_Wr1"]), "wr2T": T(ins["L0_attn_Wr2"]),
        "mwqT": T(ins["L0_ffn_Wq"]), "mwkT": T(ins["L0_ffn_Wk"]), "mwvT": T(ins["L0_ffn_Wv"]),
        "mwo": np.ascontiguousarray(np.asarray(ins["L0_ffn_Wo"], dtype=np.float32)),
        "mwr": np.ascontiguousarray(np.asarray(ins["L0_ffn_Wr"], dtype=np.float32)),
        "vecs": vecs, "consts": consts,
        "bh3": np.ascontiguousarray(np.stack(
            [np.asarray(ins["L0_ffn_bq"], dtype=np.float32).reshape(16, 64),
             np.asarray(ins["L0_ffn_bk"], dtype=np.float32).reshape(16, 64),
             np.asarray(ins["L0_ffn_bv"], dtype=np.float32).reshape(16, 64)], axis=0
        ).transpose(1, 0, 2).reshape(16, 192)),
    }
    in_maps = []
    for b in range(B):
        m = dict(shared)
        m["xT"] = np.ascontiguousarray(x[b].T)
        in_maps.append(m)

    nc = build_program(taps=tuple(_TAPS), maxphase=getattr(kernel, "_MAXPHASE", 5), hpn=getattr(kernel, "_HPN", 8))
    res = run_bass_kernel_spmd(nc, in_maps, core_ids=list(range(B)))
    outs = np.stack([np.asarray(r["out"]) for r in res.results], axis=0)
    if _TAPS:
        kernel.tap_results = [{k: np.asarray(v) for k, v in r.items() if k.startswith("tap_")}
                              for r in res.results]
    return np.ascontiguousarray(outs.transpose(0, 2, 1))


# version 20 (final)
# speedup vs baseline: 1.2553x; 1.0118x over previous
import itertools
import numpy as np
from contextlib import ExitStack

import concourse.bass as bass
import concourse.mybir as mybir
from concourse.bass_utils import run_bass_kernel_spmd

F32 = mybir.dt.float32
BF16 = mybir.dt.bfloat16
U32 = mybir.dt.uint32
AF = mybir.ActivationFunctionType
ALU = mybir.AluOpType
AX = mybir.AxisListType

D = 1024
SEQ = 2048
NDC = 8
NJT = 16
IQ = 256
NIQ = SEQ // IQ
EPS = 1e-5
DFF = 2048
ISQ8 = 0.125
ISQ32 = float(1.0 / np.sqrt(32.0))
WIN = 512
NW = SEQ // WIN
ARENA_KB = 194

_TAPS = []
_MAXPHASE = 5
_HPN = 8


class Prog:
    ENGINES = ("Y", "G", "T", "V", "S")

    def __init__(self, nc):
        self.nc = nc
        self.steps = {e: [] for e in self.ENGINES}
        self.semc = {}
        self.chain_sems = {}
        self.es = ExitStack()
        self._uid = 0

    def _nm(self, p):
        self._uid += 1
        return "%s%d" % (p, self._uid)

    def sem(self, name=None):
        h = self.es.enter_context(self.nc.semaphore(name=name or self._nm("sm")))
        self.semc[h] = 0
        return h

    def sb(self, name, shape, dt):
        return self.es.enter_context(self.nc.sbuf_tensor(name, shape, dt))

    def psum(self, name, shape):
        return self.es.enter_context(self.nc.psum_tensor(name, shape, F32))

    def init_dma_pool(self, n=12):
        self.dma_pools = {q: [self.sem("dma%s%d" % (q, i)) for i in range(n if q != "S" else 4)]
                          for q in ("Y", "G", "S")}
        self.dma_prev = {q: [None] * len(self.dma_pools[q]) for q in ("Y", "G", "S")}
        self.dma_rr = {q: 0 for q in ("Y", "G", "S")}
        self.dma_pool = self.dma_pools["Y"] + self.dma_pools["G"] + self.dma_pools["S"]

    def all_dma_toks(self):
        return [self.tok(s) for s in self.dma_pool]

    def on(self, eng, fn, waits=(), inc=None, dma=False):
        w = []
        if fn is not None and eng in self.chain_sems:
            cs = self.chain_sems[eng]
            w.append((cs, self.semc[cs]))
            if inc is None and not dma:
                inc = (cs, 1)
        def _flat(ts):
            for t in ts:
                if t is None:
                    continue
                if isinstance(t, list):
                    _flat(t)
                else:
                    w.append(t)
        _flat(waits)
        if dma:
            pool = self.dma_pools[eng]
            i = self.dma_rr[eng] % len(pool)
            self.dma_rr[eng] += 1
            if self.dma_prev[eng][i] is not None:
                w.append(self.dma_prev[eng][i])
            inc = (pool[i], 16)
        self.steps[eng].append((w, fn, inc))
        if inc is not None:
            s, k = inc
            self.semc[s] += k
            tok = (s, self.semc[s])
            if dma:
                self.dma_prev[eng][i] = tok
            return tok
        return None

    def tok(self, sem):
        return (sem, self.semc[sem])

    def barrier(self, name, dma_sems=()):
        bar = self.sem("bar_" + name)
        dw = [self.tok(s) for s in dma_sems]
        for e in self.ENGINES:
            self.on(e, lambda eng: eng.nop(), waits=dw, inc=(bar, 1))
        t = self.tok(bar)
        for e in self.ENGINES:
            self.on(e, None, waits=[t])
        return t

    def emit(self):
        nc = self.nc
        with nc.Block() as block:
            def run(eng_obj, name):
                seen = {}
                for w, fn, inc in self.steps[name]:
                    for s, v in w:
                        if seen.get(s, -1) < v:
                            eng_obj.wait_ge(s, v)
                            seen[s] = v
                    if fn is not None:
                        ins = fn(eng_obj)
                        if inc is not None:
                            if ins is None:
                                eng_obj.nop().then_inc(*inc)
                            else:
                                ins.then_inc(*inc)
                    elif inc is not None:
                        eng_obj.nop().then_inc(*inc)

            @block.sync
            def _(e):
                run(e, "Y")

            @block.gpsimd
            def _(e):
                run(e, "G")

            @block.tensor
            def _(e):
                run(e, "T")

            @block.vector
            def _(e):
                run(e, "V")

            @block.scalar
            def _(e):
                run(e, "S")


def build_program(taps=(), maxphase=5, hpn=8):
    nc = bass.Bass("TRN2", target_bir_lowering=False)
    p = Prog(nc)

    def din(name, shape, dt=F32):
        return nc.dram_tensor(name, shape, dt, kind="ExternalInput").ap()

    xT = din("xT", [D, SEQ])
    wqT = din("wqT", [D, D]); wkT = din("wkT", [D, D])
    wvT = din("wvT", [D, 512]); wqvT = din("wqvT", [D, 512])
    wfT = din("wfT", [D, D])
    wr1T = din("wr1T", [D, DFF]); wr2T = din("wr2T", [DFF, D])
    mwqT = din("mwqT", [D, D]); mwkT = din("mwkT", [D, D]); mwvT = din("mwvT", [D, D])
    mwo = din("mwo", [256, D]); mwr = din("mwr", [D, 16])
    vecs = din("vecs", [128, 128])
    consts = din("consts", [128, 2048])
    bh3 = din("bh3", [16, 192])
    out_hbm = nc.dram_tensor("out", [D, SEQ], F32, kind="ExternalOutput").ap()

    taps = set(taps)

    p.init_dma_pool(16)
    sG = p.sem("sG")
    sT = p.sem("sT"); sV = p.sem("sV"); sS = p.sem("sS")
    p.chain_sems = {"V": sV, "S": sS, "G": sG}

    psS = p.psum("psS", [128, 1024])
    pb = [psS[:, 0:512], psS[:, 512:1024]] + [p.psum("pb%d" % i, [128, 512]) for i in range(2, 8)]
    bankfree = {i: None for i in range(8)}

    arena = p.sb("arena", [128, ARENA_KB * 512], BF16)

    _audit = []

    def carve(kb, shape, dt):
        cols = shape[-1] * (2 if dt == F32 else 1)
        start = int(kb * 512)
        _audit.append((float(kb), float(kb) + cols / 512.0, tuple(shape), "f32" if dt == F32 else "bf16"))
        assert start + cols <= ARENA_KB * 512, ("carve OOB", kb, shape)
        v = arena[0:shape[0], start:start + cols]
        if dt == F32:
            v = v.bitcast(F32)
        return v
    build_program.audit = _audit

    vecs_t = p.sb("vecs_t", [128, 128], F32)
    onesPP = p.sb("onesPP", [128, 128], F32)
    ones_col_f = p.sb("ones_col_f", [128, 1], F32)
    ones_col_b = p.sb("ones_col_b", [128, 1], BF16)
    E8b = p.sb("E8b", [72, 512], BF16)
    ISTb = p.sb("ISTb", [128, 64], BF16)
    Eselb = p.sb("Eselb", [16, 128], BF16)
    Ecol4b = p.sb("Ecol4b", [4, 256], BF16)
    RowSelb = p.sb("RowSelb", [16, 1024], BF16)
    M64b = p.sb("M64b", [128, 256], BF16)
    E64f = p.sb("E64f", [4, 256], F32)
    w1b = p.sb("w1b", [128, 64], BF16)
    w2b = p.sb("w2b", [128, 32], BF16)
    negM = p.sb("negM", [128, 8], F32)
    nrm = p.sb("nrm", [1, 96], F32)
    bqs = p.sb("bqs", [128, 8], F32)
    bqvs = p.sb("bqvs", [128, 4], F32)
    scr_row = p.sb("scr_row", [33, 512], F32)
    ones8b = p.sb("ones8b", [72, 8], BF16)
    H4T = p.sb("H4T", [4, 16], BF16)
    H4E = p.sb("H4E", [16, 256], BF16)
    gat = p.sb("gat", [16, 512], F32)
    gatb = p.sb("gatb", [4, 16], BF16)
    mi8 = p.sb("mi8", [1, 8], U32)
    gexp = p.sb("gexp", [128, 2], F32)
    gexpQ = p.sb("gexpQ", [128, 2], F32)
    bcpp = p.sb("bcpp", [128, 6], F32)
    bh3t = p.sb("bh3t", [16, 192], F32)
    mnegM = p.sb("mnegM", [128, 2], F32)
    wrf = p.sb("wrf", [128, 16 * NDC], F32)

    _sizes = [("bq", 8), ("bk", 8), ("bv", 4), ("bqv", 4), ("bf", 8), ("br1", 16),
              ("br2", 8), ("g1", 8), ("b1", 8), ("g2", 8), ("b2", 8), ("bs", 1),
              ("mbr", 1), ("mbo", 8)]
    VC = {}
    _off = 0
    for _n, _k in _sizes:
        VC[_n] = (_off, _k)
        _off += _k

    def vcol(name, i=0):
        o, n = VC[name]
        assert i < n
        return vecs_t[:, o + i:o + i + 1]

    def tap(name, ap, waits):
        if name not in taps:
            return
        if not isinstance(ap, bass.AP):
            ap = ap[:, :]
        t = nc.dram_tensor("tap_" + name, [ap.shape[0], ap.shape[1]], ap.dtype,
                           kind="ExternalOutput").ap()
        p.on("G", lambda e, t=t, ap=ap: e.dma_start(out=t[:, :], in_=ap), waits=waits, dma=True)

    cstage = carve(0, [128, 2048], F32)
    t_vec = p.on("Y", lambda e: e.dma_start(out=vecs_t[:], in_=vecs[:, :]), dma=True)
    t_cst = p.on("Y", lambda e: e.dma_start(out=cstage, in_=consts[:, :]), dma=True)
    t_bh3 = p.on("Y", lambda e: e.dma_start(out=bh3t[:], in_=bh3[:, :]), dma=True)

    def vc_cast(dst, src):
        return p.on("V", lambda e, d=dst, s=src: e.tensor_copy(d, s), waits=[t_cst], inc=(sV, 1))

    vc_cast(onesPP[:, :], cstage[:, 0:128])
    vc_cast(ones_col_f[:, :], cstage[:, 0:1])
    vc_cast(ones_col_b[:, :], cstage[:, 0:1])
    vc_cast(ones8b[64:72, :], cstage[0:8, 0:8])
    vc_cast(ISTb[:, :], cstage[:, 128:192])
    vc_cast(M64b[:, :], cstage[:, 192:448])
    vc_cast(w1b[:, :], cstage[:, 448:512])
    vc_cast(w2b[:, :], cstage[:, 512:544])
    vc_cast(E8b[64:72, :], cstage[0:8, 544:1056])
    vc_cast(E8b[32:40, :], cstage[0:8, 544:1056])
    vc_cast(Eselb[:, :], cstage[0:16, 1056:1184])
    vc_cast(Ecol4b[:, :], cstage[0:4, 1184:1440])
    vc_cast(E64f[:, :], cstage[0:4, 1440:1696])
    tCONST = vc_cast(RowSelb[:, :], cstage[32:48, 544:1568])
    tb1 = p.on("G", lambda e: e.tensor_scalar_mul(bqs[:], vecs_t[:, VC["bq"][0]:VC["bq"][0] + 8], ISQ8),
               waits=[t_vec], inc=(sG, 1))
    tb2 = p.on("G", lambda e: e.tensor_scalar_mul(bqvs[:], vecs_t[:, VC["bqv"][0]:VC["bqv"][0] + 4], ISQ32),
               waits=[t_vec], inc=(sG, 1))

    xf = carve(8, [128, NDC * SEQ], F32)
    x2s = carve(72, [128, SEQ], F32)
    xs1 = carve(80, [128, SEQ], F32)
    xs2 = carve(88, [128, SEQ], F32)
    tmpr = carve(96, [1, SEQ], F32)
    tmp2r = carve(104, [1, SEQ], F32)
    r2row = carve(112, [1, SEQ], F32)
    r2rep = carve(120, [128, SEQ], F32)
    m_rep = carve(128, [128, SEQ], F32)
    r_rep = carve(136, [128, SEQ], F32)
    xn2b = carve(144, [128, NDC * SEQ], BF16)
    statrows = carve(186, [65, SEQ], F32)

    tk_x = [p.on("Y", lambda e, c=c: e.dma_start(out=xf[:, c * SEQ:(c + 1) * SEQ],
                                                 in_=xT[c * 128:(c + 1) * 128, :]), dma=True)
            for c in range(NDC)]

    prev_sq = None
    for c in range(NDC):
        t_sq = p.on("V", lambda e, c=c: e.tensor_tensor(
            x2s, xf[:, c * SEQ:(c + 1) * SEQ], xf[:, c * SEQ:(c + 1) * SEQ], ALU.mult),
            waits=[tk_x[c], prev_sq], inc=(sV, 1))
        for cc in range(4):
            p.on("T", lambda e, c=c, cc=cc: e.matmul(
                pb[cc][0:1, 0:512], ones_col_f[:, :],
                xf[:, c * SEQ + cc * 512: c * SEQ + (cc + 1) * 512],
                start=(c == 0), stop=(c == NDC - 1)), waits=[tk_x[c], tCONST])
            prev_sq = p.on("T", lambda e, c=c, cc=cc: e.matmul(
                pb[4 + cc][0:1, 0:512], ones_col_f[:, :], x2s[:, cc * 512:(cc + 1) * 512],
                start=(c == 0), stop=(c == NDC - 1)), waits=[t_sq], inc=(sT, 1))
    tSTAT = prev_sq

    tv = None
    for cc in range(4):
        p.on("V", lambda e, cc=cc: e.tensor_scalar_mul(
            statrows[0:1, cc * 512:(cc + 1) * 512], pb[cc][0:1, 0:512], 1.0 / D),
            waits=[tSTAT], inc=(sV, 1))
        tv = p.on("V", lambda e, cc=cc: e.tensor_scalar_mul(
            tmpr[0:1, cc * 512:(cc + 1) * 512], pb[4 + cc][0:1, 0:512], 1.0 / D),
            waits=[tSTAT], inc=(sV, 1))
    tv = p.on("V", lambda e: e.tensor_tensor(tmp2r, statrows[0:1, :], statrows[0:1, :], ALU.mult),
              waits=[tv], inc=(sV, 1))
    tv = p.on("V", lambda e: e.tensor_tensor(tmpr, tmpr, tmp2r, ALU.subtract), inc=(sV, 1))
    tv = p.on("V", lambda e: e.tensor_scalar_add(tmpr, tmpr, EPS), inc=(sV, 1))
    p.on("S", lambda e: e.activation(tmpr, tmpr, AF.Ln), waits=[tv], inc=(sS, 1))
    tsq_ = p.on("S", lambda e: e.activation(statrows[64:65, :], tmpr, AF.Exp, scale=-0.5),
                inc=(sS, 1))
    tv = p.on("V", lambda e: e.tensor_tensor(tmpr, statrows[64:65, :], statrows[64:65, :], ALU.mult),
              waits=[tsq_], inc=(sV, 1))
    tv = p.on("V", lambda e: e.tensor_scalar(tmpr, tmpr, -EPS, 1.0 + EPS, ALU.mult, ALU.add), inc=(sV, 1))
    p.on("S", lambda e: e.activation(tmpr, tmpr, AF.Ln), waits=[tv], inc=(sS, 1))
    tr2 = p.on("S", lambda e: e.activation(r2row, tmpr, AF.Exp, scale=-0.5), inc=(sS, 1))

    def replicate(dst, src_row, base_part, waits, bank, width=SEQ):
        last = None
        ncc = width // 512
        for cc in range(ncc):
            w = list(waits) + [bankfree[bank], last]
            tm = p.on("T", lambda e, cc=cc, dst=dst, src_row=src_row, base_part=base_part, bank=bank: e.matmul(
                pb[bank][:, 0:512], onesPP[base_part:base_part + 1, 0:128],
                src_row[0:1, cc * 512:(cc + 1) * 512], start=True, stop=True),
                waits=w, inc=(sT, 1))
            last = p.on("V", lambda e, cc=cc, dst=dst, bank=bank: e.tensor_copy(
                dst[:, cc * 512:(cc + 1) * 512], pb[bank][:, 0:512]), waits=[tm], inc=(sV, 1))
        bankfree[bank] = last
        return last

    tmr = replicate(m_rep, statrows[0:1, :], 0, [tSTAT, p.tok(sV)], 0)
    trr = replicate(r_rep, statrows[64:65, :], 64, [tsq_], 1)
    tr2r = replicate(r2rep, r2row, 0, [tr2], 2)

    tXN2 = None
    for c in range(NDC):
        ta = p.on("V", lambda e, c=c: e.tensor_tensor(
            xs1, xf[:, c * SEQ:(c + 1) * SEQ], m_rep, ALU.subtract),
            waits=[tk_x[c], tmr, trr, tr2r, tXN2], inc=(sV, 1))
        p.on("V", lambda e, c=c: e.tensor_tensor(xs2, xs1, r_rep, ALU.mult), inc=(sV, 1))
        tc2 = p.on("V", lambda e, c=c: e.tensor_tensor(xs1, xs2, r2rep, ALU.mult), inc=(sV, 1))
        tXN2 = p.on("S", lambda e, c=c: e.activation(
            xn2b[:, c * SEQ:(c + 1) * SEQ], xs1, AF.Identity,
            bias=vcol("b1", c), scale=vcol("g1", c)), waits=[tc2, t_vec], inc=(sS, 1))
    tap("ISTb", ISTb, [tCONST])
    tap("E8b", E8b[64:72, :], [tCONST])
    tap("xf0", xf[:, 0:2048], [tk_x[0], tXN2])
    tap("ones_col", ones_col_f[:, :], [tCONST])
    tap("xn2b0", xn2b[:, 0:2048], [tXN2])
    tap("statm", statrows[0:1, :], [tsq_])
    tap("statr", statrows[64:65, :], [tsq_])

    p.barrier("p1", dma_sems=p.dma_pool)

    Qb = carve(0, [128, NDC * SEQ], BF16)
    Kb = carve(32, [128, NDC * SEQ], BF16)
    Vtm = carve(64, [128, NJT * 512], BF16)
    wstage = carve(80, [128, 2048], F32)
    wbf = carve(88, [128, 8192], BF16)
    wbf2 = carve(104, [128, 8192], BF16)
    qvb = carve(120, [128, 4 * SEQ], BF16)
    bvrow = carve(136, [1, 512], F32)
    bvrep = carve(138, [128, 512], F32)
    sTt = carve(140, [16, SEQ], BF16)

    wslot_free = [None, None]

    def stream_w(hbm, nchunks, width, dst, stage, pre=None, eng="G"):
        toks = []
        sem = {"G": sG, "S": sS, "V": sV}[eng]
        for c in range(nchunks):
            s = c % 2
            w = [wslot_free[s]]
            if pre is not None and c < 2:
                w.append(pre)
            tl = p.on("Y", lambda e, c=c, s=s, hbm=hbm, width=width, stage=stage: e.dma_start(
                out=stage[:, s * 1024:s * 1024 + width], in_=hbm[c * 128:(c + 1) * 128, :]),
                waits=w, dma=True)
            if eng == "S":
                tc_ = p.on("S", lambda e, c=c, s=s, dst=dst, width=width, stage=stage: e.copy(
                    dst[:, c * width:(c + 1) * width], stage[:, s * 1024:s * 1024 + width]),
                    waits=[tl], inc=(sem, 1))
            else:
                tc_ = p.on(eng, lambda e, c=c, s=s, dst=dst, width=width, stage=stage: e.tensor_copy(
                    dst[:, c * width:(c + 1) * width], stage[:, s * 1024:s * 1024 + width]),
                    waits=[tl], inc=(sem, 1))
            wslot_free[s] = tc_
            toks.append(tc_)
        return toks

    def fm_proj(wtoks, wtile, wwidth, nmt, evac):
        out_toks = []
        bank_i = 0
        for mt in range(nmt):
            for cc in range(4):
                bki = 4 + (bank_i % 4)
                bk = pb[bki]
                last = None
                for dc in range(NDC):
                    w = [wtoks[dc], tXN2]
                    if dc == 0:
                        w.append(bankfree[bki])
                    last = p.on("T", lambda e, mt=mt, cc=cc, dc=dc, bk=bk, wtile=wtile, wwidth=wwidth: e.matmul(
                        bk[:, 0:512],
                        wtile[:, dc * wwidth + mt * 128: dc * wwidth + (mt + 1) * 128],
                        xn2b[:, dc * SEQ + cc * 512: dc * SEQ + (cc + 1) * 512],
                        start=(dc == 0), stop=(dc == NDC - 1)),
                        waits=w, inc=(sT, 1) if dc == NDC - 1 else None)
                te = evac(mt, cc, bk, last)
                bankfree[bki] = te
                out_toks.append(te)
                bank_i += 1
        return out_toks

    wq_toks = stream_w(wqT, 8, 1024, wbf, wstage, eng="S")
    tQ = fm_proj(wq_toks, wbf, 1024, 8, lambda mt, cc, bk, tk: p.on(
        "V", lambda e, mt=mt, cc=cc, bk=bk: e.tensor_scalar(
            Qb[:, mt * SEQ + cc * 512: mt * SEQ + (cc + 1) * 512], bk[:, 0:512],
            ISQ8, bqs[:, mt:mt + 1], ALU.mult, ALU.add),
        waits=[tk, tb1], inc=(sV, 1)))

    wk_toks = stream_w(wkT, 8, 1024, wbf2, wstage)
    tK = fm_proj(wk_toks, wbf2, 1024, 8, lambda mt, cc, bk, tk: p.on(
        "V", lambda e, mt=mt, cc=cc, bk=bk: e.tensor_scalar(
            Kb[:, mt * SEQ + cc * 512: mt * SEQ + (cc + 1) * 512], bk[:, 0:512],
            vcol("bk", mt), None, ALU.add),
        waits=[tk], inc=(sV, 1)))

    tbv = [p.on("G", lambda e, c=c: e.dma_start(
        out=bvrow[0:1, c * 128:(c + 1) * 128],
        in_=vecs_t[:, VC["bv"][0] + c:VC["bv"][0] + c + 1]), waits=[t_vec], dma=True)
        for c in range(4)]
    tm_ = p.on("T", lambda e: e.matmul(pb[0][:, 0:512], onesPP[0:1, 0:128], bvrow[0:1, :],
                                       start=True, stop=True), waits=tbv + [bankfree[0]], inc=(sT, 1))
    tbvrep = p.on("V", lambda e: e.tensor_copy(bvrep, pb[0][:, 0:512]), waits=[tm_], inc=(sV, 1))
    bankfree[0] = tbvrep

    wv_toks = stream_w(wvT, 8, 512, wbf, wstage, pre=tQ[-1], eng="S")
    tV = None
    for jt in range(NJT):
        bki = 2 + (jt % 2)
        bk = pb[bki]
        last = None
        for dc in range(NDC):
            w = [wv_toks[dc], tXN2]
            if dc == 0:
                w.append(bankfree[bki])
            last = p.on("T", lambda e, jt=jt, dc=dc, bk=bk: e.matmul(
                bk[:, 0:512],
                xn2b[:, dc * SEQ + jt * 128: dc * SEQ + (jt + 1) * 128],
                wbf[:, dc * 512:(dc + 1) * 512],
                start=(dc == 0), stop=(dc == NDC - 1)),
                waits=w, inc=(sT, 1) if dc == NDC - 1 else None)
        tV = p.on("V", lambda e, jt=jt, bk=bk: e.tensor_tensor(
            Vtm[:, jt * 512:(jt + 1) * 512], bk[:, 0:512], bvrep, ALU.add),
            waits=[last, tbvrep], inc=(sV, 1))
        bankfree[bki] = tV

    wqv_toks = stream_w(wqvT, 8, 512, wbf2, wstage, pre=tK[-1])
    tQV = fm_proj(wqv_toks, wbf2, 512, 4, lambda mt, cc, bk, tk: p.on(
        "V", lambda e, mt=mt, cc=cc, bk=bk: e.tensor_scalar(
            qvb[:, mt * SEQ + cc * 512: mt * SEQ + (cc + 1) * 512], bk[:, 0:512],
            ISQ32, bqvs[:, mt:mt + 1], ALU.mult, ALU.add),
        waits=[tk, tb2], inc=(sV, 1)))

    tS_T = None
    for cc2 in range(4):
        last = None
        for mt in range(4):
            w = [tQV[-1], tCONST]
            if mt == 0:
                w += [bankfree[1], tS_T]
            last = p.on("T", lambda e, cc2=cc2, mt=mt: e.matmul(
                pb[1][0:16, 0:512], w1b[:, mt * 16:(mt + 1) * 16],
                qvb[:, mt * SEQ + cc2 * 512: mt * SEQ + (cc2 + 1) * 512],
                start=(mt == 0), stop=(mt == 3)),
                waits=w, inc=(sT, 1) if mt == 3 else None)
        tS_T = p.on("V", lambda e, cc2=cc2: e.tensor_scalar(
            sTt[:, cc2 * 512:(cc2 + 1) * 512], pb[1][0:16, 0:512],
            vcol("bs", 0)[0:16, :], None, ALU.add), waits=[last], inc=(sV, 1))
        bankfree[1] = tS_T

    sqs = wbf[:, 0:2048]
    tprev = None
    for src, base, tdep in ((Qb, 0, tQ[-1]), (Kb, 32, tK[-1])):
        for hp in range(8):
            tq = p.on("V", lambda e, hp=hp, src=src: e.tensor_tensor(
                sqs, src[:, hp * SEQ:(hp + 1) * SEQ], src[:, hp * SEQ:(hp + 1) * SEQ], ALU.mult),
                waits=[tdep, tV, tprev], inc=(sV, 1))
            for cc in range(4):
                tm2 = p.on("T", lambda e, cc=cc: e.matmul(
                    pb[0][0:1, 0:512], ones_col_b[:, :], sqs[:, cc * 512:(cc + 1) * 512],
                    start=True, stop=True), waits=[tq, bankfree[0]], inc=(sT, 1))
                tprev = p.on("V", lambda e, hp=hp, cc=cc, base=base: e.reduce_max(
                    nrm[0:1, base + hp * 4 + cc: base + hp * 4 + cc + 1],
                    pb[0][0:1, 0:512], axis=AX.X), waits=[tm2], inc=(sV, 1))
                bankfree[0] = tprev
    tlastn = tprev
    for hp in range(8):
        p.on("V", lambda e, hp=hp: e.reduce_max(
            nrm[0:1, 64 + hp:64 + hp + 1], nrm[0:1, hp * 4:hp * 4 + 4], axis=AX.X),
            waits=[tlastn], inc=(sV, 1))
        tprev = p.on("V", lambda e, hp=hp: e.reduce_max(
            nrm[0:1, 72 + hp:72 + hp + 1], nrm[0:1, 32 + hp * 4:32 + hp * 4 + 4], axis=AX.X),
            inc=(sV, 1))
    t_ = p.on("V", lambda e: e.tensor_tensor(nrm[0:1, 80:88], nrm[0:1, 64:72], nrm[0:1, 72:80], ALU.mult),
              waits=[tprev], inc=(sV, 1))
    t_ = p.on("S", lambda e: e.activation(nrm[0:1, 88:96], nrm[0:1, 80:88], AF.Sqrt),
              waits=[t_], inc=(sS, 1))
    t_ = p.on("V", lambda e: e.tensor_scalar_mul(nrm[0:1, 80:88], nrm[0:1, 88:96], -1.05),
              waits=[t_], inc=(sV, 1))
    tm2 = p.on("T", lambda e: e.matmul(pb[0][:, 0:8], onesPP[0:1, 0:128], nrm[0:1, 80:88],
                                       start=True, stop=True), waits=[t_, bankfree[0]], inc=(sT, 1))
    tNEGM = p.on("V", lambda e: e.tensor_copy(negM[:], pb[0][:, 0:8]), waits=[tm2], inc=(sV, 1))
    bankfree[0] = tNEGM

    tap("Qb0", Qb[:, 0:2048], [tQ[-1]])
    tap("Kb0", Kb[:, 0:2048], [tK[-1]])
    tap("Vtm0", Vtm[:, 0:512], [tV])
    tap("sTt", sTt, [tS_T])
    tap("negM", negM, [tNEGM])

    p.barrier("p2", dma_sems=p.dma_pool)

    if maxphase <= 2:
        p.on("G", None, waits=p.all_dma_toks())
        p.emit()
        return nc

    EAB = carve(80, [128, NJT * 512], BF16)
    outTA = carve(96, [128, 4 * SEQ], BF16)
    outTB = carve(112, [128, 4 * SEQ], BF16)
    prodb = [carve(132, [128, 512], BF16), carve(133, [128, 512], BF16)]
    def carveR(row, kb, shape, dt):
        cols = shape[-1] * (2 if dt == F32 else 1)
        start = int(kb * 512)
        v = arena[row:row + shape[0], start:start + cols]
        return v.bitcast(F32) if dt == F32 else v
    _rowp = {0: 64, 1: 32}
    cS_m = {(hh, par): carveR(_rowp[par], 134 + 2 * hh, [8, 512], F32)
            for hh in (0, 1) for par in (0, 1)}
    ivd_m = {(0, 0): carveR(64, 144, [8, 512], F32), (1, 0): carveR(64, 146, [8, 512], F32),
             (0, 1): carveR(32, 128, [8, 512], F32), (1, 1): carveR(32, 130, [8, 512], F32)}
    ivr_m = {(0, 0): carveR(64, 148, [8, 512], F32), (1, 0): carveR(64, 150, [8, 512], F32),
             (0, 1): carveR(32, 140, [8, 512], F32), (1, 1): carveR(32, 142, [8, 512], F32)}
    expcB_m = {(0, 0): carveR(64, 184, [8, 512], BF16), (1, 0): carveR(64, 185, [8, 512], BF16),
               (0, 1): carveR(64, 130, [8, 512], BF16), (1, 1): carveR(64, 131, [8, 512], BF16)}
    compfs_m = {(0, 0): carveR(64, 138, [8, 512], BF16), (1, 0): carveR(64, 139, [8, 512], BF16),
                (0, 1): carveR(32, 138, [8, 512], BF16), (1, 1): carveR(32, 139, [8, 512], BF16),
                (0, 2): carveR(64, 128, [8, 512], BF16), (1, 2): carveR(64, 129, [8, 512], BF16),
                (0, 3): carveR(32, 184, [8, 512], BF16), (1, 3): carveR(32, 185, [8, 512], BF16)}
    denAB = carve(144, [33, SEQ], F32)
    finall = carve(152, [128, NDC * SEQ], BF16)

    wfb = carve(0, [128, 8192], BF16)
    wstage4 = carve(16, [128, 2048], F32)
    wf_toks = None

    tok_pv = {}
    tok_sband = {}
    tFIN = None
    gate_out = {}
    gate_den = {}
    epi = {"bkr": {7: None, 0: None, 3: None, 1: None}, "bkfinr": {4: None, 5: None},
           "crepfree": {0: [None, None], 1: [None, None]},
           "prodfree": {0: None, 1: None}, "foldfree": {0: None, 1: None}, "tfin": None}

    def emit_epilogue(hp, evdc, denv):
        epi["foldfree"][0] = denv
        epi["foldfree"][1] = evdc[0]

        def comp_chain(hh, cc2, bank):
            head = 2 * hp + hh
            par = cc2 % 2
            outT = outTA if hh == 0 else outTB
            sl = slice(cc2 * 512, (cc2 + 1) * 512)
            ivd = ivd_m[(hh, par)]
            ivr = ivr_m[(hh, par)]
            cS = cS_m[(hh, par)]
            expcB = expcB_m[(hh, par)]
            compfs = compfs_m[(hh, cc2)]
            st = {}
            bk = pb[bank]

            def s0():
                last = None
                for dc in range(4):
                    w = [evdc[dc]]
                    if dc == 0:
                        w.append(epi["bkr"][bank])
                        if bank == 3:
                            w.append(evdc[1])
                    last = p.on("T", lambda e, dc=dc: e.matmul(
                        bk[0:8, 0:512], w2b[:, dc * 8:(dc + 1) * 8],
                        outT[:, dc * SEQ + sl.start: dc * SEQ + sl.stop],
                        start=(dc == 0), stop=(dc == 3)),
                        waits=w, inc=(sT, 1) if dc == 3 else None)
                st["c"] = last

            def s1():
                st["t1"] = p.on("V", lambda e: e.tensor_copy(cS, bk[0:8, 0:512]),
                                waits=[st["c"]], inc=(sV, 1))

            def s2():
                st["tsr"] = p.on("T", lambda e: e.matmul(
                    bk[32:40, 0:512], Eselb[:, head * 8:(head + 1) * 8], sTt[:, sl],
                    start=True, stop=True), waits=[tS_T, st["t1"]], inc=(sT, 1))
                st["tir"] = p.on("T", lambda e: e.matmul(
                    bk[64:72, 0:512], onesPP[32 * hh:32 * hh + 1, 0:8],
                    denAB[32 * hh:32 * hh + 1, sl],
                    start=True, stop=True), waits=[denv], inc=(sT, 1))
                gate_den.setdefault(cc2, [None, None])[hh] = st["tir"]

            def s2b():
                p.on("S", lambda e: e.activation(ivd, bk[64:72, 0:512], AF.Ln),
                     waits=[st["tir"]], inc=(sS, 1))
                st["trec"] = p.on("S", lambda e: e.activation(ivd, ivd, AF.Exp, scale=-1.0),
                                  inc=(sS, 1))

            def s3():
                p.on("V", lambda e: e.tensor_tensor(cS, cS, ivd, ALU.mult),
                     waits=[st["trec"]], inc=(sV, 1))
                st["t3"] = p.on("V", lambda e: e.tensor_tensor(
                    cS, cS, bk[32:40, 0:512], ALU.add), waits=[st["tsr"]], inc=(sV, 1))

            def s4():
                st["t4"] = p.on("S", lambda e: e.activation(cS, cS, AF.Exp),
                                waits=[st["t3"]], inc=(sS, 1))
                st["t4b"] = p.on("S", lambda e: e.copy(expcB, cS),
                                 waits=[st["t4"]], inc=(sS, 1))

            def s5():
                st["t5"] = p.on("T", lambda e: e.matmul(
                    bk[32:40, 0:512], ones8b[64:72, :], expcB,
                    start=True, stop=True), waits=[st["t4b"], st["t3"]], inc=(sT, 1))

            def s6():
                p.on("S", lambda e: e.activation(ivr, bk[32:40, 0:512], AF.Ln),
                     waits=[st["t5"]], inc=(sS, 1))
                st["trecr"] = p.on("S", lambda e: e.activation(ivr, ivr, AF.Exp, scale=-1.0),
                                   inc=(sS, 1))

            def s7():
                st["tc1"] = p.on("G", lambda e: e.tensor_tensor(
                    cS, cS, ivd, ALU.mult),
                    waits=[st["t4b"]], inc=(sG, 1))

            def s8():
                st["tcf"] = p.on("G", lambda e: e.tensor_tensor(
                    compfs[:, :], cS, ivr, ALU.mult),
                    waits=[st["trecr"], st["tc1"]], inc=(sG, 1))
                epi["bkr"][bank] = st["tcf"]

            return [s0, s1, s2, s2b, s3, s4, s5, s6, s7, s8], st

        def fin_chain(hh, cc2, st):
            outT = outTA if hh == 0 else outTB
            sl = slice(cc2 * 512, (cc2 + 1) * 512)
            compfs = compfs_m[(hh, cc2)]
            e8r = _rowp[cc2 % 2]
            rbank = 4 + hh
            steps = []
            tprs = []

            def mk_rep_prod(dc):
                def f():
                    w = [st["tcf"], epi["bkfinr"][rbank]]
                    trep = p.on("T", lambda e: e.matmul(
                        pb[rbank][:, 0:512], E8b[e8r:e8r + 8, dc * 128:(dc + 1) * 128], compfs[:, :],
                        start=True, stop=True), waits=w, inc=(sT, 1))
                    pbuf = prodb[hh]
                    tpr = p.on("V", lambda e: e.tensor_tensor(
                        pbuf[:, :], outT[:, dc * SEQ + sl.start: dc * SEQ + sl.stop],
                        pb[rbank][:, 0:512], ALU.mult),
                        waits=[trep, epi["prodfree"][hh]], inc=(sV, 1))
                    tprs.append(tpr)
                    epi["bkfinr"][rbank] = tpr
                    fb = 6 if hh == 0 else 2
                    w2 = [tpr]
                    if dc == 0:
                        w2.append(epi["foldfree"][hh])
                    tff = p.on("T", lambda e, dc=dc, fb=fb: e.matmul(
                        pb[fb][0:64, 0:512], ISTb[:, :], pbuf,
                        start=(dc == 0), stop=(dc == 3)), waits=w2, inc=(sT, 1))
                    epi["prodfree"][hh] = tff
                    st["tff"] = tff
                return f

            for dc in range(4):
                steps.append(mk_rep_prod(dc))

            def s_ev():
                fb = 6 if hh == 0 else 2
                tfin = p.on("V", lambda e, fb=fb: e.tensor_copy(
                    finall[64 * hh:64 * (hh + 1), hp * SEQ + sl.start: hp * SEQ + sl.stop],
                    pb[fb][0:64, 0:512]), waits=[st["tff"]], inc=(sV, 1))
                epi["foldfree"][hh] = tfin
                epi["tfin"] = tfin
                if hh == 1:
                    gate_out[cc2] = tprs[-1]
            steps.append(s_ev)
            return steps

        def zrun(chains):
            for ops in itertools.zip_longest(*chains):
                for op in ops:
                    if op:
                        op()
        metas = {}
        for wave in (0, 1):
            comps = []
            for cc2 in (2 * wave, 2 * wave + 1):
                ca, sta = comp_chain(0, cc2, (7, 3)[cc2 % 2])
                cb, stb = comp_chain(1, cc2, (0, 1)[cc2 % 2])
                comps += [ca, cb]
                metas[cc2] = (sta, stb)
            if wave == 1:
                comps += [fin_chain(0, 0, metas[0][0]), fin_chain(1, 0, metas[0][1])]
            zrun(comps)
        for cc2 in range(1, 4):
            sta, stb = metas[cc2]
            zrun([fin_chain(0, cc2, sta), fin_chain(1, cc2, stb)])
        return epi["tfin"]

    for hp in range(hpn):
        tok_pv_prevhp = tok_pv
        tok_exp = {}
        tok_pv = {}
        pending_evac = None
        for iq in range(NIQ):
            def s_and_exp(j, iq=iq, hp=hp, tok_pv_prevhp=tok_pv_prevhp):
                w = [tQ[-1], tK[-1], tNEGM, tok_sband.get(0), bankfree[0], bankfree[1]]
                p.on("T", lambda e, j=j, iq=iq, hp=hp: e.matmul(
                    pb[0][:, 0:IQ],
                    Kb[0:64, hp * SEQ + j * 128: hp * SEQ + (j + 1) * 128],
                    Qb[0:64, hp * SEQ + iq * IQ: hp * SEQ + (iq + 1) * IQ],
                    start=True, stop=True, tile_position=(0, 0)), waits=w)
                tmm = p.on("T", lambda e, j=j, iq=iq, hp=hp: e.matmul(
                    pb[1][:, 0:IQ],
                    Kb[64:128, hp * SEQ + j * 128: hp * SEQ + (j + 1) * 128],
                    Qb[64:128, hp * SEQ + iq * IQ: hp * SEQ + (iq + 1) * IQ],
                    start=True, stop=True, tile_position=(64, 0)), inc=(sT, 1))
                if iq == 0:
                    prevpv = tok_pv_prevhp.get((NIQ - 1, j))
                else:
                    prevpv = tok_pv.get((iq - 1, j))
                te = p.on("S", lambda e, j=j, hp=hp: e.activation(
                    EAB[:, j * 512:(j + 1) * 512].rearrange("p (a b) -> p a b", b=IQ),
                    psS[:, 0:1024].rearrange("p (a b) -> p a b", b=512)[:, :, 0:IQ],
                    AF.Exp, bias=negM[:, hp:hp + 1]), waits=[tmm, prevpv], inc=(sS, 1))
                tok_exp[(iq, j)] = te
                tok_sband[0] = te

            s_and_exp(0)
            s_and_exp(1)
            if pending_evac is not None:
                pending_evac()
                pending_evac = None
            for j in range(NJT):
                if j + 2 <= NJT - 1:
                    s_and_exp(j + 2)
                for dc in range(4):
                    bki = 2 + dc
                    bk = pb[bki]
                    w = [tok_exp[(iq, j)], tV]
                    if j == 0:
                        w.append(bankfree[bki])
                    p.on("T", lambda e, j=j, dc=dc, bk=bk: e.matmul(
                        bk[:, 0:2 * IQ],
                        Vtm[:, j * 512 + dc * 128: j * 512 + (dc + 1) * 128],
                        EAB[:, j * 512: (j + 1) * 512],
                        start=(j == 0), stop=(j == NJT - 1)), waits=w)
                w = [tok_exp[(iq, j)]]
                if j == 0:
                    w.append(bankfree[6])
                tok_pv[(iq, j)] = p.on("T", lambda e, j=j: e.matmul(
                    pb[6][0:1, 0:2 * IQ], ones_col_b[:, :], EAB[:, j * 512: (j + 1) * 512],
                    start=(j == 0), stop=(j == NJT - 1)), waits=w, inc=(sT, 1))
            tlast = tok_pv[(iq, NJT - 1)]
            go = gate_out.get(iq // 2)
            gd = gate_den.get(iq // 2)
            w = [tlast] + (gd if gd else [])
            p.on("V", lambda e, iq=iq: e.tensor_copy(
                denAB[0:1, iq * IQ:(iq + 1) * IQ], pb[6][0:1, 0:IQ]), waits=w, inc=(sV, 1))
            bankfree[6] = p.on("V", lambda e, iq=iq: e.tensor_copy(
                denAB[32:33, iq * IQ:(iq + 1) * IQ], pb[6][0:1, IQ:2 * IQ]),
                inc=(sV, 1))
            def mk_evac(iq=iq, tlast=tlast, go=go):
                def f():
                    for dc in range(4):
                        bk = pb[2 + dc]
                        ta = p.on("S", lambda e, dc=dc, iq=iq, bk=bk: e.copy(
                            outTA[:, dc * SEQ + iq * IQ: dc * SEQ + (iq + 1) * IQ], bk[:, 0:IQ]),
                            waits=[tlast, go], inc=(sS, 1))
                        tb = p.on("V", lambda e, dc=dc, iq=iq, bk=bk: e.tensor_copy(
                            outTB[:, dc * SEQ + iq * IQ: dc * SEQ + (iq + 1) * IQ], bk[:, IQ:2 * IQ]),
                            waits=[tlast, go], inc=(sV, 1))
                        bankfree[2 + dc] = [ta, tb]
                return f
            pending_evac = mk_evac()
        pending_evac()
        pending_evac = None
        tFIN = emit_epilogue(hp, [bankfree[2 + dc] for dc in range(4)], bankfree[6])
        bankfree[0] = epi["bkr"][0]
        bankfree[1] = epi["bkr"][1]
        bankfree[2] = epi["foldfree"][1]
        bankfree[3] = epi["bkr"][3]
        bankfree[4] = epi["bkfinr"][4]
        bankfree[5] = epi["bkfinr"][5]
        bankfree[6] = epi["foldfree"][0]
        if hp == 5 and maxphase >= 4:
            wslot_free[0] = wslot_free[1] = None
            wf_toks = stream_w(wfT, 8, 1024, wfb, wstage4, pre=p.tok(sT), eng="S")

    tap("finall0", finall[:, 0:2048], [tFIN])
    tap("outTB2", outTB[:, 2 * SEQ:3 * SEQ], [p.tok(sV)])
    tap("outTBf", outTB, [p.tok(sV)])
    tap("outTAf", outTA, [p.tok(sV)])
    tap("EABlast", EAB[:, 0:1024], [p.tok(sV)])
    tap("denA", denAB[0:1, :], [p.tok(sV)])
    tap("denB", denAB[32:33, :], [p.tok(sV)])
    tap("outTA0", outTA[:, 0:2048], [p.tok(sV)])

    p.barrier("p3", dma_sems=p.dma_pool)

    if maxphase <= 3:
        p.on("G", None, waits=p.all_dma_toks())
        p.emit()
        return nc

    mrepw = carve(24, [128, WIN], F32)
    rrepw = carve(26, [128, WIN], F32)
    xw = [carve(28, [128, WIN], F32), carve(30, [128, WIN], F32)]
    wr1b = carve(32, [128, 16384], BF16)
    wr2b = carve(64, [128, 16384], BF16)
    amidw = carve(96, [128, NDC * WIN], F32)
    h1w = carve(112, [128, 16 * WIN], BF16)
    xn2w = carve(128, [128, NDC * WIN], BF16)
    outw = [carve(136, [128, WIN], F32), carve(138, [128, WIN], F32)]
    sqw = carve(140, [128, WIN], F32)
    m2row = carve(142, [1, WIN], F32)
    vtmp = carve(144, [1, WIN], F32)
    r2wrow = carve(146, [1, WIN], F32)
    m2rep = carve(148, [128, WIN], F32)
    r2wrep = carve(150, [128, WIN], F32)
    tmp4 = carve(184, [128, WIN], F32)

    if wf_toks is None:
        wslot_free[0] = wslot_free[1] = None
        wf_toks = stream_w(wfT, 8, 1024, wfb, wstage4, eng="S")
    wslot_free[0] = wslot_free[1] = None
    wr1_toks = []
    for c in range(8):
        tl = p.on("Y", lambda e, c=c: e.dma_start(
            out=wstage4[:, 0:2048], in_=wr1T[c * 128:(c + 1) * 128, :]),
            waits=[wslot_free[0], wslot_free[1]], dma=True)
        tc_ = p.on("S", lambda e, c=c: e.copy(
            wr1b[:, c * 2048:(c + 1) * 2048], wstage4[:, 0:2048]), waits=[tl], inc=(sS, 1))
        wslot_free[0] = wslot_free[1] = tc_
        wr1_toks.append(tc_)
    wr2_toks = None

    tow = [None, None]
    th1 = {}
    txn2w_last = None
    wr1done_prev = None
    fsave = {}
    for w_i in range(NW + 1):
        if w_i < NW:
            w0 = w_i * WIN
            fpP = (4, 5) if w_i % 2 == 0 else (6, 7)
            tmrw = replicate(mrepw, statrows[0:1, w0:w0 + WIN], 0, [], 0, width=WIN)
            trrw = replicate(rrepw, statrows[64:65, w0:w0 + WIN], 64, [], 1, width=WIN)
            tamid = {}
            for dc in range(NDC):
                bki = fpP[dc % 2]
                bk = pb[bki]
                last = None
                for fc in range(NDC):
                    wts = [wf_toks[fc], tFIN]
                    if fc == 0:
                        wts.append(bankfree[bki])
                    last = p.on("T", lambda e, dc=dc, fc=fc, w0=w0, bk=bk: e.matmul(
                        bk[:, 0:WIN], wfb[:, fc * 1024 + dc * 128: fc * 1024 + (dc + 1) * 128],
                        finall[:, fc * SEQ + w0: fc * SEQ + w0 + WIN],
                        start=(fc == 0), stop=(fc == NDC - 1)),
                        waits=wts, inc=(sT, 1) if fc == NDC - 1 else None)
                txl = p.on("S", lambda e, dc=dc, w0=w0: e.dma_start(
                    out=xw[dc % 2], in_=xT[dc * 128:(dc + 1) * 128, w0:w0 + WIN]),
                    waits=[tow[dc % 2]], dma=True)
                t1 = p.on("G", lambda e, dc=dc: e.tensor_tensor(
                    tmp4, xw[dc % 2], mrepw, ALU.subtract),
                    waits=[txl, tmrw, trrw, txn2w_last], inc=(sG, 1))
                t2g = p.on("G", lambda e, dc=dc: e.tensor_tensor(tmp4, tmp4, rrepw, ALU.mult),
                           inc=(sG, 1))
                t3 = p.on("V", lambda e, dc=dc, bk=bk: e.tensor_scalar(
                    sqw, bk[:, 0:WIN], vcol("bf", dc), None, ALU.add),
                    waits=[last] + ([txn2w_last] if dc == 0 else []), inc=(sV, 1))
                bankfree[bki] = t3
                t4 = p.on("V", lambda e, dc=dc: e.tensor_tensor(
                    amidw[:, dc * WIN:(dc + 1) * WIN], tmp4, sqw, ALU.add),
                    waits=[t2g], inc=(sV, 1))
                t5 = p.on("V", lambda e, dc=dc: e.tensor_tensor(
                    outw[dc % 2], xw[dc % 2], amidw[:, dc * WIN:(dc + 1) * WIN], ALU.add),
                    waits=[tow[dc % 2]], inc=(sV, 1))
                _src = (lambda dc: amidw[:, dc * WIN:(dc + 1) * WIN]) if getattr(build_program, "_plain_amid_only", False) else (lambda dc: outw[dc % 2])
                tst = p.on("G", lambda e, dc=dc, w0=w0, _src=_src: e.dma_start(
                    out=out_hbm[dc * 128:(dc + 1) * 128, w0:w0 + WIN], in_=_src(dc)),
                    waits=[t5], dma=True)
                tow[dc % 2] = tst
                tamid[dc] = t4
            fsave[w_i] = (tamid, None, fpP, w0)
        if w_i == 1:
            wr2_toks = stream_w(wr2T, 16, 1024, wr2b, wstage4, eng="V")
        if w_i >= 1:
            for mt in range(16):
                bki = (2, 3)[mt % 2]
                bk = pb[bki]
                last = None
                for dc in range(NDC):
                    wts = [wr1_toks[dc], txn2w[dc]]
                    if dc == 0:
                        wts.append(bankfree[bki])
                    last = p.on("T", lambda e, mt=mt, dc=dc, bk=bk: e.matmul(
                        bk[:, 0:WIN], wr1b[:, dc * 2048 + mt * 128: dc * 2048 + (mt + 1) * 128],
                        xn2w[:, dc * WIN:(dc + 1) * WIN],
                        start=(dc == 0), stop=(dc == NDC - 1)),
                        waits=wts, inc=(sT, 1) if dc == NDC - 1 else None)
                th1[mt] = p.on("S", lambda e, mt=mt, bk=bk: e.activation(
                    h1w[:, mt * WIN:(mt + 1) * WIN], bk[:, 0:WIN], AF.Relu,
                    bias=vcol("br1", mt)), waits=[last], inc=(sS, 1))
                bankfree[bki] = th1[mt]
            wr1done_prev = th1[15]
        if w_i < NW:
            tamid = fsave[w_i][0]
            tprev = None
            for dc in range(NDC):
                tsq2 = p.on("V", lambda e, dc=dc: e.tensor_tensor(
                    sqw, amidw[:, dc * WIN:(dc + 1) * WIN], amidw[:, dc * WIN:(dc + 1) * WIN], ALU.mult),
                    waits=[tamid[dc], tprev], inc=(sV, 1))
                w = [tamid[dc]]
                if dc == 0:
                    w += [bankfree[0], bankfree[1]]
                p.on("T", lambda e, dc=dc: e.matmul(
                    pb[0][0:1, 0:WIN], ones_col_f[:, :], amidw[:, dc * WIN:(dc + 1) * WIN],
                    start=(dc == 0), stop=(dc == NDC - 1)), waits=w)
                tprev = p.on("T", lambda e, dc=dc: e.matmul(
                    pb[1][0:1, 0:WIN], ones_col_f[:, :], sqw,
                    start=(dc == 0), stop=(dc == NDC - 1)), waits=[tsq2], inc=(sT, 1))
            tv = p.on("V", lambda e: e.tensor_scalar_mul(m2row, pb[0][0:1, 0:WIN], 1.0 / D),
                      waits=[tprev], inc=(sV, 1))
            tv = p.on("V", lambda e: e.tensor_scalar_mul(vtmp, pb[1][0:1, 0:WIN], 1.0 / D),
                      inc=(sV, 1))
            tv = p.on("V", lambda e: e.tensor_tensor(r2wrow, m2row, m2row, ALU.mult),
                      inc=(sV, 1))
            tv = p.on("V", lambda e: e.tensor_tensor(vtmp, vtmp, r2wrow, ALU.subtract),
                      inc=(sV, 1))
            tv = p.on("V", lambda e: e.tensor_scalar_add(vtmp, vtmp, EPS), inc=(sV, 1))
            p.on("S", lambda e: e.activation(vtmp, vtmp, AF.Ln),
                 waits=[tv], inc=(sS, 1))
            tsqr = p.on("S", lambda e: e.activation(r2wrow, vtmp, AF.Exp, scale=-0.5),
                        inc=(sS, 1))
            tm_a = p.on("T", lambda e: e.matmul(pb[0][:, 0:WIN], onesPP[0:1, 0:128], m2row,
                                                start=True, stop=True), waits=[tv], inc=(sT, 1))
            tm2r = p.on("V", lambda e: e.tensor_copy(m2rep, pb[0][:, 0:WIN]), waits=[tm_a], inc=(sV, 1))
            bankfree[0] = tm2r
            tm_b = p.on("T", lambda e: e.matmul(pb[1][:, 0:WIN], onesPP[0:1, 0:128], r2wrow,
                                                start=True, stop=True), waits=[tsqr], inc=(sT, 1))
            tr2w = p.on("V", lambda e: e.tensor_copy(r2wrep, pb[1][:, 0:WIN]), waits=[tm_b], inc=(sV, 1))
            bankfree[1] = tr2w
            txn2w = {}
            for dc in range(NDC):
                tb = tmp4 if dc % 2 == 0 else sqw
                w_x = [tm2r, tr2w, txn2w.get(dc - 2)]
                if dc == 0:
                    w_x.append(wr1done_prev)
                p.on("V", lambda e, dc=dc, tb=tb: e.tensor_tensor(
                    tb, amidw[:, dc * WIN:(dc + 1) * WIN], m2rep, ALU.subtract),
                    waits=w_x, inc=(sV, 1))
                t2 = p.on("V", lambda e, dc=dc, tb=tb: e.tensor_tensor(tb, tb, r2wrep, ALU.mult), inc=(sV, 1))
                txn2w[dc] = p.on("S", lambda e, dc=dc, tb=tb: e.activation(
                    xn2w[:, dc * WIN:(dc + 1) * WIN], tb, AF.Identity,
                    bias=vcol("b2", dc), scale=vcol("g2", dc)), waits=[t2], inc=(sS, 1))
                txn2w_last = txn2w[dc]
        if w_i >= 1:
            _, _, fpPp, w0p = fsave[w_i - 1]
            for mt in range(NDC):
                bki = fpPp[mt % 2]
                bk = pb[bki]
                last = None
                for hc in range(16):
                    wts = [wr2_toks[hc], th1[hc]]
                    if hc == 0:
                        wts += [bankfree[bki], tow[mt % 2]]
                    last = p.on("T", lambda e, mt=mt, hc=hc, bk=bk: e.matmul(
                        bk[:, 0:WIN], wr2b[:, hc * 1024 + mt * 128: hc * 1024 + (mt + 1) * 128],
                        h1w[:, hc * WIN:(hc + 1) * WIN],
                        start=(hc == 0), stop=(hc == 15)),
                        waits=wts, inc=(sT, 1) if hc == 15 else None)
                t1 = p.on("V", lambda e, mt=mt, bk=bk: e.tensor_scalar(
                    outw[mt % 2], bk[:, 0:WIN], vcol("br2", mt), None, ALU.add),
                    waits=[last, tow[mt % 2]], inc=(sV, 1))
                bankfree[bki] = t1
                if not getattr(build_program, "_skip_iffn_dma", False):
                    tow[mt % 2] = p.on("G", lambda e, mt=mt, w0p=w0p: e.dma_start(
                        out=out_hbm[mt * 128:(mt + 1) * 128, w0p:w0p + WIN], in_=outw[mt % 2],
                        accum_op=ALU.add), waits=[t1], dma=True)
                else:
                    tow[mt % 2] = t1

    tap("amidw", amidw, [p.tok(sV)])
    tap("xn2w", xn2w, [p.tok(sS)])
    tap("h1w", h1w, [p.tok(sS)])
    tap("wr1b0", wr1b[:, 0:2048], [wr1_toks[0]])
    tap("wr2b0", wr2b[:, 0:1024], [wr2_toks[0]])
    tap("wr2bf", wr2b, [wr2_toks[-1]])
    tap("wr1bf", wr1b, [wr1_toks[-1]])

    p.barrier("p4", dma_sems=p.dma_pool)

    m_rep5 = carve(0, [128, SEQ], F32)
    r_rep5 = carve(8, [128, SEQ], F32)
    xnb = carve(16, [128, NDC * SEQ], BF16)
    xscr = carve(48, [128, SEQ], F32)
    mwbf = carve(56, [128, 8192], BF16)
    Gb = carve(72, [128, 2048], BF16)
    wqc = carve(76, [128, 2048], BF16)
    wkc = carve(80, [128, 2048], BF16)
    wvc = carve(84, [128, 2048], BF16)
    qkb = carve(88, [128, 2 * SEQ], BF16)
    kkb = carve(96, [128, 2 * SEQ], BF16)
    vktm = carve(104, [128, NJT * 288], BF16)
    EABm = carve(114, [128, NJT * 512], BF16)
    outmA = carve(130, [65, SEQ], F32)
    outmB = carve(138, [65, SEQ], F32)
    outc = carve(146, [128, 2 * SEQ], BF16)
    mrep16 = carve(154, [16, SEQ], F32)
    rrep16 = carve(162, [16, SEQ], F32)
    logsb = carve(170, [16, SEQ], F32)
    mscr = carve(178, [128, SEQ], F32)
    grep = carve(154, [128, 256], F32)
    bvcrep = carve(155, [128, 256], F32)
    mow = [carve(130, [128, WIN], F32), carve(132, [128, WIN], F32)]

    tmr5 = replicate(m_rep5, statrows[0:1, :], 0, [], 0)
    trr5 = replicate(r_rep5, statrows[64:65, :], 64, [], 1)

    twr = p.on("Y", lambda e: e.dma_start(
        out=wrf[:].rearrange("p (c n) -> p c n", c=8), in_=mwr.rearrange("(c p) n -> p c n", p=128)),
        dma=True)

    tlog = None
    txnb = {}
    prevx = None
    for c in range(NDC):
        txl = p.on("Y", lambda e, c=c: e.dma_start(
            out=xscr, in_=xT[c * 128:(c + 1) * 128, :]), waits=[prevx, tlog], dma=True)
        for cc in range(4):
            w = [txl, twr]
            if c == 0:
                w.append(bankfree[4 + cc])
            tlog = p.on("T", lambda e, c=c, cc=cc: e.matmul(
                pb[4 + cc][0:16, 0:512], wrf[:, c * 16:(c + 1) * 16],
                xscr[:, cc * 512:(cc + 1) * 512],
                start=(c == 0), stop=(c == NDC - 1)), waits=w, inc=(sT, 1))
        t1 = p.on("V", lambda e, c=c: e.tensor_tensor(mscr, xscr, m_rep5, ALU.subtract),
                  waits=[txl, tmr5, trr5], inc=(sV, 1))
        prevx = t1
        txnb[c] = p.on("V", lambda e, c=c: e.tensor_tensor(
            xnb[:, c * SEQ:(c + 1) * SEQ], mscr, r_rep5, ALU.mult), inc=(sV, 1))
    tXNB = txnb[NDC - 1]

    tcs = None
    for c in range(NDC):
        w = [twr, tCONST]
        if c == 0:
            w.append(bankfree[0])
        tcs = p.on("T", lambda e, c=c: e.matmul(
            pb[0][0:16, 0:1], wrf[:, c * 16:(c + 1) * 16], ones_col_f[:, :],
            start=(c == 0), stop=(c == NDC - 1)), waits=w, inc=(sT, 1))
    tcsv = p.on("V", lambda e: e.tensor_copy(gat[0:16, 32:33], pb[0][0:16, 0:1]), waits=[tcs], inc=(sV, 1))
    bankfree[0] = tcsv

    trm16 = None
    for cc in range(4):
        tm_a = p.on("T", lambda e, cc=cc: e.matmul(
            pb[2][0:16, 0:512], onesPP[0:1, 0:16], statrows[0:1, cc * 512:(cc + 1) * 512],
            start=True, stop=True), waits=[trm16, bankfree[2]], inc=(sT, 1))
        tmv = p.on("V", lambda e, cc=cc: e.tensor_copy(mrep16[:, cc * 512:(cc + 1) * 512], pb[2][0:16, 0:512]),
                   waits=[tm_a], inc=(sV, 1))
        bankfree[2] = tmv
        tm_b = p.on("T", lambda e, cc=cc: e.matmul(
            pb[3][0:16, 0:512], onesPP[64:65, 0:16], statrows[64:65, cc * 512:(cc + 1) * 512],
            start=True, stop=True), waits=[tmv, bankfree[3]], inc=(sT, 1))
        trm16 = p.on("V", lambda e, cc=cc: e.tensor_copy(rrep16[:, cc * 512:(cc + 1) * 512], pb[3][0:16, 0:512]),
                     waits=[tm_b], inc=(sV, 1))
        bankfree[3] = trm16

    tlg = None
    for cc in range(4):
        p.on("V", lambda e, cc=cc: e.tensor_scalar(
            logsb[:, cc * 512:(cc + 1) * 512], mrep16[:, cc * 512:(cc + 1) * 512],
            gat[0:16, 32:33], None, ALU.mult), waits=[tlog, tcsv, trm16], inc=(sV, 1))
        p.on("V", lambda e, cc=cc: e.tensor_tensor(
            logsb[:, cc * 512:(cc + 1) * 512], pb[4 + cc][0:16, 0:512],
            logsb[:, cc * 512:(cc + 1) * 512], ALU.subtract), inc=(sV, 1))
        tlg = p.on("V", lambda e, cc=cc: e.tensor_tensor(
            logsb[:, cc * 512:(cc + 1) * 512], logsb[:, cc * 512:(cc + 1) * 512],
            rrep16[:, cc * 512:(cc + 1) * 512], ALU.mult), inc=(sV, 1))
        bankfree[4 + cc] = tlg
    tmean = p.on("V", lambda e: e.reduce_sum(gat[0:16, 33:34], logsb, axis=AX.X), waits=[tlg], inc=(sV, 1))
    tmean = p.on("V", lambda e: e.tensor_scalar(
        gat[0:16, 34:35], gat[0:16, 33:34], 1.0 / SEQ, vcol("mbr", 0)[0:16, :], ALU.mult, ALU.add),
        inc=(sV, 1))
    tdm = p.on("G", lambda e: e.dma_start(out=gat[0:1, 0:16], in_=gat[0:16, 34:35]),
               waits=[tmean], dma=True)
    t1 = p.on("V", lambda e: e.reduce_max(gat[0:1, 16:17], gat[0:1, 0:16], axis=AX.X),
              waits=[tdm], inc=(sV, 1))
    t2 = p.on("V", lambda e: e.tensor_tensor(gat[0:1, 0:16], gat[0:1, 0:16],
                                             gat[0:1, 16:17].broadcast_to([1, 16]), ALU.subtract),
              inc=(sV, 1))
    t3 = p.on("S", lambda e: e.activation(gat[0:1, 0:16], gat[0:1, 0:16], AF.Exp), waits=[t2], inc=(sS, 1))
    t4 = p.on("V", lambda e: e.reduce_sum(gat[0:1, 16:17], gat[0:1, 0:16], axis=AX.X),
              waits=[t3], inc=(sV, 1))
    t5 = p.on("V", lambda e: e.reciprocal(gat[0:1, 17:18], gat[0:1, 16:17]), inc=(sV, 1))
    tdist = p.on("V", lambda e: e.tensor_tensor(gat[0:1, 0:16], gat[0:1, 0:16],
                                                gat[0:1, 17:18].broadcast_to([1, 16]), ALU.mult),
                 inc=(sV, 1))
    p.on("V", lambda e: e.max(gat[0:1, 18:26], gat[0:1, 0:16]), waits=[tdist], inc=(sV, 1))
    ttk = p.on("V", lambda e: e.max_index(mi8[0:1, :], gat[0:1, 18:26], gat[0:1, 0:16]),
               inc=(sV, 1))
    tsp = p.on("V", lambda e: e.memset(gat[0:1, 48:64], 0.0), inc=(sV, 1))
    toh = {}
    for k in range(4):
        toh[k] = p.on("V", lambda e, k=k: e.tensor_tensor(
            gat[0:1, 64 + 16 * k:80 + 16 * k], gat[0:1, 0:16],
            gat[0:1, 18 + k:19 + k].broadcast_to([1, 16]), ALU.is_equal), waits=[ttk], inc=(sV, 1))
        p.on("V", lambda e, k=k: e.tensor_tensor(
            gat[0:1, 128 + 16 * k:144 + 16 * k], gat[0:1, 64 + 16 * k:80 + 16 * k],
            gat[0:1, 18 + k:19 + k].broadcast_to([1, 16]), ALU.mult), inc=(sV, 1))
        tsp = p.on("V", lambda e, k=k: e.tensor_tensor(
            gat[0:1, 48:64], gat[0:1, 48:64], gat[0:1, 128 + 16 * k:144 + 16 * k], ALU.add),
            inc=(sV, 1))
    tge = p.on("S", lambda e: e.activation(gat[0:1, 192:208], gat[0:1, 48:64], AF.Exp),
               waits=[tsp], inc=(sS, 1))
    p.on("V", lambda e: e.reduce_sum(gat[0:1, 16:17], gat[0:1, 192:208], axis=AX.X),
         waits=[tge], inc=(sV, 1))
    p.on("V", lambda e: e.reciprocal(gat[0:1, 17:18], gat[0:1, 16:17]), inc=(sV, 1))
    tga = p.on("V", lambda e: e.tensor_tensor(gat[0:1, 192:208], gat[0:1, 192:208],
                                              gat[0:1, 17:18].broadcast_to([1, 16]), ALU.mult),
               inc=(sV, 1))
    tgk = None
    for k in range(4):
        p.on("V", lambda e, k=k: e.tensor_tensor(
            gat[0:1, 128 + 16 * k:144 + 16 * k], gat[0:1, 192:208],
            gat[0:1, 64 + 16 * k:80 + 16 * k], ALU.mult), waits=[tga], inc=(sV, 1))
        tgk = p.on("V", lambda e, k=k: e.reduce_sum(
            gat[0:1, 26 + k:27 + k], gat[0:1, 128 + 16 * k:144 + 16 * k], axis=AX.X), inc=(sV, 1))
    tgc = p.on("G", lambda e: e.dma_start(out=gat[0:4, 40:41], in_=gat[0:1, 26:30]),
               waits=[tgk], dma=True)
    tge2 = {}
    for t2_ in range(2):
        tm_a = p.on("T", lambda e, t2_=t2_: e.matmul(
            pb[0][:, 0:1], E64f[:, t2_ * 128:(t2_ + 1) * 128], gat[0:4, 40:41],
            start=True, stop=True), waits=[tgc, bankfree[0]], inc=(sT, 1))
        tge2[t2_] = p.on("S", lambda e, t2_=t2_: e.copy(gexp[:, t2_:t2_ + 1], pb[0][:, 0:1]),
                         waits=[tm_a], inc=(sS, 1))
        bankfree[0] = tge2[t2_]
    tgq = p.on("S", lambda e: e.mul(gexpQ[:], gexp[:], ISQ8), waits=[tge2[1]], inc=(sS, 1))
    th4 = []
    for k in range(4):
        t1 = p.on("V", lambda e, k=k: e.tensor_copy(gatb[0:1, :], gat[0:1, 64 + 16 * k:80 + 16 * k]),
                  waits=[toh[k], (th4[-1] if th4 else None)], inc=(sV, 1))
        th4.append(p.on("G", lambda e, k=k: e.dma_start(out=H4T[k:k + 1, :], in_=gatb[0:1, :]),
                        waits=[t1], dma=True))
    tohc = [p.on("G", lambda e, k=k: e.dma_start(
        out=gat[0:16, 41 + k:42 + k], in_=gat[0:1, 64 + 16 * k:80 + 16 * k]),
        waits=[toh[k]], dma=True) for k in range(4)]
    tbc_last = None
    for wi in range(3):
        for k in range(4):
            tm_a = p.on("T", lambda e, wi=wi, k=k: e.matmul(
                pb[1][0:64, 0:1], bh3t[:, wi * 64:(wi + 1) * 64], gat[0:16, 41 + k:42 + k],
                start=True, stop=True), waits=[t_bh3, tohc[k], tbc_last, bankfree[1]],
                inc=(sT, 1))
            tbc_last = p.on("S", lambda e, wi=wi, k=k: e.copy(
                bcpp[64 * (k % 2):64 * (k % 2) + 64, 2 * wi + k // 2:2 * wi + k // 2 + 1],
                pb[1][0:64, 0:1]), waits=[tm_a], inc=(sS, 1))
            bankfree[1] = tbc_last
    tBC = tbc_last

    tm_a = p.on("T", lambda e: e.matmul(pb[0][0:16, 0:256], H4T[:, :], Ecol4b[:, :],
                                        start=True, stop=True), waits=th4 + [tCONST, bankfree[0]],
                inc=(sT, 1))
    th4e = p.on("V", lambda e: e.tensor_copy(H4E[:, :], pb[0][0:16, 0:256]), waits=[tm_a], inc=(sV, 1))
    bankfree[0] = th4e
    tGB = None
    for c in range(NDC):
        tm_a = p.on("T", lambda e, c=c: e.matmul(
            pb[0][:, 0:256], RowSelb[:, c * 128:(c + 1) * 128], H4E[:, :],
            start=True, stop=True), waits=[th4e, bankfree[0]], inc=(sT, 1))
        tGB = p.on("V", lambda e, c=c: e.tensor_tensor(
            Gb[:, c * 256:(c + 1) * 256], pb[0][:, 0:256], M64b[:, :], ALU.mult),
            waits=[tm_a], inc=(sV, 1))
        bankfree[0] = tGB

    wslot_free[0] = wslot_free[1] = None

    def stream_w5(hbm, nchunks, pre=None):
        toks = []
        for c in range(nchunks):
            s = c % 2
            w = [wslot_free[s]]
            if pre is not None and c < 2:
                w.append(pre)
            tl = p.on("Y", lambda e, c=c, s=s, hbm=hbm: e.dma_start(
                out=mscr[:, s * 1024:(s + 1) * 1024], in_=hbm[c * 128:(c + 1) * 128, :]),
                waits=w, dma=True)
            tc_ = p.on("S", lambda e, c=c, s=s: e.copy(
                mwbf[:, c * 1024:(c + 1) * 1024], mscr[:, s * 1024:(s + 1) * 1024]),
                waits=[tl], inc=(sS, 1))
            wslot_free[s] = tc_
            toks.append(tc_)
        return toks

    def compact_w(wtoks, dst, pre):
        last_ev = pre
        for it in range(NDC):
            last = None
            for oc in range(NDC):
                w = [wtoks[oc], tGB]
                if oc == 0:
                    w += [last_ev, bankfree[2]]
                last = p.on("T", lambda e, it=it, oc=oc: e.matmul(
                    pb[2][:, 0:256], mwbf[:, oc * 1024 + it * 128: oc * 1024 + (it + 1) * 128],
                    Gb[:, oc * 256:(oc + 1) * 256],
                    start=(oc == 0), stop=(oc == NDC - 1)),
                    waits=w, inc=(sT, 1) if oc == NDC - 1 else None)
            last_ev = p.on("V", lambda e, it=it, dst=dst: e.tensor_copy(
                dst[:, it * 256:(it + 1) * 256], pb[2][:, 0:256]), waits=[last], inc=(sV, 1))
            bankfree[2] = last_ev
        return last_ev

    tq_toks = stream_w5(mwqT, 8, pre=tXNB)
    tWQC = compact_w(tq_toks, wqc, None)
    tk_toks = stream_w5(mwkT, 8, pre=tWQC)
    tWKC = compact_w(tk_toks, wkc, None)
    tv_toks = stream_w5(mwvT, 8, pre=tWKC)
    tWVC = compact_w(tv_toks, wvc, None)

    def cproj(wc, twc, dst, bcol, gtile):
        toks = []
        for half in range(2):
            for cc in range(4):
                bki = 4 + ((half * 4 + cc) % 2)
                bk = pb[bki]
                last = None
                for dc in range(NDC):
                    w = [twc, tXNB]
                    if dc == 0:
                        w.append(bankfree[bki])
                    last = p.on("T", lambda e, half=half, cc=cc, dc=dc, bk=bk, wc=wc: e.matmul(
                        bk[:, 0:512], wc[:, dc * 256 + half * 128: dc * 256 + (half + 1) * 128],
                        xnb[:, dc * SEQ + cc * 512: dc * SEQ + (cc + 1) * 512],
                        start=(dc == 0), stop=(dc == NDC - 1)),
                        waits=w, inc=(sT, 1) if dc == NDC - 1 else None)
                te = p.on("V", lambda e, half=half, cc=cc, bk=bk, dst=dst, bcol=bcol, gtile=gtile: e.tensor_scalar(
                    dst[:, half * SEQ + cc * 512: half * SEQ + (cc + 1) * 512], bk[:, 0:512],
                    bcpp[:, bcol + half:bcol + half + 1], gtile[:, half:half + 1], ALU.add, ALU.mult),
                    waits=[last, tBC, tgq], inc=(sV, 1))
                bankfree[bki] = te
                toks.append(te)
        return toks[-1]

    tQK = cproj(wqc, tWQC, qkb, 0, gexpQ)
    tKK = cproj(wkc, tWKC, kkb, 2, gexp)

    tgrow = []
    for k in range(4):
        tgrow.append(p.on("V", lambda e, k=k: e.tensor_copy(
            scr_row[0:1, k * 64:(k + 1) * 64],
            gat[0:1, 26 + k:27 + k].broadcast_to([1, 64])), waits=[tgk], inc=(sV, 1)))
    tbvc = [p.on("G", lambda e, half=half: e.dma_start(
        out=gat[0:1, 224 + half * 128:224 + (half + 1) * 128],
        in_=bcpp[:, 4 + half:5 + half]), waits=[tBC], dma=True) for half in range(2)]
    tm_a = p.on("T", lambda e: e.matmul(pb[0][:, 0:256], onesPP[0:1, 0:128], scr_row[0:1, 0:256],
                                        start=True, stop=True), waits=tgrow + [bankfree[0]], inc=(sT, 1))
    tgrep = p.on("V", lambda e: e.tensor_copy(grep[:], pb[0][:, 0:256]), waits=[tm_a], inc=(sV, 1))
    bankfree[0] = tgrep
    tm_b = p.on("T", lambda e: e.matmul(pb[1][:, 0:256], onesPP[0:1, 0:128], gat[0:1, 224:480],
                                        start=True, stop=True), waits=tbvc + [bankfree[1]], inc=(sT, 1))
    tbvcrep = p.on("V", lambda e: e.tensor_copy(bvcrep[:], pb[1][:, 0:256]), waits=[tm_b], inc=(sV, 1))
    bankfree[1] = tbvcrep

    tms = p.on("V", lambda e: e.memset(vktm[:, :], 0.0), inc=(sV, 1))
    tones = p.on("V", lambda e: e.tensor_copy(
        vktm[:, :].rearrange("p (a b) -> p a b", b=72)[:, :, 64:65],
        ones_col_b[:, 0:1].broadcast_to([128, 64]).rearrange("p (a b) -> p a b", b=1)),
        waits=[tms], inc=(sV, 1))
    tvk_last = None
    for jt in range(NJT):
        last = None
        for dc in range(NDC):
            w = [tWVC, tXNB]
            if dc == 0:
                w.append(bankfree[2])
            last = p.on("T", lambda e, jt=jt, dc=dc: e.matmul(
                pb[2][:, 0:256], xnb[:, dc * SEQ + jt * 128: dc * SEQ + (jt + 1) * 128],
                wvc[:, dc * 256:(dc + 1) * 256],
                start=(dc == 0), stop=(dc == NDC - 1)),
                waits=w, inc=(sT, 1) if dc == NDC - 1 else None)
        t1 = p.on("V", lambda e, jt=jt: e.tensor_tensor(xscr[:, 0:256], pb[2][:, 0:256], bvcrep[:], ALU.add),
                  waits=[last, tbvcrep], inc=(sV, 1))
        bankfree[2] = t1
        p.on("V", lambda e, jt=jt: e.tensor_tensor(xscr[:, 256:512], xscr[:, 0:256], grep[:], ALU.mult),
             waits=[tgrep, tones], inc=(sV, 1))
        for k in range(4):
            tvk_last = p.on("V", lambda e, jt=jt, k=k: e.tensor_copy(
                vktm[:, jt * 288 + k * 72: jt * 288 + k * 72 + 64],
                xscr[:, 256 + k * 64:256 + (k + 1) * 64]), inc=(sV, 1))
    tVK = tvk_last

    tprev = None
    for src, base, tdep in ((qkb, 0, tQK), (kkb, 2, tKK)):
        for kp in range(2):
            tq = p.on("V", lambda e, kp=kp, src=src: e.tensor_tensor(
                mscr, src[:, kp * SEQ:(kp + 1) * SEQ], src[:, kp * SEQ:(kp + 1) * SEQ], ALU.mult),
                waits=[tdep, tprev, tVK], inc=(sV, 1))
            tqb = p.on("V", lambda e: e.tensor_copy(EABm[:, 0:SEQ], mscr), inc=(sV, 1))
            for cc in range(4):
                tm_a = p.on("T", lambda e, cc=cc: e.matmul(
                    pb[0][0:1, 0:512], ones_col_b[:, :], EABm[:, cc * 512:(cc + 1) * 512],
                    start=True, stop=True), waits=[tqb, bankfree[0]], inc=(sT, 1))
                tprev = p.on("V", lambda e, kp=kp, cc=cc, base=base: e.reduce_max(
                    gat[0:1, 240 + (base + kp) * 4 + cc:240 + (base + kp) * 4 + cc + 1],
                    pb[0][0:1, 0:512], axis=AX.X), waits=[tm_a], inc=(sV, 1))
                bankfree[0] = tprev
    for kp in range(4):
        tprev = p.on("V", lambda e, kp=kp: e.reduce_max(
            gat[0:1, 230 + kp:231 + kp], gat[0:1, 240 + kp * 4:244 + kp * 4], axis=AX.X),
            waits=[tprev], inc=(sV, 1))
    t_ = p.on("V", lambda e: e.tensor_tensor(gat[0:1, 234:236], gat[0:1, 230:232], gat[0:1, 232:234], ALU.mult),
              waits=[tprev], inc=(sV, 1))
    t_ = p.on("S", lambda e: e.activation(gat[0:1, 236:238], gat[0:1, 234:236], AF.Sqrt),
              waits=[t_], inc=(sS, 1))
    t_ = p.on("V", lambda e: e.tensor_scalar_mul(gat[0:1, 234:236], gat[0:1, 236:238], -1.05),
              waits=[t_], inc=(sV, 1))
    tm_a = p.on("T", lambda e: e.matmul(pb[0][:, 0:2], onesPP[0:1, 0:128], gat[0:1, 234:236],
                                        start=True, stop=True), waits=[t_, bankfree[0]], inc=(sT, 1))
    tMNEG = p.on("V", lambda e: e.tensor_copy(mnegM[:], pb[0][:, 0:2]), waits=[tm_a], inc=(sV, 1))
    bankfree[0] = tMNEG

    tok_pvm = {}
    tok_sbandm = {}
    for kp in range(2):
        tok_pvm_prev = tok_pvm
        tok_expm = {}
        tok_pvm = {}
        pending_evacm = None
        for iq in range(NIQ):
            def s_and_expm(j, iq=iq, kp=kp, tok_pvm_prev=tok_pvm_prev):
                w = [tQK, tKK, tMNEG, tok_sbandm.get(0)]
                p.on("T", lambda e, j=j, iq=iq, kp=kp: e.matmul(
                    pb[0][:, 0:IQ], kkb[0:64, kp * SEQ + j * 128: kp * SEQ + (j + 1) * 128],
                    qkb[0:64, kp * SEQ + iq * IQ: kp * SEQ + (iq + 1) * IQ],
                    start=True, stop=True, tile_position=(0, 0)), waits=w)
                tmm = p.on("T", lambda e, j=j, iq=iq, kp=kp: e.matmul(
                    pb[1][:, 0:IQ], kkb[64:128, kp * SEQ + j * 128: kp * SEQ + (j + 1) * 128],
                    qkb[64:128, kp * SEQ + iq * IQ: kp * SEQ + (iq + 1) * IQ],
                    start=True, stop=True, tile_position=(64, 0)), inc=(sT, 1))
                if iq == 0:
                    prevpv = tok_pvm_prev.get((NIQ - 1, j))
                else:
                    prevpv = tok_pvm.get((iq - 1, j))
                te = p.on("S", lambda e, j=j, kp=kp: e.activation(
                    EABm[:, j * 512:(j + 1) * 512].rearrange("p (a b) -> p a b", b=IQ),
                    psS[:, 0:1024].rearrange("p (a b) -> p a b", b=512)[:, :, 0:IQ],
                    AF.Exp, bias=mnegM[:, kp:kp + 1]), waits=[tmm, prevpv], inc=(sS, 1))
                tok_expm[(iq, j)] = te
                tok_sbandm[0] = te

            s_and_expm(0)
            s_and_expm(1)
            if pending_evacm is not None:
                pending_evacm()
                pending_evacm = None
            for j in range(NJT):
                if j + 2 <= NJT - 1:
                    s_and_expm(j + 2)
                tpvm = None
                for hh in range(2):
                    bki = 2 + hh
                    bk = pb[bki]
                    w = [tok_expm[(iq, j)], tVK]
                    if j == 0:
                        w.append(bankfree[bki])
                    tpvm = p.on("T", lambda e, j=j, hh=hh, kp=kp, bk=bk: e.matmul(
                        bk[0:65, 0:IQ],
                        vktm[:, j * 288 + (2 * kp + hh) * 72: j * 288 + (2 * kp + hh) * 72 + 65],
                        EABm[:, j * 512 + hh * IQ: j * 512 + (hh + 1) * IQ],
                        start=(j == 0), stop=(j == NJT - 1)), waits=w, inc=(sT, 1))
                tok_pvm[(iq, j)] = tpvm
            tlast = tok_pvm[(iq, NJT - 1)]
            def mk_evacm(iq=iq, tlast=tlast):
                def f():
                    for hh in range(2):
                        outm = outmA if hh == 0 else outmB
                        eng, sem = ("V", sV) if hh == 0 else ("S", sS)
                        if hh == 0:
                            bankfree[2 + hh] = p.on("V", lambda e, hh=hh, iq=iq, outm=outm: e.tensor_copy(
                                outm[:, iq * IQ:(iq + 1) * IQ], pb[2 + hh][0:65, 0:IQ]),
                                waits=[tlast], inc=(sV, 1))
                        else:
                            bankfree[2 + hh] = p.on("S", lambda e, hh=hh, iq=iq, outm=outm: e.copy(
                                outm[:, iq * IQ:(iq + 1) * IQ], pb[2 + hh][0:65, 0:IQ]),
                                waits=[tlast], inc=(sS, 1))
                return f
            pending_evacm = mk_evacm()
        pending_evacm()
        pending_evacm = None
        for hh in range(2):
            k = 2 * kp + hh
            outm = outmA if hh == 0 else outmB
            p.on("S", lambda e, outm=outm: e.activation(outm[64:65, :], outm[64:65, :], AF.Ln),
                 waits=[bankfree[2 + hh]], inc=(sS, 1))
            tiv = p.on("S", lambda e, outm=outm: e.activation(
                outm[64:65, :], outm[64:65, :], AF.Exp, scale=-1.0), inc=(sS, 1))
            for cc in range(4):
                tm_a = p.on("T", lambda e, cc=cc, outm=outm: e.matmul(
                    pb[6][0:64, 0:512], onesPP[64:65, 0:64], outm[64:65, cc * 512:(cc + 1) * 512],
                    start=True, stop=True), waits=[tiv, bankfree[6]], inc=(sT, 1))
                bankfree[6] = p.on("V", lambda e, cc=cc, k=k, outm=outm: e.tensor_tensor(
                    outc[64 * (k % 2):64 * (k % 2) + 64,
                         (k // 2) * SEQ + cc * 512:(k // 2) * SEQ + (cc + 1) * 512],
                    outm[0:64, cc * 512:(cc + 1) * 512], pb[6][0:64, 0:512], ALU.mult),
                    waits=[tm_a], inc=(sV, 1))
    tOUTC = bankfree[6]

    wo_toks = stream_w5(mwo, 2, pre=tMNEG)
    tmo = [None, None]
    for mt in range(NDC):
        for cc in range(4):
            par = (mt * 4 + cc) % 2
            bki = 4 + par
            bk = pb[bki]
            last = None
            for ch in range(2):
                w = [wo_toks[ch], tOUTC]
                if ch == 0:
                    w += [tmo[par], bankfree[bki]]
                last = p.on("T", lambda e, mt=mt, cc=cc, ch=ch, bk=bk: e.matmul(
                    bk[:, 0:512], mwbf[:, ch * 1024 + mt * 128: ch * 1024 + (mt + 1) * 128],
                    outc[:, ch * SEQ + cc * 512: ch * SEQ + (cc + 1) * 512],
                    start=(ch == 0), stop=(ch == 1)),
                    waits=w, inc=(sT, 1) if ch == 1 else None)
            t1 = p.on("V", lambda e, mt=mt, cc=cc, par=par, bk=bk: e.tensor_scalar(
                mow[par], bk[:, 0:512], vcol("mbo", mt), None, ALU.add),
                waits=[last, tmo[par]], inc=(sV, 1))
            bankfree[bki] = t1
            tmo[par] = p.on("G", lambda e, mt=mt, cc=cc, par=par: e.dma_start(
                out=out_hbm[mt * 128:(mt + 1) * 128, cc * 512:(cc + 1) * 512],
                in_=mow[par], accum_op=ALU.add), waits=[t1], dma=True)

    tap("dist", gat[0:1, 0:64], [tdist, ttk, tgk])
    tap("gall", gat[0:1, 192:208], [tga])
    tap("outc0", outc[:, 0:512], [tOUTC])
    tap("qkb0", qkb[:, 0:512], [tQK])
    tap("xnb0", xnb[:, 0:2048], [tXNB])

    p.on("G", None, waits=p.all_dma_toks())
    p.emit()
    return nc



def _consts_blob(Ws):
    c = np.zeros((128, 2048), dtype=np.float32)
    c[:, 0:128] = 1.0
    ist = np.zeros((128, 64), dtype=np.float32)
    for q in range(128):
        ist[q, q % 64] = 1.0
    c[:, 128:192] = ist
    m64 = np.zeros((128, 256), dtype=np.float32)
    for q in range(128):
        for k in range(4):
            m64[q, k * 64 + (q % 64)] = 1.0
    c[:, 192:448] = m64
    Ws = np.asarray(Ws, dtype=np.float32).reshape(-1)
    w1e = np.zeros((128, 64), dtype=np.float32)
    for mt in range(4):
        for q in range(128):
            w1e[q, mt * 16 + 4 * mt + q // 32] = Ws[q % 32]
    c[:, 448:512] = w1e
    w2e = np.zeros((128, 32), dtype=np.float32)
    for dc in range(4):
        for q in range(128):
            w2e[q, dc * 8 + 2 * dc + q // 64] = Ws[32 + q % 64]
    c[:, 512:544] = w2e
    e8 = np.zeros((8, 512), dtype=np.float32)
    for dc in range(4):
        for q in range(128):
            e8[2 * dc + q // 64, dc * 128 + q] = 1.0
    c[0:8, 544:1056] = e8
    esel = np.zeros((16, 128), dtype=np.float32)
    for h in range(16):
        esel[h, h * 8:(h + 1) * 8] = 1.0
    c[0:16, 1056:1184] = esel
    ecol4 = np.zeros((4, 256), dtype=np.float32)
    for k in range(4):
        ecol4[k, k * 64:(k + 1) * 64] = 1.0
    c[0:4, 1184:1440] = ecol4
    e64 = np.zeros((4, 256), dtype=np.float32)
    for t in range(2):
        for q in range(128):
            e64[2 * t + q // 64, t * 128 + q] = 1.0
    c[0:4, 1440:1696] = e64
    rowsel = np.zeros((16, 1024), dtype=np.float32)
    for col in range(1024):
        cc, q = divmod(col, 128)
        rowsel[2 * cc + q // 64, col] = 1.0
    c[32:48, 544:1568] = rowsel
    return c


def _pack_vecs(ins):
    cols = np.zeros((128, 128), dtype=np.float32)
    off = 0

    def add(arr1d, n):
        nonlocal off
        a = np.zeros((n * 128,), dtype=np.float32)
        v = np.asarray(arr1d, dtype=np.float32).reshape(-1)
        a[:v.size] = v
        cols[:, off:off + n] = a.reshape(n, 128).T
        off += n

    add(ins["L0_attn_bq"], 8)
    add(ins["L0_attn_bk"], 8)
    add(ins["L0_attn_bv"], 4)
    add(ins["L0_attn_bqv"], 4)
    add(ins["L0_attn_bf"], 8)
    add(ins["L0_attn_br1"], 16)
    add(ins["L0_attn_br2"], 8)
    add(ins["L0_attn_g1"], 8)
    add(ins["L0_attn_b1"], 8)
    add(ins["L0_attn_g2"], 8)
    add(ins["L0_attn_b2"], 8)
    add(np.full(16, float(np.asarray(ins["L0_attn_bs"]).reshape(-1)[0])), 1)
    add(ins["L0_ffn_br"], 1)
    add(ins["L0_ffn_bo"], 8)
    return cols


def kernel(**ins):
    x = np.asarray(ins["L0_attn_x"], dtype=np.float32)
    B = x.shape[0]
    consts = _consts_blob(ins["L0_attn_Ws"])
    vecs = _pack_vecs(ins)

    def T(a):
        return np.ascontiguousarray(np.asarray(a, dtype=np.float32).T)

    shared = {
        "wqT": T(ins["L0_attn_Wq"]), "wkT": T(ins["L0_attn_Wk"]),
        "wvT": T(ins["L0_attn_Wv"]), "wqvT": T(ins["L0_attn_Wqv"]),
        "wfT": T(ins["L0_attn_Wf"]),
        "wr1T": T(ins["L0_attn_Wr1"]), "wr2T": T(ins["L0_attn_Wr2"]),
        "mwqT": T(ins["L0_ffn_Wq"]), "mwkT": T(ins["L0_ffn_Wk"]), "mwvT": T(ins["L0_ffn_Wv"]),
        "mwo": np.ascontiguousarray(np.asarray(ins["L0_ffn_Wo"], dtype=np.float32)),
        "mwr": np.ascontiguousarray(np.asarray(ins["L0_ffn_Wr"], dtype=np.float32)),
        "vecs": vecs, "consts": consts,
        "bh3": np.ascontiguousarray(np.stack(
            [np.asarray(ins["L0_ffn_bq"], dtype=np.float32).reshape(16, 64),
             np.asarray(ins["L0_ffn_bk"], dtype=np.float32).reshape(16, 64),
             np.asarray(ins["L0_ffn_bv"], dtype=np.float32).reshape(16, 64)], axis=0
        ).transpose(1, 0, 2).reshape(16, 192)),
    }
    in_maps = []
    for b in range(B):
        m = dict(shared)
        m["xT"] = np.ascontiguousarray(x[b].T)
        in_maps.append(m)

    nc = build_program(taps=tuple(_TAPS), maxphase=getattr(kernel, "_MAXPHASE", 5), hpn=getattr(kernel, "_HPN", 8))
    res = run_bass_kernel_spmd(nc, in_maps, core_ids=list(range(B)))
    outs = np.stack([np.asarray(r["out"]) for r in res.results], axis=0)
    if _TAPS:
        kernel.tap_results = [{k: np.asarray(v) for k, v in r.items() if k.startswith("tap_")}
                              for r in res.results]
    return np.ascontiguousarray(outs.transpose(0, 2, 1))
